# version 16
# baseline (speedup 1.0000x reference)
import math
from contextlib import ExitStack
from dataclasses import dataclass

import numpy as np

import concourse.bass as bass
import concourse.mybir as mybir
from concourse.alu_op_type import AluOpType as ALU
from bass_rust import ReduceOp
from concourse import library_config

F32 = mybir.dt.float32
BF16 = mybir.dt.bfloat16
FP8 = mybir.dt.float8e4
AF = mybir.ActivationFunctionType
AX = mybir.AxisListType

ENGS = ("tensor", "vector", "scalar", "sync", "gpsimd")


@dataclass
class Geom:
    S: int = 8192
    D: int = 2048
    NH: int = 16
    NKV: int = 4
    HD: int = 128
    HF: int = 5632
    NC: int = 8
    TCH: int = 512
    QT: int = 512
    FSUB: int = 256
    dbg: bool = False

    @property
    def T(self): return self.S // self.NC
    @property
    def DC(self): return self.D // 128
    @property
    def NKD(self): return self.NKV * self.HD
    @property
    def HFC(self): return self.HF // 128
    @property
    def REP(self): return self.NH // self.NKV
    @property
    def NKC(self): return self.S // 128
    @property
    def NWELEM(self): return self.D * self.HF


class SemC:
    def __init__(self, nc, name, step):
        self.h = nc.alloc_semaphore(name=name)
        self.step = step
        self.v = 0

    def inc(self):
        self.v += self.step
        return self.v


class Builder:
    def __init__(self, g: Geom):
        self.g = g
        self.nc = bass.Bass("TRN2", target_bir_lowering=False)
        self.plan = None

    def L(self, eng, f):
        self.plan[eng].append(f)

    def wait(self, eng, sem: SemC, val=None):
        v = sem.v if val is None else val
        if v is not None and v > 0:
            self.L(eng, lambda e, _s=sem.h, _v=v: e.wait_ge(_s, _v))

    def drain(self, eng):
        self.L(eng, lambda e: e.drain())

    def _inc(self, instr, sem):
        if sem is not None:
            instr.then_inc(sem.h, sem.step)

    def mm(self, out, lhsT, rhs, start, stop, inc=None):
        if inc is not None: inc.inc()
        self.L("tensor", lambda e, _o=out, _l=lhsT, _r=rhs, _a=start, _b=stop, _s=inc:
               self._inc(e.matmul(_o, _l, _r, start=_a, stop=_b), _s))

    def mm3p(self, out, lhs_hl, rhs_hl, inc=None, first=True, last=True):
        combos = ((0, 0), (0, 1), (1, 0))
        for i, (a, c) in enumerate(combos):
            self.mm(out, lhs_hl[a], rhs_hl[c],
                    start=(first and i == 0), stop=(last and i == 2),
                    inc=(inc if i == 2 else None))

    def mm2p(self, out, lhs_exact, rhs_hl, inc=None, first=True, last=True):
        for i in range(2):
            self.mm(out, lhs_exact, rhs_hl[i],
                    start=(first and i == 0), stop=(last and i == 1),
                    inc=(inc if i == 1 else None))

    def tr(self, out, in_, ident, inc=None):
        if inc is not None: inc.inc()
        self.L("tensor", lambda e, _o=out, _i=in_, _d=ident, _s=inc:
               self._inc(e.transpose(_o, _i, _d), _s))

    def act(self, out, in_, func, bias=None, scale=None, accum=None, inc=None):
        if inc is not None: inc.inc()
        kw = {}
        if bias is not None: kw["bias"] = bias
        if scale is not None: kw["scale"] = scale
        if accum is not None: kw["accum_out"] = accum
        self.L("scalar", lambda e, _o=out, _i=in_, _f=func, _k=kw, _s=inc:
               self._inc(e.activation(out=_o, in_=_i, func=_f, **_k), _s))

    def acopy(self, out, in_, inc=None):
        if inc is not None: inc.inc()
        self.L("scalar", lambda e, _o=out, _i=in_, _s=inc:
               self._inc(e.copy(_o, _i), _s))

    def tt(self, out, in0, in1, op, inc=None):
        if inc is not None: inc.inc()
        self.L("vector", lambda e, _o=out, _a=in0, _b=in1, _p=op, _s=inc:
               self._inc(e.tensor_tensor(_o, _a, _b, op=_p), _s))

    def ts(self, out, in0, s1, op0, s2=None, op1=None, inc=None):
        if inc is not None: inc.inc()
        kw = dict(scalar2=s2)
        if op1 is not None: kw["op1"] = op1
        self.L("vector", lambda e, _o=out, _a=in0, _1=s1, _p=op0, _k=kw, _s=inc:
               self._inc(e.tensor_scalar(out=_o, in0=_a, scalar1=_1, op0=_p, **_k), _s))

    def stt(self, out, in0, scalar, in1, op0, op1, inc=None):
        if inc is not None: inc.inc()
        self.L("vector", lambda e, _o=out, _a=in0, _c=scalar, _b=in1, _p=op0, _q=op1, _s=inc:
               self._inc(e.scalar_tensor_tensor(out=_o, in0=_a, scalar=_c, in1=_b, op0=_p, op1=_q), _s))

    def vcopy(self, out, in_, inc=None):
        if inc is not None: inc.inc()
        self.L("vector", lambda e, _o=out, _i=in_, _s=inc:
               self._inc(e.tensor_copy(_o, _i), _s))

    def vred(self, out, in_, op, absval=False, inc=None):
        if inc is not None: inc.inc()
        self.L("vector", lambda e, _o=out, _i=in_, _p=op, _a=absval, _s=inc:
               self._inc(e.tensor_reduce(_o, _i, axis=AX.X, op=_p, apply_absolute_value=_a), _s))

    def recip(self, out, in_, inc=None):
        if inc is not None: inc.inc()
        self.L("vector", lambda e, _o=out, _i=in_, _s=inc:
               self._inc(e.reciprocal(_o, _i), _s))

    def dma(self, queue, out, in_, inc=None):
        if inc is not None: inc.inc()
        self.L(queue, lambda e, _o=out, _i=in_, _s=inc:
               self._inc(e.dma_start(_o, _i), _s))

    def pbcast(self, out, in_, inc=None):
        if inc is not None: inc.inc()
        self.L("gpsimd", lambda e, _o=out, _i=in_, _s=inc:
               self._inc(e.partition_broadcast(_o, _i), _s))

    def allred(self, out, in_, op, inc=None):
        if inc is not None: inc.inc()
        self.L("gpsimd", lambda e, _o=out, _i=in_, _p=op, _s=inc:
               self._inc(e.partition_all_reduce(_o, _i, channels=128, reduce_op=_p), _s))

    def run_stage(self, name):
        plan = self.plan
        with self.nc.named_scope(name):
            with self.nc.Block() as block:
                for eng in ENGS:
                    if plan[eng]:
                        def body(e, _th=tuple(plan[eng])):
                            for f in _th:
                                f(e)
                        getattr(block, eng)(body)
        self.plan = None


class StoreQ:
    def __init__(self, b, sems, prewait=False):
        self.b = b
        self.sems = sems
        self.i = 0
        self.prewait = prewait

    def dma(self, out, in_):
        b, sems = self.b, self.sems
        s = sems[self.i % len(sems)]
        if self.i >= len(sems) or self.prewait:
            b.wait("gpsimd", s, s.v)
        b.dma("gpsimd", out, in_, inc=s)
        self.i += 1

    def done(self):
        return [(s, s.v) for s in self.sems]

    def wait_all(self, eng):
        for s, v in self.done():
            self.b.wait(eng, s, v)


def ap2(t, p0, np_, c0, nc_):
    return t[p0:p0 + np_, c0:c0 + nc_]


def build(g: Geom):
    b = Builder(g)
    nc = b.nc
    S, D, HD, HF, T = g.S, g.D, g.HD, g.HF, g.T
    DC, NKD, NKV, HFC, REP, NKC = g.DC, g.NKD, g.NKV, g.HFC, g.REP, g.NKC
    TCH, QT, FSUB = g.TCH, g.QT, g.FSUB
    NCH = S // TCH
    OCH = T // TCH
    A0_UNITS_PER_CHUNK = -(-((D // 128) * (HF // 512) * 4 + (HF // 128) * (D // 512) * 2 + (D // 128) * (D // 512) * 2 + 8) // (S // TCH))
    NQT = T // QT
    NSUB = TCH // 128
    SCALE = 1.0 / math.sqrt(HD)

    x_t = nc.dram_tensor("x_t", [D, S], F32, kind="ExternalInput")
    x_to = nc.dram_tensor("x_to", [D, T], F32, kind="ExternalInput")
    wq_t = nc.dram_tensor("wq_t", [D, D], F32, kind="ExternalInput")
    wk_t = nc.dram_tensor("wk_t", [D, NKD], F32, kind="ExternalInput")
    wv_t = nc.dram_tensor("wv_t", [D, NKD], F32, kind="ExternalInput")
    wo_t = nc.dram_tensor("wo_t", [D, D], F32, kind="ExternalInput")
    w1_t = nc.dram_tensor("w1_t", [D, HF], F32, kind="ExternalInput")
    w2_t = nc.dram_tensor("w2_t", [D, HF], F32, kind="ExternalInput")
    w3_t = nc.dram_tensor("w3_t", [HF, D], F32, kind="ExternalInput")
    cosf = nc.dram_tensor("cosf", [S, HD], F32, kind="ExternalInput")
    sinf = nc.dram_tensor("sinf", [S, HD], F32, kind="ExternalInput")
    coso = nc.dram_tensor("coso", [T, HD], F32, kind="ExternalInput")
    sino = nc.dram_tensor("sino", [T, HD], F32, kind="ExternalInput")
    ident_in = nc.dram_tensor("ident", [128, 128], F32, kind="ExternalInput")
    ones_in = nc.dram_tensor("ones", [128, 128], F32, kind="ExternalInput")
    y_out = nc.dram_tensor("y", [T, D], F32, kind="ExternalOutput")

    scr_kh = nc.dram_tensor("scr_kh", [NKD, S], BF16)
    scr_kl = nc.dram_tensor("scr_kl", [NKD, S], BF16)
    scr_vh = nc.dram_tensor("scr_vh", [S, NKD], BF16)
    scr_vl = nc.dram_tensor("scr_vl", [S, NKD], BF16)
    scr_qh = nc.dram_tensor("scr_qh", [D, T], BF16)
    scr_ql = nc.dram_tensor("scr_ql", [D, T], BF16)
    scr_ws = [nc.dram_tensor(f"scr_w{i}", [D, HF] if i < 2 else [HF, D], FP8) for i in range(3)]
    scr_wqh = nc.dram_tensor("scr_wqh", [D, D], BF16)
    scr_wql = nc.dram_tensor("scr_wql", [D, D], BF16)
    scr_woh = nc.dram_tensor("scr_woh", [D, D], BF16)
    scr_wol = nc.dram_tensor("scr_wol", [D, D], BF16)

    if g.dbg:
        dbg_attn = nc.dram_tensor("dbg_attn", [128, DC * QT], BF16, kind="ExternalOutput")
        dbg_strm = nc.dram_tensor("dbg_strm", [128, DC * QT], F32, kind="ExternalOutput")
        dbg_xn = nc.dram_tensor("dbg_xn", [128, DC * QT], F32, kind="ExternalOutput")
        dbg_ffn = nc.dram_tensor("dbg_ffn", [128, DC * QT], F32, kind="ExternalOutput")

    sem = lambda n, s: SemC(nc, n, s)
    s_pe = sem("pe", 1)
    s_act = sem("act", 1)
    s_dve = sem("dve", 1)
    s_gp = sem("gp", 1)
    s_ld = [sem(f"ld{i}", 16) for i in range(12)]
    s_st = [sem(f"st{i}", 16) for i in range(16)]

    es_outer = ExitStack()
    sb = lambda n, sh, dt=F32: es_outer.enter_context(nc.sbuf_tensor(n, sh, dt))

    ident_sb = sb("identsb", [128, 128])
    ones_sb = sb("onessb", [128, 128])
    ones_bf = sb("onesbf", [128, 2], BF16)
    ident_bf = sb("identbf", [128, 128], BF16)
    bias_m0 = sb("bias_m0", [128, 1])
    hthr = sb("hthr", [128, 6])

    b.plan = {e: [] for e in ENGS}
    b.dma("sync", ident_sb[:, :], ident_in[:, :], inc=s_ld[3])
    b.dma("sync", ones_sb[:, :], ones_in[:, :], inc=s_ld[3])
    consts_ready = s_ld[3].v
    b.wait("vector", s_ld[3], consts_ready)
    b.vcopy(ones_bf[:, 0:1], ones_sb[:, 0:1], inc=s_dve)
    b.drain("vector")
    b.vcopy(ident_bf[:, :], ident_sb[:, :], inc=s_dve)
    b.drain("vector")
    b.L("gpsimd", lambda e: e.load_library(library_config.attnmlp))

    es_a0 = ExitStack()
    if True:
        sbA = lambda n, sh, dt=F32: es_a0.enter_context(nc.sbuf_tensor(n, sh, dt))

        def colchunks(width):
            cw = width
            while cw > 2048:
                assert cw % 2 == 0
                cw //= 2
            return cw, width // cw
        cw1, nc1 = colchunks(HF)
        cw3, nc3 = colchunks(D)
        WMAX = max(cw1, cw3, D)
        w_in = [sbA("w_in0", [128, WMAX]), sbA("w_in1", [128, WMAX])]
        s1b = [sbA("s1b0", [128, WMAX]), sbA("s1b1", [128, WMAX])]
        s2b = [sbA("s2b0", [128, WMAX]), sbA("s2b1", [128, WMAX])]
        wqb = [sbA("wqb0", [128, WMAX], FP8), sbA("wqb1", [128, WMAX], FP8)]
        wsph = [sbA("wsph0", [128, D], BF16), sbA("wsph1", [128, D], BF16)]
        wspl = [sbA("wspl0", [128, D], BF16), sbA("wspl1", [128, D], BF16)]
        n_t1 = (D // 128) * nc1 * 2 + (HF // 128) * nc3
        accb = sbA("accb", [128, n_t1 + 8])
        gtot = sbA("gtot", [128, 3])

        wsrc = [w1_t, w2_t, w3_t]
        wtiles = [[(r, c, cw1) for r in range(D // 128) for c in range(nc1)],
                  [(r, c, cw1) for r in range(D // 128) for c in range(nc1)],
                  [(r, c, cw3) for r in range(HF // 128) for c in range(nc3)]]

    def a0_work():
            col = 0
            wcol = []
            dve_done = {}
            ti = 0
            for wi, tiles in enumerate(wtiles):
                wcol.append(col)
                for (r, cc, width) in tiles:
                    sl = ti % 2
                    if ti >= 2:
                        b.wait("sync", s_act, dve_done[ti - 2])
                    b.dma("sync", ap2(w_in[sl], 0, 128, 0, width),
                          ap2(wsrc[wi], r * 128, 128, cc * width, width), inc=s_ld[6 + sl])
                    b.wait("scalar", s_ld[6 + sl], s_ld[6 + sl].v)
                    b.act(ap2(s1b[sl], 0, 128, 0, width), ap2(w_in[sl], 0, 128, 0, width),
                          AF.Abs, accum=accb[:, col:col + 1], inc=s_act)
                    dve_done[ti] = s_act.v
                    col += 1
                    ti += 1
                    yield
            b.wait("vector", s_act, s_act.v)
            for wi in range(3):
                c0, c1 = wcol[wi], wcol[wi] + len(wtiles[wi])
                b.vred(gtot[:, wi:wi + 1], accb[:, c0:c1], ALU.add, inc=s_dve)
                b.drain("vector")
            b.wait("gpsimd", s_dve)
            for wi in range(3):
                b.allred(gtot[:, wi:wi + 1], gtot[:, wi:wi + 1], ReduceOp.add, inc=s_gp)
            b.wait("vector", s_gp)
            for wi in range(3):
                nw = float(g.NWELEM)
                b.ts(hthr[:, 2 * wi:2 * wi + 1], gtot[:, wi:wi + 1], -0.5 / nw, ALU.mult,
                     s2=-0.5e-8, op1=ALU.add, inc=s_dve)
                b.ts(hthr[:, 2 * wi + 1:2 * wi + 2], gtot[:, wi:wi + 1], 0.5 / nw, ALU.mult,
                     s2=0.5e-8, op1=ALU.add, inc=s_dve)
                b.drain("vector")
            hready = s_dve.v
            b.wait("scalar", s_dve, hready)

            b.wait("sync", s_dve, hready)
            stq = StoreQ(b, [s_st[8], s_st[9]])
            ti = 0
            act_done = {}
            dve_done = {}
            st_done = {}
            for wi, tiles in enumerate(wtiles):
                for (r, cc, width) in tiles:
                    sl = ti % 2
                    if ti >= 2:
                        b.wait("sync", s_act, act_done[ti - 2])
                    b.dma("sync", ap2(w_in[sl], 0, 128, 0, width),
                          ap2(wsrc[wi], r * 128, 128, cc * width, width), inc=s_ld[6 + sl])
                    b.wait("scalar", s_ld[6 + sl], s_ld[6 + sl].v)
                    if ti >= 2:
                        b.wait("scalar", s_dve, dve_done[ti - 2])
                    b.act(ap2(s1b[sl], 0, 128, 0, width), ap2(w_in[sl], 0, 128, 0, width),
                          AF.Sign, bias=hthr[:, 2 * wi:2 * wi + 1], inc=s_act)
                    b.act(ap2(s2b[sl], 0, 128, 0, width), ap2(w_in[sl], 0, 128, 0, width),
                          AF.Sign, bias=hthr[:, 2 * wi + 1:2 * wi + 2], inc=s_act)
                    act_done[ti] = s_act.v
                    b.wait("vector", s_act, act_done[ti])
                    if ti >= 2:
                        for _s, _v in st_done[ti - 2]:
                            b.wait("vector", _s, _v)
                    b.drain("vector")
                    b.tt(ap2(wqb[sl], 0, 128, 0, width), ap2(s1b[sl], 0, 128, 0, width),
                         ap2(s2b[sl], 0, 128, 0, width), ALU.add, inc=s_dve)
                    dve_done[ti] = s_dve.v
                    b.wait("gpsimd", s_dve, s_dve.v)
                    stq.dma(ap2(scr_ws[wi], r * 128, 128, cc * width, width),
                            ap2(wqb[sl], 0, 128, 0, width))
                    st_done[ti] = stq.done()
                    ti += 1
                    yield
            stq.wait_all("gpsimd")

            b.wait("sync", s_dve, s_dve.v)
            stq_h = StoreQ(b, [s_st[10], s_st[11]])
            stq_l = StoreQ(b, [s_st[12], s_st[13]])
            sp_src = [(wq_t, scr_wqh, scr_wql), (wo_t, scr_woh, scr_wol)]
            ti = 0
            dve_done = {}
            st_done = {}
            for (srcw, dsth, dstl) in sp_src:
                for r in range(D // 128):
                    sl = ti % 2
                    if ti >= 2:
                        b.wait("sync", s_dve, dve_done[ti - 2])
                    b.dma("sync", ap2(w_in[sl], 0, 128, 0, D),
                          ap2(srcw, r * 128, 128, 0, D), inc=s_ld[6 + sl])
                    b.wait("vector", s_ld[6 + sl], s_ld[6 + sl].v)
                    if ti >= 2:
                        for _s, _v in st_done[ti - 2]:
                            b.wait("vector", _s, _v)
                    b.drain("vector")
                    b.vcopy(ap2(wsph[sl], 0, 128, 0, D), ap2(w_in[sl], 0, 128, 0, D))
                    b.drain("vector")
                    b.tt(ap2(wspl[sl], 0, 128, 0, D), ap2(w_in[sl], 0, 128, 0, D),
                         ap2(wsph[sl], 0, 128, 0, D), ALU.subtract, inc=s_dve)
                    dve_done[ti] = s_dve.v
                    b.drain("vector")
                    b.wait("gpsimd", s_dve, s_dve.v)
                    stq_h.dma(ap2(dsth, r * 128, 128, 0, D), ap2(wsph[sl], 0, 128, 0, D))
                    stq_l.dma(ap2(dstl, r * 128, 128, 0, D), ap2(wspl[sl], 0, 128, 0, D))
                    st_done[ti] = stq_h.done() + stq_l.done()
                    ti += 1
                    yield
            stq_h.wait_all("gpsimd")
            stq_l.wait_all("gpsimd")
            b.scr_wq_ready = stq_h.done() + stq_l.done()
            yield


    a0_gen = a0_work()
    for _ in a0_gen:
        pass
    b.run_stage("A0")
    es_a0.close()

    with ExitStack() as es:
        sbA = lambda n, sh, dt=F32: es.enter_context(nc.sbuf_tensor(n, sh, dt))
        xt32 = sbA("xt32", [128, DC * TCH])
        xth = [sbA("xth0", [128, DC * TCH], BF16), sbA("xth1", [128, DC * TCH], BF16)]
        xtl = [sbA("xtl0", [128, DC * TCH], BF16), sbA("xtl1", [128, DC * TCH], BF16)]
        wkv32 = sbA("wkv32", [128, NKD])
        wkvh = sbA("wkvh", [128, DC * 2 * NKD], BF16)
        wkvl = sbA("wkvl", [128, DC * 2 * NKD], BF16)
        cs = [sbA("cs0", [128, NSUB * 2 * HD])]
        t1b = sbA("t1b", [128, max(NKD, 512)])
        t2b = sbA("t2b", [128, max(NKD, 512)])
        kpb = sbA("kpb", [128, max(NKD, 512)])
        kpbh = sbA("kpbh", [128, max(NKD, 512)], BF16)
        kpbl = sbA("kpbl", [128, max(NKD, 512)], BF16)
        ksth = sbA("ksth", [128, max(NKV, 4) * TCH], BF16)
        kstl = sbA("kstl", [128, max(NKV, 4) * TCH], BF16)
        vsth = sbA("vsth", [128, NSUB * NKD], BF16)
        vstl = sbA("vstl", [128, NSUB * NKD], BF16)
        wqtlh = [sbA("wqtlh0", [128, 512], BF16), sbA("wqtlh1", [128, 512], BF16)]
        wqtll = [sbA("wqtll0", [128, 512], BF16), sbA("wqtll1", [128, 512], BF16)]
        qsth, qstl = ksth, kstl

        ps = [es.enter_context(nc.psum_tensor(f"psA{i}", [128, 512], F32)) for i in range(4)]
        psKb, psVb = ps[0:2], ps[2:4]
        psQ = ps[0:4]
        psT = [es.enter_context(nc.psum_tensor(f"psAT{i}", [128, 512], BF16)) for i in range(2)]

        b.plan = {e: [] for e in ENGS}
        for dc in range(DC):
            for which, srcw in ((0, wk_t), (1, wv_t)):
                b.dma("sync", wkv32[:, :], ap2(srcw, dc * 128, 128, 0, NKD), inc=s_ld[2])
                b.wait("vector", s_ld[2], s_ld[2].v)
                if dc > 0 or which > 0:
                    b.drain("vector")
                co0 = dc * 2 * NKD + which * NKD
                b.vcopy(ap2(wkvh, 0, 128, co0, NKD), wkv32[:, :])
                b.drain("vector")
                b.tt(ap2(wkvl, 0, 128, co0, NKD), wkv32[:, :],
                     ap2(wkvh, 0, 128, co0, NKD), ALU.subtract, inc=s_dve)
                b.wait("sync", s_dve, s_dve.v)
        wkv_ready = s_dve.v

        tr_bank_free = [None, None]
        kpb_tr_pe = None
        xt32_free_dve = 0
        cs_free_dve = 0
        rope2_done = {}
        vevac2_done = {}
        pe_after_chunk = {}
        stq_kh = StoreQ(b, [s_st[0], s_st[1]])
        stq_kl = StoreQ(b, [s_st[2], s_st[3]])
        stq_vh = StoreQ(b, [s_st[4], s_st[5]])
        stq_vl = StoreQ(b, [s_st[6], s_st[7]])

        def load_xt_cs(sl, src_x, src_cos, src_sin, col0, slot_free_pe):
            nonlocal xt32_free_dve, cs_free_dve
            if cs_free_dve:
                b.wait("sync", s_dve, cs_free_dve)
            b.dma("sync", bass.AP(cs[0], 0, [[NSUB * 2 * HD, 128], [2 * HD, NSUB], [1, HD]]),
                  bass.AP(src_cos, col0 * HD, [[HD, 128], [128 * HD, NSUB], [1, HD]]), inc=s_ld[sl])
            b.dma("sync", bass.AP(cs[0], HD, [[NSUB * 2 * HD, 128], [2 * HD, NSUB], [1, HD]]),
                  bass.AP(src_sin, col0 * HD, [[HD, 128], [128 * HD, NSUB], [1, HD]]), inc=s_ld[sl])
            if xt32_free_dve:
                b.wait("sync", s_dve, xt32_free_dve)
            b.dma("sync", xt32[:, :],
                  bass.AP(src_x, col0, [[src_x.shape[1], 128], [128 * src_x.shape[1], DC], [1, TCH]]),
                  inc=s_ld[sl])
            b.wait("vector", s_ld[sl], s_ld[sl].v)
            if slot_free_pe:
                b.wait("vector", s_pe, slot_free_pe)
            b.drain("vector")
            b.vcopy(xth[sl][:, :], xt32[:, :])
            b.drain("vector")
            b.tt(xtl[sl][:, :], xt32[:, :], xth[sl][:, :], ALU.subtract, inc=s_dve)
            xt32_free_dve = s_dve.v
            b.drain("vector")
            return s_ld[sl].v, s_dve.v

        def rope(psrc, nheads, width, sl, sub, dst):
            base = sub * 2 * HD
            cosap = bass.AP(cs[0], base, [[NSUB * 2 * HD, 128], [0, nheads], [1, HD]])
            sin4 = bass.AP(cs[0], base + HD, [[NSUB * 2 * HD, 128], [0, nheads], [HD // 2, 2], [1, HD // 2]])
            kap = bass.AP(psrc, 0, [[512, 128], [HD, nheads], [1, HD]])
            kshift = bass.AP(psrc, HD // 2, [[512, 128], [HD, nheads], [-(HD // 2), 2], [1, HD // 2]])
            t1ap = bass.AP(t1b, 0, [[t1b.shape[1], 128], [HD, nheads], [1, HD]])
            t2ap = bass.AP(t2b, 0, [[t2b.shape[1], 128], [HD, nheads], [HD // 2, 2], [1, HD // 2]])
            b.tt(t1ap, kap, cosap, ALU.mult)
            b.tt(t2ap, kshift, sin4, ALU.mult, inc=s_dve)
            b.drain("vector")
            b.tt(ap2(dst, 0, 128, 0, width), ap2(t1b, 0, 128, 0, width),
                 ap2(t2b, 0, 128, 0, width), ALU.add, inc=s_dve)
            b.drain("vector")
            b.vcopy(ap2(kpbh, 0, 128, 0, width), ap2(dst, 0, 128, 0, width))
            b.drain("vector")
            b.tt(ap2(kpbl, 0, 128, 0, width), ap2(dst, 0, 128, 0, width),
                 ap2(kpbh, 0, 128, 0, width), ALU.subtract, inc=s_dve)
            out = s_dve.v
            b.drain("vector")
            return out

        def transpose_pair(nslices, dsth, dstl, dcol_fn, ready_dve):
            nonlocal kpb_tr_pe
            for i in range(nslices):
                for which, (srcb, dstb) in enumerate(((kpbh, dsth), (kpbl, dstl))):
                    bank = (2 * i + which) % 2
                    b.wait("tensor", s_act, tr_bank_free[bank])
                    b.wait("tensor", s_dve, ready_dve)
                    b.tr(ap2(psT[bank], 0, 128, 0, 128),
                         ap2(srcb, 0, 128, i * HD, HD), ident_bf[:, :], inc=s_pe)
                    b.wait("scalar", s_pe, s_pe.v)
                    b.acopy(ap2(dstb, 0, 128, dcol_fn(i), 128),
                            ap2(psT[bank], 0, 128, 0, 128), inc=s_act)
                    tr_bank_free[bank] = s_act.v
            kpb_tr_pe = s_pe.v

        chunk_loaded = {}
        chunk_loaded[0] = load_xt_cs(0, x_t, cosf, sinf, 0, 0)
        for c in range(NCH):
            sl = c % 2
            xt_ready, xs_done = chunk_loaded[c]
            b.wait("tensor", s_dve, xs_done)

            for sub in range(NSUB):
                si = (c * NSUB + sub) % 2
                psK, psV = psKb[si], psVb[si]
                for which, pdst in ((0, psK), (1, psV)):
                    if c * NSUB + sub >= 2:
                        if which == 0:
                            b.wait("tensor", s_dve, rope2_done.get((c * NSUB + sub - 2), 0))
                        else:
                            b.wait("tensor", s_dve, vevac2_done.get((c * NSUB + sub - 2), 0))
                    for dc in range(DC):
                        lhs_hl = (ap2(xth[sl], 0, 128, dc * TCH + sub * 128, 128),
                                  ap2(xtl[sl], 0, 128, dc * TCH + sub * 128, 128))
                        rhs_hl = (ap2(wkvh, 0, 128, dc * 2 * NKD + which * NKD, NKD),
                                  ap2(wkvl, 0, 128, dc * 2 * NKD + which * NKD, NKD))
                        b.mm3p(ap2(pdst, 0, 128, 0, NKD), lhs_hl, rhs_hl,
                               first=(dc == 0), last=(dc == DC - 1),
                               inc=(s_pe if dc == DC - 1 else None))
                pe_kv = s_pe.v
                b.wait("scalar", s_pe, pe_kv)
                if c > 0:
                    for _s, _v in vstage_stored:
                        b.wait("scalar", _s, _v)
                b.acopy(ap2(vsth, 0, 128, sub * NKD, NKD), ap2(psV, 0, 128, 0, NKD), inc=s_act)
                b.wait("vector", s_act, s_act.v)
                if c > 0:
                    for _s, _v in vstage_stored:
                        b.wait("vector", _s, _v)
                b.drain("vector")
                b.tt(ap2(vstl, 0, 128, sub * NKD, NKD), ap2(psV, 0, 128, 0, NKD),
                     ap2(vsth, 0, 128, sub * NKD, NKD), ALU.subtract, inc=s_dve)
                vevac2_done[c * NSUB + sub] = s_dve.v
                b.drain("vector")
                b.wait("vector", s_ld[sl], xt_ready)
                b.wait("vector", s_pe, pe_kv)
                if c > 0:
                    for _s, _v in kstage_stored:
                        b.wait("vector", _s, _v)
                if kpb_tr_pe is not None:
                    b.wait("vector", s_pe, kpb_tr_pe)
                prev_rope = rope(psK, NKV, NKD, sl, sub, kpb)
                rope2_done[c * NSUB + sub] = prev_rope
                cs_free_dve = prev_rope
                transpose_pair(NKV, ksth, kstl,
                               lambda i, _sub=sub: i * TCH + _sub * 128, prev_rope)
            pe_after_chunk[c] = s_pe.v
            b.wait("gpsimd", s_act, s_act.v)
            b.wait("gpsimd", s_dve, s_dve.v)
            for kv in range(NKV):
                stq_kh.dma(ap2(scr_kh, kv * 128, 128, c * TCH, TCH),
                           ap2(ksth, 0, 128, kv * TCH, TCH))
                stq_kl.dma(ap2(scr_kl, kv * 128, 128, c * TCH, TCH),
                           ap2(kstl, 0, 128, kv * TCH, TCH))
            kstage_stored = stq_kh.done() + stq_kl.done()
            for sub in range(NSUB):
                stq_vh.dma(ap2(scr_vh, c * TCH + sub * 128, 128, 0, NKD),
                           ap2(vsth, 0, 128, sub * NKD, NKD))
                stq_vl.dma(ap2(scr_vl, c * TCH + sub * 128, 128, 0, NKD),
                           ap2(vstl, 0, 128, sub * NKD, NKD))
            vstage_stored = stq_vh.done() + stq_vl.done()
            if c + 1 < NCH:
                chunk_loaded[c + 1] = load_xt_cs((c + 1) % 2, x_t, cosf, sinf, (c + 1) * TCH,
                                                 pe_after_chunk.get(c - 1, 0))

        wq_pe = {}
        wq_ld = 0
        stq_qh = StoreQ(b, [s_st[8], s_st[9]], prewait=True)
        stq_ql = StoreQ(b, [s_st[0], s_st[1]], prewait=True)
        for _s, _v in b.scr_wq_ready:
            b.wait("sync", _s, _v)
        for q_ in (stq_kh, stq_kl, stq_vh, stq_vl):
            q_.wait_all("scalar")
            q_.wait_all("vector")
        for co in range(OCH):
            sl = co % 2
            b.wait("sync", s_pe, s_pe.v)
            b.wait("sync", s_dve, s_dve.v)
            xt_ready, xs_done = load_xt_cs(sl, x_to, coso, sino, co * TCH, s_pe.v)
            b.wait("tensor", s_dve, xs_done)
            for ob in range(D // 512):
                for dc in range(DC):
                    wsl = wq_ld % 2
                    if wq_ld >= 2:
                        b.wait("sync", s_pe, wq_pe[wq_ld - 2])
                    b.dma("sync", ap2(wqtlh[wsl], 0, 128, 0, 512),
                          ap2(scr_wqh, dc * 128, 128, ob * 512, 512), inc=s_ld[4 + wsl])
                    b.dma("sync", ap2(wqtll[wsl], 0, 128, 0, 512),
                          ap2(scr_wql, dc * 128, 128, ob * 512, 512), inc=s_ld[4 + wsl])
                    b.wait("tensor", s_ld[4 + wsl], s_ld[4 + wsl].v)
                    for sub in range(NSUB):
                        if dc == 0 and (ob > 0 or co > 0):
                            b.wait("tensor", s_dve, q_rope_done.get(sub, 0))
                        lhs_hl = (ap2(xth[sl], 0, 128, dc * TCH + sub * 128, 128),
                                  ap2(xtl[sl], 0, 128, dc * TCH + sub * 128, 128))
                        rhs_hl = (ap2(wqtlh[wsl], 0, 128, 0, 512),
                                  ap2(wqtll[wsl], 0, 128, 0, 512))
                        b.mm3p(ap2(psQ[sub], 0, 128, 0, 512), lhs_hl, rhs_hl,
                               first=(dc == 0), last=(dc == DC - 1),
                               inc=(s_pe if sub == NSUB - 1 else None))
                    wq_pe[wq_ld] = s_pe.v
                    wq_ld += 1
                if co == 0 and ob == 0:
                    q_rope_done = {}
                pe_q = s_pe.v
                for sub in range(NSUB):
                    b.wait("vector", s_pe, pe_q)
                    b.wait("vector", s_ld[sl], xt_ready)
                    if not (co == 0 and ob == 0):
                        for _s, _v in qstage_stored:
                            b.wait("vector", _s, _v)
                    if kpb_tr_pe is not None:
                        b.wait("vector", s_pe, kpb_tr_pe)
                    qr = rope(psQ[sub], 4, 512, sl, sub, kpb)
                    cs_free_dve = qr
                    q_rope_done[sub] = qr
                    transpose_pair(4, qsth, qstl,
                                   lambda i, _sub=sub: i * TCH + _sub * 128, qr)
                b.wait("gpsimd", s_act, s_act.v)
                for hs in range(4):
                    stq_qh.dma(ap2(scr_qh, ob * 512 + hs * 128, 128, co * TCH, TCH),
                               ap2(qsth, 0, 128, hs * TCH, TCH))
                    stq_ql.dma(ap2(scr_ql, ob * 512 + hs * 128, 128, co * TCH, TCH),
                               ap2(qstl, 0, 128, hs * TCH, TCH))
                qstage_stored = stq_qh.done() + stq_ql.done()
        for q_ in (stq_kh, stq_kl, stq_vh, stq_vl, stq_qh, stq_ql):
            q_.wait_all("gpsimd")
        b.run_stage("A01")
    es_a0.close()

    first_m0 = True
    for qt in range(NQT):
        es_qt = ExitStack()
        attn_oh = es_qt.enter_context(nc.sbuf_tensor(f"attn_oh_{qt}", [128, DC * QT], BF16))
        attn_ol = es_qt.enter_context(nc.sbuf_tensor(f"attn_ol_{qt}", [128, DC * QT], BF16))

        with ExitStack() as es:
            sbB = lambda n, sh, dt=F32: es.enter_context(nc.sbuf_tensor(f"{n}_{qt}", sh, dt))
            qtbh = sbB("qtbh", [128, DC * QT], BF16)
            qtbl = sbB("qtbl", [128, DC * QT], BF16)
            kgh = sbB("kgh", [128, S], BF16)
            kgl = sbB("kgl", [128, S], BF16)
            vgh = sbB("vgh", [128, NKC * HD], BF16)
            vgl = sbB("vgl", [128, NKC * HD], BF16)
            p32 = [sbB(f"p32{i}", [128, QT]) for i in range(3)]
            phi = [sbB(f"phi{i}", [128, QT], BF16) for i in range(3)]
            plo = [sbB(f"plo{i}", [128, QT], BF16) for i in range(3)]
            ptmp = sbB("ptmp", [128, QT])
            rcp = sbB("rcp", [1, QT])
            bcst = sbB("bcst", [128, QT])
            mx0 = sbB("mx0", [128, 1])
            ps = [es.enter_context(nc.psum_tensor(f"psB{i}_{qt}", [128, 512], F32)) for i in range(7)]
            psS = ps[0:3]
            psPVb, psDb = ps[3:5], ps[5:7]

            b.plan = {e: [] for e in ENGS}
            prev_tail_pv = None
            evac_done_by_bank = {}
            b.dma("sync", qtbh[:, :],
                  bass.AP(scr_qh, qt * QT, [[T, 128], [128 * T, DC], [1, QT]]), inc=s_ld[0])
            b.dma("sync", qtbl[:, :],
                  bass.AP(scr_ql, qt * QT, [[T, 128], [128 * T, DC], [1, QT]]), inc=s_ld[0])
            q_ready = s_ld[0].v
            b.wait("tensor", s_ld[0], q_ready)
            for gidx in range(NKV):
                if gidx > 0:
                    b.wait("sync", s_pe, s_pe.v)
                b.dma("sync", kgh[:, :], ap2(scr_kh, gidx * 128, 128, 0, S), inc=s_ld[1])
                b.dma("sync", kgl[:, :], ap2(scr_kl, gidx * 128, 128, 0, S), inc=s_ld[1])
                vsplit = max(1, NKC // 8)
                for srcv, dstv in ((scr_vh, vgh), (scr_vl, vgl)):
                    for i in range(0, NKC, vsplit):
                        n = min(vsplit, NKC - i)
                        b.dma("sync", bass.AP(dstv, i * HD, [[NKC * HD, 128], [HD, n], [1, HD]]),
                              bass.AP(srcv, (i * 128) * NKD + gidx * HD, [[NKD, 128], [128 * NKD, n], [1, HD]]),
                              inc=s_ld[1])
                kv_ready = s_ld[1].v
                b.wait("tensor", s_ld[1], kv_ready)
                for h in range(REP):
                    hh = gidx * REP + h
                    psPV = psPVb[hh % 2]
                    psD = psDb[hh % 2]
                    act_exp = {}
                    dve_plo = {}
                    pe_pv = {}

                    def emit_pv(kp, psPV=psPV, psD=psD):
                        pbp = kp % 3
                        if kp == 0 and evac_done_by_bank.get(hh % 2):
                            b.wait("tensor", s_dve, evac_done_by_bank[hh % 2])
                        b.wait("tensor", s_act, act_exp[kp])
                        b.wait("tensor", s_dve, dve_plo[kp])
                        vh = bass.AP(vgh, kp * HD, [[NKC * HD, 128], [1, HD]])
                        vl = bass.AP(vgl, kp * HD, [[NKC * HD, 128], [1, HD]])
                        for i, (lv, rp) in enumerate(((vh, phi[pbp]), (vh, plo[pbp]), (vl, phi[pbp]))):
                            b.mm(ap2(psPV, 0, 128, 0, QT), lv, rp[:, :],
                                 start=(kp == 0 and i == 0), stop=(kp == NKC - 1 and i == 2),
                                 inc=(s_pe if i == 2 else None))
                        b.mm(ap2(psD, 0, 1, 0, QT), ones_bf[:, 0:1], phi[pbp][:, :],
                             start=(kp == 0), stop=(kp == NKC - 1), inc=s_pe)
                        pe_pv[kp] = s_pe.v

                    for kc in range(NKC):
                        pb = kc % 3
                        if kc >= 3:
                            b.wait("tensor", s_act, act_exp[kc - 3])
                        lhs_hl = (ap2(kgh, 0, 128, kc * 128, 128),
                                  ap2(kgl, 0, 128, kc * 128, 128))
                        rhs_hl = (ap2(qtbh, 0, 128, hh * QT, QT),
                                  ap2(qtbl, 0, 128, hh * QT, QT))
                        b.mm3p(ap2(psS[kc % 3], 0, 128, 0, QT), lhs_hl, rhs_hl, inc=s_pe)
                        pe_s = s_pe.v
                        if first_m0 and kc == 0:
                            b.wait("vector", s_pe, pe_s)
                            b.vred(mx0[:, :], ap2(psS[pb], 0, 128, 0, QT), ALU.max, inc=s_dve)
                            b.drain("vector")
                            b.wait("gpsimd", s_dve, s_dve.v)
                            b.allred(mx0[:, :], mx0[:, :], ReduceOp.max, inc=s_gp)
                            b.wait("vector", s_gp, s_gp.v)
                            b.ts(bias_m0[:, :], mx0[:, :], -SCALE, ALU.mult, inc=s_dve)
                            b.drain("vector")
                            b.wait("scalar", s_dve, s_dve.v)
                        if kc < 3 and prev_tail_pv is not None:
                            b.wait("scalar", s_pe, prev_tail_pv)
                        if kc >= 3:
                            b.wait("scalar", s_dve, dve_plo[kc - 3])
                            b.wait("scalar", s_pe, pe_pv[kc - 3])
                        b.wait("scalar", s_pe, pe_s)
                        b.act(p32[pb][:, :], ap2(psS[pb], 0, 128, 0, QT), AF.Exp,
                              bias=bias_m0[:, :], scale=SCALE)
                        b.act(phi[pb][:, :], ap2(psS[pb], 0, 128, 0, QT), AF.Exp,
                              bias=bias_m0[:, :], scale=SCALE, inc=s_act)
                        act_exp[kc] = s_act.v
                        b.wait("vector", s_act, act_exp[kc])
                        b.drain("vector")
                        b.tt(plo[pb][:, :], p32[pb][:, :], phi[pb][:, :], ALU.subtract, inc=s_dve)
                        dve_plo[kc] = s_dve.v
                        b.drain("vector")
                        if kc >= 2:
                            emit_pv(kc - 2)
                    emit_pv(NKC - 2)
                    emit_pv(NKC - 1)
                    pe_tail = s_pe.v
                    b.wait("vector", s_pe, pe_tail)
                    b.recip(rcp[:, :], ap2(psD, 0, 1, 0, QT), inc=s_dve)
                    b.drain("vector")
                    b.wait("gpsimd", s_dve, s_dve.v)
                    b.pbcast(bcst[:, :], rcp[:, :], inc=s_gp)
                    b.wait("vector", s_gp, s_gp.v)
                    b.tt(ptmp[:, :], ap2(psPV, 0, 128, 0, QT), bcst[:, :], ALU.mult, inc=s_dve)
                    b.drain("vector")
                    b.wait("scalar", s_dve, s_dve.v)
                    b.acopy(ap2(attn_oh, 0, 128, hh * QT, QT), ptmp[:, :], inc=s_act)
                    b.wait("vector", s_act, s_act.v)
                    b.tt(ap2(attn_ol, 0, 128, hh * QT, QT), ptmp[:, :],
                         ap2(attn_oh, 0, 128, hh * QT, QT), ALU.subtract, inc=s_dve)
                    b.drain("vector")
                    evac_done_by_bank[hh % 2] = s_dve.v
                    prev_tail_pv = pe_tail
                    first_m0 = False
            b.run_stage(f"B{qt}")

        with ExitStack() as es:
            sbC = lambda n, sh, dt=F32: es.enter_context(nc.sbuf_tensor(f"{n}_{qt}", sh, dt))
            strm = sbC("strm", [128, DC * QT])
            xn = sbC("xn", [128, DC * QT])
            xnh = sbC("xnh", [128, DC * QT], BF16)
            xnl = sbC("xnl", [128, DC * QT], BF16)
            hbh = sbC("hbh", [128, HFC * FSUB], BF16)
            hbl = sbC("hbl", [128, HFC * FSUB], BF16)
            htmp = sbC("htmp", [128, FSUB])
            wtlh = [sbC("wtlh0", [128, 512], BF16), sbC("wtlh1", [128, 512], BF16)]
            wtll = [sbC("wtll0", [128, 512], BF16), sbC("wtll1", [128, 512], BF16)]
            wf8 = [sbC(f"wf8{i}", [128, 512], FP8) for i in range(4)]
            w2f8 = [sbC(f"w2f8{i}", [128, 512], FP8) for i in range(4)]
            sgb = sbC("sgb", [128, FSUB])
            stat = sbC("stat", [1, QT])
            stat2 = sbC("stat2", [1, QT])
            statm = sbC("statm", [1, QT])
            sqb = [sbC("sqb0", [128, QT]), sbC("sqb1", [128, QT])]
            bcstC = sbC("bcstC", [128, QT])
            bcstC2 = sbC("bcstC2", [128, QT])
            ybuf = sbC("ybuf", [128, D])
            ps = [es.enter_context(nc.psum_tensor(f"psC{i}_{qt}", [128, 512], F32)) for i in range(8)]

            b.plan = {e: [] for e in ENGS}
            b.dma("sync", strm[:, :],
                  bass.AP(x_to, qt * QT, [[T, 128], [128 * T, DC], [1, QT]]), inc=s_ld[0])
            x_ready = s_ld[0].v
            wo_ld = 0
            pe_oc = {}
            dve_oc = {}
            for oc in range(DC):
                psO = ps[oc % 2]
                if oc >= 2:
                    b.wait("tensor", s_dve, dve_oc[oc - 2])
                for hc in range(DC):
                    wsl = wo_ld % 2
                    if wo_ld >= 2:
                        b.wait("sync", s_pe, pe_oc[wo_ld - 2])
                    b.dma("sync", ap2(wtlh[wsl], 0, 128, 0, 128),
                          ap2(scr_woh, hc * 128, 128, oc * 128, 128), inc=s_ld[2 + wsl])
                    b.dma("sync", ap2(wtll[wsl], 0, 128, 0, 128),
                          ap2(scr_wol, hc * 128, 128, oc * 128, 128), inc=s_ld[2 + wsl])
                    b.wait("tensor", s_ld[2 + wsl], s_ld[2 + wsl].v)
                    lhs_hl = (ap2(wtlh[wsl], 0, 128, 0, 128), ap2(wtll[wsl], 0, 128, 0, 128))
                    rhs_hl = (ap2(attn_oh, 0, 128, hc * QT, QT), ap2(attn_ol, 0, 128, hc * QT, QT))
                    b.mm3p(ap2(psO, 0, 128, 0, QT), lhs_hl, rhs_hl,
                           first=(hc == 0), last=(hc == DC - 1), inc=s_pe)
                    pe_oc[wo_ld] = s_pe.v
                    wo_ld += 1
                b.wait("vector", s_pe, s_pe.v)
                if oc == 0:
                    b.wait("vector", s_ld[0], x_ready)
                b.drain("vector")
                b.tt(ap2(strm, 0, 128, oc * QT, QT), ap2(psO, 0, 128, 0, QT),
                     ap2(strm, 0, 128, oc * QT, QT), ALU.add, inc=s_dve)
                dve_oc[oc] = s_dve.v
                b.drain("vector")

            def layer_norm(src, dst, src_ready_dve):
                b.wait("tensor", s_dve, src_ready_dve)
                for dc in range(DC):
                    b.mm(ap2(ps[2], 0, 1, 0, QT), ones_sb[:, 0:1], ap2(src, 0, 128, dc * QT, QT),
                         start=(dc == 0), stop=(dc == DC - 1), inc=(s_pe if dc == DC - 1 else None))
                b.wait("vector", s_pe, s_pe.v)
                b.ts(stat[:, :], ap2(ps[2], 0, 1, 0, QT), 1.0 / D, ALU.mult, inc=s_dve)
                b.drain("vector")
                b.wait("scalar", s_dve, src_ready_dve)
                pe_sq = {}
                for dc in range(DC):
                    if dc >= 2:
                        b.wait("scalar", s_pe, pe_sq[dc - 2])
                    b.act(sqb[dc % 2][:, :], ap2(src, 0, 128, dc * QT, QT), AF.Square, inc=s_act)
                    b.wait("tensor", s_act, s_act.v)
                    b.mm(ap2(ps[3], 0, 1, 0, QT), ones_sb[:, 0:1], sqb[dc % 2][:, :],
                         start=(dc == 0), stop=(dc == DC - 1), inc=s_pe)
                    pe_sq[dc] = s_pe.v
                b.wait("vector", s_pe, s_pe.v)
                b.ts(stat2[:, :], ap2(ps[3], 0, 1, 0, QT), 1.0 / D, ALU.mult, inc=s_dve)
                b.drain("vector")
                b.tt(statm[:, :], stat[:, :], stat[:, :], ALU.mult, inc=s_dve)
                b.drain("vector")
                b.stt(stat2[:, :], statm[:, :], -1.0, stat2[:, :], ALU.mult, ALU.add, inc=s_dve)
                b.drain("vector")
                b.ts(stat2[:, :], stat2[:, :], 1.0, ALU.mult, s2=1e-5, op1=ALU.add, inc=s_dve)
                b.drain("vector")
                b.wait("scalar", s_dve, s_dve.v)
                b.act(statm[:, :], stat2[:, :], AF.Sqrt, inc=s_act)
                b.wait("vector", s_act, s_act.v)
                b.recip(stat2[:, :], statm[:, :], inc=s_dve)
                b.drain("vector")
                b.tt(statm[:, :], stat[:, :], stat2[:, :], ALU.mult, inc=s_dve)
                b.drain("vector")
                b.wait("gpsimd", s_dve, s_dve.v)
                b.pbcast(bcstC2[:, :], stat2[:, :], inc=s_gp)
                b.pbcast(bcstC[:, :], statm[:, :], inc=s_gp)
                b.wait("vector", s_gp, s_gp.v)
                for dc in range(DC):
                    b.tt(ap2(dst, 0, 128, dc * QT, QT), ap2(src, 0, 128, dc * QT, QT),
                         bcstC2[:, :], ALU.mult, inc=s_dve)
                    b.drain("vector")
                    b.tt(ap2(dst, 0, 128, dc * QT, QT), ap2(dst, 0, 128, dc * QT, QT),
                         bcstC[:, :], ALU.subtract, inc=s_dve)
                    b.drain("vector")
                return s_dve.v

            if g.dbg and qt == 0:
                b.wait("gpsimd", s_dve, s_dve.v)
                b.dma("gpsimd", dbg_strm[:, :], strm[:, :], inc=s_st[0])
                b.wait("gpsimd", s_st[0], s_st[0].v)
            xn_ready = layer_norm(strm, xn, s_dve.v)
            b.vcopy(xnh[:, :], xn[:, :])
            b.drain("vector")
            b.tt(xnl[:, :], xn[:, :], xnh[:, :], ALU.subtract, inc=s_dve)
            xn_split = s_dve.v
            b.drain("vector")
            if g.dbg and qt == 0:
                b.wait("gpsimd", s_dve, xn_split)
                b.dma("gpsimd", dbg_xn[:, :], xn[:, :], inc=s_st[0])
                b.wait("gpsimd", s_st[0], s_st[0].v)

            w_pe = {}
            w_ld = 0
            m3_pe = {}
            m3_ld = 0
            for fs in range(QT // FSUB):
                for hbg in range(HFC // 4):
                    ps1, ps2 = ps[0:4], ps[4:8]
                    if hbg == 0:
                        b.wait("sync", s_pe, s_pe.v)
                    if not (fs == 0 and hbg == 0):
                        b.wait("tensor", s_dve, s_dve.v)
                    for dc in range(DC):
                        wsl = dc % 4
                        lsem1, lsem2 = s_ld[2 + wsl], s_ld[8 + wsl]
                        if w_ld >= 4:
                            b.wait("sync", s_pe, w_pe[w_ld - 4])
                        b.dma("sync", ap2(wf8[wsl], 0, 128, 0, 512),
                              ap2(scr_ws[0], dc * 128, 128, hbg * 512, 512), inc=lsem1)
                        b.dma("sync", ap2(w2f8[wsl], 0, 128, 0, 512),
                              ap2(scr_ws[1], dc * 128, 128, hbg * 512, 512), inc=lsem2)
                        b.wait("tensor", lsem1, lsem1.v)
                        b.wait("tensor", lsem2, lsem2.v)
                        if dc == 0 and hbg == 0 and fs == 0:
                            b.wait("tensor", s_dve, xn_split)
                        rhs_hl = (ap2(xnh, 0, 128, dc * QT + fs * FSUB, FSUB),
                                  ap2(xnl, 0, 128, dc * QT + fs * FSUB, FSUB))
                        for j in range(4):
                            b.mm2p(ap2(ps1[j], 0, 128, 0, FSUB), ap2(wf8[wsl], 0, 128, j * 128, 128),
                                   rhs_hl, first=(dc == 0), last=(dc == DC - 1), inc=None)
                            b.mm2p(ap2(ps2[j], 0, 128, 0, FSUB), ap2(w2f8[wsl], 0, 128, j * 128, 128),
                                   rhs_hl, first=(dc == 0), last=(dc == DC - 1),
                                   inc=(s_pe if j == 3 else None))
                        w_pe[w_ld] = s_pe.v
                        w_ld += 1
                    pe_12 = s_pe.v
                    b.wait("scalar", s_pe, pe_12)
                    for j in range(4):
                        hb = hbg * 4 + j
                        if j > 0:
                            b.wait("scalar", s_dve, s_dve.v)
                        b.act(sgb[:, :], ap2(ps1[j], 0, 128, 0, FSUB), AF.Sigmoid, scale=0.5, inc=s_act)
                        b.wait("vector", s_act, s_act.v)
                        b.tt(htmp[:, :], ap2(ps2[j], 0, 128, 0, FSUB), sgb[:, :], ALU.mult, inc=s_dve)
                        b.drain("vector")
                        b.wait("scalar", s_dve, s_dve.v)
                        b.acopy(ap2(hbh, 0, 128, hb * FSUB, FSUB), htmp[:, :], inc=s_act)
                        b.wait("vector", s_act, s_act.v)
                        b.tt(ap2(hbl, 0, 128, hb * FSUB, FSUB), htmp[:, :],
                             ap2(hbh, 0, 128, hb * FSUB, FSUB), ALU.subtract, inc=s_dve)
                        b.drain("vector")
                h_act_done = s_act.v
                h_done = s_dve.v
                for ocg in range(DC // 4):
                    ps3 = ps[0:4]
                    if ocg == 0:
                        b.wait("sync", s_pe, s_pe.v)
                    if not (fs == 0 and ocg == 0):
                        b.wait("tensor", s_dve, m3_evac)
                    for hb in range(HFC):
                        wsl = hb % 4
                        lsem = s_ld[2 + wsl]
                        if m3_ld >= 4:
                            b.wait("sync", s_pe, m3_pe[m3_ld - 4])
                        b.dma("sync", ap2(wf8[wsl], 0, 128, 0, 512),
                              ap2(scr_ws[2], hb * 128, 128, ocg * 512, 512), inc=lsem)
                        b.wait("tensor", lsem, lsem.v)
                        if hb == 0 and ocg == 0:
                            b.wait("tensor", s_dve, h_done)
                            b.wait("tensor", s_act, h_act_done)
                        rhs_hl = (ap2(hbh, 0, 128, hb * FSUB, FSUB),
                                  ap2(hbl, 0, 128, hb * FSUB, FSUB))
                        for j in range(4):
                            b.mm2p(ap2(ps3[j], 0, 128, 0, FSUB), ap2(wf8[wsl], 0, 128, j * 128, 128),
                                   rhs_hl, first=(hb == 0), last=(hb == HFC - 1),
                                   inc=(s_pe if j == 3 else None))
                        m3_pe[m3_ld] = s_pe.v
                        m3_ld += 1
                    b.wait("vector", s_pe, s_pe.v)
                    for j in range(4):
                        oc = ocg * 4 + j
                        b.stt(ap2(strm, 0, 128, oc * QT + fs * FSUB, FSUB),
                              ap2(ps3[j], 0, 128, 0, FSUB), 0.25,
                              ap2(xn, 0, 128, oc * QT + fs * FSUB, FSUB),
                              ALU.mult, ALU.add, inc=s_dve)
                        b.drain("vector")
                    m3_evac = s_dve.v

            if g.dbg and qt == 0:
                b.wait("gpsimd", s_dve, s_dve.v)
                b.dma("gpsimd", dbg_ffn[:, :], strm[:, :], inc=s_st[0])
                b.wait("gpsimd", s_st[0], s_st[0].v)
            out_ready = layer_norm(strm, xn, s_dve.v)
            tr_free = [None, None]
            stq_y = StoreQ(b, [s_st[6], s_st[7]])
            yst = stq_y.done()
            for sub in range(QT // 128):
                for oc in range(DC):
                    bank = 4 + (oc % 2)
                    b.wait("tensor", s_act, tr_free[oc % 2])
                    b.wait("tensor", s_dve, out_ready)
                    b.tr(ap2(ps[bank], 0, 128, 0, 128),
                         ap2(xn, 0, 128, oc * QT + sub * 128, 128), ident_sb[:, :], inc=s_pe)
                    b.wait("scalar", s_pe, s_pe.v)
                    if sub > 0 and oc == 0:
                        for _s, _v in yst:
                            b.wait("scalar", _s, _v)
                    b.acopy(ap2(ybuf, 0, 128, oc * 128, 128), ap2(ps[bank], 0, 128, 0, 128), inc=s_act)
                    tr_free[oc % 2] = s_act.v
                b.wait("gpsimd", s_act, s_act.v)
                stq_y.dma(ap2(y_out, qt * QT + sub * 128, 128, 0, D),
                          ap2(ybuf, 0, 128, 0, D))
                yst = stq_y.done()
            stq_y.wait_all("gpsimd")
            b.run_stage(f"C{qt}")
        es_qt.close()

    es_outer.close()
    mybir.codegen_inst_isa_subclasses(nc)
    return nc



def rope_tables(S, HD):
    import jax
    import jax.numpy as jnp
    with jax.default_device(jax.devices("cpu")[0]):
        inv = 1.0 / 10000.0 ** (jnp.arange(0, HD, 2, dtype=jnp.float32) / HD)
        t = jnp.arange(S, dtype=jnp.float32)
        freqs = jnp.einsum("i,j->ij", t, inv)
        emb = jnp.concatenate([freqs, freqs], axis=-1)
        cos = np.asarray(jnp.cos(emb), dtype=np.float32)
        sin = np.asarray(jnp.sin(emb), dtype=np.float32)
    return cos, sin


def prep_core_inputs(g: Geom, core, x, Wq, Wk, Wv, Wo, W1, W2, W3):
    S, D, HD, T = g.S, g.D, g.HD, g.T
    x2 = np.ascontiguousarray(x.reshape(S, D))
    cosf, sinf = rope_tables(S, HD)
    sin_signed = sinf.copy()
    sin_signed[:, :HD // 2] *= -1.0
    sl = slice(core * T, (core + 1) * T)
    return {
        "x_t": np.ascontiguousarray(x2.T),
        "x_to": np.ascontiguousarray(x2.T[:, sl]),
        "wq_t": np.ascontiguousarray(Wq.T),
        "wk_t": np.ascontiguousarray(Wk.T),
        "wv_t": np.ascontiguousarray(Wv.T),
        "wo_t": np.ascontiguousarray(Wo.T),
        "w1_t": np.ascontiguousarray(W1.T),
        "w2_t": np.ascontiguousarray(W2.T),
        "w3_t": np.ascontiguousarray(W3.T),
        "cosf": cosf, "sinf": sin_signed,
        "coso": np.ascontiguousarray(cosf[sl]), "sino": np.ascontiguousarray(sin_signed[sl]),
        "ident": np.eye(128, dtype=np.float32),
        "ones": np.ones((128, 128), dtype=np.float32),
    }





from concourse.bass_utils import run_bass_kernel_spmd


def kernel(L0_attn_x, L0_attn_Wq, L0_attn_Wk, L0_attn_Wv, L0_attn_Wo,
           L0_ffn_w1, L0_ffn_w2, L0_ffn_w3):
    g = Geom()
    x = np.ascontiguousarray(np.asarray(L0_attn_x, dtype=np.float32).reshape(g.S, g.D))
    Wq = np.asarray(L0_attn_Wq, dtype=np.float32)
    Wk = np.asarray(L0_attn_Wk, dtype=np.float32)
    Wv = np.asarray(L0_attn_Wv, dtype=np.float32)
    Wo = np.asarray(L0_attn_Wo, dtype=np.float32)
    W1 = np.asarray(L0_ffn_w1, dtype=np.float32)
    W2 = np.asarray(L0_ffn_w2, dtype=np.float32)
    W3 = np.asarray(L0_ffn_w3, dtype=np.float32)

    nc = build(g)

    x_t = np.ascontiguousarray(x.T)
    cosf, sinf = rope_tables(g.S, g.HD)
    sin_signed = sinf.copy()
    sin_signed[:, :g.HD // 2] *= -1.0
    shared = {
        "x_t": x_t,
        "wq_t": np.ascontiguousarray(Wq.T),
        "wk_t": np.ascontiguousarray(Wk.T),
        "wv_t": np.ascontiguousarray(Wv.T),
        "wo_t": np.ascontiguousarray(Wo.T),
        "w1_t": np.ascontiguousarray(W1.T),
        "w2_t": np.ascontiguousarray(W2.T),
        "w3_t": np.ascontiguousarray(W3.T),
        "cosf": cosf, "sinf": sin_signed,
        "ident": np.eye(128, dtype=np.float32),
        "ones": np.ones((128, 128), dtype=np.float32),
    }
    in_maps = []
    for c in range(g.NC):
        sl = slice(c * g.T, (c + 1) * g.T)
        m = dict(shared)
        m["x_to"] = np.ascontiguousarray(x_t[:, sl])
        m["coso"] = np.ascontiguousarray(cosf[sl])
        m["sino"] = np.ascontiguousarray(sin_signed[sl])
        in_maps.append(m)

    res = run_bass_kernel_spmd(nc, in_maps, core_ids=list(range(g.NC)))
    y = np.concatenate([np.asarray(res.results[c]["y"]) for c in range(g.NC)], axis=0)
    return y.reshape(1, g.S, g.D).astype(np.float32, copy=False)


# version 17
# speedup vs baseline: 1.0534x; 1.0534x over previous
import math
from contextlib import ExitStack
from dataclasses import dataclass

import numpy as np

import concourse.bass as bass
import concourse.mybir as mybir
from concourse.alu_op_type import AluOpType as ALU
from bass_rust import ReduceOp
from concourse import library_config

F32 = mybir.dt.float32
BF16 = mybir.dt.bfloat16
FP8 = mybir.dt.float8e4
AF = mybir.ActivationFunctionType
AX = mybir.AxisListType

ENGS = ("tensor", "vector", "scalar", "sync", "gpsimd")


@dataclass
class Geom:
    S: int = 8192
    D: int = 2048
    NH: int = 16
    NKV: int = 4
    HD: int = 128
    HF: int = 5632
    NC: int = 8
    TCH: int = 512
    QT: int = 512
    FSUB: int = 256
    dbg: bool = False

    @property
    def T(self): return self.S // self.NC
    @property
    def DC(self): return self.D // 128
    @property
    def NKD(self): return self.NKV * self.HD
    @property
    def HFC(self): return self.HF // 128
    @property
    def REP(self): return self.NH // self.NKV
    @property
    def NKC(self): return self.S // 128
    @property
    def NWELEM(self): return self.D * self.HF


class SemC:
    def __init__(self, nc, name, step):
        self.h = nc.alloc_semaphore(name=name)
        self.step = step
        self.v = 0

    def inc(self):
        self.v += self.step
        return self.v


class Builder:
    def __init__(self, g: Geom):
        self.g = g
        self.nc = bass.Bass("TRN2", target_bir_lowering=False)
        self.plan = None

    def L(self, eng, f):
        self.plan[eng].append(f)

    def wait(self, eng, sem: SemC, val=None):
        v = sem.v if val is None else val
        if v is not None and v > 0:
            self.L(eng, lambda e, _s=sem.h, _v=v: e.wait_ge(_s, _v))

    def drain(self, eng):
        self.L(eng, lambda e: e.drain())

    def _inc(self, instr, sem):
        if sem is not None:
            instr.then_inc(sem.h, sem.step)

    def mm(self, out, lhsT, rhs, start, stop, inc=None):
        if inc is not None: inc.inc()
        self.L("tensor", lambda e, _o=out, _l=lhsT, _r=rhs, _a=start, _b=stop, _s=inc:
               self._inc(e.matmul(_o, _l, _r, start=_a, stop=_b), _s))

    def mm3p(self, out, lhs_hl, rhs_hl, inc=None, first=True, last=True):
        combos = ((0, 0), (0, 1), (1, 0))
        for i, (a, c) in enumerate(combos):
            self.mm(out, lhs_hl[a], rhs_hl[c],
                    start=(first and i == 0), stop=(last and i == 2),
                    inc=(inc if i == 2 else None))

    def mm2p(self, out, lhs_exact, rhs_hl, inc=None, first=True, last=True):
        for i in range(2):
            self.mm(out, lhs_exact, rhs_hl[i],
                    start=(first and i == 0), stop=(last and i == 1),
                    inc=(inc if i == 1 else None))

    def tr(self, out, in_, ident, inc=None):
        if inc is not None: inc.inc()
        self.L("tensor", lambda e, _o=out, _i=in_, _d=ident, _s=inc:
               self._inc(e.transpose(_o, _i, _d), _s))

    def act(self, out, in_, func, bias=None, scale=None, accum=None, inc=None):
        if inc is not None: inc.inc()
        kw = {}
        if bias is not None: kw["bias"] = bias
        if scale is not None: kw["scale"] = scale
        if accum is not None: kw["accum_out"] = accum
        self.L("scalar", lambda e, _o=out, _i=in_, _f=func, _k=kw, _s=inc:
               self._inc(e.activation(out=_o, in_=_i, func=_f, **_k), _s))

    def acopy(self, out, in_, inc=None):
        if inc is not None: inc.inc()
        self.L("scalar", lambda e, _o=out, _i=in_, _s=inc:
               self._inc(e.copy(_o, _i), _s))

    def tt(self, out, in0, in1, op, inc=None):
        if inc is not None: inc.inc()
        self.L("vector", lambda e, _o=out, _a=in0, _b=in1, _p=op, _s=inc:
               self._inc(e.tensor_tensor(_o, _a, _b, op=_p), _s))

    def ts(self, out, in0, s1, op0, s2=None, op1=None, inc=None):
        if inc is not None: inc.inc()
        kw = dict(scalar2=s2)
        if op1 is not None: kw["op1"] = op1
        self.L("vector", lambda e, _o=out, _a=in0, _1=s1, _p=op0, _k=kw, _s=inc:
               self._inc(e.tensor_scalar(out=_o, in0=_a, scalar1=_1, op0=_p, **_k), _s))

    def stt(self, out, in0, scalar, in1, op0, op1, inc=None):
        if inc is not None: inc.inc()
        self.L("vector", lambda e, _o=out, _a=in0, _c=scalar, _b=in1, _p=op0, _q=op1, _s=inc:
               self._inc(e.scalar_tensor_tensor(out=_o, in0=_a, scalar=_c, in1=_b, op0=_p, op1=_q), _s))

    def vcopy(self, out, in_, inc=None):
        if inc is not None: inc.inc()
        self.L("vector", lambda e, _o=out, _i=in_, _s=inc:
               self._inc(e.tensor_copy(_o, _i), _s))

    def vred(self, out, in_, op, absval=False, inc=None):
        if inc is not None: inc.inc()
        self.L("vector", lambda e, _o=out, _i=in_, _p=op, _a=absval, _s=inc:
               self._inc(e.tensor_reduce(_o, _i, axis=AX.X, op=_p, apply_absolute_value=_a), _s))

    def recip(self, out, in_, inc=None):
        if inc is not None: inc.inc()
        self.L("vector", lambda e, _o=out, _i=in_, _s=inc:
               self._inc(e.reciprocal(_o, _i), _s))

    def dma(self, queue, out, in_, inc=None):
        if inc is not None: inc.inc()
        self.L(queue, lambda e, _o=out, _i=in_, _s=inc:
               self._inc(e.dma_start(_o, _i), _s))

    def pbcast(self, out, in_, inc=None):
        if inc is not None: inc.inc()
        self.L("gpsimd", lambda e, _o=out, _i=in_, _s=inc:
               self._inc(e.partition_broadcast(_o, _i), _s))

    def allred(self, out, in_, op, inc=None):
        if inc is not None: inc.inc()
        self.L("gpsimd", lambda e, _o=out, _i=in_, _p=op, _s=inc:
               self._inc(e.partition_all_reduce(_o, _i, channels=128, reduce_op=_p), _s))

    def run_stage(self, name):
        plan = self.plan
        with self.nc.named_scope(name):
            with self.nc.Block() as block:
                for eng in ENGS:
                    if plan[eng]:
                        def body(e, _th=tuple(plan[eng])):
                            for f in _th:
                                f(e)
                        getattr(block, eng)(body)
        self.plan = None


class StoreQ:
    def __init__(self, b, sems, prewait=False):
        self.b = b
        self.sems = sems
        self.i = 0
        self.prewait = prewait

    def dma(self, out, in_):
        b, sems = self.b, self.sems
        s = sems[self.i % len(sems)]
        if self.i >= len(sems) or self.prewait:
            b.wait("gpsimd", s, s.v)
        b.dma("gpsimd", out, in_, inc=s)
        self.i += 1

    def done(self):
        return [(s, s.v) for s in self.sems]

    def wait_all(self, eng):
        for s, v in self.done():
            self.b.wait(eng, s, v)


def ap2(t, p0, np_, c0, nc_):
    return t[p0:p0 + np_, c0:c0 + nc_]


def make_self_cc(nc_cores):
    def self_cc(e, kind, op, ins, outs, sem):
        e.collective_compute(kind, op, replica_groups=[list(range(nc_cores))],
                             ins=ins, outs=outs).then_inc(sem.h, sem.step)
    return self_cc


def build(g: Geom):
    b = Builder(g)
    nc = b.nc
    S, D, HD, HF, T = g.S, g.D, g.HD, g.HF, g.T
    DC, NKD, NKV, HFC, REP, NKC = g.DC, g.NKD, g.NKV, g.HFC, g.REP, g.NKC
    TCH, QT, FSUB = g.TCH, g.QT, g.FSUB
    NCH = S // TCH
    OCH = T // TCH
    A0_UNITS_PER_CHUNK = -(-((D // 128) * (HF // 512) * 4 + (HF // 128) * (D // 512) * 2 + (D // 128) * (D // 512) * 2 + 8) // (S // TCH))
    NQT = T // QT
    NSUB = TCH // 128
    SCALE = 1.0 / math.sqrt(HD)
    self_cc = make_self_cc(g.NC)

    x_t = nc.dram_tensor("x_t", [D, S], F32, kind="ExternalInput")
    x_to = nc.dram_tensor("x_to", [D, T], F32, kind="ExternalInput")
    DS = D // g.NC
    HFP = ((HF + 128 * g.NC - 1) // (128 * g.NC)) * 128 * g.NC
    HFS = HFP // g.NC
    wq_s = nc.dram_tensor("wq_s", [DS, D], F32, kind="ExternalInput")
    wk_t = nc.dram_tensor("wk_t", [D, NKD], F32, kind="ExternalInput")
    wv_t = nc.dram_tensor("wv_t", [D, NKD], F32, kind="ExternalInput")
    wo_s = nc.dram_tensor("wo_s", [DS, D], F32, kind="ExternalInput")
    w1_s = nc.dram_tensor("w1_s", [DS, HF], F32, kind="ExternalInput")
    w2_s = nc.dram_tensor("w2_s", [DS, HF], F32, kind="ExternalInput")
    w3_s = nc.dram_tensor("w3_s", [HFS, D], F32, kind="ExternalInput")
    cosf = nc.dram_tensor("cosf", [S, HD], F32, kind="ExternalInput")
    sinf = nc.dram_tensor("sinf", [S, HD], F32, kind="ExternalInput")
    coso = nc.dram_tensor("coso", [T, HD], F32, kind="ExternalInput")
    sino = nc.dram_tensor("sino", [T, HD], F32, kind="ExternalInput")
    ident_in = nc.dram_tensor("ident", [128, 128], F32, kind="ExternalInput")
    ones_in = nc.dram_tensor("ones", [128, 128], F32, kind="ExternalInput")
    y_out = nc.dram_tensor("y", [T, D], F32, kind="ExternalOutput")

    scr_kh = nc.dram_tensor("scr_kh", [NKD, S], BF16)
    scr_kl = nc.dram_tensor("scr_kl", [NKD, S], BF16)
    scr_vh = nc.dram_tensor("scr_vh", [S, NKD], BF16)
    scr_vl = nc.dram_tensor("scr_vl", [S, NKD], BF16)
    scr_qh = nc.dram_tensor("scr_qh", [D, T], BF16)
    scr_ql = nc.dram_tensor("scr_ql", [D, T], BF16)
    CCSP = "Shared" if g.NC > 4 else "Local"
    cc_ws = [nc.dram_tensor(f"cc_w{i}", [DS, HF] if i < 2 else [HFS, D], FP8) for i in range(3)]
    scr_ws = [nc.dram_tensor(f"scr_w{i}", [D, HF] if i < 2 else [HFP, D], FP8, addr_space=CCSP) for i in range(3)]
    cc_wsp = [nc.dram_tensor(f"cc_wsp{i}", [DS, D], BF16) for i in range(4)]
    scr_wqh = nc.dram_tensor("scr_wqh", [D, D], BF16, addr_space=CCSP)
    scr_wql = nc.dram_tensor("scr_wql", [D, D], BF16, addr_space=CCSP)
    scr_woh = nc.dram_tensor("scr_woh", [D, D], BF16, addr_space=CCSP)
    scr_wol = nc.dram_tensor("scr_wol", [D, D], BF16, addr_space=CCSP)
    cc_g_in = nc.dram_tensor("cc_g_in", [1, 3], F32)
    cc_g_out = nc.dram_tensor("cc_g_out", [1, 3], F32)

    if g.dbg:
        dbg_attn = nc.dram_tensor("dbg_attn", [128, DC * QT], BF16, kind="ExternalOutput")
        dbg_strm = nc.dram_tensor("dbg_strm", [128, DC * QT], F32, kind="ExternalOutput")
        dbg_xn = nc.dram_tensor("dbg_xn", [128, DC * QT], F32, kind="ExternalOutput")
        dbg_ffn = nc.dram_tensor("dbg_ffn", [128, DC * QT], F32, kind="ExternalOutput")

    sem = lambda n, s: SemC(nc, n, s)
    s_pe = sem("pe", 1)
    s_act = sem("act", 1)
    s_dve = sem("dve", 1)
    s_gp = sem("gp", 1)
    s_ld = [sem(f"ld{i}", 16) for i in range(12)]
    s_st = [sem(f"st{i}", 16) for i in range(16)]

    es_outer = ExitStack()
    sb = lambda n, sh, dt=F32: es_outer.enter_context(nc.sbuf_tensor(n, sh, dt))

    ident_sb = sb("identsb", [128, 128])
    ones_sb = sb("onessb", [128, 128])
    ones_bf = sb("onesbf", [128, 2], BF16)
    ident_bf = sb("identbf", [128, 128], BF16)
    bias_m0 = sb("bias_m0", [128, 1])
    hthr = sb("hthr", [128, 6])

    b.plan = {e: [] for e in ENGS}
    b.dma("sync", ident_sb[:, :], ident_in[:, :], inc=s_ld[3])
    b.dma("sync", ones_sb[:, :], ones_in[:, :], inc=s_ld[3])
    consts_ready = s_ld[3].v
    b.wait("vector", s_ld[3], consts_ready)
    b.vcopy(ones_bf[:, 0:1], ones_sb[:, 0:1], inc=s_dve)
    b.drain("vector")
    b.vcopy(ident_bf[:, :], ident_sb[:, :], inc=s_dve)
    b.drain("vector")
    b.L("gpsimd", lambda e: e.load_library(library_config.attnmlp))

    es_a0 = ExitStack()
    if True:
        sbA = lambda n, sh, dt=F32: es_a0.enter_context(nc.sbuf_tensor(n, sh, dt))

        def colchunks(width):
            cw = width
            while cw > 2048:
                assert cw % 2 == 0
                cw //= 2
            return cw, width // cw
        cw1, nc1 = colchunks(HF)
        cw3, nc3 = colchunks(D)
        WMAX = max(cw1, cw3, D)
        g_sb = es_a0.enter_context(nc.sbuf_tensor("g_sb", [128, 4], F32))
        w_in = [sbA("w_in0", [128, WMAX]), sbA("w_in1", [128, WMAX])]
        s1b = [sbA("s1b0", [128, WMAX]), sbA("s1b1", [128, WMAX])]
        s2b = [sbA("s2b0", [128, WMAX]), sbA("s2b1", [128, WMAX])]
        wqb = [sbA("wqb0", [128, WMAX], FP8), sbA("wqb1", [128, WMAX], FP8)]
        wsph = [sbA("wsph0", [128, D], BF16), sbA("wsph1", [128, D], BF16)]
        wspl = [sbA("wspl0", [128, D], BF16), sbA("wspl1", [128, D], BF16)]
        n_t1 = (D // 128) * nc1 * 2 + (HF // 128) * nc3
        accb = sbA("accb", [128, n_t1 + 8])
        gtot = sbA("gtot", [128, 3])

        wsrc = [w1_s, w2_s, w3_s]
        wtiles = [[(r, c, cw1) for r in range(DS // 128) for c in range(nc1)],
                  [(r, c, cw1) for r in range(DS // 128) for c in range(nc1)],
                  [(r, c, cw3) for r in range(HFS // 128) for c in range(nc3)]]

    def a0_work():
            col = 0
            wcol = []
            dve_done = {}
            ti = 0
            for wi, tiles in enumerate(wtiles):
                wcol.append(col)
                for (r, cc, width) in tiles:
                    sl = ti % 2
                    if ti >= 2:
                        b.wait("sync", s_act, dve_done[ti - 2])
                    b.dma("sync", ap2(w_in[sl], 0, 128, 0, width),
                          ap2(wsrc[wi], r * 128, 128, cc * width, width), inc=s_ld[6 + sl])
                    b.wait("scalar", s_ld[6 + sl], s_ld[6 + sl].v)
                    b.act(ap2(s1b[sl], 0, 128, 0, width), ap2(w_in[sl], 0, 128, 0, width),
                          AF.Abs, accum=accb[:, col:col + 1], inc=s_act)
                    dve_done[ti] = s_act.v
                    col += 1
                    ti += 1
                    yield
            b.wait("vector", s_act, s_act.v)
            for wi in range(3):
                c0, c1 = wcol[wi], wcol[wi] + len(wtiles[wi])
                b.vred(gtot[:, wi:wi + 1], accb[:, c0:c1], ALU.add, inc=s_dve)
                b.drain("vector")
            b.wait("gpsimd", s_dve)
            for wi in range(3):
                b.allred(gtot[:, wi:wi + 1], gtot[:, wi:wi + 1], ReduceOp.add, inc=s_gp)
            b.wait("gpsimd", s_gp, s_gp.v)
            b.dma("gpsimd", cc_g_in[0:1, 0:3], gtot[0:1, 0:3], inc=s_st[14])
            b.wait("gpsimd", s_st[14], s_st[14].v)
            b.L("gpsimd", lambda e: self_cc(e, "AllReduce", ALU.add, [cc_g_in[:, :]], [cc_g_out[:, :]], s_gp))
            s_gp.inc()
            b.wait("gpsimd", s_gp, s_gp.v)
            b.dma("gpsimd", g_sb[0:1, 0:3], cc_g_out[0:1, 0:3], inc=s_st[14])
            b.wait("gpsimd", s_st[14], s_st[14].v)
            b.pbcast(gtot[:, 0:3], g_sb[0:1, 0:3], inc=s_gp)
            b.wait("vector", s_gp)
            for wi in range(3):
                nw = float(g.NWELEM)
                b.ts(hthr[:, 2 * wi:2 * wi + 1], gtot[:, wi:wi + 1], -0.5 / nw, ALU.mult,
                     s2=-0.5e-8, op1=ALU.add, inc=s_dve)
                b.ts(hthr[:, 2 * wi + 1:2 * wi + 2], gtot[:, wi:wi + 1], 0.5 / nw, ALU.mult,
                     s2=0.5e-8, op1=ALU.add, inc=s_dve)
                b.drain("vector")
            hready = s_dve.v
            b.wait("scalar", s_dve, hready)

            b.wait("sync", s_dve, hready)
            stq = StoreQ(b, [s_st[8], s_st[9]])
            ti = 0
            act_done = {}
            dve_done = {}
            st_done = {}
            for wi, tiles in enumerate(wtiles):
                for (r, cc, width) in tiles:
                    sl = ti % 2
                    if ti >= 2:
                        b.wait("sync", s_act, act_done[ti - 2])
                    b.dma("sync", ap2(w_in[sl], 0, 128, 0, width),
                          ap2(wsrc[wi], r * 128, 128, cc * width, width), inc=s_ld[6 + sl])
                    b.wait("scalar", s_ld[6 + sl], s_ld[6 + sl].v)
                    if ti >= 2:
                        b.wait("scalar", s_dve, dve_done[ti - 2])
                    b.act(ap2(s1b[sl], 0, 128, 0, width), ap2(w_in[sl], 0, 128, 0, width),
                          AF.Sign, bias=hthr[:, 2 * wi:2 * wi + 1], inc=s_act)
                    b.act(ap2(s2b[sl], 0, 128, 0, width), ap2(w_in[sl], 0, 128, 0, width),
                          AF.Sign, bias=hthr[:, 2 * wi + 1:2 * wi + 2], inc=s_act)
                    act_done[ti] = s_act.v
                    b.wait("vector", s_act, act_done[ti])
                    if ti >= 2:
                        for _s, _v in st_done[ti - 2]:
                            b.wait("vector", _s, _v)
                    b.drain("vector")
                    b.tt(ap2(wqb[sl], 0, 128, 0, width), ap2(s1b[sl], 0, 128, 0, width),
                         ap2(s2b[sl], 0, 128, 0, width), ALU.add, inc=s_dve)
                    dve_done[ti] = s_dve.v
                    b.wait("gpsimd", s_dve, s_dve.v)
                    stq.dma(ap2(cc_ws[wi], r * 128, 128, cc * width, width),
                            ap2(wqb[sl], 0, 128, 0, width))
                    st_done[ti] = stq.done()
                    ti += 1
                    yield
            stq.wait_all("gpsimd")
            for wi in range(3):
                b.L("gpsimd", lambda e, _w=wi: self_cc(e, "AllGather", ALU.bypass,
                    [cc_ws[_w][:, :]], [scr_ws[_w][:, :]], s_gp))
                s_gp.inc()

            b.wait("sync", s_dve, s_dve.v)
            stq_h = StoreQ(b, [s_st[10], s_st[11]])
            stq_l = StoreQ(b, [s_st[12], s_st[13]])
            sp_src = [(wq_s, cc_wsp[0], cc_wsp[1]), (wo_s, cc_wsp[2], cc_wsp[3])]
            ti = 0
            dve_done = {}
            st_done = {}
            for (srcw, dsth, dstl) in sp_src:
                for r in range(DS // 128):
                    sl = ti % 2
                    if ti >= 2:
                        b.wait("sync", s_dve, dve_done[ti - 2])
                    b.dma("sync", ap2(w_in[sl], 0, 128, 0, D),
                          ap2(srcw, r * 128, 128, 0, D), inc=s_ld[6 + sl])
                    b.wait("vector", s_ld[6 + sl], s_ld[6 + sl].v)
                    if ti >= 2:
                        for _s, _v in st_done[ti - 2]:
                            b.wait("vector", _s, _v)
                    b.drain("vector")
                    b.vcopy(ap2(wsph[sl], 0, 128, 0, D), ap2(w_in[sl], 0, 128, 0, D))
                    b.drain("vector")
                    b.tt(ap2(wspl[sl], 0, 128, 0, D), ap2(w_in[sl], 0, 128, 0, D),
                         ap2(wsph[sl], 0, 128, 0, D), ALU.subtract, inc=s_dve)
                    dve_done[ti] = s_dve.v
                    b.drain("vector")
                    b.wait("gpsimd", s_dve, s_dve.v)
                    stq_h.dma(ap2(dsth, r * 128, 128, 0, D), ap2(wsph[sl], 0, 128, 0, D))
                    stq_l.dma(ap2(dstl, r * 128, 128, 0, D), ap2(wspl[sl], 0, 128, 0, D))
                    st_done[ti] = stq_h.done() + stq_l.done()
                    ti += 1
                    yield
            stq_h.wait_all("gpsimd")
            stq_l.wait_all("gpsimd")
            for _ci, _co in ((cc_wsp[0], scr_wqh), (cc_wsp[1], scr_wql),
                             (cc_wsp[2], scr_woh), (cc_wsp[3], scr_wol)):
                b.L("gpsimd", lambda e, _a=_ci, _b=_co: self_cc(e, "AllGather", ALU.bypass,
                    [_a[:, :]], [_b[:, :]], s_gp))
                s_gp.inc()
            b.wait("gpsimd", s_gp, s_gp.v)
            b.scr_wq_ready = [(s_gp, s_gp.v)]
            yield


    a0_gen = a0_work()
    for _ in a0_gen:
        pass
    b.run_stage("A0")
    es_a0.close()

    with ExitStack() as es:
        sbA = lambda n, sh, dt=F32: es.enter_context(nc.sbuf_tensor(n, sh, dt))
        xt32 = sbA("xt32", [128, DC * TCH])
        xth = [sbA("xth0", [128, DC * TCH], BF16), sbA("xth1", [128, DC * TCH], BF16)]
        xtl = [sbA("xtl0", [128, DC * TCH], BF16), sbA("xtl1", [128, DC * TCH], BF16)]
        wkv32 = sbA("wkv32", [128, NKD])
        wkvh = sbA("wkvh", [128, DC * 2 * NKD], BF16)
        wkvl = sbA("wkvl", [128, DC * 2 * NKD], BF16)
        cs = [sbA("cs0", [128, NSUB * 2 * HD])]
        t1b = sbA("t1b", [128, max(NKD, 512)])
        t2b = sbA("t2b", [128, max(NKD, 512)])
        kpb = sbA("kpb", [128, max(NKD, 512)])
        kpbh = sbA("kpbh", [128, max(NKD, 512)], BF16)
        kpbl = sbA("kpbl", [128, max(NKD, 512)], BF16)
        ksth = sbA("ksth", [128, max(NKV, 4) * TCH], BF16)
        kstl = sbA("kstl", [128, max(NKV, 4) * TCH], BF16)
        vsth = sbA("vsth", [128, NSUB * NKD], BF16)
        vstl = sbA("vstl", [128, NSUB * NKD], BF16)
        wqtlh = [sbA("wqtlh0", [128, 512], BF16), sbA("wqtlh1", [128, 512], BF16)]
        wqtll = [sbA("wqtll0", [128, 512], BF16), sbA("wqtll1", [128, 512], BF16)]
        qsth, qstl = ksth, kstl

        ps = [es.enter_context(nc.psum_tensor(f"psA{i}", [128, 512], F32)) for i in range(4)]
        psKb, psVb = ps[0:2], ps[2:4]
        psQ = ps[0:4]
        psT = [es.enter_context(nc.psum_tensor(f"psAT{i}", [128, 512], BF16)) for i in range(2)]

        b.plan = {e: [] for e in ENGS}
        for dc in range(DC):
            for which, srcw in ((0, wk_t), (1, wv_t)):
                b.dma("sync", wkv32[:, :], ap2(srcw, dc * 128, 128, 0, NKD), inc=s_ld[2])
                b.wait("vector", s_ld[2], s_ld[2].v)
                if dc > 0 or which > 0:
                    b.drain("vector")
                co0 = dc * 2 * NKD + which * NKD
                b.vcopy(ap2(wkvh, 0, 128, co0, NKD), wkv32[:, :])
                b.drain("vector")
                b.tt(ap2(wkvl, 0, 128, co0, NKD), wkv32[:, :],
                     ap2(wkvh, 0, 128, co0, NKD), ALU.subtract, inc=s_dve)
                b.wait("sync", s_dve, s_dve.v)
        wkv_ready = s_dve.v

        tr_bank_free = [None, None]
        kpb_tr_pe = None
        xt32_free_dve = 0
        cs_free_dve = 0
        rope2_done = {}
        vevac2_done = {}
        pe_after_chunk = {}
        stq_kh = StoreQ(b, [s_st[0], s_st[1]])
        stq_kl = StoreQ(b, [s_st[2], s_st[3]])
        stq_vh = StoreQ(b, [s_st[4], s_st[5]])
        stq_vl = StoreQ(b, [s_st[6], s_st[7]])

        def load_xt_cs(sl, src_x, src_cos, src_sin, col0, slot_free_pe):
            nonlocal xt32_free_dve, cs_free_dve
            if cs_free_dve:
                b.wait("sync", s_dve, cs_free_dve)
            b.dma("sync", bass.AP(cs[0], 0, [[NSUB * 2 * HD, 128], [2 * HD, NSUB], [1, HD]]),
                  bass.AP(src_cos, col0 * HD, [[HD, 128], [128 * HD, NSUB], [1, HD]]), inc=s_ld[sl])
            b.dma("sync", bass.AP(cs[0], HD, [[NSUB * 2 * HD, 128], [2 * HD, NSUB], [1, HD]]),
                  bass.AP(src_sin, col0 * HD, [[HD, 128], [128 * HD, NSUB], [1, HD]]), inc=s_ld[sl])
            if xt32_free_dve:
                b.wait("sync", s_dve, xt32_free_dve)
            b.dma("sync", xt32[:, :],
                  bass.AP(src_x, col0, [[src_x.shape[1], 128], [128 * src_x.shape[1], DC], [1, TCH]]),
                  inc=s_ld[sl])
            b.wait("vector", s_ld[sl], s_ld[sl].v)
            if slot_free_pe:
                b.wait("vector", s_pe, slot_free_pe)
            b.drain("vector")
            b.vcopy(xth[sl][:, :], xt32[:, :])
            b.drain("vector")
            b.tt(xtl[sl][:, :], xt32[:, :], xth[sl][:, :], ALU.subtract, inc=s_dve)
            xt32_free_dve = s_dve.v
            b.drain("vector")
            return s_ld[sl].v, s_dve.v

        def rope(psrc, nheads, width, sl, sub, dst):
            base = sub * 2 * HD
            cosap = bass.AP(cs[0], base, [[NSUB * 2 * HD, 128], [0, nheads], [1, HD]])
            sin4 = bass.AP(cs[0], base + HD, [[NSUB * 2 * HD, 128], [0, nheads], [HD // 2, 2], [1, HD // 2]])
            kap = bass.AP(psrc, 0, [[512, 128], [HD, nheads], [1, HD]])
            kshift = bass.AP(psrc, HD // 2, [[512, 128], [HD, nheads], [-(HD // 2), 2], [1, HD // 2]])
            t1ap = bass.AP(t1b, 0, [[t1b.shape[1], 128], [HD, nheads], [1, HD]])
            t2ap = bass.AP(t2b, 0, [[t2b.shape[1], 128], [HD, nheads], [HD // 2, 2], [1, HD // 2]])
            b.tt(t1ap, kap, cosap, ALU.mult)
            b.tt(t2ap, kshift, sin4, ALU.mult, inc=s_dve)
            b.drain("vector")
            b.tt(ap2(dst, 0, 128, 0, width), ap2(t1b, 0, 128, 0, width),
                 ap2(t2b, 0, 128, 0, width), ALU.add, inc=s_dve)
            b.drain("vector")
            b.vcopy(ap2(kpbh, 0, 128, 0, width), ap2(dst, 0, 128, 0, width))
            b.drain("vector")
            b.tt(ap2(kpbl, 0, 128, 0, width), ap2(dst, 0, 128, 0, width),
                 ap2(kpbh, 0, 128, 0, width), ALU.subtract, inc=s_dve)
            out = s_dve.v
            b.drain("vector")
            return out

        def transpose_pair(nslices, dsth, dstl, dcol_fn, ready_dve):
            nonlocal kpb_tr_pe
            for i in range(nslices):
                for which, (srcb, dstb) in enumerate(((kpbh, dsth), (kpbl, dstl))):
                    bank = (2 * i + which) % 2
                    b.wait("tensor", s_act, tr_bank_free[bank])
                    b.wait("tensor", s_dve, ready_dve)
                    b.tr(ap2(psT[bank], 0, 128, 0, 128),
                         ap2(srcb, 0, 128, i * HD, HD), ident_bf[:, :], inc=s_pe)
                    b.wait("scalar", s_pe, s_pe.v)
                    b.acopy(ap2(dstb, 0, 128, dcol_fn(i), 128),
                            ap2(psT[bank], 0, 128, 0, 128), inc=s_act)
                    tr_bank_free[bank] = s_act.v
            kpb_tr_pe = s_pe.v

        chunk_loaded = {}
        chunk_loaded[0] = load_xt_cs(0, x_t, cosf, sinf, 0, 0)
        for c in range(NCH):
            sl = c % 2
            xt_ready, xs_done = chunk_loaded[c]
            b.wait("tensor", s_dve, xs_done)

            for sub in range(NSUB):
                si = (c * NSUB + sub) % 2
                psK, psV = psKb[si], psVb[si]
                for which, pdst in ((0, psK), (1, psV)):
                    if c * NSUB + sub >= 2:
                        if which == 0:
                            b.wait("tensor", s_dve, rope2_done.get((c * NSUB + sub - 2), 0))
                        else:
                            b.wait("tensor", s_dve, vevac2_done.get((c * NSUB + sub - 2), 0))
                    for dc in range(DC):
                        lhs_hl = (ap2(xth[sl], 0, 128, dc * TCH + sub * 128, 128),
                                  ap2(xtl[sl], 0, 128, dc * TCH + sub * 128, 128))
                        rhs_hl = (ap2(wkvh, 0, 128, dc * 2 * NKD + which * NKD, NKD),
                                  ap2(wkvl, 0, 128, dc * 2 * NKD + which * NKD, NKD))
                        b.mm3p(ap2(pdst, 0, 128, 0, NKD), lhs_hl, rhs_hl,
                               first=(dc == 0), last=(dc == DC - 1),
                               inc=(s_pe if dc == DC - 1 else None))
                pe_kv = s_pe.v
                b.wait("scalar", s_pe, pe_kv)
                if c > 0:
                    for _s, _v in vstage_stored:
                        b.wait("scalar", _s, _v)
                b.acopy(ap2(vsth, 0, 128, sub * NKD, NKD), ap2(psV, 0, 128, 0, NKD), inc=s_act)
                b.wait("vector", s_act, s_act.v)
                if c > 0:
                    for _s, _v in vstage_stored:
                        b.wait("vector", _s, _v)
                b.drain("vector")
                b.tt(ap2(vstl, 0, 128, sub * NKD, NKD), ap2(psV, 0, 128, 0, NKD),
                     ap2(vsth, 0, 128, sub * NKD, NKD), ALU.subtract, inc=s_dve)
                vevac2_done[c * NSUB + sub] = s_dve.v
                b.drain("vector")
                b.wait("vector", s_ld[sl], xt_ready)
                b.wait("vector", s_pe, pe_kv)
                if c > 0:
                    for _s, _v in kstage_stored:
                        b.wait("vector", _s, _v)
                if kpb_tr_pe is not None:
                    b.wait("vector", s_pe, kpb_tr_pe)
                prev_rope = rope(psK, NKV, NKD, sl, sub, kpb)
                rope2_done[c * NSUB + sub] = prev_rope
                cs_free_dve = prev_rope
                transpose_pair(NKV, ksth, kstl,
                               lambda i, _sub=sub: i * TCH + _sub * 128, prev_rope)
            pe_after_chunk[c] = s_pe.v
            b.wait("gpsimd", s_act, s_act.v)
            b.wait("gpsimd", s_dve, s_dve.v)
            for kv in range(NKV):
                stq_kh.dma(ap2(scr_kh, kv * 128, 128, c * TCH, TCH),
                           ap2(ksth, 0, 128, kv * TCH, TCH))
                stq_kl.dma(ap2(scr_kl, kv * 128, 128, c * TCH, TCH),
                           ap2(kstl, 0, 128, kv * TCH, TCH))
            kstage_stored = stq_kh.done() + stq_kl.done()
            for sub in range(NSUB):
                stq_vh.dma(ap2(scr_vh, c * TCH + sub * 128, 128, 0, NKD),
                           ap2(vsth, 0, 128, sub * NKD, NKD))
                stq_vl.dma(ap2(scr_vl, c * TCH + sub * 128, 128, 0, NKD),
                           ap2(vstl, 0, 128, sub * NKD, NKD))
            vstage_stored = stq_vh.done() + stq_vl.done()
            if c + 1 < NCH:
                chunk_loaded[c + 1] = load_xt_cs((c + 1) % 2, x_t, cosf, sinf, (c + 1) * TCH,
                                                 pe_after_chunk.get(c - 1, 0))

        wq_pe = {}
        wq_ld = 0
        stq_qh = StoreQ(b, [s_st[8], s_st[9]], prewait=True)
        stq_ql = StoreQ(b, [s_st[0], s_st[1]], prewait=True)
        for _s, _v in b.scr_wq_ready:
            b.wait("sync", _s, _v)
        for q_ in (stq_kh, stq_kl, stq_vh, stq_vl):
            q_.wait_all("scalar")
            q_.wait_all("vector")
        for co in range(OCH):
            sl = co % 2
            b.wait("sync", s_pe, s_pe.v)
            b.wait("sync", s_dve, s_dve.v)
            xt_ready, xs_done = load_xt_cs(sl, x_to, coso, sino, co * TCH, s_pe.v)
            b.wait("tensor", s_dve, xs_done)
            for ob in range(D // 512):
                for dc in range(DC):
                    wsl = wq_ld % 2
                    if wq_ld >= 2:
                        b.wait("sync", s_pe, wq_pe[wq_ld - 2])
                    b.dma("sync", ap2(wqtlh[wsl], 0, 128, 0, 512),
                          ap2(scr_wqh, dc * 128, 128, ob * 512, 512), inc=s_ld[4 + wsl])
                    b.dma("sync", ap2(wqtll[wsl], 0, 128, 0, 512),
                          ap2(scr_wql, dc * 128, 128, ob * 512, 512), inc=s_ld[4 + wsl])
                    b.wait("tensor", s_ld[4 + wsl], s_ld[4 + wsl].v)
                    for sub in range(NSUB):
                        if dc == 0 and (ob > 0 or co > 0):
                            b.wait("tensor", s_dve, q_rope_done.get(sub, 0))
                        lhs_hl = (ap2(xth[sl], 0, 128, dc * TCH + sub * 128, 128),
                                  ap2(xtl[sl], 0, 128, dc * TCH + sub * 128, 128))
                        rhs_hl = (ap2(wqtlh[wsl], 0, 128, 0, 512),
                                  ap2(wqtll[wsl], 0, 128, 0, 512))
                        b.mm3p(ap2(psQ[sub], 0, 128, 0, 512), lhs_hl, rhs_hl,
                               first=(dc == 0), last=(dc == DC - 1),
                               inc=(s_pe if sub == NSUB - 1 else None))
                    wq_pe[wq_ld] = s_pe.v
                    wq_ld += 1
                if co == 0 and ob == 0:
                    q_rope_done = {}
                pe_q = s_pe.v
                for sub in range(NSUB):
                    b.wait("vector", s_pe, pe_q)
                    b.wait("vector", s_ld[sl], xt_ready)
                    if not (co == 0 and ob == 0):
                        for _s, _v in qstage_stored:
                            b.wait("vector", _s, _v)
                    if kpb_tr_pe is not None:
                        b.wait("vector", s_pe, kpb_tr_pe)
                    qr = rope(psQ[sub], 4, 512, sl, sub, kpb)
                    cs_free_dve = qr
                    q_rope_done[sub] = qr
                    transpose_pair(4, qsth, qstl,
                                   lambda i, _sub=sub: i * TCH + _sub * 128, qr)
                b.wait("gpsimd", s_act, s_act.v)
                for hs in range(4):
                    stq_qh.dma(ap2(scr_qh, ob * 512 + hs * 128, 128, co * TCH, TCH),
                               ap2(qsth, 0, 128, hs * TCH, TCH))
                    stq_ql.dma(ap2(scr_ql, ob * 512 + hs * 128, 128, co * TCH, TCH),
                               ap2(qstl, 0, 128, hs * TCH, TCH))
                qstage_stored = stq_qh.done() + stq_ql.done()
        for q_ in (stq_kh, stq_kl, stq_vh, stq_vl, stq_qh, stq_ql):
            q_.wait_all("gpsimd")
        b.run_stage("A01")
    es_a0.close()

    first_m0 = True
    for qt in range(NQT):
        es_qt = ExitStack()
        attn_oh = es_qt.enter_context(nc.sbuf_tensor(f"attn_oh_{qt}", [128, DC * QT], BF16))
        attn_ol = es_qt.enter_context(nc.sbuf_tensor(f"attn_ol_{qt}", [128, DC * QT], BF16))

        with ExitStack() as es:
            sbB = lambda n, sh, dt=F32: es.enter_context(nc.sbuf_tensor(f"{n}_{qt}", sh, dt))
            qtbh = sbB("qtbh", [128, DC * QT], BF16)
            qtbl = sbB("qtbl", [128, DC * QT], BF16)
            kgh = sbB("kgh", [128, S], BF16)
            kgl = sbB("kgl", [128, S], BF16)
            vgh = sbB("vgh", [128, NKC * HD], BF16)
            vgl = sbB("vgl", [128, NKC * HD], BF16)
            p32 = [sbB(f"p32{i}", [128, QT]) for i in range(3)]
            phi = [sbB(f"phi{i}", [128, QT], BF16) for i in range(3)]
            plo = [sbB(f"plo{i}", [128, QT], BF16) for i in range(3)]
            ptmp = sbB("ptmp", [128, QT])
            rcp = sbB("rcp", [1, QT])
            bcst = sbB("bcst", [128, QT])
            mx0 = sbB("mx0", [128, 1])
            ps = [es.enter_context(nc.psum_tensor(f"psB{i}_{qt}", [128, 512], F32)) for i in range(7)]
            psS = ps[0:3]
            psPVb, psDb = ps[3:5], ps[5:7]

            b.plan = {e: [] for e in ENGS}
            prev_tail_pv = None
            evac_done_by_bank = {}
            b.dma("sync", qtbh[:, :],
                  bass.AP(scr_qh, qt * QT, [[T, 128], [128 * T, DC], [1, QT]]), inc=s_ld[0])
            b.dma("sync", qtbl[:, :],
                  bass.AP(scr_ql, qt * QT, [[T, 128], [128 * T, DC], [1, QT]]), inc=s_ld[0])
            q_ready = s_ld[0].v
            b.wait("tensor", s_ld[0], q_ready)
            for gidx in range(NKV):
                if gidx > 0:
                    b.wait("sync", s_pe, s_pe.v)
                b.dma("sync", kgh[:, :], ap2(scr_kh, gidx * 128, 128, 0, S), inc=s_ld[1])
                b.dma("sync", kgl[:, :], ap2(scr_kl, gidx * 128, 128, 0, S), inc=s_ld[1])
                vsplit = max(1, NKC // 8)
                for srcv, dstv in ((scr_vh, vgh), (scr_vl, vgl)):
                    for i in range(0, NKC, vsplit):
                        n = min(vsplit, NKC - i)
                        b.dma("sync", bass.AP(dstv, i * HD, [[NKC * HD, 128], [HD, n], [1, HD]]),
                              bass.AP(srcv, (i * 128) * NKD + gidx * HD, [[NKD, 128], [128 * NKD, n], [1, HD]]),
                              inc=s_ld[1])
                kv_ready = s_ld[1].v
                b.wait("tensor", s_ld[1], kv_ready)
                for h in range(REP):
                    hh = gidx * REP + h
                    psPV = psPVb[hh % 2]
                    psD = psDb[hh % 2]
                    act_exp = {}
                    dve_plo = {}
                    pe_pv = {}

                    def emit_pv(kp, psPV=psPV, psD=psD):
                        pbp = kp % 3
                        if kp == 0 and evac_done_by_bank.get(hh % 2):
                            b.wait("tensor", s_dve, evac_done_by_bank[hh % 2])
                        b.wait("tensor", s_act, act_exp[kp])
                        b.wait("tensor", s_dve, dve_plo[kp])
                        vh = bass.AP(vgh, kp * HD, [[NKC * HD, 128], [1, HD]])
                        vl = bass.AP(vgl, kp * HD, [[NKC * HD, 128], [1, HD]])
                        for i, (lv, rp) in enumerate(((vh, phi[pbp]), (vh, plo[pbp]), (vl, phi[pbp]))):
                            b.mm(ap2(psPV, 0, 128, 0, QT), lv, rp[:, :],
                                 start=(kp == 0 and i == 0), stop=(kp == NKC - 1 and i == 2),
                                 inc=(s_pe if i == 2 else None))
                        b.mm(ap2(psD, 0, 1, 0, QT), ones_bf[:, 0:1], phi[pbp][:, :],
                             start=(kp == 0), stop=(kp == NKC - 1), inc=s_pe)
                        pe_pv[kp] = s_pe.v

                    for kc in range(NKC):
                        pb = kc % 3
                        if kc >= 3:
                            b.wait("tensor", s_act, act_exp[kc - 3])
                        lhs_hl = (ap2(kgh, 0, 128, kc * 128, 128),
                                  ap2(kgl, 0, 128, kc * 128, 128))
                        rhs_hl = (ap2(qtbh, 0, 128, hh * QT, QT),
                                  ap2(qtbl, 0, 128, hh * QT, QT))
                        b.mm3p(ap2(psS[kc % 3], 0, 128, 0, QT), lhs_hl, rhs_hl, inc=s_pe)
                        pe_s = s_pe.v
                        if first_m0 and kc == 0:
                            b.wait("vector", s_pe, pe_s)
                            b.vred(mx0[:, :], ap2(psS[pb], 0, 128, 0, QT), ALU.max, inc=s_dve)
                            b.drain("vector")
                            b.wait("gpsimd", s_dve, s_dve.v)
                            b.allred(mx0[:, :], mx0[:, :], ReduceOp.max, inc=s_gp)
                            b.wait("vector", s_gp, s_gp.v)
                            b.ts(bias_m0[:, :], mx0[:, :], -SCALE, ALU.mult, inc=s_dve)
                            b.drain("vector")
                            b.wait("scalar", s_dve, s_dve.v)
                        if kc < 3 and prev_tail_pv is not None:
                            b.wait("scalar", s_pe, prev_tail_pv)
                        if kc >= 3:
                            b.wait("scalar", s_dve, dve_plo[kc - 3])
                            b.wait("scalar", s_pe, pe_pv[kc - 3])
                        b.wait("scalar", s_pe, pe_s)
                        b.act(p32[pb][:, :], ap2(psS[pb], 0, 128, 0, QT), AF.Exp,
                              bias=bias_m0[:, :], scale=SCALE)
                        b.act(phi[pb][:, :], ap2(psS[pb], 0, 128, 0, QT), AF.Exp,
                              bias=bias_m0[:, :], scale=SCALE, inc=s_act)
                        act_exp[kc] = s_act.v
                        b.wait("vector", s_act, act_exp[kc])
                        b.drain("vector")
                        b.tt(plo[pb][:, :], p32[pb][:, :], phi[pb][:, :], ALU.subtract, inc=s_dve)
                        dve_plo[kc] = s_dve.v
                        b.drain("vector")
                        if kc >= 2:
                            emit_pv(kc - 2)
                    emit_pv(NKC - 2)
                    emit_pv(NKC - 1)
                    pe_tail = s_pe.v
                    b.wait("vector", s_pe, pe_tail)
                    b.recip(rcp[:, :], ap2(psD, 0, 1, 0, QT), inc=s_dve)
                    b.drain("vector")
                    b.wait("gpsimd", s_dve, s_dve.v)
                    b.pbcast(bcst[:, :], rcp[:, :], inc=s_gp)
                    b.wait("vector", s_gp, s_gp.v)
                    b.tt(ptmp[:, :], ap2(psPV, 0, 128, 0, QT), bcst[:, :], ALU.mult, inc=s_dve)
                    b.drain("vector")
                    b.wait("scalar", s_dve, s_dve.v)
                    b.acopy(ap2(attn_oh, 0, 128, hh * QT, QT), ptmp[:, :], inc=s_act)
                    b.wait("vector", s_act, s_act.v)
                    b.tt(ap2(attn_ol, 0, 128, hh * QT, QT), ptmp[:, :],
                         ap2(attn_oh, 0, 128, hh * QT, QT), ALU.subtract, inc=s_dve)
                    b.drain("vector")
                    evac_done_by_bank[hh % 2] = s_dve.v
                    prev_tail_pv = pe_tail
                    first_m0 = False
            b.run_stage(f"B{qt}")

        with ExitStack() as es:
            sbC = lambda n, sh, dt=F32: es.enter_context(nc.sbuf_tensor(f"{n}_{qt}", sh, dt))
            strm = sbC("strm", [128, DC * QT])
            xn = sbC("xn", [128, DC * QT])
            xnh = sbC("xnh", [128, DC * QT], BF16)
            xnl = sbC("xnl", [128, DC * QT], BF16)
            hbh = sbC("hbh", [128, HFC * FSUB], BF16)
            hbl = sbC("hbl", [128, HFC * FSUB], BF16)
            htmp = sbC("htmp", [128, FSUB])
            wtlh = [sbC("wtlh0", [128, 512], BF16), sbC("wtlh1", [128, 512], BF16)]
            wtll = [sbC("wtll0", [128, 512], BF16), sbC("wtll1", [128, 512], BF16)]
            wf8 = [sbC(f"wf8{i}", [128, 512], FP8) for i in range(4)]
            w2f8 = [sbC(f"w2f8{i}", [128, 512], FP8) for i in range(4)]
            sgb = sbC("sgb", [128, FSUB])
            stat = sbC("stat", [1, QT])
            stat2 = sbC("stat2", [1, QT])
            statm = sbC("statm", [1, QT])
            sqb = [sbC("sqb0", [128, QT]), sbC("sqb1", [128, QT])]
            bcstC = sbC("bcstC", [128, QT])
            bcstC2 = sbC("bcstC2", [128, QT])
            ybuf = sbC("ybuf", [128, D])
            ps = [es.enter_context(nc.psum_tensor(f"psC{i}_{qt}", [128, 512], F32)) for i in range(8)]

            b.plan = {e: [] for e in ENGS}
            b.dma("sync", strm[:, :],
                  bass.AP(x_to, qt * QT, [[T, 128], [128 * T, DC], [1, QT]]), inc=s_ld[0])
            x_ready = s_ld[0].v
            wo_ld = 0
            pe_oc = {}
            dve_oc = {}
            for oc in range(DC):
                psO = ps[oc % 2]
                if oc >= 2:
                    b.wait("tensor", s_dve, dve_oc[oc - 2])
                for hc in range(DC):
                    wsl = wo_ld % 2
                    if wo_ld >= 2:
                        b.wait("sync", s_pe, pe_oc[wo_ld - 2])
                    b.dma("sync", ap2(wtlh[wsl], 0, 128, 0, 128),
                          ap2(scr_woh, hc * 128, 128, oc * 128, 128), inc=s_ld[2 + wsl])
                    b.dma("sync", ap2(wtll[wsl], 0, 128, 0, 128),
                          ap2(scr_wol, hc * 128, 128, oc * 128, 128), inc=s_ld[2 + wsl])
                    b.wait("tensor", s_ld[2 + wsl], s_ld[2 + wsl].v)
                    lhs_hl = (ap2(wtlh[wsl], 0, 128, 0, 128), ap2(wtll[wsl], 0, 128, 0, 128))
                    rhs_hl = (ap2(attn_oh, 0, 128, hc * QT, QT), ap2(attn_ol, 0, 128, hc * QT, QT))
                    b.mm3p(ap2(psO, 0, 128, 0, QT), lhs_hl, rhs_hl,
                           first=(hc == 0), last=(hc == DC - 1), inc=s_pe)
                    pe_oc[wo_ld] = s_pe.v
                    wo_ld += 1
                b.wait("vector", s_pe, s_pe.v)
                if oc == 0:
                    b.wait("vector", s_ld[0], x_ready)
                b.drain("vector")
                b.tt(ap2(strm, 0, 128, oc * QT, QT), ap2(psO, 0, 128, 0, QT),
                     ap2(strm, 0, 128, oc * QT, QT), ALU.add, inc=s_dve)
                dve_oc[oc] = s_dve.v
                b.drain("vector")

            def layer_norm(src, dst, src_ready_dve):
                b.wait("tensor", s_dve, src_ready_dve)
                for dc in range(DC):
                    b.mm(ap2(ps[2], 0, 1, 0, QT), ones_sb[:, 0:1], ap2(src, 0, 128, dc * QT, QT),
                         start=(dc == 0), stop=(dc == DC - 1), inc=(s_pe if dc == DC - 1 else None))
                b.wait("vector", s_pe, s_pe.v)
                b.ts(stat[:, :], ap2(ps[2], 0, 1, 0, QT), 1.0 / D, ALU.mult, inc=s_dve)
                b.drain("vector")
                b.wait("scalar", s_dve, src_ready_dve)
                pe_sq = {}
                for dc in range(DC):
                    if dc >= 2:
                        b.wait("scalar", s_pe, pe_sq[dc - 2])
                    b.act(sqb[dc % 2][:, :], ap2(src, 0, 128, dc * QT, QT), AF.Square, inc=s_act)
                    b.wait("tensor", s_act, s_act.v)
                    b.mm(ap2(ps[3], 0, 1, 0, QT), ones_sb[:, 0:1], sqb[dc % 2][:, :],
                         start=(dc == 0), stop=(dc == DC - 1), inc=s_pe)
                    pe_sq[dc] = s_pe.v
                b.wait("vector", s_pe, s_pe.v)
                b.ts(stat2[:, :], ap2(ps[3], 0, 1, 0, QT), 1.0 / D, ALU.mult, inc=s_dve)
                b.drain("vector")
                b.tt(statm[:, :], stat[:, :], stat[:, :], ALU.mult, inc=s_dve)
                b.drain("vector")
                b.stt(stat2[:, :], statm[:, :], -1.0, stat2[:, :], ALU.mult, ALU.add, inc=s_dve)
                b.drain("vector")
                b.ts(stat2[:, :], stat2[:, :], 1.0, ALU.mult, s2=1e-5, op1=ALU.add, inc=s_dve)
                b.drain("vector")
                b.wait("scalar", s_dve, s_dve.v)
                b.act(statm[:, :], stat2[:, :], AF.Sqrt, inc=s_act)
                b.wait("vector", s_act, s_act.v)
                b.recip(stat2[:, :], statm[:, :], inc=s_dve)
                b.drain("vector")
                b.tt(statm[:, :], stat[:, :], stat2[:, :], ALU.mult, inc=s_dve)
                b.drain("vector")
                b.wait("gpsimd", s_dve, s_dve.v)
                b.pbcast(bcstC2[:, :], stat2[:, :], inc=s_gp)
                b.pbcast(bcstC[:, :], statm[:, :], inc=s_gp)
                b.wait("vector", s_gp, s_gp.v)
                for dc in range(DC):
                    b.tt(ap2(dst, 0, 128, dc * QT, QT), ap2(src, 0, 128, dc * QT, QT),
                         bcstC2[:, :], ALU.mult, inc=s_dve)
                    b.drain("vector")
                    b.tt(ap2(dst, 0, 128, dc * QT, QT), ap2(dst, 0, 128, dc * QT, QT),
                         bcstC[:, :], ALU.subtract, inc=s_dve)
                    b.drain("vector")
                return s_dve.v

            if g.dbg and qt == 0:
                b.wait("gpsimd", s_dve, s_dve.v)
                b.dma("gpsimd", dbg_strm[:, :], strm[:, :], inc=s_st[0])
                b.wait("gpsimd", s_st[0], s_st[0].v)
            xn_ready = layer_norm(strm, xn, s_dve.v)
            b.vcopy(xnh[:, :], xn[:, :])
            b.drain("vector")
            b.tt(xnl[:, :], xn[:, :], xnh[:, :], ALU.subtract, inc=s_dve)
            xn_split = s_dve.v
            b.drain("vector")
            if g.dbg and qt == 0:
                b.wait("gpsimd", s_dve, xn_split)
                b.dma("gpsimd", dbg_xn[:, :], xn[:, :], inc=s_st[0])
                b.wait("gpsimd", s_st[0], s_st[0].v)

            w_pe = {}
            w_ld = 0
            m3_pe = {}
            m3_ld = 0
            for fs in range(QT // FSUB):
                for hbg in range(HFC // 4):
                    ps1, ps2 = ps[0:4], ps[4:8]
                    if hbg == 0:
                        b.wait("sync", s_pe, s_pe.v)
                    if not (fs == 0 and hbg == 0):
                        b.wait("tensor", s_dve, s_dve.v)
                    for dc in range(DC):
                        wsl = dc % 4
                        lsem1, lsem2 = s_ld[2 + wsl], s_ld[8 + wsl]
                        if w_ld >= 4:
                            b.wait("sync", s_pe, w_pe[w_ld - 4])
                        b.dma("sync", ap2(wf8[wsl], 0, 128, 0, 512),
                              ap2(scr_ws[0], dc * 128, 128, hbg * 512, 512), inc=lsem1)
                        b.dma("sync", ap2(w2f8[wsl], 0, 128, 0, 512),
                              ap2(scr_ws[1], dc * 128, 128, hbg * 512, 512), inc=lsem2)
                        b.wait("tensor", lsem1, lsem1.v)
                        b.wait("tensor", lsem2, lsem2.v)
                        if dc == 0 and hbg == 0 and fs == 0:
                            b.wait("tensor", s_dve, xn_split)
                        rhs_hl = (ap2(xnh, 0, 128, dc * QT + fs * FSUB, FSUB),
                                  ap2(xnl, 0, 128, dc * QT + fs * FSUB, FSUB))
                        for j in range(4):
                            b.mm2p(ap2(ps1[j], 0, 128, 0, FSUB), ap2(wf8[wsl], 0, 128, j * 128, 128),
                                   rhs_hl, first=(dc == 0), last=(dc == DC - 1), inc=None)
                            b.mm2p(ap2(ps2[j], 0, 128, 0, FSUB), ap2(w2f8[wsl], 0, 128, j * 128, 128),
                                   rhs_hl, first=(dc == 0), last=(dc == DC - 1),
                                   inc=(s_pe if j == 3 else None))
                        w_pe[w_ld] = s_pe.v
                        w_ld += 1
                    pe_12 = s_pe.v
                    b.wait("scalar", s_pe, pe_12)
                    for j in range(4):
                        hb = hbg * 4 + j
                        if j > 0:
                            b.wait("scalar", s_dve, s_dve.v)
                        b.act(sgb[:, :], ap2(ps1[j], 0, 128, 0, FSUB), AF.Sigmoid, scale=0.5, inc=s_act)
                        b.wait("vector", s_act, s_act.v)
                        b.tt(htmp[:, :], ap2(ps2[j], 0, 128, 0, FSUB), sgb[:, :], ALU.mult, inc=s_dve)
                        b.drain("vector")
                        b.wait("scalar", s_dve, s_dve.v)
                        b.acopy(ap2(hbh, 0, 128, hb * FSUB, FSUB), htmp[:, :], inc=s_act)
                        b.wait("vector", s_act, s_act.v)
                        b.tt(ap2(hbl, 0, 128, hb * FSUB, FSUB), htmp[:, :],
                             ap2(hbh, 0, 128, hb * FSUB, FSUB), ALU.subtract, inc=s_dve)
                        b.drain("vector")
                h_act_done = s_act.v
                h_done = s_dve.v
                for ocg in range(DC // 4):
                    ps3 = ps[0:4]
                    if ocg == 0:
                        b.wait("sync", s_pe, s_pe.v)
                    if not (fs == 0 and ocg == 0):
                        b.wait("tensor", s_dve, m3_evac)
                    for hb in range(HFC):
                        wsl = hb % 4
                        lsem = s_ld[2 + wsl]
                        if m3_ld >= 4:
                            b.wait("sync", s_pe, m3_pe[m3_ld - 4])
                        b.dma("sync", ap2(wf8[wsl], 0, 128, 0, 512),
                              ap2(scr_ws[2], hb * 128, 128, ocg * 512, 512), inc=lsem)
                        b.wait("tensor", lsem, lsem.v)
                        if hb == 0 and ocg == 0:
                            b.wait("tensor", s_dve, h_done)
                            b.wait("tensor", s_act, h_act_done)
                        rhs_hl = (ap2(hbh, 0, 128, hb * FSUB, FSUB),
                                  ap2(hbl, 0, 128, hb * FSUB, FSUB))
                        for j in range(4):
                            b.mm2p(ap2(ps3[j], 0, 128, 0, FSUB), ap2(wf8[wsl], 0, 128, j * 128, 128),
                                   rhs_hl, first=(hb == 0), last=(hb == HFC - 1),
                                   inc=(s_pe if j == 3 else None))
                        m3_pe[m3_ld] = s_pe.v
                        m3_ld += 1
                    b.wait("vector", s_pe, s_pe.v)
                    for j in range(4):
                        oc = ocg * 4 + j
                        b.stt(ap2(strm, 0, 128, oc * QT + fs * FSUB, FSUB),
                              ap2(ps3[j], 0, 128, 0, FSUB), 0.25,
                              ap2(xn, 0, 128, oc * QT + fs * FSUB, FSUB),
                              ALU.mult, ALU.add, inc=s_dve)
                        b.drain("vector")
                    m3_evac = s_dve.v

            if g.dbg and qt == 0:
                b.wait("gpsimd", s_dve, s_dve.v)
                b.dma("gpsimd", dbg_ffn[:, :], strm[:, :], inc=s_st[0])
                b.wait("gpsimd", s_st[0], s_st[0].v)
            out_ready = layer_norm(strm, xn, s_dve.v)
            tr_free = [None, None]
            stq_y = StoreQ(b, [s_st[6], s_st[7]])
            yst = stq_y.done()
            for sub in range(QT // 128):
                for oc in range(DC):
                    bank = 4 + (oc % 2)
                    b.wait("tensor", s_act, tr_free[oc % 2])
                    b.wait("tensor", s_dve, out_ready)
                    b.tr(ap2(ps[bank], 0, 128, 0, 128),
                         ap2(xn, 0, 128, oc * QT + sub * 128, 128), ident_sb[:, :], inc=s_pe)
                    b.wait("scalar", s_pe, s_pe.v)
                    if sub > 0 and oc == 0:
                        for _s, _v in yst:
                            b.wait("scalar", _s, _v)
                    b.acopy(ap2(ybuf, 0, 128, oc * 128, 128), ap2(ps[bank], 0, 128, 0, 128), inc=s_act)
                    tr_free[oc % 2] = s_act.v
                b.wait("gpsimd", s_act, s_act.v)
                stq_y.dma(ap2(y_out, qt * QT + sub * 128, 128, 0, D),
                          ap2(ybuf, 0, 128, 0, D))
                yst = stq_y.done()
            stq_y.wait_all("gpsimd")
            b.run_stage(f"C{qt}")
        es_qt.close()

    es_outer.close()
    mybir.codegen_inst_isa_subclasses(nc)
    return nc



def rope_tables(S, HD):
    import jax
    import jax.numpy as jnp
    with jax.default_device(jax.devices("cpu")[0]):
        inv = 1.0 / 10000.0 ** (jnp.arange(0, HD, 2, dtype=jnp.float32) / HD)
        t = jnp.arange(S, dtype=jnp.float32)
        freqs = jnp.einsum("i,j->ij", t, inv)
        emb = jnp.concatenate([freqs, freqs], axis=-1)
        cos = np.asarray(jnp.cos(emb), dtype=np.float32)
        sin = np.asarray(jnp.sin(emb), dtype=np.float32)
    return cos, sin


def prep_core_inputs(g: Geom, core, x, Wq, Wk, Wv, Wo, W1, W2, W3):
    S, D, HD, T = g.S, g.D, g.HD, g.T
    x2 = np.ascontiguousarray(x.reshape(S, D))
    cosf, sinf = rope_tables(S, HD)
    sin_signed = sinf.copy()
    sin_signed[:, :HD // 2] *= -1.0
    sl = slice(core * T, (core + 1) * T)
    NC = g.NC
    DS = D // NC
    HFP = ((g.HF + 128 * NC - 1) // (128 * NC)) * 128 * NC
    w3p = np.zeros((HFP, D), dtype=np.float32)
    w3p[:g.HF] = W3.T
    dsl = slice(core * DS, (core + 1) * DS)
    hsl = slice(core * (HFP // NC), (core + 1) * (HFP // NC))
    return {
        "x_t": np.ascontiguousarray(x2.T),
        "x_to": np.ascontiguousarray(x2.T[:, sl]),
        "wq_s": np.ascontiguousarray(Wq.T[dsl]),
        "wk_t": np.ascontiguousarray(Wk.T),
        "wv_t": np.ascontiguousarray(Wv.T),
        "wo_s": np.ascontiguousarray(Wo.T[dsl]),
        "w1_s": np.ascontiguousarray(W1.T[dsl]),
        "w2_s": np.ascontiguousarray(W2.T[dsl]),
        "w3_s": np.ascontiguousarray(w3p[hsl]),
        "cosf": cosf, "sinf": sin_signed,
        "coso": np.ascontiguousarray(cosf[sl]), "sino": np.ascontiguousarray(sin_signed[sl]),
        "ident": np.eye(128, dtype=np.float32),
        "ones": np.ones((128, 128), dtype=np.float32),
    }




from concourse.bass_utils import run_bass_kernel_spmd


def kernel(L0_attn_x, L0_attn_Wq, L0_attn_Wk, L0_attn_Wv, L0_attn_Wo,
           L0_ffn_w1, L0_ffn_w2, L0_ffn_w3):
    g = Geom()
    x = np.ascontiguousarray(np.asarray(L0_attn_x, dtype=np.float32).reshape(g.S, g.D))
    Wq = np.asarray(L0_attn_Wq, dtype=np.float32)
    Wk = np.asarray(L0_attn_Wk, dtype=np.float32)
    Wv = np.asarray(L0_attn_Wv, dtype=np.float32)
    Wo = np.asarray(L0_attn_Wo, dtype=np.float32)
    W1 = np.asarray(L0_ffn_w1, dtype=np.float32)
    W2 = np.asarray(L0_ffn_w2, dtype=np.float32)
    W3 = np.asarray(L0_ffn_w3, dtype=np.float32)

    nc = build(g)

    x_t = np.ascontiguousarray(x.T)
    cosf, sinf = rope_tables(g.S, g.HD)
    sin_signed = sinf.copy()
    sin_signed[:, :g.HD // 2] *= -1.0
    wq_T = np.ascontiguousarray(Wq.T)
    wo_T = np.ascontiguousarray(Wo.T)
    w1_T = np.ascontiguousarray(W1.T)
    w2_T = np.ascontiguousarray(W2.T)
    NCn = g.NC
    DS = g.D // NCn
    HFP = ((g.HF + 128 * NCn - 1) // (128 * NCn)) * 128 * NCn
    HFS = HFP // NCn
    w3p = np.zeros((HFP, g.D), dtype=np.float32)
    w3p[:g.HF] = W3.T
    shared = {
        "x_t": x_t,
        "wk_t": np.ascontiguousarray(Wk.T),
        "wv_t": np.ascontiguousarray(Wv.T),
        "cosf": cosf, "sinf": sin_signed,
        "ident": np.eye(128, dtype=np.float32),
        "ones": np.ones((128, 128), dtype=np.float32),
    }
    in_maps = []
    for c in range(NCn):
        sl = slice(c * g.T, (c + 1) * g.T)
        dsl = slice(c * DS, (c + 1) * DS)
        hsl = slice(c * HFS, (c + 1) * HFS)
        m = dict(shared)
        m["x_to"] = np.ascontiguousarray(x_t[:, sl])
        m["coso"] = np.ascontiguousarray(cosf[sl])
        m["sino"] = np.ascontiguousarray(sin_signed[sl])
        m["wq_s"] = np.ascontiguousarray(wq_T[dsl])
        m["wo_s"] = np.ascontiguousarray(wo_T[dsl])
        m["w1_s"] = np.ascontiguousarray(w1_T[dsl])
        m["w2_s"] = np.ascontiguousarray(w2_T[dsl])
        m["w3_s"] = np.ascontiguousarray(w3p[hsl])
        in_maps.append(m)

    res = run_bass_kernel_spmd(nc, in_maps, core_ids=list(range(g.NC)))
    y = np.concatenate([np.asarray(res.results[c]["y"]) for c in range(g.NC)], axis=0)
    return y.reshape(1, g.S, g.D).astype(np.float32, copy=False)


# version 18
# speedup vs baseline: 1.2439x; 1.1809x over previous
import math
from contextlib import ExitStack
from dataclasses import dataclass

import numpy as np

import concourse.bass as bass
import concourse.mybir as mybir
from concourse.alu_op_type import AluOpType as ALU
from bass_rust import ReduceOp
from concourse import library_config

F32 = mybir.dt.float32
BF16 = mybir.dt.bfloat16
FP8 = mybir.dt.float8e4
AF = mybir.ActivationFunctionType
AX = mybir.AxisListType

ENGS = ("tensor", "vector", "scalar", "sync", "gpsimd")


@dataclass
class Geom:
    S: int = 8192
    D: int = 2048
    NH: int = 16
    NKV: int = 4
    HD: int = 128
    HF: int = 5632
    NC: int = 8
    TCH: int = 512
    QT: int = 512
    FSUB: int = 256
    dbg: bool = False

    @property
    def T(self): return self.S // self.NC
    @property
    def DC(self): return self.D // 128
    @property
    def NKD(self): return self.NKV * self.HD
    @property
    def HFC(self): return self.HF // 128
    @property
    def REP(self): return self.NH // self.NKV
    @property
    def NKC(self): return self.S // 128
    @property
    def NWELEM(self): return self.D * self.HF


class SemC:
    def __init__(self, nc, name, step):
        self.h = nc.alloc_semaphore(name=name)
        self.step = step
        self.v = 0

    def inc(self):
        self.v += self.step
        return self.v


class Builder:
    def __init__(self, g: Geom):
        self.g = g
        self.nc = bass.Bass("TRN2", target_bir_lowering=False)
        self.plan = None

    def L(self, eng, f):
        self.plan[eng].append(f)

    def wait(self, eng, sem: SemC, val=None):
        v = sem.v if val is None else val
        if v is not None and v > 0:
            self.L(eng, lambda e, _s=sem.h, _v=v: e.wait_ge(_s, _v))

    def drain(self, eng):
        self.L(eng, lambda e: e.drain())

    def _inc(self, instr, sem):
        if sem is not None:
            instr.then_inc(sem.h, sem.step)

    def mm(self, out, lhsT, rhs, start, stop, inc=None):
        if inc is not None: inc.inc()
        self.L("tensor", lambda e, _o=out, _l=lhsT, _r=rhs, _a=start, _b=stop, _s=inc:
               self._inc(e.matmul(_o, _l, _r, start=_a, stop=_b), _s))

    def mm3p(self, out, lhs_hl, rhs_hl, inc=None, first=True, last=True):
        combos = ((0, 0), (0, 1), (1, 0))
        for i, (a, c) in enumerate(combos):
            self.mm(out, lhs_hl[a], rhs_hl[c],
                    start=(first and i == 0), stop=(last and i == 2),
                    inc=(inc if i == 2 else None))

    def mm2p(self, out, lhs_exact, rhs_hl, inc=None, first=True, last=True):
        for i in range(2):
            self.mm(out, lhs_exact, rhs_hl[i],
                    start=(first and i == 0), stop=(last and i == 1),
                    inc=(inc if i == 1 else None))

    def tr(self, out, in_, ident, inc=None):
        if inc is not None: inc.inc()
        self.L("tensor", lambda e, _o=out, _i=in_, _d=ident, _s=inc:
               self._inc(e.transpose(_o, _i, _d), _s))

    def act(self, out, in_, func, bias=None, scale=None, accum=None, inc=None):
        if inc is not None: inc.inc()
        kw = {}
        if bias is not None: kw["bias"] = bias
        if scale is not None: kw["scale"] = scale
        if accum is not None: kw["accum_out"] = accum
        self.L("scalar", lambda e, _o=out, _i=in_, _f=func, _k=kw, _s=inc:
               self._inc(e.activation(out=_o, in_=_i, func=_f, **_k), _s))

    def acopy(self, out, in_, inc=None):
        if inc is not None: inc.inc()
        self.L("scalar", lambda e, _o=out, _i=in_, _s=inc:
               self._inc(e.copy(_o, _i), _s))

    def tt(self, out, in0, in1, op, inc=None):
        if inc is not None: inc.inc()
        self.L("vector", lambda e, _o=out, _a=in0, _b=in1, _p=op, _s=inc:
               self._inc(e.tensor_tensor(_o, _a, _b, op=_p), _s))

    def ts(self, out, in0, s1, op0, s2=None, op1=None, inc=None):
        if inc is not None: inc.inc()
        kw = dict(scalar2=s2)
        if op1 is not None: kw["op1"] = op1
        self.L("vector", lambda e, _o=out, _a=in0, _1=s1, _p=op0, _k=kw, _s=inc:
               self._inc(e.tensor_scalar(out=_o, in0=_a, scalar1=_1, op0=_p, **_k), _s))

    def stt(self, out, in0, scalar, in1, op0, op1, inc=None):
        if inc is not None: inc.inc()
        self.L("vector", lambda e, _o=out, _a=in0, _c=scalar, _b=in1, _p=op0, _q=op1, _s=inc:
               self._inc(e.scalar_tensor_tensor(out=_o, in0=_a, scalar=_c, in1=_b, op0=_p, op1=_q), _s))

    def vcopy(self, out, in_, inc=None):
        if inc is not None: inc.inc()
        self.L("vector", lambda e, _o=out, _i=in_, _s=inc:
               self._inc(e.tensor_copy(_o, _i), _s))

    def vred(self, out, in_, op, absval=False, inc=None):
        if inc is not None: inc.inc()
        self.L("vector", lambda e, _o=out, _i=in_, _p=op, _a=absval, _s=inc:
               self._inc(e.tensor_reduce(_o, _i, axis=AX.X, op=_p, apply_absolute_value=_a), _s))

    def recip(self, out, in_, inc=None):
        if inc is not None: inc.inc()
        self.L("vector", lambda e, _o=out, _i=in_, _s=inc:
               self._inc(e.reciprocal(_o, _i), _s))

    def dma(self, queue, out, in_, inc=None):
        if inc is not None: inc.inc()
        self.L(queue, lambda e, _o=out, _i=in_, _s=inc:
               self._inc(e.dma_start(_o, _i), _s))

    def pbcast(self, out, in_, inc=None):
        if inc is not None: inc.inc()
        self.L("gpsimd", lambda e, _o=out, _i=in_, _s=inc:
               self._inc(e.partition_broadcast(_o, _i), _s))

    def allred(self, out, in_, op, inc=None):
        if inc is not None: inc.inc()
        self.L("gpsimd", lambda e, _o=out, _i=in_, _p=op, _s=inc:
               self._inc(e.partition_all_reduce(_o, _i, channels=128, reduce_op=_p), _s))

    def run_stage(self, name):
        plan = self.plan
        with self.nc.named_scope(name):
            with self.nc.Block() as block:
                for eng in ENGS:
                    if plan[eng]:
                        def body(e, _th=tuple(plan[eng])):
                            for f in _th:
                                f(e)
                        getattr(block, eng)(body)
        self.plan = None


class StoreQ:
    def __init__(self, b, sems, prewait=False):
        self.b = b
        self.sems = sems
        self.i = 0
        self.prewait = prewait

    def dma(self, out, in_):
        b, sems = self.b, self.sems
        s = sems[self.i % len(sems)]
        if self.i >= len(sems) or self.prewait:
            b.wait("gpsimd", s, s.v)
        b.dma("gpsimd", out, in_, inc=s)
        self.i += 1

    def done(self):
        return [(s, s.v) for s in self.sems]

    def wait_all(self, eng):
        for s, v in self.done():
            self.b.wait(eng, s, v)


def ap2(t, p0, np_, c0, nc_):
    return t[p0:p0 + np_, c0:c0 + nc_]


def make_self_cc(nc_cores):
    def self_cc(e, kind, op, ins, outs, sem):
        e.collective_compute(kind, op, replica_groups=[list(range(nc_cores))],
                             ins=ins, outs=outs).then_inc(sem.h, sem.step)
    return self_cc


def build(g: Geom):
    b = Builder(g)
    nc = b.nc
    S, D, HD, HF, T = g.S, g.D, g.HD, g.HF, g.T
    DC, NKD, NKV, HFC, REP, NKC = g.DC, g.NKD, g.NKV, g.HFC, g.REP, g.NKC
    TCH, QT, FSUB = g.TCH, g.QT, g.FSUB
    NCH = S // TCH
    OCH = T // TCH
    A0_UNITS_PER_CHUNK = -(-((D // 128) * (HF // 512) * 4 + (HF // 128) * (D // 512) * 2 + (D // 128) * (D // 512) * 2 + 8) // (S // TCH))
    NQT = T // QT
    NSUB = TCH // 128
    SCALE = 1.0 / math.sqrt(HD)
    self_cc = make_self_cc(g.NC)

    x_t = nc.dram_tensor("x_t", [D, S], F32, kind="ExternalInput")
    x_to = nc.dram_tensor("x_to", [D, T], F32, kind="ExternalInput")
    DS = D // g.NC
    HFP = ((HF + 128 * g.NC - 1) // (128 * g.NC)) * 128 * g.NC
    HFS = HFP // g.NC
    wq_s = nc.dram_tensor("wq_s", [DS, D], F32, kind="ExternalInput")
    NSH = g.NC // NKV
    S_KV = S // NSH
    x_kv = nc.dram_tensor("x_kv", [D, S_KV], F32, kind="ExternalInput")
    cos_kv = nc.dram_tensor("cos_kv", [S_KV, HD], F32, kind="ExternalInput")
    sin_kv = nc.dram_tensor("sin_kv", [S_KV, HD], F32, kind="ExternalInput")
    wkv_s = nc.dram_tensor("wkv_s", [D, 256], F32, kind="ExternalInput")
    wo_s = nc.dram_tensor("wo_s", [DS, D], F32, kind="ExternalInput")
    w1_s = nc.dram_tensor("w1_s", [DS, HF], F32, kind="ExternalInput")
    w2_s = nc.dram_tensor("w2_s", [DS, HF], F32, kind="ExternalInput")
    w3_s = nc.dram_tensor("w3_s", [HFS, D], F32, kind="ExternalInput")
    cosf = nc.dram_tensor("cosf", [S, HD], F32, kind="ExternalInput")
    sinf = nc.dram_tensor("sinf", [S, HD], F32, kind="ExternalInput")
    coso = nc.dram_tensor("coso", [T, HD], F32, kind="ExternalInput")
    sino = nc.dram_tensor("sino", [T, HD], F32, kind="ExternalInput")
    ident_in = nc.dram_tensor("ident", [128, 128], F32, kind="ExternalInput")
    ones_in = nc.dram_tensor("ones", [128, 128], F32, kind="ExternalInput")
    y_out = nc.dram_tensor("y", [T, D], F32, kind="ExternalOutput")

    cc_kh = nc.dram_tensor("cc_kh", [128, S_KV], BF16)
    cc_kl = nc.dram_tensor("cc_kl", [128, S_KV], BF16)
    cc_vh = nc.dram_tensor("cc_vh", [S_KV, 128], BF16)
    cc_vl = nc.dram_tensor("cc_vl", [S_KV, 128], BF16)
    scr_kh = nc.dram_tensor("scr_kh", [g.NC * 128, S_KV], BF16, addr_space=("Shared" if g.NC > 4 else "Local"))
    scr_kl = nc.dram_tensor("scr_kl", [g.NC * 128, S_KV], BF16, addr_space=("Shared" if g.NC > 4 else "Local"))
    scr_vh = nc.dram_tensor("scr_vh", [g.NC * S_KV, 128], BF16, addr_space=("Shared" if g.NC > 4 else "Local"))
    scr_vl = nc.dram_tensor("scr_vl", [g.NC * S_KV, 128], BF16, addr_space=("Shared" if g.NC > 4 else "Local"))
    scr_qh = nc.dram_tensor("scr_qh", [D, T], BF16)
    scr_ql = nc.dram_tensor("scr_ql", [D, T], BF16)
    CCSP = "Shared" if g.NC > 4 else "Local"
    cc_ws = [nc.dram_tensor(f"cc_w{i}", [DS, HF] if i < 2 else [HFS, D], FP8) for i in range(3)]
    scr_ws = [nc.dram_tensor(f"scr_w{i}", [D, HF] if i < 2 else [HFP, D], FP8, addr_space=CCSP) for i in range(3)]
    cc_wsp = [nc.dram_tensor(f"cc_wsp{i}", [DS, D], BF16) for i in range(4)]
    scr_wqh = nc.dram_tensor("scr_wqh", [D, D], BF16, addr_space=CCSP)
    scr_wql = nc.dram_tensor("scr_wql", [D, D], BF16, addr_space=CCSP)
    scr_woh = nc.dram_tensor("scr_woh", [D, D], BF16, addr_space=CCSP)
    scr_wol = nc.dram_tensor("scr_wol", [D, D], BF16, addr_space=CCSP)
    cc_g_in = nc.dram_tensor("cc_g_in", [1, 3], F32)
    cc_g_out = nc.dram_tensor("cc_g_out", [1, 3], F32)

    if g.dbg:
        dbg_attn = nc.dram_tensor("dbg_attn", [128, DC * QT], BF16, kind="ExternalOutput")
        dbg_strm = nc.dram_tensor("dbg_strm", [128, DC * QT], F32, kind="ExternalOutput")
        dbg_xn = nc.dram_tensor("dbg_xn", [128, DC * QT], F32, kind="ExternalOutput")
        dbg_ffn = nc.dram_tensor("dbg_ffn", [128, DC * QT], F32, kind="ExternalOutput")

    sem = lambda n, s: SemC(nc, n, s)
    s_pe = sem("pe", 1)
    s_act = sem("act", 1)
    s_dve = sem("dve", 1)
    s_gp = sem("gp", 1)
    s_ld = [sem(f"ld{i}", 16) for i in range(12)]
    s_st = [sem(f"st{i}", 16) for i in range(16)]

    es_outer = ExitStack()
    sb = lambda n, sh, dt=F32: es_outer.enter_context(nc.sbuf_tensor(n, sh, dt))

    ident_sb = sb("identsb", [128, 128])
    ones_sb = sb("onessb", [128, 128])
    ones_bf = sb("onesbf", [128, 2], BF16)
    ident_bf = sb("identbf", [128, 128], BF16)
    bias_m0 = sb("bias_m0", [128, 1])
    hthr = sb("hthr", [128, 6])

    b.plan = {e: [] for e in ENGS}
    b.dma("sync", ident_sb[:, :], ident_in[:, :], inc=s_ld[3])
    b.dma("sync", ones_sb[:, :], ones_in[:, :], inc=s_ld[3])
    consts_ready = s_ld[3].v
    b.wait("vector", s_ld[3], consts_ready)
    b.vcopy(ones_bf[:, 0:1], ones_sb[:, 0:1], inc=s_dve)
    b.drain("vector")
    b.vcopy(ident_bf[:, :], ident_sb[:, :], inc=s_dve)
    b.drain("vector")
    b.L("gpsimd", lambda e: e.load_library(library_config.attnmlp))

    es_a0 = ExitStack()
    if True:
        sbA = lambda n, sh, dt=F32: es_a0.enter_context(nc.sbuf_tensor(n, sh, dt))

        def colchunks(width):
            cw = width
            while cw > 2048:
                assert cw % 2 == 0
                cw //= 2
            return cw, width // cw
        cw1, nc1 = colchunks(HF)
        cw3, nc3 = colchunks(D)
        WMAX = max(cw1, cw3, D)
        g_sb = es_a0.enter_context(nc.sbuf_tensor("g_sb", [128, 4], F32))
        w_in = [sbA("w_in0", [128, WMAX]), sbA("w_in1", [128, WMAX])]
        s1b = [sbA("s1b0", [128, WMAX]), sbA("s1b1", [128, WMAX])]
        s2b = [sbA("s2b0", [128, WMAX]), sbA("s2b1", [128, WMAX])]
        wqb = [sbA("wqb0", [128, WMAX], FP8), sbA("wqb1", [128, WMAX], FP8)]
        wsph = [sbA("wsph0", [128, D], BF16), sbA("wsph1", [128, D], BF16)]
        wspl = [sbA("wspl0", [128, D], BF16), sbA("wspl1", [128, D], BF16)]
        n_t1 = (D // 128) * nc1 * 2 + (HF // 128) * nc3
        accb = sbA("accb", [128, n_t1 + 8])
        gtot = sbA("gtot", [128, 3])

        wsrc = [w1_s, w2_s, w3_s]
        wtiles = [[(r, c, cw1) for r in range(DS // 128) for c in range(nc1)],
                  [(r, c, cw1) for r in range(DS // 128) for c in range(nc1)],
                  [(r, c, cw3) for r in range(HFS // 128) for c in range(nc3)]]

    def a0_work():
            col = 0
            wcol = []
            dve_done = {}
            ti = 0
            for wi, tiles in enumerate(wtiles):
                wcol.append(col)
                for (r, cc, width) in tiles:
                    sl = ti % 2
                    if ti >= 2:
                        b.wait("sync", s_act, dve_done[ti - 2])
                    b.dma("sync", ap2(w_in[sl], 0, 128, 0, width),
                          ap2(wsrc[wi], r * 128, 128, cc * width, width), inc=s_ld[6 + sl])
                    b.wait("scalar", s_ld[6 + sl], s_ld[6 + sl].v)
                    b.act(ap2(s1b[sl], 0, 128, 0, width), ap2(w_in[sl], 0, 128, 0, width),
                          AF.Abs, accum=accb[:, col:col + 1], inc=s_act)
                    dve_done[ti] = s_act.v
                    col += 1
                    ti += 1
                    yield
            b.wait("vector", s_act, s_act.v)
            for wi in range(3):
                c0, c1 = wcol[wi], wcol[wi] + len(wtiles[wi])
                b.vred(gtot[:, wi:wi + 1], accb[:, c0:c1], ALU.add, inc=s_dve)
                b.drain("vector")
            b.wait("gpsimd", s_dve)
            for wi in range(3):
                b.allred(gtot[:, wi:wi + 1], gtot[:, wi:wi + 1], ReduceOp.add, inc=s_gp)
            b.wait("gpsimd", s_gp, s_gp.v)
            b.dma("gpsimd", cc_g_in[0:1, 0:3], gtot[0:1, 0:3], inc=s_st[14])
            b.wait("gpsimd", s_st[14], s_st[14].v)
            b.L("gpsimd", lambda e: self_cc(e, "AllReduce", ALU.add, [cc_g_in[:, :]], [cc_g_out[:, :]], s_gp))
            s_gp.inc()
            b.wait("gpsimd", s_gp, s_gp.v)
            b.dma("gpsimd", g_sb[0:1, 0:3], cc_g_out[0:1, 0:3], inc=s_st[14])
            b.wait("gpsimd", s_st[14], s_st[14].v)
            b.pbcast(gtot[:, 0:3], g_sb[0:1, 0:3], inc=s_gp)
            b.wait("vector", s_gp)
            for wi in range(3):
                nw = float(g.NWELEM)
                b.ts(hthr[:, 2 * wi:2 * wi + 1], gtot[:, wi:wi + 1], -0.5 / nw, ALU.mult,
                     s2=-0.5e-8, op1=ALU.add, inc=s_dve)
                b.ts(hthr[:, 2 * wi + 1:2 * wi + 2], gtot[:, wi:wi + 1], 0.5 / nw, ALU.mult,
                     s2=0.5e-8, op1=ALU.add, inc=s_dve)
                b.drain("vector")
            hready = s_dve.v
            b.wait("scalar", s_dve, hready)

            b.wait("sync", s_dve, hready)
            stq = StoreQ(b, [s_st[8], s_st[9]])
            ti = 0
            act_done = {}
            dve_done = {}
            st_done = {}
            for wi, tiles in enumerate(wtiles):
                for (r, cc, width) in tiles:
                    sl = ti % 2
                    if ti >= 2:
                        b.wait("sync", s_act, act_done[ti - 2])
                    b.dma("sync", ap2(w_in[sl], 0, 128, 0, width),
                          ap2(wsrc[wi], r * 128, 128, cc * width, width), inc=s_ld[6 + sl])
                    b.wait("scalar", s_ld[6 + sl], s_ld[6 + sl].v)
                    if ti >= 2:
                        b.wait("scalar", s_dve, dve_done[ti - 2])
                    b.act(ap2(s1b[sl], 0, 128, 0, width), ap2(w_in[sl], 0, 128, 0, width),
                          AF.Sign, bias=hthr[:, 2 * wi:2 * wi + 1], inc=s_act)
                    b.act(ap2(s2b[sl], 0, 128, 0, width), ap2(w_in[sl], 0, 128, 0, width),
                          AF.Sign, bias=hthr[:, 2 * wi + 1:2 * wi + 2], inc=s_act)
                    act_done[ti] = s_act.v
                    b.wait("vector", s_act, act_done[ti])
                    if ti >= 2:
                        for _s, _v in st_done[ti - 2]:
                            b.wait("vector", _s, _v)
                    b.drain("vector")
                    b.tt(ap2(wqb[sl], 0, 128, 0, width), ap2(s1b[sl], 0, 128, 0, width),
                         ap2(s2b[sl], 0, 128, 0, width), ALU.add, inc=s_dve)
                    dve_done[ti] = s_dve.v
                    b.wait("gpsimd", s_dve, s_dve.v)
                    stq.dma(ap2(cc_ws[wi], r * 128, 128, cc * width, width),
                            ap2(wqb[sl], 0, 128, 0, width))
                    st_done[ti] = stq.done()
                    ti += 1
                    yield
            stq.wait_all("gpsimd")
            for wi in range(3):
                b.L("gpsimd", lambda e, _w=wi: self_cc(e, "AllGather", ALU.bypass,
                    [cc_ws[_w][:, :]], [scr_ws[_w][:, :]], s_gp))
                s_gp.inc()

            b.wait("sync", s_dve, s_dve.v)
            stq_h = StoreQ(b, [s_st[10], s_st[11]])
            stq_l = StoreQ(b, [s_st[12], s_st[13]])
            sp_src = [(wq_s, cc_wsp[0], cc_wsp[1]), (wo_s, cc_wsp[2], cc_wsp[3])]
            ti = 0
            dve_done = {}
            st_done = {}
            for (srcw, dsth, dstl) in sp_src:
                for r in range(DS // 128):
                    sl = ti % 2
                    if ti >= 2:
                        b.wait("sync", s_dve, dve_done[ti - 2])
                    b.dma("sync", ap2(w_in[sl], 0, 128, 0, D),
                          ap2(srcw, r * 128, 128, 0, D), inc=s_ld[6 + sl])
                    b.wait("vector", s_ld[6 + sl], s_ld[6 + sl].v)
                    if ti >= 2:
                        for _s, _v in st_done[ti - 2]:
                            b.wait("vector", _s, _v)
                    b.drain("vector")
                    b.vcopy(ap2(wsph[sl], 0, 128, 0, D), ap2(w_in[sl], 0, 128, 0, D))
                    b.drain("vector")
                    b.tt(ap2(wspl[sl], 0, 128, 0, D), ap2(w_in[sl], 0, 128, 0, D),
                         ap2(wsph[sl], 0, 128, 0, D), ALU.subtract, inc=s_dve)
                    dve_done[ti] = s_dve.v
                    b.drain("vector")
                    b.wait("gpsimd", s_dve, s_dve.v)
                    stq_h.dma(ap2(dsth, r * 128, 128, 0, D), ap2(wsph[sl], 0, 128, 0, D))
                    stq_l.dma(ap2(dstl, r * 128, 128, 0, D), ap2(wspl[sl], 0, 128, 0, D))
                    st_done[ti] = stq_h.done() + stq_l.done()
                    ti += 1
                    yield
            stq_h.wait_all("gpsimd")
            stq_l.wait_all("gpsimd")
            for _ci, _co in ((cc_wsp[0], scr_wqh), (cc_wsp[1], scr_wql),
                             (cc_wsp[2], scr_woh), (cc_wsp[3], scr_wol)):
                b.L("gpsimd", lambda e, _a=_ci, _b=_co: self_cc(e, "AllGather", ALU.bypass,
                    [_a[:, :]], [_b[:, :]], s_gp))
                s_gp.inc()
            b.wait("gpsimd", s_gp, s_gp.v)
            b.scr_wq_ready = [(s_gp, s_gp.v)]
            yield


    a0_gen = a0_work()
    for _ in a0_gen:
        pass
    b.run_stage("A0")
    es_a0.close()

    with ExitStack() as es:
        sbA = lambda n, sh, dt=F32: es.enter_context(nc.sbuf_tensor(n, sh, dt))
        xt32 = sbA("xt32", [128, DC * TCH])
        xth = [sbA("xth0", [128, DC * TCH], BF16), sbA("xth1", [128, DC * TCH], BF16)]
        xtl = [sbA("xtl0", [128, DC * TCH], BF16), sbA("xtl1", [128, DC * TCH], BF16)]
        wkv32 = sbA("wkv32", [128, max(NKD, 256)])
        wkvh = sbA("wkvh", [128, DC * 256], BF16)
        wkvl = sbA("wkvl", [128, DC * 256], BF16)
        cs = [sbA("cs0", [128, NSUB * 2 * HD])]
        t1b = sbA("t1b", [128, max(NKD, 512)])
        t2b = sbA("t2b", [128, max(NKD, 512)])
        kpb = sbA("kpb", [128, max(NKD, 512)])
        kpbh = sbA("kpbh", [128, max(NKD, 512)], BF16)
        kpbl = sbA("kpbl", [128, max(NKD, 512)], BF16)
        ksth = sbA("ksth", [128, max(NKV, 4) * TCH], BF16)
        kstl = sbA("kstl", [128, max(NKV, 4) * TCH], BF16)
        vsth = sbA("vsth", [128, NSUB * 128], BF16)
        vstl = sbA("vstl", [128, NSUB * 128], BF16)
        wqtlh = [sbA("wqtlh0", [128, 512], BF16), sbA("wqtlh1", [128, 512], BF16)]
        wqtll = [sbA("wqtll0", [128, 512], BF16), sbA("wqtll1", [128, 512], BF16)]
        qsth, qstl = ksth, kstl

        ps = [es.enter_context(nc.psum_tensor(f"psA{i}", [128, 512], F32)) for i in range(4)]
        psKb, psVb = ps[0:2], ps[2:4]
        psQ = ps[0:4]
        psT = [es.enter_context(nc.psum_tensor(f"psAT{i}", [128, 512], BF16)) for i in range(2)]

        b.plan = {e: [] for e in ENGS}
        for dc in range(DC):
            b.dma("sync", wkv32[:, 0:256], ap2(wkv_s, dc * 128, 128, 0, 256), inc=s_ld[2])
            b.wait("vector", s_ld[2], s_ld[2].v)
            if dc > 0:
                b.drain("vector")
            co0 = dc * 256
            b.vcopy(ap2(wkvh, 0, 128, co0, 256), wkv32[:, 0:256])
            b.drain("vector")
            b.tt(ap2(wkvl, 0, 128, co0, 256), wkv32[:, 0:256],
                 ap2(wkvh, 0, 128, co0, 256), ALU.subtract, inc=s_dve)
            b.wait("sync", s_dve, s_dve.v)
        wkv_ready = s_dve.v

        tr_bank_free = [None, None]
        kpb_tr_pe = None
        xt32_free_dve = 0
        cs_free_dve = 0
        rope2_done = {}
        vevac2_done = {}
        pe_after_chunk = {}
        stq_kh = StoreQ(b, [s_st[0], s_st[1]])
        stq_kl = StoreQ(b, [s_st[2], s_st[3]])
        stq_vh = StoreQ(b, [s_st[4], s_st[5]])
        stq_vl = StoreQ(b, [s_st[6], s_st[7]])

        def load_xt_cs(sl, src_x, src_cos, src_sin, col0, slot_free_pe):
            nonlocal xt32_free_dve, cs_free_dve
            if cs_free_dve:
                b.wait("sync", s_dve, cs_free_dve)
            b.dma("sync", bass.AP(cs[0], 0, [[NSUB * 2 * HD, 128], [2 * HD, NSUB], [1, HD]]),
                  bass.AP(src_cos, col0 * HD, [[HD, 128], [128 * HD, NSUB], [1, HD]]), inc=s_ld[sl])
            b.dma("sync", bass.AP(cs[0], HD, [[NSUB * 2 * HD, 128], [2 * HD, NSUB], [1, HD]]),
                  bass.AP(src_sin, col0 * HD, [[HD, 128], [128 * HD, NSUB], [1, HD]]), inc=s_ld[sl])
            if xt32_free_dve:
                b.wait("sync", s_dve, xt32_free_dve)
            b.dma("sync", xt32[:, :],
                  bass.AP(src_x, col0, [[src_x.shape[1], 128], [128 * src_x.shape[1], DC], [1, TCH]]),
                  inc=s_ld[sl])
            b.wait("vector", s_ld[sl], s_ld[sl].v)
            if slot_free_pe:
                b.wait("vector", s_pe, slot_free_pe)
            b.drain("vector")
            b.vcopy(xth[sl][:, :], xt32[:, :])
            b.drain("vector")
            b.tt(xtl[sl][:, :], xt32[:, :], xth[sl][:, :], ALU.subtract, inc=s_dve)
            xt32_free_dve = s_dve.v
            b.drain("vector")
            return s_ld[sl].v, s_dve.v

        def rope(psrc, nheads, width, sl, sub, dst):
            base = sub * 2 * HD
            cosap = bass.AP(cs[0], base, [[NSUB * 2 * HD, 128], [0, nheads], [1, HD]])
            sin4 = bass.AP(cs[0], base + HD, [[NSUB * 2 * HD, 128], [0, nheads], [HD // 2, 2], [1, HD // 2]])
            kap = bass.AP(psrc, 0, [[512, 128], [HD, nheads], [1, HD]])
            kshift = bass.AP(psrc, HD // 2, [[512, 128], [HD, nheads], [-(HD // 2), 2], [1, HD // 2]])
            t1ap = bass.AP(t1b, 0, [[t1b.shape[1], 128], [HD, nheads], [1, HD]])
            t2ap = bass.AP(t2b, 0, [[t2b.shape[1], 128], [HD, nheads], [HD // 2, 2], [1, HD // 2]])
            b.tt(t1ap, kap, cosap, ALU.mult)
            b.tt(t2ap, kshift, sin4, ALU.mult, inc=s_dve)
            b.drain("vector")
            b.tt(ap2(dst, 0, 128, 0, width), ap2(t1b, 0, 128, 0, width),
                 ap2(t2b, 0, 128, 0, width), ALU.add, inc=s_dve)
            b.drain("vector")
            b.vcopy(ap2(kpbh, 0, 128, 0, width), ap2(dst, 0, 128, 0, width))
            b.drain("vector")
            b.tt(ap2(kpbl, 0, 128, 0, width), ap2(dst, 0, 128, 0, width),
                 ap2(kpbh, 0, 128, 0, width), ALU.subtract, inc=s_dve)
            out = s_dve.v
            b.drain("vector")
            return out

        def transpose_pair(nslices, dsth, dstl, dcol_fn, ready_dve):
            nonlocal kpb_tr_pe
            for i in range(nslices):
                for which, (srcb, dstb) in enumerate(((kpbh, dsth), (kpbl, dstl))):
                    bank = (2 * i + which) % 2
                    b.wait("tensor", s_act, tr_bank_free[bank])
                    b.wait("tensor", s_dve, ready_dve)
                    b.tr(ap2(psT[bank], 0, 128, 0, 128),
                         ap2(srcb, 0, 128, i * HD, HD), ident_bf[:, :], inc=s_pe)
                    b.wait("scalar", s_pe, s_pe.v)
                    b.acopy(ap2(dstb, 0, 128, dcol_fn(i), 128),
                            ap2(psT[bank], 0, 128, 0, 128), inc=s_act)
                    tr_bank_free[bank] = s_act.v
            kpb_tr_pe = s_pe.v

        NCHK = S_KV // TCH
        chunk_loaded = {}
        chunk_loaded[0] = load_xt_cs(0, x_kv, cos_kv, sin_kv, 0, 0)
        for c in range(NCHK):
            sl = c % 2
            xt_ready, xs_done = chunk_loaded[c]
            b.wait("tensor", s_dve, xs_done)

            for sub in range(NSUB):
                si = (c * NSUB + sub) % 2
                psKV = psKb[si]
                idx = c * NSUB + sub
                if idx >= 2:
                    b.wait("tensor", s_dve, rope2_done.get(idx - 2, 0))
                    b.wait("tensor", s_dve, vevac2_done.get(idx - 2, 0))
                for dc in range(DC):
                    lhs_hl = (ap2(xth[sl], 0, 128, dc * TCH + sub * 128, 128),
                              ap2(xtl[sl], 0, 128, dc * TCH + sub * 128, 128))
                    rhs_hl = (ap2(wkvh, 0, 128, dc * 256, 256),
                              ap2(wkvl, 0, 128, dc * 256, 256))
                    b.mm3p(ap2(psKV, 0, 128, 0, 256), lhs_hl, rhs_hl,
                           first=(dc == 0), last=(dc == DC - 1),
                           inc=(s_pe if dc == DC - 1 else None))
                pe_kv = s_pe.v
                b.wait("scalar", s_pe, pe_kv)
                if c > 0:
                    for _s, _v in vstage_stored:
                        b.wait("scalar", _s, _v)
                b.acopy(ap2(vsth, 0, 128, sub * 128, 128), ap2(psKV, 0, 128, 128, 128), inc=s_act)
                b.wait("vector", s_act, s_act.v)
                if c > 0:
                    for _s, _v in vstage_stored:
                        b.wait("vector", _s, _v)
                b.drain("vector")
                b.tt(ap2(vstl, 0, 128, sub * 128, 128), ap2(psKV, 0, 128, 128, 128),
                     ap2(vsth, 0, 128, sub * 128, 128), ALU.subtract, inc=s_dve)
                vevac2_done[idx] = s_dve.v
                b.drain("vector")
                b.wait("vector", s_ld[sl], xt_ready)
                b.wait("vector", s_pe, pe_kv)
                if c > 0:
                    for _s, _v in kstage_stored:
                        b.wait("vector", _s, _v)
                if kpb_tr_pe is not None:
                    b.wait("vector", s_pe, kpb_tr_pe)
                prev_rope = rope(psKV, 1, 128, sl, sub, kpb)
                rope2_done[idx] = prev_rope
                cs_free_dve = prev_rope
                transpose_pair(1, ksth, kstl,
                               lambda i, _sub=sub: _sub * 128, prev_rope)
            pe_after_chunk[c] = s_pe.v
            b.wait("gpsimd", s_act, s_act.v)
            b.wait("gpsimd", s_dve, s_dve.v)
            stq_kh.dma(ap2(cc_kh, 0, 128, c * TCH, TCH), ap2(ksth, 0, 128, 0, TCH))
            stq_kl.dma(ap2(cc_kl, 0, 128, c * TCH, TCH), ap2(kstl, 0, 128, 0, TCH))
            kstage_stored = stq_kh.done() + stq_kl.done()
            for sub in range(NSUB):
                stq_vh.dma(ap2(cc_vh, c * TCH + sub * 128, 128, 0, 128),
                           ap2(vsth, 0, 128, sub * 128, 128))
                stq_vl.dma(ap2(cc_vl, c * TCH + sub * 128, 128, 0, 128),
                           ap2(vstl, 0, 128, sub * 128, 128))
            vstage_stored = stq_vh.done() + stq_vl.done()
            if c + 1 < NCHK:
                chunk_loaded[c + 1] = load_xt_cs((c + 1) % 2, x_kv, cos_kv, sin_kv, (c + 1) * TCH,
                                                 pe_after_chunk.get(c - 1, 0))
        for q_ in (stq_kh, stq_kl, stq_vh, stq_vl):
            q_.wait_all("gpsimd")
        for _ci, _co in ((cc_kh, scr_kh), (cc_kl, scr_kl), (cc_vh, scr_vh), (cc_vl, scr_vl)):
            b.L("gpsimd", lambda e, _a=_ci, _b=_co: self_cc(e, "AllGather", ALU.bypass,
                [_a[:, :]], [_b[:, :]], s_gp))
            s_gp.inc()
        kv_gathered = s_gp.v

        wq_pe = {}
        wq_ld = 0
        stq_qh = StoreQ(b, [s_st[8], s_st[9]], prewait=True)
        stq_ql = StoreQ(b, [s_st[0], s_st[1]], prewait=True)
        for _s, _v in b.scr_wq_ready:
            b.wait("sync", _s, _v)
        for q_ in (stq_kh, stq_kl, stq_vh, stq_vl):
            q_.wait_all("scalar")
            q_.wait_all("vector")
        for co in range(OCH):
            sl = co % 2
            b.wait("sync", s_pe, s_pe.v)
            b.wait("sync", s_dve, s_dve.v)
            xt_ready, xs_done = load_xt_cs(sl, x_to, coso, sino, co * TCH, s_pe.v)
            b.wait("tensor", s_dve, xs_done)
            for ob in range(D // 512):
                for dc in range(DC):
                    wsl = wq_ld % 2
                    if wq_ld >= 2:
                        b.wait("sync", s_pe, wq_pe[wq_ld - 2])
                    b.dma("sync", ap2(wqtlh[wsl], 0, 128, 0, 512),
                          ap2(scr_wqh, dc * 128, 128, ob * 512, 512), inc=s_ld[4 + wsl])
                    b.dma("sync", ap2(wqtll[wsl], 0, 128, 0, 512),
                          ap2(scr_wql, dc * 128, 128, ob * 512, 512), inc=s_ld[4 + wsl])
                    b.wait("tensor", s_ld[4 + wsl], s_ld[4 + wsl].v)
                    for sub in range(NSUB):
                        if dc == 0 and (ob > 0 or co > 0):
                            b.wait("tensor", s_dve, q_rope_done.get(sub, 0))
                        lhs_hl = (ap2(xth[sl], 0, 128, dc * TCH + sub * 128, 128),
                                  ap2(xtl[sl], 0, 128, dc * TCH + sub * 128, 128))
                        rhs_hl = (ap2(wqtlh[wsl], 0, 128, 0, 512),
                                  ap2(wqtll[wsl], 0, 128, 0, 512))
                        b.mm3p(ap2(psQ[sub], 0, 128, 0, 512), lhs_hl, rhs_hl,
                               first=(dc == 0), last=(dc == DC - 1),
                               inc=(s_pe if sub == NSUB - 1 else None))
                    wq_pe[wq_ld] = s_pe.v
                    wq_ld += 1
                if co == 0 and ob == 0:
                    q_rope_done = {}
                pe_q = s_pe.v
                for sub in range(NSUB):
                    b.wait("vector", s_pe, pe_q)
                    b.wait("vector", s_ld[sl], xt_ready)
                    if not (co == 0 and ob == 0):
                        for _s, _v in qstage_stored:
                            b.wait("vector", _s, _v)
                    if kpb_tr_pe is not None:
                        b.wait("vector", s_pe, kpb_tr_pe)
                    qr = rope(psQ[sub], 4, 512, sl, sub, kpb)
                    cs_free_dve = qr
                    q_rope_done[sub] = qr
                    transpose_pair(4, qsth, qstl,
                                   lambda i, _sub=sub: i * TCH + _sub * 128, qr)
                b.wait("gpsimd", s_act, s_act.v)
                for hs in range(4):
                    stq_qh.dma(ap2(scr_qh, ob * 512 + hs * 128, 128, co * TCH, TCH),
                               ap2(qsth, 0, 128, hs * TCH, TCH))
                    stq_ql.dma(ap2(scr_ql, ob * 512 + hs * 128, 128, co * TCH, TCH),
                               ap2(qstl, 0, 128, hs * TCH, TCH))
                qstage_stored = stq_qh.done() + stq_ql.done()
        for q_ in (stq_kh, stq_kl, stq_vh, stq_vl, stq_qh, stq_ql):
            q_.wait_all("gpsimd")
        b.run_stage("A01")
    es_a0.close()

    first_m0 = True
    for qt in range(NQT):
        es_qt = ExitStack()
        attn_oh = es_qt.enter_context(nc.sbuf_tensor(f"attn_oh_{qt}", [128, DC * QT], BF16))
        attn_ol = es_qt.enter_context(nc.sbuf_tensor(f"attn_ol_{qt}", [128, DC * QT], BF16))

        with ExitStack() as es:
            sbB = lambda n, sh, dt=F32: es.enter_context(nc.sbuf_tensor(f"{n}_{qt}", sh, dt))
            qtbh = sbB("qtbh", [128, DC * QT], BF16)
            qtbl = sbB("qtbl", [128, DC * QT], BF16)
            kgh = sbB("kgh", [128, S], BF16)
            kgl = sbB("kgl", [128, S], BF16)
            vgh = sbB("vgh", [128, NKC * HD], BF16)
            vgl = sbB("vgl", [128, NKC * HD], BF16)
            p32 = [sbB(f"p32{i}", [128, QT]) for i in range(3)]
            phi = [sbB(f"phi{i}", [128, QT], BF16) for i in range(3)]
            plo = [sbB(f"plo{i}", [128, QT], BF16) for i in range(3)]
            ptmp = sbB("ptmp", [128, QT])
            rcp = sbB("rcp", [1, QT])
            bcst = sbB("bcst", [128, QT])
            mx0 = sbB("mx0", [128, 1])
            ps = [es.enter_context(nc.psum_tensor(f"psB{i}_{qt}", [128, 512], F32)) for i in range(7)]
            psS = ps[0:3]
            psPVb, psDb = ps[3:5], ps[5:7]

            b.plan = {e: [] for e in ENGS}
            prev_tail_pv = None
            evac_done_by_bank = {}
            b.wait("sync", s_gp, kv_gathered)
            b.dma("sync", qtbh[:, :],
                  bass.AP(scr_qh, qt * QT, [[T, 128], [128 * T, DC], [1, QT]]), inc=s_ld[0])
            b.dma("sync", qtbl[:, :],
                  bass.AP(scr_ql, qt * QT, [[T, 128], [128 * T, DC], [1, QT]]), inc=s_ld[0])
            q_ready = s_ld[0].v
            b.wait("tensor", s_ld[0], q_ready)
            for gidx in range(NKV):
                if gidx > 0:
                    b.wait("sync", s_pe, s_pe.v)
                NSHl = g.NC // NKV
                S_KVl = S // NSHl
                for half in range(NSHl):
                    blk_ = gidx + half * NKV
                    b.dma("sync", kgh[:, half * S_KVl:(half + 1) * S_KVl],
                          ap2(scr_kh, blk_ * 128, 128, 0, S_KVl), inc=s_ld[1])
                    b.dma("sync", kgl[:, half * S_KVl:(half + 1) * S_KVl],
                          ap2(scr_kl, blk_ * 128, 128, 0, S_KVl), inc=s_ld[1])
                NKCH = S_KVl // 128
                vsplit = max(1, NKCH // 8)
                for srcv, dstv in ((scr_vh, vgh), (scr_vl, vgl)):
                    for half in range(NSHl):
                        blk_ = gidx + half * NKV
                        for i in range(0, NKCH, vsplit):
                            n = min(vsplit, NKCH - i)
                            b.dma("sync",
                                  bass.AP(dstv, (half * NKCH + i) * HD, [[NKC * HD, 128], [HD, n], [1, HD]]),
                                  bass.AP(srcv, (blk_ * S_KVl + i * 128) * 128, [[128, 128], [128 * 128, n], [1, 128]]),
                                  inc=s_ld[1])
                kv_ready = s_ld[1].v
                b.wait("tensor", s_ld[1], kv_ready)
                for h in range(REP):
                    hh = gidx * REP + h
                    psPV = psPVb[hh % 2]
                    psD = psDb[hh % 2]
                    act_exp = {}
                    dve_plo = {}
                    pe_pv = {}

                    def emit_pv(kp, psPV=psPV, psD=psD):
                        pbp = kp % 3
                        if kp == 0 and evac_done_by_bank.get(hh % 2):
                            b.wait("tensor", s_dve, evac_done_by_bank[hh % 2])
                        b.wait("tensor", s_act, act_exp[kp])
                        b.wait("tensor", s_dve, dve_plo[kp])
                        vh = bass.AP(vgh, kp * HD, [[NKC * HD, 128], [1, HD]])
                        vl = bass.AP(vgl, kp * HD, [[NKC * HD, 128], [1, HD]])
                        for i, (lv, rp) in enumerate(((vh, phi[pbp]), (vh, plo[pbp]), (vl, phi[pbp]))):
                            b.mm(ap2(psPV, 0, 128, 0, QT), lv, rp[:, :],
                                 start=(kp == 0 and i == 0), stop=(kp == NKC - 1 and i == 2),
                                 inc=(s_pe if i == 2 else None))
                        b.mm(ap2(psD, 0, 1, 0, QT), ones_bf[:, 0:1], phi[pbp][:, :],
                             start=(kp == 0), stop=(kp == NKC - 1), inc=s_pe)
                        pe_pv[kp] = s_pe.v

                    for kc in range(NKC):
                        pb = kc % 3
                        if kc >= 3:
                            b.wait("tensor", s_act, act_exp[kc - 3])
                        lhs_hl = (ap2(kgh, 0, 128, kc * 128, 128),
                                  ap2(kgl, 0, 128, kc * 128, 128))
                        rhs_hl = (ap2(qtbh, 0, 128, hh * QT, QT),
                                  ap2(qtbl, 0, 128, hh * QT, QT))
                        b.mm3p(ap2(psS[kc % 3], 0, 128, 0, QT), lhs_hl, rhs_hl, inc=s_pe)
                        pe_s = s_pe.v
                        if first_m0 and kc == 0:
                            b.wait("vector", s_pe, pe_s)
                            b.vred(mx0[:, :], ap2(psS[pb], 0, 128, 0, QT), ALU.max, inc=s_dve)
                            b.drain("vector")
                            b.wait("gpsimd", s_dve, s_dve.v)
                            b.allred(mx0[:, :], mx0[:, :], ReduceOp.max, inc=s_gp)
                            b.wait("vector", s_gp, s_gp.v)
                            b.ts(bias_m0[:, :], mx0[:, :], -SCALE, ALU.mult, inc=s_dve)
                            b.drain("vector")
                            b.wait("scalar", s_dve, s_dve.v)
                        if kc < 3 and prev_tail_pv is not None:
                            b.wait("scalar", s_pe, prev_tail_pv)
                        if kc >= 3:
                            b.wait("scalar", s_dve, dve_plo[kc - 3])
                            b.wait("scalar", s_pe, pe_pv[kc - 3])
                        b.wait("scalar", s_pe, pe_s)
                        b.act(p32[pb][:, :], ap2(psS[pb], 0, 128, 0, QT), AF.Exp,
                              bias=bias_m0[:, :], scale=SCALE)
                        b.act(phi[pb][:, :], ap2(psS[pb], 0, 128, 0, QT), AF.Exp,
                              bias=bias_m0[:, :], scale=SCALE, inc=s_act)
                        act_exp[kc] = s_act.v
                        b.wait("vector", s_act, act_exp[kc])
                        b.drain("vector")
                        b.tt(plo[pb][:, :], p32[pb][:, :], phi[pb][:, :], ALU.subtract, inc=s_dve)
                        dve_plo[kc] = s_dve.v
                        b.drain("vector")
                        if kc >= 2:
                            emit_pv(kc - 2)
                    emit_pv(NKC - 2)
                    emit_pv(NKC - 1)
                    pe_tail = s_pe.v
                    b.wait("vector", s_pe, pe_tail)
                    b.recip(rcp[:, :], ap2(psD, 0, 1, 0, QT), inc=s_dve)
                    b.drain("vector")
                    b.wait("gpsimd", s_dve, s_dve.v)
                    b.pbcast(bcst[:, :], rcp[:, :], inc=s_gp)
                    b.wait("vector", s_gp, s_gp.v)
                    b.tt(ptmp[:, :], ap2(psPV, 0, 128, 0, QT), bcst[:, :], ALU.mult, inc=s_dve)
                    b.drain("vector")
                    b.wait("scalar", s_dve, s_dve.v)
                    b.acopy(ap2(attn_oh, 0, 128, hh * QT, QT), ptmp[:, :], inc=s_act)
                    b.wait("vector", s_act, s_act.v)
                    b.tt(ap2(attn_ol, 0, 128, hh * QT, QT), ptmp[:, :],
                         ap2(attn_oh, 0, 128, hh * QT, QT), ALU.subtract, inc=s_dve)
                    b.drain("vector")
                    evac_done_by_bank[hh % 2] = s_dve.v
                    prev_tail_pv = pe_tail
                    first_m0 = False
            b.run_stage(f"B{qt}")

        with ExitStack() as es:
            sbC = lambda n, sh, dt=F32: es.enter_context(nc.sbuf_tensor(f"{n}_{qt}", sh, dt))
            strm = sbC("strm", [128, DC * QT])
            xn = sbC("xn", [128, DC * QT])
            xnh = sbC("xnh", [128, DC * QT], BF16)
            xnl = sbC("xnl", [128, DC * QT], BF16)
            hbh = sbC("hbh", [128, HFC * FSUB], BF16)
            hbl = sbC("hbl", [128, HFC * FSUB], BF16)
            htmp = sbC("htmp", [128, FSUB])
            wtlh = [sbC("wtlh0", [128, 512], BF16), sbC("wtlh1", [128, 512], BF16)]
            wtll = [sbC("wtll0", [128, 512], BF16), sbC("wtll1", [128, 512], BF16)]
            wf8 = [sbC(f"wf8{i}", [128, 512], FP8) for i in range(4)]
            w2f8 = [sbC(f"w2f8{i}", [128, 512], FP8) for i in range(4)]
            sgb = sbC("sgb", [128, FSUB])
            stat = sbC("stat", [1, QT])
            stat2 = sbC("stat2", [1, QT])
            statm = sbC("statm", [1, QT])
            sqb = [sbC("sqb0", [128, QT]), sbC("sqb1", [128, QT])]
            bcstC = sbC("bcstC", [128, QT])
            bcstC2 = sbC("bcstC2", [128, QT])
            ybuf = sbC("ybuf", [128, D])
            ps = [es.enter_context(nc.psum_tensor(f"psC{i}_{qt}", [128, 512], F32)) for i in range(8)]

            b.plan = {e: [] for e in ENGS}
            b.dma("sync", strm[:, :],
                  bass.AP(x_to, qt * QT, [[T, 128], [128 * T, DC], [1, QT]]), inc=s_ld[0])
            x_ready = s_ld[0].v
            wo_ld = 0
            pe_oc = {}
            dve_oc = {}
            for oc in range(DC):
                psO = ps[oc % 2]
                if oc >= 2:
                    b.wait("tensor", s_dve, dve_oc[oc - 2])
                for hc in range(DC):
                    wsl = wo_ld % 2
                    if wo_ld >= 2:
                        b.wait("sync", s_pe, pe_oc[wo_ld - 2])
                    b.dma("sync", ap2(wtlh[wsl], 0, 128, 0, 128),
                          ap2(scr_woh, hc * 128, 128, oc * 128, 128), inc=s_ld[2 + wsl])
                    b.dma("sync", ap2(wtll[wsl], 0, 128, 0, 128),
                          ap2(scr_wol, hc * 128, 128, oc * 128, 128), inc=s_ld[2 + wsl])
                    b.wait("tensor", s_ld[2 + wsl], s_ld[2 + wsl].v)
                    lhs_hl = (ap2(wtlh[wsl], 0, 128, 0, 128), ap2(wtll[wsl], 0, 128, 0, 128))
                    rhs_hl = (ap2(attn_oh, 0, 128, hc * QT, QT), ap2(attn_ol, 0, 128, hc * QT, QT))
                    b.mm3p(ap2(psO, 0, 128, 0, QT), lhs_hl, rhs_hl,
                           first=(hc == 0), last=(hc == DC - 1), inc=s_pe)
                    pe_oc[wo_ld] = s_pe.v
                    wo_ld += 1
                b.wait("vector", s_pe, s_pe.v)
                if oc == 0:
                    b.wait("vector", s_ld[0], x_ready)
                b.drain("vector")
                b.tt(ap2(strm, 0, 128, oc * QT, QT), ap2(psO, 0, 128, 0, QT),
                     ap2(strm, 0, 128, oc * QT, QT), ALU.add, inc=s_dve)
                dve_oc[oc] = s_dve.v
                b.drain("vector")

            def layer_norm(src, dst, src_ready_dve):
                b.wait("tensor", s_dve, src_ready_dve)
                for dc in range(DC):
                    b.mm(ap2(ps[2], 0, 1, 0, QT), ones_sb[:, 0:1], ap2(src, 0, 128, dc * QT, QT),
                         start=(dc == 0), stop=(dc == DC - 1), inc=(s_pe if dc == DC - 1 else None))
                b.wait("vector", s_pe, s_pe.v)
                b.ts(stat[:, :], ap2(ps[2], 0, 1, 0, QT), 1.0 / D, ALU.mult, inc=s_dve)
                b.drain("vector")
                b.wait("scalar", s_dve, src_ready_dve)
                pe_sq = {}
                for dc in range(DC):
                    if dc >= 2:
                        b.wait("scalar", s_pe, pe_sq[dc - 2])
                    b.act(sqb[dc % 2][:, :], ap2(src, 0, 128, dc * QT, QT), AF.Square, inc=s_act)
                    b.wait("tensor", s_act, s_act.v)
                    b.mm(ap2(ps[3], 0, 1, 0, QT), ones_sb[:, 0:1], sqb[dc % 2][:, :],
                         start=(dc == 0), stop=(dc == DC - 1), inc=s_pe)
                    pe_sq[dc] = s_pe.v
                b.wait("vector", s_pe, s_pe.v)
                b.ts(stat2[:, :], ap2(ps[3], 0, 1, 0, QT), 1.0 / D, ALU.mult, inc=s_dve)
                b.drain("vector")
                b.tt(statm[:, :], stat[:, :], stat[:, :], ALU.mult, inc=s_dve)
                b.drain("vector")
                b.stt(stat2[:, :], statm[:, :], -1.0, stat2[:, :], ALU.mult, ALU.add, inc=s_dve)
                b.drain("vector")
                b.ts(stat2[:, :], stat2[:, :], 1.0, ALU.mult, s2=1e-5, op1=ALU.add, inc=s_dve)
                b.drain("vector")
                b.wait("scalar", s_dve, s_dve.v)
                b.act(statm[:, :], stat2[:, :], AF.Sqrt, inc=s_act)
                b.wait("vector", s_act, s_act.v)
                b.recip(stat2[:, :], statm[:, :], inc=s_dve)
                b.drain("vector")
                b.tt(statm[:, :], stat[:, :], stat2[:, :], ALU.mult, inc=s_dve)
                b.drain("vector")
                b.wait("gpsimd", s_dve, s_dve.v)
                b.pbcast(bcstC2[:, :], stat2[:, :], inc=s_gp)
                b.pbcast(bcstC[:, :], statm[:, :], inc=s_gp)
                b.wait("vector", s_gp, s_gp.v)
                for dc in range(DC):
                    b.tt(ap2(dst, 0, 128, dc * QT, QT), ap2(src, 0, 128, dc * QT, QT),
                         bcstC2[:, :], ALU.mult, inc=s_dve)
                    b.drain("vector")
                    b.tt(ap2(dst, 0, 128, dc * QT, QT), ap2(dst, 0, 128, dc * QT, QT),
                         bcstC[:, :], ALU.subtract, inc=s_dve)
                    b.drain("vector")
                return s_dve.v

            if g.dbg and qt == 0:
                b.wait("gpsimd", s_dve, s_dve.v)
                b.dma("gpsimd", dbg_strm[:, :], strm[:, :], inc=s_st[0])
                b.wait("gpsimd", s_st[0], s_st[0].v)
            xn_ready = layer_norm(strm, xn, s_dve.v)
            b.vcopy(xnh[:, :], xn[:, :])
            b.drain("vector")
            b.tt(xnl[:, :], xn[:, :], xnh[:, :], ALU.subtract, inc=s_dve)
            xn_split = s_dve.v
            b.drain("vector")
            if g.dbg and qt == 0:
                b.wait("gpsimd", s_dve, xn_split)
                b.dma("gpsimd", dbg_xn[:, :], xn[:, :], inc=s_st[0])
                b.wait("gpsimd", s_st[0], s_st[0].v)

            w_pe = {}
            w_ld = 0
            m3_pe = {}
            m3_ld = 0
            for fs in range(QT // FSUB):
                for hbg in range(HFC // 4):
                    ps1, ps2 = ps[0:4], ps[4:8]
                    if hbg == 0:
                        b.wait("sync", s_pe, s_pe.v)
                    if not (fs == 0 and hbg == 0):
                        b.wait("tensor", s_dve, s_dve.v)
                    for dc in range(DC):
                        wsl = dc % 4
                        lsem1, lsem2 = s_ld[2 + wsl], s_ld[8 + wsl]
                        if w_ld >= 4:
                            b.wait("sync", s_pe, w_pe[w_ld - 4])
                        b.dma("sync", ap2(wf8[wsl], 0, 128, 0, 512),
                              ap2(scr_ws[0], dc * 128, 128, hbg * 512, 512), inc=lsem1)
                        b.dma("sync", ap2(w2f8[wsl], 0, 128, 0, 512),
                              ap2(scr_ws[1], dc * 128, 128, hbg * 512, 512), inc=lsem2)
                        b.wait("tensor", lsem1, lsem1.v)
                        b.wait("tensor", lsem2, lsem2.v)
                        if dc == 0 and hbg == 0 and fs == 0:
                            b.wait("tensor", s_dve, xn_split)
                        rhs_hl = (ap2(xnh, 0, 128, dc * QT + fs * FSUB, FSUB),
                                  ap2(xnl, 0, 128, dc * QT + fs * FSUB, FSUB))
                        for j in range(4):
                            b.mm2p(ap2(ps1[j], 0, 128, 0, FSUB), ap2(wf8[wsl], 0, 128, j * 128, 128),
                                   rhs_hl, first=(dc == 0), last=(dc == DC - 1), inc=None)
                            b.mm2p(ap2(ps2[j], 0, 128, 0, FSUB), ap2(w2f8[wsl], 0, 128, j * 128, 128),
                                   rhs_hl, first=(dc == 0), last=(dc == DC - 1),
                                   inc=(s_pe if j == 3 else None))
                        w_pe[w_ld] = s_pe.v
                        w_ld += 1
                    pe_12 = s_pe.v
                    b.wait("scalar", s_pe, pe_12)
                    for j in range(4):
                        hb = hbg * 4 + j
                        if j > 0:
                            b.wait("scalar", s_dve, s_dve.v)
                        b.act(sgb[:, :], ap2(ps1[j], 0, 128, 0, FSUB), AF.Sigmoid, scale=0.5, inc=s_act)
                        b.wait("vector", s_act, s_act.v)
                        b.tt(htmp[:, :], ap2(ps2[j], 0, 128, 0, FSUB), sgb[:, :], ALU.mult, inc=s_dve)
                        b.drain("vector")
                        b.wait("scalar", s_dve, s_dve.v)
                        b.acopy(ap2(hbh, 0, 128, hb * FSUB, FSUB), htmp[:, :], inc=s_act)
                        b.wait("vector", s_act, s_act.v)
                        b.tt(ap2(hbl, 0, 128, hb * FSUB, FSUB), htmp[:, :],
                             ap2(hbh, 0, 128, hb * FSUB, FSUB), ALU.subtract, inc=s_dve)
                        b.drain("vector")
                h_act_done = s_act.v
                h_done = s_dve.v
                for ocg in range(DC // 4):
                    ps3 = ps[0:4]
                    if ocg == 0:
                        b.wait("sync", s_pe, s_pe.v)
                    if not (fs == 0 and ocg == 0):
                        b.wait("tensor", s_dve, m3_evac)
                    for hb in range(HFC):
                        wsl = hb % 4
                        lsem = s_ld[2 + wsl]
                        if m3_ld >= 4:
                            b.wait("sync", s_pe, m3_pe[m3_ld - 4])
                        b.dma("sync", ap2(wf8[wsl], 0, 128, 0, 512),
                              ap2(scr_ws[2], hb * 128, 128, ocg * 512, 512), inc=lsem)
                        b.wait("tensor", lsem, lsem.v)
                        if hb == 0 and ocg == 0:
                            b.wait("tensor", s_dve, h_done)
                            b.wait("tensor", s_act, h_act_done)
                        rhs_hl = (ap2(hbh, 0, 128, hb * FSUB, FSUB),
                                  ap2(hbl, 0, 128, hb * FSUB, FSUB))
                        for j in range(4):
                            b.mm2p(ap2(ps3[j], 0, 128, 0, FSUB), ap2(wf8[wsl], 0, 128, j * 128, 128),
                                   rhs_hl, first=(hb == 0), last=(hb == HFC - 1),
                                   inc=(s_pe if j == 3 else None))
                        m3_pe[m3_ld] = s_pe.v
                        m3_ld += 1
                    b.wait("vector", s_pe, s_pe.v)
                    for j in range(4):
                        oc = ocg * 4 + j
                        b.stt(ap2(strm, 0, 128, oc * QT + fs * FSUB, FSUB),
                              ap2(ps3[j], 0, 128, 0, FSUB), 0.25,
                              ap2(xn, 0, 128, oc * QT + fs * FSUB, FSUB),
                              ALU.mult, ALU.add, inc=s_dve)
                        b.drain("vector")
                    m3_evac = s_dve.v

            if g.dbg and qt == 0:
                b.wait("gpsimd", s_dve, s_dve.v)
                b.dma("gpsimd", dbg_ffn[:, :], strm[:, :], inc=s_st[0])
                b.wait("gpsimd", s_st[0], s_st[0].v)
            out_ready = layer_norm(strm, xn, s_dve.v)
            tr_free = [None, None]
            stq_y = StoreQ(b, [s_st[6], s_st[7]])
            yst = stq_y.done()
            for sub in range(QT // 128):
                for oc in range(DC):
                    bank = 4 + (oc % 2)
                    b.wait("tensor", s_act, tr_free[oc % 2])
                    b.wait("tensor", s_dve, out_ready)
                    b.tr(ap2(ps[bank], 0, 128, 0, 128),
                         ap2(xn, 0, 128, oc * QT + sub * 128, 128), ident_sb[:, :], inc=s_pe)
                    b.wait("scalar", s_pe, s_pe.v)
                    if sub > 0 and oc == 0:
                        for _s, _v in yst:
                            b.wait("scalar", _s, _v)
                    b.acopy(ap2(ybuf, 0, 128, oc * 128, 128), ap2(ps[bank], 0, 128, 0, 128), inc=s_act)
                    tr_free[oc % 2] = s_act.v
                b.wait("gpsimd", s_act, s_act.v)
                stq_y.dma(ap2(y_out, qt * QT + sub * 128, 128, 0, D),
                          ap2(ybuf, 0, 128, 0, D))
                yst = stq_y.done()
            stq_y.wait_all("gpsimd")
            b.run_stage(f"C{qt}")
        es_qt.close()

    es_outer.close()
    mybir.codegen_inst_isa_subclasses(nc)
    return nc



def rope_tables(S, HD):
    import jax
    import jax.numpy as jnp
    with jax.default_device(jax.devices("cpu")[0]):
        inv = 1.0 / 10000.0 ** (jnp.arange(0, HD, 2, dtype=jnp.float32) / HD)
        t = jnp.arange(S, dtype=jnp.float32)
        freqs = jnp.einsum("i,j->ij", t, inv)
        emb = jnp.concatenate([freqs, freqs], axis=-1)
        cos = np.asarray(jnp.cos(emb), dtype=np.float32)
        sin = np.asarray(jnp.sin(emb), dtype=np.float32)
    return cos, sin


def prep_core_inputs(g: Geom, core, x, Wq, Wk, Wv, Wo, W1, W2, W3):
    S, D, HD, T = g.S, g.D, g.HD, g.T
    x2 = np.ascontiguousarray(x.reshape(S, D))
    cosf, sinf = rope_tables(S, HD)
    sin_signed = sinf.copy()
    sin_signed[:, :HD // 2] *= -1.0
    sl = slice(core * T, (core + 1) * T)
    NC = g.NC
    DS = D // NC
    HFP = ((g.HF + 128 * NC - 1) // (128 * NC)) * 128 * NC
    w3p = np.zeros((HFP, D), dtype=np.float32)
    w3p[:g.HF] = W3.T
    dsl = slice(core * DS, (core + 1) * DS)
    hsl = slice(core * (HFP // NC), (core + 1) * (HFP // NC))
    hc = core % g.NKV
    th = core // g.NKV
    S_KV = S // (NC // g.NKV)
    ksl = slice(th * S_KV, (th + 1) * S_KV)
    wkv_slice = np.concatenate([Wk.T[:, hc * 128:(hc + 1) * 128],
                                Wv.T[:, hc * 128:(hc + 1) * 128]], axis=1)
    return {
        "x_t": np.ascontiguousarray(x2.T),
        "x_to": np.ascontiguousarray(x2.T[:, sl]),
        "x_kv": np.ascontiguousarray(x2.T[:, ksl]),
        "cos_kv": np.ascontiguousarray(cosf[ksl]),
        "sin_kv": np.ascontiguousarray(sin_signed[ksl]),
        "wkv_s": np.ascontiguousarray(wkv_slice),
        "wq_s": np.ascontiguousarray(Wq.T[dsl]),
        "wo_s": np.ascontiguousarray(Wo.T[dsl]),
        "w1_s": np.ascontiguousarray(W1.T[dsl]),
        "w2_s": np.ascontiguousarray(W2.T[dsl]),
        "w3_s": np.ascontiguousarray(w3p[hsl]),
        "cosf": cosf, "sinf": sin_signed,
        "coso": np.ascontiguousarray(cosf[sl]), "sino": np.ascontiguousarray(sin_signed[sl]),
        "ident": np.eye(128, dtype=np.float32),
        "ones": np.ones((128, 128), dtype=np.float32),
    }




from concourse.bass_utils import run_bass_kernel_spmd


def kernel(L0_attn_x, L0_attn_Wq, L0_attn_Wk, L0_attn_Wv, L0_attn_Wo,
           L0_ffn_w1, L0_ffn_w2, L0_ffn_w3):
    g = Geom()
    x = np.ascontiguousarray(np.asarray(L0_attn_x, dtype=np.float32).reshape(g.S, g.D))
    Wq = np.asarray(L0_attn_Wq, dtype=np.float32)
    Wk = np.asarray(L0_attn_Wk, dtype=np.float32)
    Wv = np.asarray(L0_attn_Wv, dtype=np.float32)
    Wo = np.asarray(L0_attn_Wo, dtype=np.float32)
    W1 = np.asarray(L0_ffn_w1, dtype=np.float32)
    W2 = np.asarray(L0_ffn_w2, dtype=np.float32)
    W3 = np.asarray(L0_ffn_w3, dtype=np.float32)

    nc = build(g)

    x_t = np.ascontiguousarray(x.T)
    cosf, sinf = rope_tables(g.S, g.HD)
    sin_signed = sinf.copy()
    sin_signed[:, :g.HD // 2] *= -1.0
    wq_T = np.ascontiguousarray(Wq.T)
    wo_T = np.ascontiguousarray(Wo.T)
    w1_T = np.ascontiguousarray(W1.T)
    w2_T = np.ascontiguousarray(W2.T)
    NCn = g.NC
    DS = g.D // NCn
    HFP = ((g.HF + 128 * NCn - 1) // (128 * NCn)) * 128 * NCn
    HFS = HFP // NCn
    w3p = np.zeros((HFP, g.D), dtype=np.float32)
    w3p[:g.HF] = W3.T
    wk_T = np.ascontiguousarray(Wk.T)
    wv_T = np.ascontiguousarray(Wv.T)
    shared = {
        "x_t": x_t,
        "cosf": cosf, "sinf": sin_signed,
        "ident": np.eye(128, dtype=np.float32),
        "ones": np.ones((128, 128), dtype=np.float32),
    }
    in_maps = []
    for c in range(NCn):
        sl = slice(c * g.T, (c + 1) * g.T)
        dsl = slice(c * DS, (c + 1) * DS)
        hsl = slice(c * HFS, (c + 1) * HFS)
        m = dict(shared)
        m["x_to"] = np.ascontiguousarray(x_t[:, sl])
        m["coso"] = np.ascontiguousarray(cosf[sl])
        m["sino"] = np.ascontiguousarray(sin_signed[sl])
        m["wq_s"] = np.ascontiguousarray(wq_T[dsl])
        m["wo_s"] = np.ascontiguousarray(wo_T[dsl])
        m["w1_s"] = np.ascontiguousarray(w1_T[dsl])
        m["w2_s"] = np.ascontiguousarray(w2_T[dsl])
        m["w3_s"] = np.ascontiguousarray(w3p[hsl])
        hc = c % g.NKV
        th = c // g.NKV
        S_KV = g.S // (NCn // g.NKV)
        ksl = slice(th * S_KV, (th + 1) * S_KV)
        m["x_kv"] = np.ascontiguousarray(x_t[:, ksl])
        m["cos_kv"] = np.ascontiguousarray(cosf[ksl])
        m["sin_kv"] = np.ascontiguousarray(sin_signed[ksl])
        m["wkv_s"] = np.ascontiguousarray(np.concatenate(
            [wk_T[:, hc * 128:(hc + 1) * 128], wv_T[:, hc * 128:(hc + 1) * 128]], axis=1))
        in_maps.append(m)

    res = run_bass_kernel_spmd(nc, in_maps, core_ids=list(range(g.NC)))
    y = np.concatenate([np.asarray(res.results[c]["y"]) for c in range(g.NC)], axis=0)
    return y.reshape(1, g.S, g.D).astype(np.float32, copy=False)


# version 20
# speedup vs baseline: 1.2742x; 1.0243x over previous
import math
from contextlib import ExitStack
from dataclasses import dataclass

import numpy as np

import concourse.bass as bass
import concourse.mybir as mybir
from concourse.alu_op_type import AluOpType as ALU
from bass_rust import ReduceOp
from concourse import library_config

F32 = mybir.dt.float32
BF16 = mybir.dt.bfloat16
FP8 = mybir.dt.float8e4
AF = mybir.ActivationFunctionType
AX = mybir.AxisListType

ENGS = ("tensor", "vector", "scalar", "sync", "gpsimd")


@dataclass
class Geom:
    S: int = 8192
    D: int = 2048
    NH: int = 16
    NKV: int = 4
    HD: int = 128
    HF: int = 5632
    NC: int = 8
    TCH: int = 512
    QT: int = 512
    FSUB: int = 256
    dbg: bool = False

    @property
    def T(self): return self.S // self.NC
    @property
    def DC(self): return self.D // 128
    @property
    def NKD(self): return self.NKV * self.HD
    @property
    def HFC(self): return self.HF // 128
    @property
    def REP(self): return self.NH // self.NKV
    @property
    def NKC(self): return self.S // 128
    @property
    def NWELEM(self): return self.D * self.HF


class SemC:
    def __init__(self, nc, name, step):
        self.h = nc.alloc_semaphore(name=name)
        self.step = step
        self.v = 0

    def inc(self):
        self.v += self.step
        return self.v


class Builder:
    def __init__(self, g: Geom):
        self.g = g
        self.nc = bass.Bass("TRN2", target_bir_lowering=False)
        self.plan = None

    def L(self, eng, f):
        self.plan[eng].append(f)

    def wait(self, eng, sem: SemC, val=None):
        v = sem.v if val is None else val
        if v is not None and v > 0:
            self.L(eng, lambda e, _s=sem.h, _v=v: e.wait_ge(_s, _v))

    def drain(self, eng):
        self.L(eng, lambda e: e.drain())

    def _inc(self, instr, sem):
        if sem is not None:
            instr.then_inc(sem.h, sem.step)

    def mm(self, out, lhsT, rhs, start, stop, inc=None):
        if inc is not None: inc.inc()
        self.L("tensor", lambda e, _o=out, _l=lhsT, _r=rhs, _a=start, _b=stop, _s=inc:
               self._inc(e.matmul(_o, _l, _r, start=_a, stop=_b), _s))

    def mm3p(self, out, lhs_hl, rhs_hl, inc=None, first=True, last=True):
        combos = ((0, 0), (0, 1), (1, 0))
        for i, (a, c) in enumerate(combos):
            self.mm(out, lhs_hl[a], rhs_hl[c],
                    start=(first and i == 0), stop=(last and i == 2),
                    inc=(inc if i == 2 else None))

    def mm2p(self, out, lhs_exact, rhs_hl, inc=None, first=True, last=True):
        for i in range(2):
            self.mm(out, lhs_exact, rhs_hl[i],
                    start=(first and i == 0), stop=(last and i == 1),
                    inc=(inc if i == 1 else None))

    def tr(self, out, in_, ident, inc=None):
        if inc is not None: inc.inc()
        self.L("tensor", lambda e, _o=out, _i=in_, _d=ident, _s=inc:
               self._inc(e.transpose(_o, _i, _d), _s))

    def act(self, out, in_, func, bias=None, scale=None, accum=None, inc=None):
        if inc is not None: inc.inc()
        kw = {}
        if bias is not None: kw["bias"] = bias
        if scale is not None: kw["scale"] = scale
        if accum is not None: kw["accum_out"] = accum
        self.L("scalar", lambda e, _o=out, _i=in_, _f=func, _k=kw, _s=inc:
               self._inc(e.activation(out=_o, in_=_i, func=_f, **_k), _s))

    def acopy(self, out, in_, inc=None):
        if inc is not None: inc.inc()
        self.L("scalar", lambda e, _o=out, _i=in_, _s=inc:
               self._inc(e.copy(_o, _i), _s))

    def tt(self, out, in0, in1, op, inc=None):
        if inc is not None: inc.inc()
        self.L("vector", lambda e, _o=out, _a=in0, _b=in1, _p=op, _s=inc:
               self._inc(e.tensor_tensor(_o, _a, _b, op=_p), _s))

    def ts(self, out, in0, s1, op0, s2=None, op1=None, inc=None):
        if inc is not None: inc.inc()
        kw = dict(scalar2=s2)
        if op1 is not None: kw["op1"] = op1
        self.L("vector", lambda e, _o=out, _a=in0, _1=s1, _p=op0, _k=kw, _s=inc:
               self._inc(e.tensor_scalar(out=_o, in0=_a, scalar1=_1, op0=_p, **_k), _s))

    def stt(self, out, in0, scalar, in1, op0, op1, inc=None):
        if inc is not None: inc.inc()
        self.L("vector", lambda e, _o=out, _a=in0, _c=scalar, _b=in1, _p=op0, _q=op1, _s=inc:
               self._inc(e.scalar_tensor_tensor(out=_o, in0=_a, scalar=_c, in1=_b, op0=_p, op1=_q), _s))

    def vcopy(self, out, in_, inc=None):
        if inc is not None: inc.inc()
        self.L("vector", lambda e, _o=out, _i=in_, _s=inc:
               self._inc(e.tensor_copy(_o, _i), _s))

    def vred(self, out, in_, op, absval=False, inc=None):
        if inc is not None: inc.inc()
        self.L("vector", lambda e, _o=out, _i=in_, _p=op, _a=absval, _s=inc:
               self._inc(e.tensor_reduce(_o, _i, axis=AX.X, op=_p, apply_absolute_value=_a), _s))

    def recip(self, out, in_, inc=None):
        if inc is not None: inc.inc()
        self.L("vector", lambda e, _o=out, _i=in_, _s=inc:
               self._inc(e.reciprocal(_o, _i), _s))

    def dma(self, queue, out, in_, inc=None):
        if inc is not None: inc.inc()
        self.L(queue, lambda e, _o=out, _i=in_, _s=inc:
               self._inc(e.dma_start(_o, _i), _s))

    def pbcast(self, out, in_, inc=None):
        if inc is not None: inc.inc()
        self.L("gpsimd", lambda e, _o=out, _i=in_, _s=inc:
               self._inc(e.partition_broadcast(_o, _i), _s))

    def allred(self, out, in_, op, inc=None):
        if inc is not None: inc.inc()
        self.L("gpsimd", lambda e, _o=out, _i=in_, _p=op, _s=inc:
               self._inc(e.partition_all_reduce(_o, _i, channels=128, reduce_op=_p), _s))

    def run_stage(self, name):
        plan = self.plan
        with self.nc.named_scope(name):
            with self.nc.Block() as block:
                for eng in ENGS:
                    if plan[eng]:
                        def body(e, _th=tuple(plan[eng])):
                            for f in _th:
                                f(e)
                        getattr(block, eng)(body)
        self.plan = None


class StoreQ:
    def __init__(self, b, sems, prewait=False):
        self.b = b
        self.sems = sems
        self.i = 0
        self.prewait = prewait

    def dma(self, out, in_):
        b, sems = self.b, self.sems
        s = sems[self.i % len(sems)]
        if self.i >= len(sems) or self.prewait:
            b.wait("gpsimd", s, s.v)
        b.dma("gpsimd", out, in_, inc=s)
        self.i += 1

    def done(self):
        return [(s, s.v) for s in self.sems]

    def wait_all(self, eng):
        for s, v in self.done():
            self.b.wait(eng, s, v)


def ap2(t, p0, np_, c0, nc_):
    return t[p0:p0 + np_, c0:c0 + nc_]


def make_self_cc(nc_cores):
    def self_cc(e, kind, op, ins, outs, sem):
        e.collective_compute(kind, op, replica_groups=[list(range(nc_cores))],
                             ins=ins, outs=outs).then_inc(sem.h, sem.step)
    return self_cc


def build(g: Geom):
    b = Builder(g)
    nc = b.nc
    S, D, HD, HF, T = g.S, g.D, g.HD, g.HF, g.T
    DC, NKD, NKV, HFC, REP, NKC = g.DC, g.NKD, g.NKV, g.HFC, g.REP, g.NKC
    TCH, QT, FSUB = g.TCH, g.QT, g.FSUB
    NCH = S // TCH
    OCH = T // TCH
    A0_UNITS_PER_CHUNK = -(-((D // 128) * (HF // 512) * 4 + (HF // 128) * (D // 512) * 2 + (D // 128) * (D // 512) * 2 + 8) // (S // TCH))
    NQT = T // QT
    NSUB = TCH // 128
    SCALE = 1.0 / math.sqrt(HD)
    self_cc = make_self_cc(g.NC)

    x_t = nc.dram_tensor("x_t", [D, S], F32, kind="ExternalInput")
    x_to = nc.dram_tensor("x_to", [D, T], F32, kind="ExternalInput")
    DS = D // g.NC
    HFP = ((HF + 128 * g.NC - 1) // (128 * g.NC)) * 128 * g.NC
    HFS = HFP // g.NC
    wq_s = nc.dram_tensor("wq_s", [DS, D], F32, kind="ExternalInput")
    NSH = g.NC // NKV
    S_KV = S // NSH
    x_kv = nc.dram_tensor("x_kv", [D, S_KV], F32, kind="ExternalInput")
    cos_kv = nc.dram_tensor("cos_kv", [S_KV, HD], F32, kind="ExternalInput")
    sin_kv = nc.dram_tensor("sin_kv", [S_KV, HD], F32, kind="ExternalInput")
    wkv_s = nc.dram_tensor("wkv_s", [D, 256], F32, kind="ExternalInput")
    wo_s = nc.dram_tensor("wo_s", [DS, D], F32, kind="ExternalInput")
    w1_s = nc.dram_tensor("w1_s", [DS, HF], F32, kind="ExternalInput")
    w2_s = nc.dram_tensor("w2_s", [DS, HF], F32, kind="ExternalInput")
    w3_s = nc.dram_tensor("w3_s", [HFS, D], F32, kind="ExternalInput")
    cosf = nc.dram_tensor("cosf", [S, HD], F32, kind="ExternalInput")
    sinf = nc.dram_tensor("sinf", [S, HD], F32, kind="ExternalInput")
    coso = nc.dram_tensor("coso", [T, HD], F32, kind="ExternalInput")
    sino = nc.dram_tensor("sino", [T, HD], F32, kind="ExternalInput")
    ident_in = nc.dram_tensor("ident", [128, 128], F32, kind="ExternalInput")
    ones_in = nc.dram_tensor("ones", [128, 128], F32, kind="ExternalInput")
    y_out = nc.dram_tensor("y", [T, D], F32, kind="ExternalOutput")

    cc_kh = nc.dram_tensor("cc_kh", [128, S_KV], BF16)
    cc_kl = nc.dram_tensor("cc_kl", [128, S_KV], BF16)
    cc_vh = nc.dram_tensor("cc_vh", [S_KV, 128], BF16)
    cc_vl = nc.dram_tensor("cc_vl", [S_KV, 128], BF16)
    scr_kh = nc.dram_tensor("scr_kh", [g.NC * 128, S_KV], BF16, addr_space=("Shared" if g.NC > 4 else "Local"))
    scr_kl = nc.dram_tensor("scr_kl", [g.NC * 128, S_KV], BF16, addr_space=("Shared" if g.NC > 4 else "Local"))
    scr_vh = nc.dram_tensor("scr_vh", [g.NC * S_KV, 128], BF16, addr_space=("Shared" if g.NC > 4 else "Local"))
    scr_vl = nc.dram_tensor("scr_vl", [g.NC * S_KV, 128], BF16, addr_space=("Shared" if g.NC > 4 else "Local"))
    scr_qh = nc.dram_tensor("scr_qh", [D, T], BF16)
    scr_ql = nc.dram_tensor("scr_ql", [D, T], BF16)
    CCSP = "Shared" if g.NC > 4 else "Local"
    cc_ws = [nc.dram_tensor(f"cc_w{i}", [DS, HF] if i < 2 else [HFS, D], FP8) for i in range(3)]
    scr_ws = [nc.dram_tensor(f"scr_w{i}", [D, HF] if i < 2 else [HFP, D], FP8, addr_space=CCSP) for i in range(3)]
    cc_wsp = [nc.dram_tensor(f"cc_wsp{i}", [DS, D], BF16) for i in range(4)]
    scr_wqh = nc.dram_tensor("scr_wqh", [D, D], BF16, addr_space=CCSP)
    scr_wql = nc.dram_tensor("scr_wql", [D, D], BF16, addr_space=CCSP)
    scr_woh = nc.dram_tensor("scr_woh", [D, D], BF16, addr_space=CCSP)
    scr_wol = nc.dram_tensor("scr_wol", [D, D], BF16, addr_space=CCSP)
    cc_g_in = nc.dram_tensor("cc_g_in", [1, 3], F32)
    cc_g_out = nc.dram_tensor("cc_g_out", [1, 3], F32)

    if g.dbg:
        dbg_attn = nc.dram_tensor("dbg_attn", [128, DC * QT], BF16, kind="ExternalOutput")
        dbg_strm = nc.dram_tensor("dbg_strm", [128, DC * QT], F32, kind="ExternalOutput")
        dbg_xn = nc.dram_tensor("dbg_xn", [128, DC * QT], F32, kind="ExternalOutput")
        dbg_ffn = nc.dram_tensor("dbg_ffn", [128, DC * QT], F32, kind="ExternalOutput")

    sem = lambda n, s: SemC(nc, n, s)
    s_pe = sem("pe", 1)
    s_act = sem("act", 1)
    s_dve = sem("dve", 1)
    s_gp = sem("gp", 1)
    s_ld = [sem(f"ld{i}", 16) for i in range(12)]
    s_st = [sem(f"st{i}", 16) for i in range(16)]

    es_outer = ExitStack()
    sb = lambda n, sh, dt=F32: es_outer.enter_context(nc.sbuf_tensor(n, sh, dt))

    ident_sb = sb("identsb", [128, 128])
    ones_sb = sb("onessb", [128, 128])
    ones_bf = sb("onesbf", [128, 2], BF16)
    ident_bf = sb("identbf", [128, 128], BF16)
    bias_m0 = sb("bias_m0", [128, 1])
    hthr = sb("hthr", [128, 6])

    b.plan = {e: [] for e in ENGS}
    b.dma("sync", ident_sb[:, :], ident_in[:, :], inc=s_ld[3])
    b.dma("sync", ones_sb[:, :], ones_in[:, :], inc=s_ld[3])
    consts_ready = s_ld[3].v
    b.wait("vector", s_ld[3], consts_ready)
    b.vcopy(ones_bf[:, 0:1], ones_sb[:, 0:1], inc=s_dve)
    b.drain("vector")
    b.vcopy(ident_bf[:, :], ident_sb[:, :], inc=s_dve)
    b.drain("vector")
    b.L("gpsimd", lambda e: e.load_library(library_config.attnmlp))

    es_a0 = ExitStack()
    if True:
        sbA = lambda n, sh, dt=F32: es_a0.enter_context(nc.sbuf_tensor(n, sh, dt))

        def colchunks(width):
            cw = width
            while cw > 2048:
                assert cw % 2 == 0
                cw //= 2
            return cw, width // cw
        cw1, nc1 = colchunks(HF)
        cw3, nc3 = colchunks(D)
        WMAX = max(cw1, cw3, D)
        g_sb = es_a0.enter_context(nc.sbuf_tensor("g_sb", [128, 4], F32))
        w_in = [sbA("w_in0", [128, WMAX]), sbA("w_in1", [128, WMAX])]
        s1b = [sbA("s1b0", [128, WMAX]), sbA("s1b1", [128, WMAX])]
        s2b = [sbA("s2b0", [128, WMAX]), sbA("s2b1", [128, WMAX])]
        wqb = [sbA("wqb0", [128, WMAX], FP8), sbA("wqb1", [128, WMAX], FP8)]
        wsph = [sbA("wsph0", [128, D], BF16), sbA("wsph1", [128, D], BF16)]
        wspl = [sbA("wspl0", [128, D], BF16), sbA("wspl1", [128, D], BF16)]
        n_t1 = (D // 128) * nc1 * 2 + (HF // 128) * nc3
        accb = sbA("accb", [128, n_t1 + 8])
        gtot = sbA("gtot", [128, 3])

        wsrc = [w1_s, w2_s, w3_s]
        wtiles = [[(r, c, cw1) for r in range(DS // 128) for c in range(nc1)],
                  [(r, c, cw1) for r in range(DS // 128) for c in range(nc1)],
                  [(r, c, cw3) for r in range(HFS // 128) for c in range(nc3)]]

    def a0_work():
            col = 0
            wcol = []
            dve_done = {}
            ti = 0
            for wi, tiles in enumerate(wtiles):
                wcol.append(col)
                for (r, cc, width) in tiles:
                    sl = ti % 2
                    if ti >= 2:
                        b.wait("sync", s_act, dve_done[ti - 2])
                    b.dma("sync", ap2(w_in[sl], 0, 128, 0, width),
                          ap2(wsrc[wi], r * 128, 128, cc * width, width), inc=s_ld[6 + sl])
                    b.wait("scalar", s_ld[6 + sl], s_ld[6 + sl].v)
                    b.act(ap2(s1b[sl], 0, 128, 0, width), ap2(w_in[sl], 0, 128, 0, width),
                          AF.Abs, accum=accb[:, col:col + 1], inc=s_act)
                    dve_done[ti] = s_act.v
                    col += 1
                    ti += 1
                    yield
            b.wait("vector", s_act, s_act.v)
            for wi in range(3):
                c0, c1 = wcol[wi], wcol[wi] + len(wtiles[wi])
                b.vred(gtot[:, wi:wi + 1], accb[:, c0:c1], ALU.add, inc=s_dve)
                b.drain("vector")
            b.wait("gpsimd", s_dve)
            for wi in range(3):
                b.allred(gtot[:, wi:wi + 1], gtot[:, wi:wi + 1], ReduceOp.add, inc=s_gp)
            b.wait("gpsimd", s_gp, s_gp.v)
            b.dma("gpsimd", cc_g_in[0:1, 0:3], gtot[0:1, 0:3], inc=s_st[14])
            b.wait("gpsimd", s_st[14], s_st[14].v)
            b.L("gpsimd", lambda e: self_cc(e, "AllReduce", ALU.add, [cc_g_in[:, :]], [cc_g_out[:, :]], s_gp))
            s_gp.inc()
            b.wait("gpsimd", s_gp, s_gp.v)
            b.dma("gpsimd", g_sb[0:1, 0:3], cc_g_out[0:1, 0:3], inc=s_st[14])
            b.wait("gpsimd", s_st[14], s_st[14].v)
            b.pbcast(gtot[:, 0:3], g_sb[0:1, 0:3], inc=s_gp)
            b.wait("vector", s_gp)
            for wi in range(3):
                nw = float(g.NWELEM)
                b.ts(hthr[:, 2 * wi:2 * wi + 1], gtot[:, wi:wi + 1], -0.5 / nw, ALU.mult,
                     s2=-0.5e-8, op1=ALU.add, inc=s_dve)
                b.ts(hthr[:, 2 * wi + 1:2 * wi + 2], gtot[:, wi:wi + 1], 0.5 / nw, ALU.mult,
                     s2=0.5e-8, op1=ALU.add, inc=s_dve)
                b.drain("vector")
            hready = s_dve.v
            b.wait("scalar", s_dve, hready)

            b.wait("sync", s_act, s_act.v)
            b.wait("sync", s_dve, s_dve.v)
            stq_h = StoreQ(b, [s_st[10], s_st[11]])
            stq_l = StoreQ(b, [s_st[12], s_st[13]])
            sp_src = [(wq_s, cc_wsp[0], cc_wsp[1]), (wo_s, cc_wsp[2], cc_wsp[3])]
            ti = 0
            dve_done = {}
            st_done = {}
            for (srcw, dsth, dstl) in sp_src:
                for r in range(DS // 128):
                    sl = ti % 2
                    if ti >= 2:
                        b.wait("sync", s_dve, dve_done[ti - 2])
                    b.dma("sync", ap2(w_in[sl], 0, 128, 0, D),
                          ap2(srcw, r * 128, 128, 0, D), inc=s_ld[6 + sl])
                    b.wait("vector", s_ld[6 + sl], s_ld[6 + sl].v)
                    if ti >= 2:
                        for _s, _v in st_done[ti - 2]:
                            b.wait("vector", _s, _v)
                    b.drain("vector")
                    b.vcopy(ap2(wsph[sl], 0, 128, 0, D), ap2(w_in[sl], 0, 128, 0, D))
                    b.drain("vector")
                    b.tt(ap2(wspl[sl], 0, 128, 0, D), ap2(w_in[sl], 0, 128, 0, D),
                         ap2(wsph[sl], 0, 128, 0, D), ALU.subtract, inc=s_dve)
                    dve_done[ti] = s_dve.v
                    b.drain("vector")
                    b.wait("gpsimd", s_dve, s_dve.v)
                    stq_h.dma(ap2(dsth, r * 128, 128, 0, D), ap2(wsph[sl], 0, 128, 0, D))
                    stq_l.dma(ap2(dstl, r * 128, 128, 0, D), ap2(wspl[sl], 0, 128, 0, D))
                    st_done[ti] = stq_h.done() + stq_l.done()
                    ti += 1
                    yield
            stq_h.wait_all("gpsimd")
            stq_l.wait_all("gpsimd")
            for _ci, _co in ((cc_wsp[0], scr_wqh), (cc_wsp[1], scr_wql),
                             (cc_wsp[2], scr_woh), (cc_wsp[3], scr_wol)):
                b.L("gpsimd", lambda e, _a=_ci, _b=_co: self_cc(e, "AllGather", ALU.bypass,
                    [_a[:, :]], [_b[:, :]], s_gp))
                s_gp.inc()
            b.scr_wq_ready = [(s_gp, s_gp.v)]
            yield


            b.wait("sync", s_dve, s_dve.v)
            stq = StoreQ(b, [s_st[8], s_st[9]])
            ti = 0
            act_done = {}
            dve_done = {}
            st_done = {}
            for wi, tiles in enumerate(wtiles):
                for (r, cc, width) in tiles:
                    sl = ti % 2
                    if ti >= 2:
                        b.wait("sync", s_act, act_done[ti - 2])
                    b.dma("sync", ap2(w_in[sl], 0, 128, 0, width),
                          ap2(wsrc[wi], r * 128, 128, cc * width, width), inc=s_ld[6 + sl])
                    b.wait("scalar", s_ld[6 + sl], s_ld[6 + sl].v)
                    if ti >= 2:
                        b.wait("scalar", s_dve, dve_done[ti - 2])
                    b.act(ap2(s1b[sl], 0, 128, 0, width), ap2(w_in[sl], 0, 128, 0, width),
                          AF.Sign, bias=hthr[:, 2 * wi:2 * wi + 1], inc=s_act)
                    b.act(ap2(s2b[sl], 0, 128, 0, width), ap2(w_in[sl], 0, 128, 0, width),
                          AF.Sign, bias=hthr[:, 2 * wi + 1:2 * wi + 2], inc=s_act)
                    act_done[ti] = s_act.v
                    b.wait("vector", s_act, act_done[ti])
                    if ti >= 2:
                        for _s, _v in st_done[ti - 2]:
                            b.wait("vector", _s, _v)
                    b.drain("vector")
                    b.tt(ap2(wqb[sl], 0, 128, 0, width), ap2(s1b[sl], 0, 128, 0, width),
                         ap2(s2b[sl], 0, 128, 0, width), ALU.add, inc=s_dve)
                    dve_done[ti] = s_dve.v
                    b.wait("gpsimd", s_dve, s_dve.v)
                    stq.dma(ap2(cc_ws[wi], r * 128, 128, cc * width, width),
                            ap2(wqb[sl], 0, 128, 0, width))
                    st_done[ti] = stq.done()
                    ti += 1
                    yield
            stq.wait_all("gpsimd")
            for wi in range(3):
                b.L("gpsimd", lambda e, _w=wi: self_cc(e, "AllGather", ALU.bypass,
                    [cc_ws[_w][:, :]], [scr_ws[_w][:, :]], s_gp))
                s_gp.inc()
            b.scr_ffn_ready = [(s_gp, s_gp.v)]

    a0_gen = a0_work()
    for _ in a0_gen:
        pass
    b.run_stage("A0")
    es_a0.close()

    with ExitStack() as es:
        sbA = lambda n, sh, dt=F32: es.enter_context(nc.sbuf_tensor(n, sh, dt))
        xt32 = sbA("xt32", [128, DC * TCH])
        xth = [sbA("xth0", [128, DC * TCH], BF16), sbA("xth1", [128, DC * TCH], BF16)]
        xtl = [sbA("xtl0", [128, DC * TCH], BF16), sbA("xtl1", [128, DC * TCH], BF16)]
        wkv32 = sbA("wkv32", [128, max(NKD, 256)])
        wkvh = sbA("wkvh", [128, DC * 256], BF16)
        wkvl = sbA("wkvl", [128, DC * 256], BF16)
        cs = [sbA("cs0", [128, NSUB * 2 * HD])]
        t1b = sbA("t1b", [128, max(NKD, 512)])
        t2b = sbA("t2b", [128, max(NKD, 512)])
        kpb = sbA("kpb", [128, max(NKD, 512)])
        kpbh = sbA("kpbh", [128, max(NKD, 512)], BF16)
        kpbl = sbA("kpbl", [128, max(NKD, 512)], BF16)
        ksth = sbA("ksth", [128, max(NKV, 4) * TCH], BF16)
        kstl = sbA("kstl", [128, max(NKV, 4) * TCH], BF16)
        vsth = sbA("vsth", [128, NSUB * 128], BF16)
        vstl = sbA("vstl", [128, NSUB * 128], BF16)
        wqtlh = [sbA("wqtlh0", [128, 512], BF16), sbA("wqtlh1", [128, 512], BF16)]
        wqtll = [sbA("wqtll0", [128, 512], BF16), sbA("wqtll1", [128, 512], BF16)]
        qsth, qstl = ksth, kstl

        ps = [es.enter_context(nc.psum_tensor(f"psA{i}", [128, 512], F32)) for i in range(4)]
        psKb, psVb = ps[0:2], ps[2:4]
        psQ = ps[0:4]
        psT = [es.enter_context(nc.psum_tensor(f"psAT{i}", [128, 512], BF16)) for i in range(2)]

        b.plan = {e: [] for e in ENGS}
        for dc in range(DC):
            b.dma("sync", wkv32[:, 0:256], ap2(wkv_s, dc * 128, 128, 0, 256), inc=s_ld[2])
            b.wait("vector", s_ld[2], s_ld[2].v)
            if dc > 0:
                b.drain("vector")
            co0 = dc * 256
            b.vcopy(ap2(wkvh, 0, 128, co0, 256), wkv32[:, 0:256])
            b.drain("vector")
            b.tt(ap2(wkvl, 0, 128, co0, 256), wkv32[:, 0:256],
                 ap2(wkvh, 0, 128, co0, 256), ALU.subtract, inc=s_dve)
            b.wait("sync", s_dve, s_dve.v)
        wkv_ready = s_dve.v

        tr_bank_free = [None, None]
        kpb_tr_pe = None
        xt32_free_dve = 0
        cs_free_dve = 0
        rope2_done = {}
        vevac2_done = {}
        pe_after_chunk = {}
        stq_kh = StoreQ(b, [s_st[0], s_st[1]])
        stq_kl = StoreQ(b, [s_st[2], s_st[3]])
        stq_vh = StoreQ(b, [s_st[4], s_st[5]])
        stq_vl = StoreQ(b, [s_st[6], s_st[7]])

        def load_xt_cs(sl, src_x, src_cos, src_sin, col0, slot_free_pe):
            nonlocal xt32_free_dve, cs_free_dve
            if cs_free_dve:
                b.wait("sync", s_dve, cs_free_dve)
            b.dma("sync", bass.AP(cs[0], 0, [[NSUB * 2 * HD, 128], [2 * HD, NSUB], [1, HD]]),
                  bass.AP(src_cos, col0 * HD, [[HD, 128], [128 * HD, NSUB], [1, HD]]), inc=s_ld[sl])
            b.dma("sync", bass.AP(cs[0], HD, [[NSUB * 2 * HD, 128], [2 * HD, NSUB], [1, HD]]),
                  bass.AP(src_sin, col0 * HD, [[HD, 128], [128 * HD, NSUB], [1, HD]]), inc=s_ld[sl])
            if xt32_free_dve:
                b.wait("sync", s_dve, xt32_free_dve)
            b.dma("sync", xt32[:, :],
                  bass.AP(src_x, col0, [[src_x.shape[1], 128], [128 * src_x.shape[1], DC], [1, TCH]]),
                  inc=s_ld[sl])
            b.wait("vector", s_ld[sl], s_ld[sl].v)
            if slot_free_pe:
                b.wait("vector", s_pe, slot_free_pe)
            b.drain("vector")
            b.vcopy(xth[sl][:, :], xt32[:, :])
            b.drain("vector")
            b.tt(xtl[sl][:, :], xt32[:, :], xth[sl][:, :], ALU.subtract, inc=s_dve)
            xt32_free_dve = s_dve.v
            b.drain("vector")
            return s_ld[sl].v, s_dve.v

        def rope(psrc, nheads, width, sl, sub, dst):
            base = sub * 2 * HD
            cosap = bass.AP(cs[0], base, [[NSUB * 2 * HD, 128], [0, nheads], [1, HD]])
            sin4 = bass.AP(cs[0], base + HD, [[NSUB * 2 * HD, 128], [0, nheads], [HD // 2, 2], [1, HD // 2]])
            kap = bass.AP(psrc, 0, [[512, 128], [HD, nheads], [1, HD]])
            kshift = bass.AP(psrc, HD // 2, [[512, 128], [HD, nheads], [-(HD // 2), 2], [1, HD // 2]])
            t1ap = bass.AP(t1b, 0, [[t1b.shape[1], 128], [HD, nheads], [1, HD]])
            t2ap = bass.AP(t2b, 0, [[t2b.shape[1], 128], [HD, nheads], [HD // 2, 2], [1, HD // 2]])
            b.tt(t1ap, kap, cosap, ALU.mult)
            b.tt(t2ap, kshift, sin4, ALU.mult, inc=s_dve)
            b.drain("vector")
            b.tt(ap2(dst, 0, 128, 0, width), ap2(t1b, 0, 128, 0, width),
                 ap2(t2b, 0, 128, 0, width), ALU.add, inc=s_dve)
            b.drain("vector")
            b.vcopy(ap2(kpbh, 0, 128, 0, width), ap2(dst, 0, 128, 0, width))
            b.drain("vector")
            b.tt(ap2(kpbl, 0, 128, 0, width), ap2(dst, 0, 128, 0, width),
                 ap2(kpbh, 0, 128, 0, width), ALU.subtract, inc=s_dve)
            out = s_dve.v
            b.drain("vector")
            return out

        def transpose_pair(nslices, dsth, dstl, dcol_fn, ready_dve):
            nonlocal kpb_tr_pe
            for i in range(nslices):
                for which, (srcb, dstb) in enumerate(((kpbh, dsth), (kpbl, dstl))):
                    bank = (2 * i + which) % 2
                    b.wait("tensor", s_act, tr_bank_free[bank])
                    b.wait("tensor", s_dve, ready_dve)
                    b.tr(ap2(psT[bank], 0, 128, 0, 128),
                         ap2(srcb, 0, 128, i * HD, HD), ident_bf[:, :], inc=s_pe)
                    b.wait("scalar", s_pe, s_pe.v)
                    b.acopy(ap2(dstb, 0, 128, dcol_fn(i), 128),
                            ap2(psT[bank], 0, 128, 0, 128), inc=s_act)
                    tr_bank_free[bank] = s_act.v
            kpb_tr_pe = s_pe.v

        NCHK = S_KV // TCH
        chunk_loaded = {}
        chunk_loaded[0] = load_xt_cs(0, x_kv, cos_kv, sin_kv, 0, 0)
        for c in range(NCHK):
            sl = c % 2
            xt_ready, xs_done = chunk_loaded[c]
            b.wait("tensor", s_dve, xs_done)

            for sub in range(NSUB):
                si = (c * NSUB + sub) % 2
                psKV = psKb[si]
                idx = c * NSUB + sub
                if idx >= 2:
                    b.wait("tensor", s_dve, rope2_done.get(idx - 2, 0))
                    b.wait("tensor", s_dve, vevac2_done.get(idx - 2, 0))
                for dc in range(DC):
                    lhs_hl = (ap2(xth[sl], 0, 128, dc * TCH + sub * 128, 128),
                              ap2(xtl[sl], 0, 128, dc * TCH + sub * 128, 128))
                    rhs_hl = (ap2(wkvh, 0, 128, dc * 256, 256),
                              ap2(wkvl, 0, 128, dc * 256, 256))
                    b.mm3p(ap2(psKV, 0, 128, 0, 256), lhs_hl, rhs_hl,
                           first=(dc == 0), last=(dc == DC - 1),
                           inc=(s_pe if dc == DC - 1 else None))
                pe_kv = s_pe.v
                b.wait("scalar", s_pe, pe_kv)
                if c > 0:
                    for _s, _v in vstage_stored:
                        b.wait("scalar", _s, _v)
                b.acopy(ap2(vsth, 0, 128, sub * 128, 128), ap2(psKV, 0, 128, 128, 128), inc=s_act)
                b.wait("vector", s_act, s_act.v)
                if c > 0:
                    for _s, _v in vstage_stored:
                        b.wait("vector", _s, _v)
                b.drain("vector")
                b.tt(ap2(vstl, 0, 128, sub * 128, 128), ap2(psKV, 0, 128, 128, 128),
                     ap2(vsth, 0, 128, sub * 128, 128), ALU.subtract, inc=s_dve)
                vevac2_done[idx] = s_dve.v
                b.drain("vector")
                b.wait("vector", s_ld[sl], xt_ready)
                b.wait("vector", s_pe, pe_kv)
                if c > 0:
                    for _s, _v in kstage_stored:
                        b.wait("vector", _s, _v)
                if kpb_tr_pe is not None:
                    b.wait("vector", s_pe, kpb_tr_pe)
                prev_rope = rope(psKV, 1, 128, sl, sub, kpb)
                rope2_done[idx] = prev_rope
                cs_free_dve = prev_rope
                transpose_pair(1, ksth, kstl,
                               lambda i, _sub=sub: _sub * 128, prev_rope)
            pe_after_chunk[c] = s_pe.v
            b.wait("gpsimd", s_act, s_act.v)
            b.wait("gpsimd", s_dve, s_dve.v)
            stq_kh.dma(ap2(cc_kh, 0, 128, c * TCH, TCH), ap2(ksth, 0, 128, 0, TCH))
            stq_kl.dma(ap2(cc_kl, 0, 128, c * TCH, TCH), ap2(kstl, 0, 128, 0, TCH))
            kstage_stored = stq_kh.done() + stq_kl.done()
            for sub in range(NSUB):
                stq_vh.dma(ap2(cc_vh, c * TCH + sub * 128, 128, 0, 128),
                           ap2(vsth, 0, 128, sub * 128, 128))
                stq_vl.dma(ap2(cc_vl, c * TCH + sub * 128, 128, 0, 128),
                           ap2(vstl, 0, 128, sub * 128, 128))
            vstage_stored = stq_vh.done() + stq_vl.done()
            if c + 1 < NCHK:
                chunk_loaded[c + 1] = load_xt_cs((c + 1) % 2, x_kv, cos_kv, sin_kv, (c + 1) * TCH,
                                                 pe_after_chunk.get(c - 1, 0))
        for q_ in (stq_kh, stq_kl, stq_vh, stq_vl):
            q_.wait_all("gpsimd")
        for _ci, _co in ((cc_kh, scr_kh), (cc_kl, scr_kl), (cc_vh, scr_vh), (cc_vl, scr_vl)):
            b.L("gpsimd", lambda e, _a=_ci, _b=_co: self_cc(e, "AllGather", ALU.bypass,
                [_a[:, :]], [_b[:, :]], s_gp))
            s_gp.inc()
        kv_gathered = s_gp.v

        wq_pe = {}
        wq_ld = 0
        stq_qh = StoreQ(b, [s_st[8], s_st[9]], prewait=True)
        stq_ql = StoreQ(b, [s_st[0], s_st[1]], prewait=True)
        for _s, _v in b.scr_wq_ready:
            b.wait("sync", _s, _v)
        for q_ in (stq_kh, stq_kl, stq_vh, stq_vl):
            q_.wait_all("scalar")
            q_.wait_all("vector")
        for co in range(OCH):
            sl = co % 2
            b.wait("sync", s_pe, s_pe.v)
            b.wait("sync", s_dve, s_dve.v)
            xt_ready, xs_done = load_xt_cs(sl, x_to, coso, sino, co * TCH, s_pe.v)
            b.wait("tensor", s_dve, xs_done)
            for ob in range(D // 512):
                for dc in range(DC):
                    wsl = wq_ld % 2
                    if wq_ld >= 2:
                        b.wait("sync", s_pe, wq_pe[wq_ld - 2])
                    b.dma("sync", ap2(wqtlh[wsl], 0, 128, 0, 512),
                          ap2(scr_wqh, dc * 128, 128, ob * 512, 512), inc=s_ld[4 + wsl])
                    b.dma("sync", ap2(wqtll[wsl], 0, 128, 0, 512),
                          ap2(scr_wql, dc * 128, 128, ob * 512, 512), inc=s_ld[4 + wsl])
                    b.wait("tensor", s_ld[4 + wsl], s_ld[4 + wsl].v)
                    for sub in range(NSUB):
                        if dc == 0 and (ob > 0 or co > 0):
                            b.wait("tensor", s_dve, q_rope_done.get(sub, 0))
                        lhs_hl = (ap2(xth[sl], 0, 128, dc * TCH + sub * 128, 128),
                                  ap2(xtl[sl], 0, 128, dc * TCH + sub * 128, 128))
                        rhs_hl = (ap2(wqtlh[wsl], 0, 128, 0, 512),
                                  ap2(wqtll[wsl], 0, 128, 0, 512))
                        b.mm3p(ap2(psQ[sub], 0, 128, 0, 512), lhs_hl, rhs_hl,
                               first=(dc == 0), last=(dc == DC - 1),
                               inc=(s_pe if sub == NSUB - 1 else None))
                    wq_pe[wq_ld] = s_pe.v
                    wq_ld += 1
                if co == 0 and ob == 0:
                    q_rope_done = {}
                pe_q = s_pe.v
                for sub in range(NSUB):
                    b.wait("vector", s_pe, pe_q)
                    b.wait("vector", s_ld[sl], xt_ready)
                    if not (co == 0 and ob == 0):
                        for _s, _v in qstage_stored:
                            b.wait("vector", _s, _v)
                    if kpb_tr_pe is not None:
                        b.wait("vector", s_pe, kpb_tr_pe)
                    qr = rope(psQ[sub], 4, 512, sl, sub, kpb)
                    cs_free_dve = qr
                    q_rope_done[sub] = qr
                    transpose_pair(4, qsth, qstl,
                                   lambda i, _sub=sub: i * TCH + _sub * 128, qr)
                b.wait("gpsimd", s_act, s_act.v)
                for hs in range(4):
                    stq_qh.dma(ap2(scr_qh, ob * 512 + hs * 128, 128, co * TCH, TCH),
                               ap2(qsth, 0, 128, hs * TCH, TCH))
                    stq_ql.dma(ap2(scr_ql, ob * 512 + hs * 128, 128, co * TCH, TCH),
                               ap2(qstl, 0, 128, hs * TCH, TCH))
                qstage_stored = stq_qh.done() + stq_ql.done()
        for q_ in (stq_kh, stq_kl, stq_vh, stq_vl, stq_qh, stq_ql):
            q_.wait_all("gpsimd")
        b.run_stage("A01")
    es_a0.close()

    first_m0 = True
    for qt in range(NQT):
        es_qt = ExitStack()
        attn_oh = es_qt.enter_context(nc.sbuf_tensor(f"attn_oh_{qt}", [128, DC * QT], BF16))
        attn_ol = es_qt.enter_context(nc.sbuf_tensor(f"attn_ol_{qt}", [128, DC * QT], BF16))

        with ExitStack() as es:
            sbB = lambda n, sh, dt=F32: es.enter_context(nc.sbuf_tensor(f"{n}_{qt}", sh, dt))
            qtbh = sbB("qtbh", [128, DC * QT], BF16)
            qtbl = sbB("qtbl", [128, DC * QT], BF16)
            kgh = sbB("kgh", [128, S], BF16)
            kgl = sbB("kgl", [128, S], BF16)
            vgh = sbB("vgh", [128, NKC * HD], BF16)
            vgl = sbB("vgl", [128, NKC * HD], BF16)
            p32 = [[sbB(f"p32{h}{i}", [128, QT]) for i in range(3)] for h in range(2)]
            phi = [[sbB(f"phi{h}{i}", [128, QT], BF16) for i in range(3)] for h in range(2)]
            plo = [[sbB(f"plo{h}{i}", [128, QT], BF16) for i in range(3)] for h in range(2)]
            ptmp = sbB("ptmp", [128, QT])
            rcp = sbB("rcp", [1, QT])
            bcst = sbB("bcst", [128, QT])
            mx0 = sbB("mx0", [128, 1])
            ps = [es.enter_context(nc.psum_tensor(f"psB{i}_{qt}", [128, 512], F32)) for i in range(8)]

            b.plan = {e: [] for e in ENGS}
            prev_tail_pv = None
            evac_done_by_bank = {}
            b.wait("sync", s_gp, kv_gathered)
            b.dma("sync", qtbh[:, :],
                  bass.AP(scr_qh, qt * QT, [[T, 128], [128 * T, DC], [1, QT]]), inc=s_ld[0])
            b.dma("sync", qtbl[:, :],
                  bass.AP(scr_ql, qt * QT, [[T, 128], [128 * T, DC], [1, QT]]), inc=s_ld[0])
            q_ready = s_ld[0].v
            b.wait("tensor", s_ld[0], q_ready)
            for gidx in range(NKV):
                if gidx > 0:
                    b.wait("sync", s_pe, s_pe.v)
                NSHl = g.NC // NKV
                S_KVl = S // NSHl
                for half in range(NSHl):
                    blk_ = gidx + half * NKV
                    b.dma("sync", kgh[:, half * S_KVl:(half + 1) * S_KVl],
                          ap2(scr_kh, blk_ * 128, 128, 0, S_KVl), inc=s_ld[1])
                    b.dma("sync", kgl[:, half * S_KVl:(half + 1) * S_KVl],
                          ap2(scr_kl, blk_ * 128, 128, 0, S_KVl), inc=s_ld[1])
                NKCH = S_KVl // 128
                vsplit = max(1, NKCH // 8)
                for srcv, dstv in ((scr_vh, vgh), (scr_vl, vgl)):
                    for half in range(NSHl):
                        blk_ = gidx + half * NKV
                        for i in range(0, NKCH, vsplit):
                            n = min(vsplit, NKCH - i)
                            b.dma("sync",
                                  bass.AP(dstv, (half * NKCH + i) * HD, [[NKC * HD, 128], [HD, n], [1, HD]]),
                                  bass.AP(srcv, (blk_ * S_KVl + i * 128) * 128, [[128, 128], [128 * 128, n], [1, 128]]),
                                  inc=s_ld[1])
                kv_ready = s_ld[1].v
                b.wait("tensor", s_ld[1], kv_ready)
                assert REP % 2 == 0
                for hp in range(REP // 2):
                    hhA = gidx * REP + 2 * hp
                    hhB = hhA + 1
                    psS = [ps[0:2], ps[2:4]]
                    psPV = [ps[4], ps[5]]
                    psD = [ps[6], ps[7]]
                    act_exp = {}
                    dve_plo = {}
                    pe_pv = {}

                    def emit_pv(kp):
                        pbp = kp % 3
                        if kp == 0 and evac_done_by_bank.get(0):
                            b.wait("tensor", s_dve, evac_done_by_bank[0])
                        b.wait("tensor", s_act, act_exp[kp])
                        b.wait("tensor", s_dve, dve_plo[kp])
                        vh = bass.AP(vgh, kp * HD, [[NKC * HD, 128], [1, HD]])
                        vl = bass.AP(vgl, kp * HD, [[NKC * HD, 128], [1, HD]])
                        first = (kp == 0)
                        last = (kp == NKC - 1)
                        for hd in range(2):
                            b.mm(ap2(psPV[hd], 0, 128, 0, QT), vh, phi[hd][pbp][:, :],
                                 start=(first and True), stop=False, inc=None)
                            b.mm(ap2(psPV[hd], 0, 128, 0, QT), vh, plo[hd][pbp][:, :],
                                 start=False, stop=False, inc=None)
                        for hd in range(2):
                            b.mm(ap2(psPV[hd], 0, 128, 0, QT), vl, phi[hd][pbp][:, :],
                                 start=False, stop=last, inc=s_pe)
                            b.mm(ap2(psD[hd], 0, 1, 0, QT), ones_bf[:, 0:1], phi[hd][pbp][:, :],
                                 start=first, stop=last, inc=s_pe)
                        pe_pv[kp] = s_pe.v

                    for kc in range(NKC):
                        pb2 = (kc // 1) % 2
                        pb = kc % 3
                        if kc >= 2:
                            b.wait("tensor", s_act, act_exp[kc - 2])
                        kh_ap = ap2(kgh, 0, 128, kc * 128, 128)
                        kl_ap = ap2(kgl, 0, 128, kc * 128, 128)
                        qs = [(ap2(qtbh, 0, 128, hhA * QT, QT), ap2(qtbl, 0, 128, hhA * QT, QT)),
                              (ap2(qtbh, 0, 128, hhB * QT, QT), ap2(qtbl, 0, 128, hhB * QT, QT))]
                        for hd in range(2):
                            b.mm(ap2(psS[hd][pb2], 0, 128, 0, QT), kh_ap, qs[hd][0],
                                 start=True, stop=False, inc=None)
                            b.mm(ap2(psS[hd][pb2], 0, 128, 0, QT), kh_ap, qs[hd][1],
                                 start=False, stop=False, inc=None)
                        for hd in range(2):
                            b.mm(ap2(psS[hd][pb2], 0, 128, 0, QT), kl_ap, qs[hd][0],
                                 start=False, stop=True, inc=(s_pe if hd == 1 else None))
                        pe_s = s_pe.v
                        if first_m0 and kc == 0:
                            b.wait("vector", s_pe, pe_s)
                            b.vred(mx0[:, :], ap2(psS[0][0], 0, 128, 0, QT), ALU.max, inc=s_dve)
                            b.drain("vector")
                            b.wait("gpsimd", s_dve, s_dve.v)
                            b.allred(mx0[:, :], mx0[:, :], ReduceOp.max, inc=s_gp)
                            b.wait("vector", s_gp, s_gp.v)
                            b.ts(bias_m0[:, :], mx0[:, :], -SCALE, ALU.mult, inc=s_dve)
                            b.drain("vector")
                            b.wait("scalar", s_dve, s_dve.v)
                        if kc < 3 and prev_tail_pv is not None:
                            b.wait("scalar", s_pe, prev_tail_pv)
                        if kc >= 3:
                            b.wait("scalar", s_dve, dve_plo[kc - 3])
                            b.wait("scalar", s_pe, pe_pv[kc - 3])
                        b.wait("scalar", s_pe, pe_s)
                        for hd in range(2):
                            b.act(p32[hd][pb][:, :], ap2(psS[hd][pb2], 0, 128, 0, QT), AF.Exp,
                                  bias=bias_m0[:, :], scale=SCALE)
                            b.act(phi[hd][pb][:, :], ap2(psS[hd][pb2], 0, 128, 0, QT), AF.Exp,
                                  bias=bias_m0[:, :], scale=SCALE, inc=(s_act if hd == 1 else None))
                        act_exp[kc] = s_act.v
                        b.wait("vector", s_act, act_exp[kc])
                        for hd in range(2):
                            b.drain("vector")
                            b.tt(plo[hd][pb][:, :], p32[hd][pb][:, :], phi[hd][pb][:, :],
                                 ALU.subtract, inc=(s_dve if hd == 1 else None))
                        dve_plo[kc] = s_dve.v
                        b.drain("vector")
                        if kc >= 2:
                            emit_pv(kc - 2)
                    emit_pv(NKC - 2)
                    emit_pv(NKC - 1)
                    pe_tail = s_pe.v
                    for hd, hh in ((0, hhA), (1, hhB)):
                        b.wait("vector", s_pe, pe_tail)
                        b.recip(rcp[:, :], ap2(psD[hd], 0, 1, 0, QT), inc=s_dve)
                        b.drain("vector")
                        b.wait("gpsimd", s_dve, s_dve.v)
                        b.pbcast(bcst[:, :], rcp[:, :], inc=s_gp)
                        b.wait("vector", s_gp, s_gp.v)
                        b.tt(ptmp[:, :], ap2(psPV[hd], 0, 128, 0, QT), bcst[:, :], ALU.mult, inc=s_dve)
                        b.drain("vector")
                        b.wait("scalar", s_dve, s_dve.v)
                        b.acopy(ap2(attn_oh, 0, 128, hh * QT, QT), ptmp[:, :], inc=s_act)
                        b.wait("vector", s_act, s_act.v)
                        b.tt(ap2(attn_ol, 0, 128, hh * QT, QT), ptmp[:, :],
                             ap2(attn_oh, 0, 128, hh * QT, QT), ALU.subtract, inc=s_dve)
                        b.drain("vector")
                    b.wait("tensor", s_dve, s_dve.v)
                    evac_done_by_bank[0] = s_dve.v
                    prev_tail_pv = pe_tail
                    first_m0 = False
            b.run_stage(f"B{qt}")

        with ExitStack() as es:
            sbC = lambda n, sh, dt=F32: es.enter_context(nc.sbuf_tensor(f"{n}_{qt}", sh, dt))
            strm = sbC("strm", [128, DC * QT])
            xn = sbC("xn", [128, DC * QT])
            xnh = sbC("xnh", [128, DC * QT], BF16)
            xnl = sbC("xnl", [128, DC * QT], BF16)
            hbh = sbC("hbh", [128, HFC * FSUB], BF16)
            hbl = sbC("hbl", [128, HFC * FSUB], BF16)
            htmp = sbC("htmp", [128, FSUB])
            wtlh = [sbC("wtlh0", [128, 512], BF16), sbC("wtlh1", [128, 512], BF16)]
            wtll = [sbC("wtll0", [128, 512], BF16), sbC("wtll1", [128, 512], BF16)]
            wf8 = [sbC(f"wf8{i}", [128, 512], FP8) for i in range(4)]
            w2f8 = [sbC(f"w2f8{i}", [128, 512], FP8) for i in range(4)]
            sgb = sbC("sgb", [128, FSUB])
            stat = sbC("stat", [1, QT])
            stat2 = sbC("stat2", [1, QT])
            statm = sbC("statm", [1, QT])
            sqb = [sbC("sqb0", [128, QT]), sbC("sqb1", [128, QT])]
            bcstC = sbC("bcstC", [128, QT])
            bcstC2 = sbC("bcstC2", [128, QT])
            ybuf = sbC("ybuf", [128, D])
            ps = [es.enter_context(nc.psum_tensor(f"psC{i}_{qt}", [128, 512], F32)) for i in range(8)]

            b.plan = {e: [] for e in ENGS}
            b.dma("sync", strm[:, :],
                  bass.AP(x_to, qt * QT, [[T, 128], [128 * T, DC], [1, QT]]), inc=s_ld[0])
            x_ready = s_ld[0].v
            wo_ld = 0
            pe_oc = {}
            dve_oc = {}
            for oc in range(DC):
                psO = ps[oc % 2]
                if oc >= 2:
                    b.wait("tensor", s_dve, dve_oc[oc - 2])
                for hc in range(DC):
                    wsl = wo_ld % 2
                    if wo_ld >= 2:
                        b.wait("sync", s_pe, pe_oc[wo_ld - 2])
                    b.dma("sync", ap2(wtlh[wsl], 0, 128, 0, 128),
                          ap2(scr_woh, hc * 128, 128, oc * 128, 128), inc=s_ld[2 + wsl])
                    b.dma("sync", ap2(wtll[wsl], 0, 128, 0, 128),
                          ap2(scr_wol, hc * 128, 128, oc * 128, 128), inc=s_ld[2 + wsl])
                    b.wait("tensor", s_ld[2 + wsl], s_ld[2 + wsl].v)
                    lhs_hl = (ap2(wtlh[wsl], 0, 128, 0, 128), ap2(wtll[wsl], 0, 128, 0, 128))
                    rhs_hl = (ap2(attn_oh, 0, 128, hc * QT, QT), ap2(attn_ol, 0, 128, hc * QT, QT))
                    b.mm3p(ap2(psO, 0, 128, 0, QT), lhs_hl, rhs_hl,
                           first=(hc == 0), last=(hc == DC - 1), inc=s_pe)
                    pe_oc[wo_ld] = s_pe.v
                    wo_ld += 1
                b.wait("vector", s_pe, s_pe.v)
                if oc == 0:
                    b.wait("vector", s_ld[0], x_ready)
                b.drain("vector")
                b.tt(ap2(strm, 0, 128, oc * QT, QT), ap2(psO, 0, 128, 0, QT),
                     ap2(strm, 0, 128, oc * QT, QT), ALU.add, inc=s_dve)
                dve_oc[oc] = s_dve.v
                b.drain("vector")

            def layer_norm(src, dst, src_ready_dve):
                b.wait("tensor", s_dve, src_ready_dve)
                for dc in range(DC):
                    b.mm(ap2(ps[2], 0, 1, 0, QT), ones_sb[:, 0:1], ap2(src, 0, 128, dc * QT, QT),
                         start=(dc == 0), stop=(dc == DC - 1), inc=(s_pe if dc == DC - 1 else None))
                b.wait("vector", s_pe, s_pe.v)
                b.ts(stat[:, :], ap2(ps[2], 0, 1, 0, QT), 1.0 / D, ALU.mult, inc=s_dve)
                b.drain("vector")
                b.wait("scalar", s_dve, src_ready_dve)
                pe_sq = {}
                for dc in range(DC):
                    if dc >= 2:
                        b.wait("scalar", s_pe, pe_sq[dc - 2])
                    b.act(sqb[dc % 2][:, :], ap2(src, 0, 128, dc * QT, QT), AF.Square, inc=s_act)
                    b.wait("tensor", s_act, s_act.v)
                    b.mm(ap2(ps[3], 0, 1, 0, QT), ones_sb[:, 0:1], sqb[dc % 2][:, :],
                         start=(dc == 0), stop=(dc == DC - 1), inc=s_pe)
                    pe_sq[dc] = s_pe.v
                b.wait("vector", s_pe, s_pe.v)
                b.ts(stat2[:, :], ap2(ps[3], 0, 1, 0, QT), 1.0 / D, ALU.mult, inc=s_dve)
                b.drain("vector")
                b.tt(statm[:, :], stat[:, :], stat[:, :], ALU.mult, inc=s_dve)
                b.drain("vector")
                b.stt(stat2[:, :], statm[:, :], -1.0, stat2[:, :], ALU.mult, ALU.add, inc=s_dve)
                b.drain("vector")
                b.ts(stat2[:, :], stat2[:, :], 1.0, ALU.mult, s2=1e-5, op1=ALU.add, inc=s_dve)
                b.drain("vector")
                b.wait("scalar", s_dve, s_dve.v)
                b.act(statm[:, :], stat2[:, :], AF.Sqrt, inc=s_act)
                b.wait("vector", s_act, s_act.v)
                b.recip(stat2[:, :], statm[:, :], inc=s_dve)
                b.drain("vector")
                b.tt(statm[:, :], stat[:, :], stat2[:, :], ALU.mult, inc=s_dve)
                b.drain("vector")
                b.wait("gpsimd", s_dve, s_dve.v)
                b.pbcast(bcstC2[:, :], stat2[:, :], inc=s_gp)
                b.pbcast(bcstC[:, :], statm[:, :], inc=s_gp)
                b.wait("vector", s_gp, s_gp.v)
                for dc in range(DC):
                    b.tt(ap2(dst, 0, 128, dc * QT, QT), ap2(src, 0, 128, dc * QT, QT),
                         bcstC2[:, :], ALU.mult, inc=s_dve)
                    b.drain("vector")
                    b.tt(ap2(dst, 0, 128, dc * QT, QT), ap2(dst, 0, 128, dc * QT, QT),
                         bcstC[:, :], ALU.subtract, inc=s_dve)
                    b.drain("vector")
                return s_dve.v

            if g.dbg and qt == 0:
                b.wait("gpsimd", s_dve, s_dve.v)
                b.dma("gpsimd", dbg_strm[:, :], strm[:, :], inc=s_st[0])
                b.wait("gpsimd", s_st[0], s_st[0].v)
            xn_ready = layer_norm(strm, xn, s_dve.v)
            b.vcopy(xnh[:, :], xn[:, :])
            b.drain("vector")
            b.tt(xnl[:, :], xn[:, :], xnh[:, :], ALU.subtract, inc=s_dve)
            xn_split = s_dve.v
            b.drain("vector")
            if g.dbg and qt == 0:
                b.wait("gpsimd", s_dve, xn_split)
                b.dma("gpsimd", dbg_xn[:, :], xn[:, :], inc=s_st[0])
                b.wait("gpsimd", s_st[0], s_st[0].v)

            b.wait("sync", *b.scr_ffn_ready[0])
            w_pe = {}
            w_ld = 0
            m3_pe = {}
            m3_ld = 0
            for fs in range(QT // FSUB):
                for hbg in range(HFC // 4):
                    ps1, ps2 = ps[0:4], ps[4:8]
                    if hbg == 0:
                        b.wait("sync", s_pe, s_pe.v)
                    if not (fs == 0 and hbg == 0):
                        b.wait("tensor", s_dve, s_dve.v)
                    for dc in range(DC):
                        wsl = dc % 4
                        lsem1, lsem2 = s_ld[2 + wsl], s_ld[8 + wsl]
                        if w_ld >= 4:
                            b.wait("sync", s_pe, w_pe[w_ld - 4])
                        b.dma("sync", ap2(wf8[wsl], 0, 128, 0, 512),
                              ap2(scr_ws[0], dc * 128, 128, hbg * 512, 512), inc=lsem1)
                        b.dma("sync", ap2(w2f8[wsl], 0, 128, 0, 512),
                              ap2(scr_ws[1], dc * 128, 128, hbg * 512, 512), inc=lsem2)
                        b.wait("tensor", lsem1, lsem1.v)
                        b.wait("tensor", lsem2, lsem2.v)
                        if dc == 0 and hbg == 0 and fs == 0:
                            b.wait("tensor", s_dve, xn_split)
                        rhs_hl = (ap2(xnh, 0, 128, dc * QT + fs * FSUB, FSUB),
                                  ap2(xnl, 0, 128, dc * QT + fs * FSUB, FSUB))
                        for j in range(4):
                            b.mm2p(ap2(ps1[j], 0, 128, 0, FSUB), ap2(wf8[wsl], 0, 128, j * 128, 128),
                                   rhs_hl, first=(dc == 0), last=(dc == DC - 1), inc=None)
                            b.mm2p(ap2(ps2[j], 0, 128, 0, FSUB), ap2(w2f8[wsl], 0, 128, j * 128, 128),
                                   rhs_hl, first=(dc == 0), last=(dc == DC - 1),
                                   inc=(s_pe if j == 3 else None))
                        w_pe[w_ld] = s_pe.v
                        w_ld += 1
                    pe_12 = s_pe.v
                    b.wait("scalar", s_pe, pe_12)
                    for j in range(4):
                        hb = hbg * 4 + j
                        if j > 0:
                            b.wait("scalar", s_dve, s_dve.v)
                        b.act(sgb[:, :], ap2(ps1[j], 0, 128, 0, FSUB), AF.Sigmoid, scale=0.5, inc=s_act)
                        b.wait("vector", s_act, s_act.v)
                        b.tt(htmp[:, :], ap2(ps2[j], 0, 128, 0, FSUB), sgb[:, :], ALU.mult, inc=s_dve)
                        b.drain("vector")
                        b.wait("scalar", s_dve, s_dve.v)
                        b.acopy(ap2(hbh, 0, 128, hb * FSUB, FSUB), htmp[:, :], inc=s_act)
                        b.wait("vector", s_act, s_act.v)
                        b.tt(ap2(hbl, 0, 128, hb * FSUB, FSUB), htmp[:, :],
                             ap2(hbh, 0, 128, hb * FSUB, FSUB), ALU.subtract, inc=s_dve)
                        b.drain("vector")
                h_act_done = s_act.v
                h_done = s_dve.v
                for ocg in range(DC // 4):
                    ps3 = ps[0:4]
                    if ocg == 0:
                        b.wait("sync", s_pe, s_pe.v)
                    if not (fs == 0 and ocg == 0):
                        b.wait("tensor", s_dve, m3_evac)
                    for hb in range(HFC):
                        wsl = hb % 4
                        lsem = s_ld[2 + wsl]
                        if m3_ld >= 4:
                            b.wait("sync", s_pe, m3_pe[m3_ld - 4])
                        b.dma("sync", ap2(wf8[wsl], 0, 128, 0, 512),
                              ap2(scr_ws[2], hb * 128, 128, ocg * 512, 512), inc=lsem)
                        b.wait("tensor", lsem, lsem.v)
                        if hb == 0 and ocg == 0:
                            b.wait("tensor", s_dve, h_done)
                            b.wait("tensor", s_act, h_act_done)
                        rhs_hl = (ap2(hbh, 0, 128, hb * FSUB, FSUB),
                                  ap2(hbl, 0, 128, hb * FSUB, FSUB))
                        for j in range(4):
                            b.mm2p(ap2(ps3[j], 0, 128, 0, FSUB), ap2(wf8[wsl], 0, 128, j * 128, 128),
                                   rhs_hl, first=(hb == 0), last=(hb == HFC - 1),
                                   inc=(s_pe if j == 3 else None))
                        m3_pe[m3_ld] = s_pe.v
                        m3_ld += 1
                    b.wait("vector", s_pe, s_pe.v)
                    for j in range(4):
                        oc = ocg * 4 + j
                        b.stt(ap2(strm, 0, 128, oc * QT + fs * FSUB, FSUB),
                              ap2(ps3[j], 0, 128, 0, FSUB), 0.25,
                              ap2(xn, 0, 128, oc * QT + fs * FSUB, FSUB),
                              ALU.mult, ALU.add, inc=s_dve)
                        b.drain("vector")
                    m3_evac = s_dve.v

            if g.dbg and qt == 0:
                b.wait("gpsimd", s_dve, s_dve.v)
                b.dma("gpsimd", dbg_ffn[:, :], strm[:, :], inc=s_st[0])
                b.wait("gpsimd", s_st[0], s_st[0].v)
            out_ready = layer_norm(strm, xn, s_dve.v)
            tr_free = [None, None]
            stq_y = StoreQ(b, [s_st[6], s_st[7]])
            yst = stq_y.done()
            for sub in range(QT // 128):
                for oc in range(DC):
                    bank = 4 + (oc % 2)
                    b.wait("tensor", s_act, tr_free[oc % 2])
                    b.wait("tensor", s_dve, out_ready)
                    b.tr(ap2(ps[bank], 0, 128, 0, 128),
                         ap2(xn, 0, 128, oc * QT + sub * 128, 128), ident_sb[:, :], inc=s_pe)
                    b.wait("scalar", s_pe, s_pe.v)
                    if sub > 0 and oc == 0:
                        for _s, _v in yst:
                            b.wait("scalar", _s, _v)
                    b.acopy(ap2(ybuf, 0, 128, oc * 128, 128), ap2(ps[bank], 0, 128, 0, 128), inc=s_act)
                    tr_free[oc % 2] = s_act.v
                b.wait("gpsimd", s_act, s_act.v)
                stq_y.dma(ap2(y_out, qt * QT + sub * 128, 128, 0, D),
                          ap2(ybuf, 0, 128, 0, D))
                yst = stq_y.done()
            stq_y.wait_all("gpsimd")
            b.run_stage(f"C{qt}")
        es_qt.close()

    es_outer.close()
    mybir.codegen_inst_isa_subclasses(nc)
    return nc



def rope_tables(S, HD):
    import jax
    import jax.numpy as jnp
    with jax.default_device(jax.devices("cpu")[0]):
        inv = 1.0 / 10000.0 ** (jnp.arange(0, HD, 2, dtype=jnp.float32) / HD)
        t = jnp.arange(S, dtype=jnp.float32)
        freqs = jnp.einsum("i,j->ij", t, inv)
        emb = jnp.concatenate([freqs, freqs], axis=-1)
        cos = np.asarray(jnp.cos(emb), dtype=np.float32)
        sin = np.asarray(jnp.sin(emb), dtype=np.float32)
    return cos, sin


def prep_core_inputs(g: Geom, core, x, Wq, Wk, Wv, Wo, W1, W2, W3):
    S, D, HD, T = g.S, g.D, g.HD, g.T
    x2 = np.ascontiguousarray(x.reshape(S, D))
    cosf, sinf = rope_tables(S, HD)
    sin_signed = sinf.copy()
    sin_signed[:, :HD // 2] *= -1.0
    sl = slice(core * T, (core + 1) * T)
    NC = g.NC
    DS = D // NC
    HFP = ((g.HF + 128 * NC - 1) // (128 * NC)) * 128 * NC
    w3p = np.zeros((HFP, D), dtype=np.float32)
    w3p[:g.HF] = W3.T
    dsl = slice(core * DS, (core + 1) * DS)
    hsl = slice(core * (HFP // NC), (core + 1) * (HFP // NC))
    hc = core % g.NKV
    th = core // g.NKV
    S_KV = S // (NC // g.NKV)
    ksl = slice(th * S_KV, (th + 1) * S_KV)
    wkv_slice = np.concatenate([Wk.T[:, hc * 128:(hc + 1) * 128],
                                Wv.T[:, hc * 128:(hc + 1) * 128]], axis=1)
    return {
        "x_t": np.ascontiguousarray(x2.T),
        "x_to": np.ascontiguousarray(x2.T[:, sl]),
        "x_kv": np.ascontiguousarray(x2.T[:, ksl]),
        "cos_kv": np.ascontiguousarray(cosf[ksl]),
        "sin_kv": np.ascontiguousarray(sin_signed[ksl]),
        "wkv_s": np.ascontiguousarray(wkv_slice),
        "wq_s": np.ascontiguousarray(Wq.T[dsl]),
        "wo_s": np.ascontiguousarray(Wo.T[dsl]),
        "w1_s": np.ascontiguousarray(W1.T[dsl]),
        "w2_s": np.ascontiguousarray(W2.T[dsl]),
        "w3_s": np.ascontiguousarray(w3p[hsl]),
        "cosf": cosf, "sinf": sin_signed,
        "coso": np.ascontiguousarray(cosf[sl]), "sino": np.ascontiguousarray(sin_signed[sl]),
        "ident": np.eye(128, dtype=np.float32),
        "ones": np.ones((128, 128), dtype=np.float32),
    }




from concourse.bass_utils import run_bass_kernel_spmd


def kernel(L0_attn_x, L0_attn_Wq, L0_attn_Wk, L0_attn_Wv, L0_attn_Wo,
           L0_ffn_w1, L0_ffn_w2, L0_ffn_w3):
    g = Geom()
    x = np.ascontiguousarray(np.asarray(L0_attn_x, dtype=np.float32).reshape(g.S, g.D))
    Wq = np.asarray(L0_attn_Wq, dtype=np.float32)
    Wk = np.asarray(L0_attn_Wk, dtype=np.float32)
    Wv = np.asarray(L0_attn_Wv, dtype=np.float32)
    Wo = np.asarray(L0_attn_Wo, dtype=np.float32)
    W1 = np.asarray(L0_ffn_w1, dtype=np.float32)
    W2 = np.asarray(L0_ffn_w2, dtype=np.float32)
    W3 = np.asarray(L0_ffn_w3, dtype=np.float32)

    nc = build(g)

    x_t = np.ascontiguousarray(x.T)
    cosf, sinf = rope_tables(g.S, g.HD)
    sin_signed = sinf.copy()
    sin_signed[:, :g.HD // 2] *= -1.0
    wq_T = np.ascontiguousarray(Wq.T)
    wo_T = np.ascontiguousarray(Wo.T)
    w1_T = np.ascontiguousarray(W1.T)
    w2_T = np.ascontiguousarray(W2.T)
    NCn = g.NC
    DS = g.D // NCn
    HFP = ((g.HF + 128 * NCn - 1) // (128 * NCn)) * 128 * NCn
    HFS = HFP // NCn
    w3p = np.zeros((HFP, g.D), dtype=np.float32)
    w3p[:g.HF] = W3.T
    wk_T = np.ascontiguousarray(Wk.T)
    wv_T = np.ascontiguousarray(Wv.T)
    shared = {
        "x_t": x_t,
        "cosf": cosf, "sinf": sin_signed,
        "ident": np.eye(128, dtype=np.float32),
        "ones": np.ones((128, 128), dtype=np.float32),
    }
    in_maps = []
    for c in range(NCn):
        sl = slice(c * g.T, (c + 1) * g.T)
        dsl = slice(c * DS, (c + 1) * DS)
        hsl = slice(c * HFS, (c + 1) * HFS)
        m = dict(shared)
        m["x_to"] = np.ascontiguousarray(x_t[:, sl])
        m["coso"] = np.ascontiguousarray(cosf[sl])
        m["sino"] = np.ascontiguousarray(sin_signed[sl])
        m["wq_s"] = np.ascontiguousarray(wq_T[dsl])
        m["wo_s"] = np.ascontiguousarray(wo_T[dsl])
        m["w1_s"] = np.ascontiguousarray(w1_T[dsl])
        m["w2_s"] = np.ascontiguousarray(w2_T[dsl])
        m["w3_s"] = np.ascontiguousarray(w3p[hsl])
        hc = c % g.NKV
        th = c // g.NKV
        S_KV = g.S // (NCn // g.NKV)
        ksl = slice(th * S_KV, (th + 1) * S_KV)
        m["x_kv"] = np.ascontiguousarray(x_t[:, ksl])
        m["cos_kv"] = np.ascontiguousarray(cosf[ksl])
        m["sin_kv"] = np.ascontiguousarray(sin_signed[ksl])
        m["wkv_s"] = np.ascontiguousarray(np.concatenate(
            [wk_T[:, hc * 128:(hc + 1) * 128], wv_T[:, hc * 128:(hc + 1) * 128]], axis=1))
        in_maps.append(m)

    res = run_bass_kernel_spmd(nc, in_maps, core_ids=list(range(g.NC)))
    y = np.concatenate([np.asarray(res.results[c]["y"]) for c in range(g.NC)], axis=0)
    return y.reshape(1, g.S, g.D).astype(np.float32, copy=False)


# version 24
# speedup vs baseline: 1.3653x; 1.0715x over previous
import math
from contextlib import ExitStack
from dataclasses import dataclass

import numpy as np

import concourse.bass as bass
import concourse.mybir as mybir
from concourse.alu_op_type import AluOpType as ALU
from bass_rust import ReduceOp
from concourse import library_config

F32 = mybir.dt.float32
BF16 = mybir.dt.bfloat16
FP8 = mybir.dt.float8e4
AF = mybir.ActivationFunctionType
AX = mybir.AxisListType

ENGS = ("tensor", "vector", "scalar", "sync", "gpsimd")


@dataclass
class Geom:
    S: int = 8192
    D: int = 2048
    NH: int = 16
    NKV: int = 4
    HD: int = 128
    HF: int = 5632
    NC: int = 8
    TCH: int = 512
    QT: int = 512
    FSUB: int = 256
    dbg: bool = False

    @property
    def T(self): return self.S // self.NC
    @property
    def DC(self): return self.D // 128
    @property
    def NKD(self): return self.NKV * self.HD
    @property
    def HFC(self): return self.HF // 128
    @property
    def REP(self): return self.NH // self.NKV
    @property
    def NKC(self): return self.S // 128
    @property
    def NWELEM(self): return self.D * self.HF


class SemC:
    def __init__(self, nc, name, step):
        self.h = nc.alloc_semaphore(name=name)
        self.step = step
        self.v = 0

    def inc(self):
        self.v += self.step
        return self.v


class Builder:
    def __init__(self, g: Geom):
        self.g = g
        self.nc = bass.Bass("TRN2", target_bir_lowering=False)
        self.plan = None

    def L(self, eng, f):
        self.plan[eng].append(f)

    def wait(self, eng, sem: SemC, val=None):
        v = sem.v if val is None else val
        if v is not None and v > 0:
            self.L(eng, lambda e, _s=sem.h, _v=v: e.wait_ge(_s, _v))

    def drain(self, eng):
        self.L(eng, lambda e: e.drain())

    def _inc(self, instr, sem):
        if sem is not None:
            instr.then_inc(sem.h, sem.step)

    def mm(self, out, lhsT, rhs, start, stop, inc=None):
        if inc is not None: inc.inc()
        self.L("tensor", lambda e, _o=out, _l=lhsT, _r=rhs, _a=start, _b=stop, _s=inc:
               self._inc(e.matmul(_o, _l, _r, start=_a, stop=_b), _s))

    def mm3p(self, out, lhs_hl, rhs_hl, inc=None, first=True, last=True):
        combos = ((0, 0), (0, 1), (1, 0))
        for i, (a, c) in enumerate(combos):
            self.mm(out, lhs_hl[a], rhs_hl[c],
                    start=(first and i == 0), stop=(last and i == 2),
                    inc=(inc if i == 2 else None))

    def mm2p(self, out, lhs_exact, rhs_hl, inc=None, first=True, last=True):
        for i in range(2):
            self.mm(out, lhs_exact, rhs_hl[i],
                    start=(first and i == 0), stop=(last and i == 1),
                    inc=(inc if i == 1 else None))

    def tr(self, out, in_, ident, inc=None):
        if inc is not None: inc.inc()
        self.L("tensor", lambda e, _o=out, _i=in_, _d=ident, _s=inc:
               self._inc(e.transpose(_o, _i, _d), _s))

    def act(self, out, in_, func, bias=None, scale=None, accum=None, inc=None):
        if inc is not None: inc.inc()
        kw = {}
        if bias is not None: kw["bias"] = bias
        if scale is not None: kw["scale"] = scale
        if accum is not None: kw["accum_out"] = accum
        self.L("scalar", lambda e, _o=out, _i=in_, _f=func, _k=kw, _s=inc:
               self._inc(e.activation(out=_o, in_=_i, func=_f, **_k), _s))

    def acopy(self, out, in_, inc=None):
        if inc is not None: inc.inc()
        self.L("scalar", lambda e, _o=out, _i=in_, _s=inc:
               self._inc(e.copy(_o, _i), _s))

    def tt(self, out, in0, in1, op, inc=None):
        if inc is not None: inc.inc()
        self.L("vector", lambda e, _o=out, _a=in0, _b=in1, _p=op, _s=inc:
               self._inc(e.tensor_tensor(_o, _a, _b, op=_p), _s))

    def ts(self, out, in0, s1, op0, s2=None, op1=None, inc=None):
        if inc is not None: inc.inc()
        kw = dict(scalar2=s2)
        if op1 is not None: kw["op1"] = op1
        self.L("vector", lambda e, _o=out, _a=in0, _1=s1, _p=op0, _k=kw, _s=inc:
               self._inc(e.tensor_scalar(out=_o, in0=_a, scalar1=_1, op0=_p, **_k), _s))

    def stt(self, out, in0, scalar, in1, op0, op1, inc=None):
        if inc is not None: inc.inc()
        self.L("vector", lambda e, _o=out, _a=in0, _c=scalar, _b=in1, _p=op0, _q=op1, _s=inc:
               self._inc(e.scalar_tensor_tensor(out=_o, in0=_a, scalar=_c, in1=_b, op0=_p, op1=_q), _s))

    def vcopy(self, out, in_, inc=None):
        if inc is not None: inc.inc()
        self.L("vector", lambda e, _o=out, _i=in_, _s=inc:
               self._inc(e.tensor_copy(_o, _i), _s))

    def vred(self, out, in_, op, absval=False, inc=None):
        if inc is not None: inc.inc()
        self.L("vector", lambda e, _o=out, _i=in_, _p=op, _a=absval, _s=inc:
               self._inc(e.tensor_reduce(_o, _i, axis=AX.X, op=_p, apply_absolute_value=_a), _s))

    def recip(self, out, in_, inc=None):
        if inc is not None: inc.inc()
        self.L("vector", lambda e, _o=out, _i=in_, _s=inc:
               self._inc(e.reciprocal(_o, _i), _s))

    def dma(self, queue, out, in_, inc=None):
        if inc is not None: inc.inc()
        self.L(queue, lambda e, _o=out, _i=in_, _s=inc:
               self._inc(e.dma_start(_o, _i), _s))

    def pbcast(self, out, in_, inc=None):
        if inc is not None: inc.inc()
        self.L("gpsimd", lambda e, _o=out, _i=in_, _s=inc:
               self._inc(e.partition_broadcast(_o, _i), _s))

    def allred(self, out, in_, op, inc=None):
        if inc is not None: inc.inc()
        self.L("gpsimd", lambda e, _o=out, _i=in_, _p=op, _s=inc:
               self._inc(e.partition_all_reduce(_o, _i, channels=128, reduce_op=_p), _s))

    def run_stage(self, name):
        plan = self.plan
        with self.nc.named_scope(name):
            with self.nc.Block() as block:
                for eng in ENGS:
                    if plan[eng]:
                        def body(e, _th=tuple(plan[eng])):
                            for f in _th:
                                f(e)
                        getattr(block, eng)(body)
        self.plan = None


class StoreQ:
    def __init__(self, b, sems, prewait=False):
        self.b = b
        self.sems = sems
        self.i = 0
        self.prewait = prewait

    def dma(self, out, in_):
        b, sems = self.b, self.sems
        s = sems[self.i % len(sems)]
        if self.i >= len(sems) or self.prewait:
            b.wait("gpsimd", s, s.v)
        b.dma("gpsimd", out, in_, inc=s)
        self.i += 1

    def done(self):
        return [(s, s.v) for s in self.sems]

    def wait_all(self, eng):
        for s, v in self.done():
            self.b.wait(eng, s, v)


def ap2(t, p0, np_, c0, nc_):
    return t[p0:p0 + np_, c0:c0 + nc_]


def make_self_cc(nc_cores):
    def self_cc(e, kind, op, ins, outs, sem):
        e.collective_compute(kind, op, replica_groups=[list(range(nc_cores))],
                             ins=ins, outs=outs).then_inc(sem.h, sem.step)
    return self_cc


def build(g: Geom):
    b = Builder(g)
    nc = b.nc
    S, D, HD, HF, T = g.S, g.D, g.HD, g.HF, g.T
    DC, NKD, NKV, HFC, REP, NKC = g.DC, g.NKD, g.NKV, g.HFC, g.REP, g.NKC
    TCH, QT, FSUB = g.TCH, g.QT, g.FSUB
    NCH = S // TCH
    OCH = T // TCH
    A0_UNITS_PER_CHUNK = -(-((D // 128) * (HF // 512) * 4 + (HF // 128) * (D // 512) * 2 + (D // 128) * (D // 512) * 2 + 8) // (S // TCH))
    NQT = T // QT
    NSUB = TCH // 128
    SCALE = 1.0 / math.sqrt(HD)
    self_cc = make_self_cc(g.NC)

    x_t = nc.dram_tensor("x_t", [D, S], F32, kind="ExternalInput")
    x_to = nc.dram_tensor("x_to", [D, T], F32, kind="ExternalInput")
    DS = D // g.NC
    HFP = ((HF + 128 * g.NC - 1) // (128 * g.NC)) * 128 * g.NC
    HFS = HFP // g.NC
    wq_s = nc.dram_tensor("wq_s", [DS, D], F32, kind="ExternalInput")
    NSH = g.NC // NKV
    S_KV = S // NSH
    x_kv = nc.dram_tensor("x_kv", [D, S_KV], F32, kind="ExternalInput")
    cos_kv = nc.dram_tensor("cos_kv", [S_KV, HD], F32, kind="ExternalInput")
    sin_kv = nc.dram_tensor("sin_kv", [S_KV, HD], F32, kind="ExternalInput")
    wkv_s = nc.dram_tensor("wkv_s", [D, 256], F32, kind="ExternalInput")
    wo_s = nc.dram_tensor("wo_s", [DS, D], F32, kind="ExternalInput")
    w1_s = nc.dram_tensor("w1_s", [DS, HF], F32, kind="ExternalInput")
    w2_s = nc.dram_tensor("w2_s", [DS, HF], F32, kind="ExternalInput")
    w3_s = nc.dram_tensor("w3_s", [HFS, D], F32, kind="ExternalInput")
    cosf = nc.dram_tensor("cosf", [S, HD], F32, kind="ExternalInput")
    sinf = nc.dram_tensor("sinf", [S, HD], F32, kind="ExternalInput")
    coso = nc.dram_tensor("coso", [T, HD], F32, kind="ExternalInput")
    sino = nc.dram_tensor("sino", [T, HD], F32, kind="ExternalInput")
    ident_in = nc.dram_tensor("ident", [128, 128], F32, kind="ExternalInput")
    ones_in = nc.dram_tensor("ones", [128, 128], F32, kind="ExternalInput")
    y_out = nc.dram_tensor("y", [T, D], F32, kind="ExternalOutput")

    cc_k = nc.dram_tensor("cc_k", [256, S_KV], BF16)
    cc_v = nc.dram_tensor("cc_v", [2 * S_KV, 128], BF16)
    scr_k = nc.dram_tensor("scr_k", [g.NC * 256, S_KV], BF16, addr_space=("Shared" if g.NC > 4 else "Local"))
    scr_v = nc.dram_tensor("scr_v", [g.NC * 2 * S_KV, 128], BF16, addr_space=("Shared" if g.NC > 4 else "Local"))
    scr_ah = nc.dram_tensor("scr_ah", [D, T], BF16)
    scr_al = nc.dram_tensor("scr_al", [D, T], BF16)
    scr_qh = nc.dram_tensor("scr_qh", [D, T], BF16)
    scr_ql = nc.dram_tensor("scr_ql", [D, T], BF16)
    CCSP = "Shared" if g.NC > 4 else "Local"
    cc_w12 = nc.dram_tensor("cc_w12", [2 * DS, HF], FP8)
    cc_w3 = nc.dram_tensor("cc_w3", [HFS, D], FP8)
    scr_w12 = nc.dram_tensor("scr_w12", [g.NC * 2 * DS, HF], FP8, addr_space=CCSP)
    scr_w3g = nc.dram_tensor("scr_w3g", [HFP, D], FP8, addr_space=CCSP)

    def w12_row(kind, r):
        rank, off = divmod(r, DS)
        return rank * 2 * DS + kind * DS + off
    cc_wsp_all = nc.dram_tensor("cc_wsp_all", [4 * DS, D], BF16)
    scr_wsp = nc.dram_tensor("scr_wsp", [g.NC * 4 * DS, D], BF16, addr_space=CCSP)

    def wsp_row(kind, r):
        rank, off = divmod(r, DS)
        return rank * 4 * DS + kind * DS + off
    cc_g_in = nc.dram_tensor("cc_g_in", [1, 3], F32)
    cc_g_out = nc.dram_tensor("cc_g_out", [1, 3], F32)

    if g.dbg:
        dbg_attn = nc.dram_tensor("dbg_attn", [128, DC * QT], BF16, kind="ExternalOutput")
        dbg_strm = nc.dram_tensor("dbg_strm", [128, DC * QT], F32, kind="ExternalOutput")
        dbg_xn = nc.dram_tensor("dbg_xn", [128, DC * QT], F32, kind="ExternalOutput")
        dbg_ffn = nc.dram_tensor("dbg_ffn", [128, DC * QT], F32, kind="ExternalOutput")

    sem = lambda n, s: SemC(nc, n, s)
    s_pe = sem("pe", 1)
    s_act = sem("act", 1)
    s_dve = sem("dve", 1)
    s_gp = sem("gp", 1)
    s_ld = [sem(f"ld{i}", 16) for i in range(12)]
    s_st = [sem(f"st{i}", 16) for i in range(16)]

    es_outer = ExitStack()
    sb = lambda n, sh, dt=F32: es_outer.enter_context(nc.sbuf_tensor(n, sh, dt))

    ident_sb = sb("identsb", [128, 128])
    ones_sb = sb("onessb", [128, 128])
    ones_bf = sb("onesbf", [128, 2], BF16)
    ident_bf = sb("identbf", [128, 128], BF16)
    bias_m0 = sb("bias_m0", [128, 1])
    hthr = sb("hthr", [128, 6])

    b.plan = {e: [] for e in ENGS}
    b.dma("sync", ident_sb[:, :], ident_in[:, :], inc=s_ld[3])
    b.dma("sync", ones_sb[:, :], ones_in[:, :], inc=s_ld[3])
    consts_ready = s_ld[3].v
    b.wait("vector", s_ld[3], consts_ready)
    b.vcopy(ones_bf[:, 0:1], ones_sb[:, 0:1], inc=s_dve)
    b.drain("vector")
    b.vcopy(ident_bf[:, :], ident_sb[:, :], inc=s_dve)
    b.drain("vector")
    b.L("gpsimd", lambda e: e.load_library(library_config.attnmlp))

    es_a0 = ExitStack()
    if True:
        sbA = lambda n, sh, dt=F32: es_a0.enter_context(nc.sbuf_tensor(n, sh, dt))

        def colchunks(width):
            cw = width
            while cw > 2048:
                assert cw % 2 == 0
                cw //= 2
            return cw, width // cw
        cw1, nc1 = colchunks(HF)
        cw3, nc3 = colchunks(D)
        WMAX = max(cw1, cw3, D)
        g_sb = es_a0.enter_context(nc.sbuf_tensor("g_sb", [128, 4], F32))
        w_in = [sbA("w_in0", [128, WMAX]), sbA("w_in1", [128, WMAX])]
        s1b = [sbA("s1b0", [128, WMAX]), sbA("s1b1", [128, WMAX])]
        s2b = [sbA("s2b0", [128, WMAX]), sbA("s2b1", [128, WMAX])]
        wqb = [sbA("wqb0", [128, WMAX], FP8), sbA("wqb1", [128, WMAX], FP8)]
        wsph = [sbA("wsph0", [128, D], BF16), sbA("wsph1", [128, D], BF16)]
        wspl = [sbA("wspl0", [128, D], BF16), sbA("wspl1", [128, D], BF16)]
        n_t1 = (D // 128) * nc1 * 2 + (HF // 128) * nc3
        accb = sbA("accb", [128, n_t1 + 8])
        gtot = sbA("gtot", [128, 3])

        wsrc = [w1_s, w2_s, w3_s]
        wtiles = [[(r, c, cw1) for r in range(DS // 128) for c in range(nc1)],
                  [(r, c, cw1) for r in range(DS // 128) for c in range(nc1)],
                  [(r, c, cw3) for r in range(HFS // 128) for c in range(nc3)]]

    def a0_work():
            col = 0
            wcol = []
            dve_done = {}
            ti = 0
            for wi, tiles in enumerate(wtiles):
                wcol.append(col)
                for (r, cc, width) in tiles:
                    sl = ti % 2
                    if ti >= 2:
                        b.wait("sync", s_act, dve_done[ti - 2])
                    b.dma("sync", ap2(w_in[sl], 0, 128, 0, width),
                          ap2(wsrc[wi], r * 128, 128, cc * width, width), inc=s_ld[6 + sl])
                    b.wait("scalar", s_ld[6 + sl], s_ld[6 + sl].v)
                    b.act(ap2(s1b[sl], 0, 128, 0, width), ap2(w_in[sl], 0, 128, 0, width),
                          AF.Abs, accum=accb[:, col:col + 1], inc=s_act)
                    dve_done[ti] = s_act.v
                    col += 1
                    ti += 1
                    yield
            b.wait("vector", s_act, s_act.v)
            for wi in range(3):
                c0, c1 = wcol[wi], wcol[wi] + len(wtiles[wi])
                b.vred(gtot[:, wi:wi + 1], accb[:, c0:c1], ALU.add, inc=s_dve)
                b.drain("vector")
            b.wait("gpsimd", s_dve)
            for wi in range(3):
                b.allred(gtot[:, wi:wi + 1], gtot[:, wi:wi + 1], ReduceOp.add, inc=s_gp)
            b.wait("gpsimd", s_gp, s_gp.v)
            b.dma("gpsimd", cc_g_in[0:1, 0:3], gtot[0:1, 0:3], inc=s_st[14])
            b.wait("gpsimd", s_st[14], s_st[14].v)
            b.L("gpsimd", lambda e: self_cc(e, "AllReduce", ALU.add, [cc_g_in[:, :]], [cc_g_out[:, :]], s_gp))
            s_gp.inc()
            b.wait("gpsimd", s_gp, s_gp.v)
            b.dma("gpsimd", g_sb[0:1, 0:3], cc_g_out[0:1, 0:3], inc=s_st[14])
            b.wait("gpsimd", s_st[14], s_st[14].v)
            b.pbcast(gtot[:, 0:3], g_sb[0:1, 0:3], inc=s_gp)
            b.wait("vector", s_gp)
            for wi in range(3):
                nw = float(g.NWELEM)
                b.ts(hthr[:, 2 * wi:2 * wi + 1], gtot[:, wi:wi + 1], -0.5 / nw, ALU.mult,
                     s2=-0.5e-8, op1=ALU.add, inc=s_dve)
                b.ts(hthr[:, 2 * wi + 1:2 * wi + 2], gtot[:, wi:wi + 1], 0.5 / nw, ALU.mult,
                     s2=0.5e-8, op1=ALU.add, inc=s_dve)
                b.drain("vector")
            hready = s_dve.v
            b.wait("scalar", s_dve, hready)

            b.wait("sync", s_act, s_act.v)
            b.wait("sync", s_dve, s_dve.v)
            stq_h = StoreQ(b, [s_st[10], s_st[11]])
            stq_l = StoreQ(b, [s_st[12], s_st[13]])
            sp_src = [(wq_s, 0, 1), (wo_s, 2, 3)]
            ti = 0
            dve_done = {}
            st_done = {}
            for (srcw, kh_, kl_) in sp_src:
                for r in range(DS // 128):
                    sl = ti % 2
                    if ti >= 2:
                        b.wait("sync", s_dve, dve_done[ti - 2])
                    b.dma("sync", ap2(w_in[sl], 0, 128, 0, D),
                          ap2(srcw, r * 128, 128, 0, D), inc=s_ld[6 + sl])
                    b.wait("vector", s_ld[6 + sl], s_ld[6 + sl].v)
                    if ti >= 2:
                        for _s, _v in st_done[ti - 2]:
                            b.wait("vector", _s, _v)
                    b.drain("vector")
                    b.vcopy(ap2(wsph[sl], 0, 128, 0, D), ap2(w_in[sl], 0, 128, 0, D))
                    b.drain("vector")
                    b.tt(ap2(wspl[sl], 0, 128, 0, D), ap2(w_in[sl], 0, 128, 0, D),
                         ap2(wsph[sl], 0, 128, 0, D), ALU.subtract, inc=s_dve)
                    dve_done[ti] = s_dve.v
                    b.drain("vector")
                    b.wait("gpsimd", s_dve, s_dve.v)
                    stq_h.dma(ap2(cc_wsp_all, kh_ * DS + r * 128, 128, 0, D), ap2(wsph[sl], 0, 128, 0, D))
                    stq_l.dma(ap2(cc_wsp_all, kl_ * DS + r * 128, 128, 0, D), ap2(wspl[sl], 0, 128, 0, D))
                    st_done[ti] = stq_h.done() + stq_l.done()
                    ti += 1
                    yield
            stq_h.wait_all("gpsimd")
            stq_l.wait_all("gpsimd")
            b.L("gpsimd", lambda e: self_cc(e, "AllGather", ALU.bypass,
                [cc_wsp_all[:, :]], [scr_wsp[:, :]], s_gp))
            s_gp.inc()
            b.scr_wq_ready = [(s_gp, s_gp.v)]
            yield


            b.wait("sync", s_dve, s_dve.v)
            stq = StoreQ(b, [s_st[8], s_st[9]])
            ti = 0
            act_done = {}
            dve_done = {}
            st_done = {}
            for wi, tiles in enumerate(wtiles):
                for (r, cc, width) in tiles:
                    sl = ti % 2
                    if ti >= 2:
                        b.wait("sync", s_act, act_done[ti - 2])
                    b.dma("sync", ap2(w_in[sl], 0, 128, 0, width),
                          ap2(wsrc[wi], r * 128, 128, cc * width, width), inc=s_ld[6 + sl])
                    b.wait("scalar", s_ld[6 + sl], s_ld[6 + sl].v)
                    if ti >= 2:
                        b.wait("scalar", s_dve, dve_done[ti - 2])
                    b.act(ap2(s1b[sl], 0, 128, 0, width), ap2(w_in[sl], 0, 128, 0, width),
                          AF.Sign, bias=hthr[:, 2 * wi:2 * wi + 1], inc=s_act)
                    b.act(ap2(s2b[sl], 0, 128, 0, width), ap2(w_in[sl], 0, 128, 0, width),
                          AF.Sign, bias=hthr[:, 2 * wi + 1:2 * wi + 2], inc=s_act)
                    act_done[ti] = s_act.v
                    b.wait("vector", s_act, act_done[ti])
                    if ti >= 2:
                        for _s, _v in st_done[ti - 2]:
                            b.wait("vector", _s, _v)
                    b.drain("vector")
                    b.tt(ap2(wqb[sl], 0, 128, 0, width), ap2(s1b[sl], 0, 128, 0, width),
                         ap2(s2b[sl], 0, 128, 0, width), ALU.add, inc=s_dve)
                    dve_done[ti] = s_dve.v
                    b.wait("gpsimd", s_dve, s_dve.v)
                    if wi < 2:
                        stq.dma(ap2(cc_w12, wi * DS + r * 128, 128, cc * width, width),
                                ap2(wqb[sl], 0, 128, 0, width))
                    else:
                        stq.dma(ap2(cc_w3, r * 128, 128, cc * width, width),
                                ap2(wqb[sl], 0, 128, 0, width))
                    st_done[ti] = stq.done()
                    ti += 1
                    yield
            stq.wait_all("gpsimd")
            for _ci, _co in ((cc_w12, scr_w12), (cc_w3, scr_w3g)):
                b.L("gpsimd", lambda e, _a=_ci, _b=_co: self_cc(e, "AllGather", ALU.bypass,
                    [_a[:, :]], [_b[:, :]], s_gp))
                s_gp.inc()
            b.scr_ffn_ready = [(s_gp, s_gp.v)]

    a0_gen = a0_work()
    for _ in a0_gen:
        pass
    b.run_stage("A0")
    es_a0.close()

    with ExitStack() as es:
        sbA = lambda n, sh, dt=F32: es.enter_context(nc.sbuf_tensor(n, sh, dt))
        xt32 = sbA("xt32", [128, DC * TCH])
        xth = [sbA("xth0", [128, DC * TCH], BF16), sbA("xth1", [128, DC * TCH], BF16)]
        xtl = [sbA("xtl0", [128, DC * TCH], BF16), sbA("xtl1", [128, DC * TCH], BF16)]
        wkv32 = sbA("wkv32", [128, max(NKD, 256)])
        wkvh = sbA("wkvh", [128, DC * 256], BF16)
        wkvl = sbA("wkvl", [128, DC * 256], BF16)
        cs = [sbA("cs0", [128, NSUB * 2 * HD])]
        t1b = sbA("t1b", [128, max(NKD, 512)])
        t2b = sbA("t2b", [128, max(NKD, 512)])
        kpb = sbA("kpb", [128, max(NKD, 512)])
        kpbh = sbA("kpbh", [128, max(NKD, 512)], BF16)
        kpbl = sbA("kpbl", [128, max(NKD, 512)], BF16)
        ksth = sbA("ksth", [128, max(NKV, 4) * TCH], BF16)
        kstl = sbA("kstl", [128, max(NKV, 4) * TCH], BF16)
        vsth = sbA("vsth", [128, NSUB * 128], BF16)
        vstl = sbA("vstl", [128, NSUB * 128], BF16)
        wqtlh = [sbA("wqtlh0", [128, 512], BF16), sbA("wqtlh1", [128, 512], BF16)]
        wqtll = [sbA("wqtll0", [128, 512], BF16), sbA("wqtll1", [128, 512], BF16)]
        qsth, qstl = ksth, kstl

        ps = [es.enter_context(nc.psum_tensor(f"psA{i}", [128, 512], F32)) for i in range(4)]
        psKb, psVb = ps[0:2], ps[2:4]
        psQ = ps[0:4]
        psT = [es.enter_context(nc.psum_tensor(f"psAT{i}", [128, 512], BF16)) for i in range(2)]

        b.plan = {e: [] for e in ENGS}
        for dc in range(DC):
            b.dma("sync", wkv32[:, 0:256], ap2(wkv_s, dc * 128, 128, 0, 256), inc=s_ld[2])
            b.wait("vector", s_ld[2], s_ld[2].v)
            if dc > 0:
                b.drain("vector")
            co0 = dc * 256
            b.vcopy(ap2(wkvh, 0, 128, co0, 256), wkv32[:, 0:256])
            b.drain("vector")
            b.tt(ap2(wkvl, 0, 128, co0, 256), wkv32[:, 0:256],
                 ap2(wkvh, 0, 128, co0, 256), ALU.subtract, inc=s_dve)
            b.wait("sync", s_dve, s_dve.v)
        wkv_ready = s_dve.v

        tr_bank_free = [None, None]
        kpb_tr_pe = None
        xt32_free_dve = 0
        cs_free_dve = 0
        rope2_done = {}
        vevac2_done = {}
        pe_after_chunk = {}
        stq_kh = StoreQ(b, [s_st[0], s_st[1]])
        stq_kl = StoreQ(b, [s_st[2], s_st[3]])
        stq_vh = StoreQ(b, [s_st[4], s_st[5]])
        stq_vl = StoreQ(b, [s_st[6], s_st[7]])

        def load_xt_cs(sl, src_x, src_cos, src_sin, col0, slot_free_pe):
            nonlocal xt32_free_dve, cs_free_dve
            if cs_free_dve:
                b.wait("sync", s_dve, cs_free_dve)
            b.dma("sync", bass.AP(cs[0], 0, [[NSUB * 2 * HD, 128], [2 * HD, NSUB], [1, HD]]),
                  bass.AP(src_cos, col0 * HD, [[HD, 128], [128 * HD, NSUB], [1, HD]]), inc=s_ld[sl])
            b.dma("sync", bass.AP(cs[0], HD, [[NSUB * 2 * HD, 128], [2 * HD, NSUB], [1, HD]]),
                  bass.AP(src_sin, col0 * HD, [[HD, 128], [128 * HD, NSUB], [1, HD]]), inc=s_ld[sl])
            if xt32_free_dve:
                b.wait("sync", s_dve, xt32_free_dve)
            b.dma("sync", xt32[:, :],
                  bass.AP(src_x, col0, [[src_x.shape[1], 128], [128 * src_x.shape[1], DC], [1, TCH]]),
                  inc=s_ld[sl])
            b.wait("vector", s_ld[sl], s_ld[sl].v)
            if slot_free_pe:
                b.wait("vector", s_pe, slot_free_pe)
            b.drain("vector")
            b.vcopy(xth[sl][:, :], xt32[:, :])
            b.drain("vector")
            b.tt(xtl[sl][:, :], xt32[:, :], xth[sl][:, :], ALU.subtract, inc=s_dve)
            xt32_free_dve = s_dve.v
            b.drain("vector")
            return s_ld[sl].v, s_dve.v

        def rope(psrc, nheads, width, sl, sub, dst):
            base = sub * 2 * HD
            cosap = bass.AP(cs[0], base, [[NSUB * 2 * HD, 128], [0, nheads], [1, HD]])
            sin4 = bass.AP(cs[0], base + HD, [[NSUB * 2 * HD, 128], [0, nheads], [HD // 2, 2], [1, HD // 2]])
            kap = bass.AP(psrc, 0, [[512, 128], [HD, nheads], [1, HD]])
            kshift = bass.AP(psrc, HD // 2, [[512, 128], [HD, nheads], [-(HD // 2), 2], [1, HD // 2]])
            t1ap = bass.AP(t1b, 0, [[t1b.shape[1], 128], [HD, nheads], [1, HD]])
            t2ap = bass.AP(t2b, 0, [[t2b.shape[1], 128], [HD, nheads], [HD // 2, 2], [1, HD // 2]])
            b.tt(t1ap, kap, cosap, ALU.mult)
            b.tt(t2ap, kshift, sin4, ALU.mult, inc=s_dve)
            b.drain("vector")
            b.tt(ap2(dst, 0, 128, 0, width), ap2(t1b, 0, 128, 0, width),
                 ap2(t2b, 0, 128, 0, width), ALU.add, inc=s_dve)
            b.drain("vector")
            b.vcopy(ap2(kpbh, 0, 128, 0, width), ap2(dst, 0, 128, 0, width))
            b.drain("vector")
            b.tt(ap2(kpbl, 0, 128, 0, width), ap2(dst, 0, 128, 0, width),
                 ap2(kpbh, 0, 128, 0, width), ALU.subtract, inc=s_dve)
            out = s_dve.v
            b.drain("vector")
            return out

        def transpose_pair(nslices, dsth, dstl, dcol_fn, ready_dve):
            nonlocal kpb_tr_pe
            for i in range(nslices):
                for which, (srcb, dstb) in enumerate(((kpbh, dsth), (kpbl, dstl))):
                    bank = (2 * i + which) % 2
                    b.wait("tensor", s_act, tr_bank_free[bank])
                    b.wait("tensor", s_dve, ready_dve)
                    b.tr(ap2(psT[bank], 0, 128, 0, 128),
                         ap2(srcb, 0, 128, i * HD, HD), ident_bf[:, :], inc=s_pe)
                    b.wait("scalar", s_pe, s_pe.v)
                    b.acopy(ap2(dstb, 0, 128, dcol_fn(i), 128),
                            ap2(psT[bank], 0, 128, 0, 128), inc=s_act)
                    tr_bank_free[bank] = s_act.v
            kpb_tr_pe = s_pe.v

        NCHK = S_KV // TCH
        chunk_loaded = {}
        chunk_loaded[0] = load_xt_cs(0, x_kv, cos_kv, sin_kv, 0, 0)
        for c in range(NCHK):
            sl = c % 2
            xt_ready, xs_done = chunk_loaded[c]
            b.wait("tensor", s_dve, xs_done)

            for sub in range(NSUB):
                si = (c * NSUB + sub) % 2
                psKV = psKb[si]
                idx = c * NSUB + sub
                if idx >= 2:
                    b.wait("tensor", s_dve, rope2_done.get(idx - 2, 0))
                    b.wait("tensor", s_dve, vevac2_done.get(idx - 2, 0))
                for dc in range(DC):
                    lhs_hl = (ap2(xth[sl], 0, 128, dc * TCH + sub * 128, 128),
                              ap2(xtl[sl], 0, 128, dc * TCH + sub * 128, 128))
                    rhs_hl = (ap2(wkvh, 0, 128, dc * 256, 256),
                              ap2(wkvl, 0, 128, dc * 256, 256))
                    b.mm3p(ap2(psKV, 0, 128, 0, 256), lhs_hl, rhs_hl,
                           first=(dc == 0), last=(dc == DC - 1),
                           inc=(s_pe if dc == DC - 1 else None))
                pe_kv = s_pe.v
                b.wait("scalar", s_pe, pe_kv)
                if c > 0:
                    for _s, _v in vstage_stored:
                        b.wait("scalar", _s, _v)
                b.acopy(ap2(vsth, 0, 128, sub * 128, 128), ap2(psKV, 0, 128, 128, 128), inc=s_act)
                b.wait("vector", s_act, s_act.v)
                if c > 0:
                    for _s, _v in vstage_stored:
                        b.wait("vector", _s, _v)
                b.drain("vector")
                b.tt(ap2(vstl, 0, 128, sub * 128, 128), ap2(psKV, 0, 128, 128, 128),
                     ap2(vsth, 0, 128, sub * 128, 128), ALU.subtract, inc=s_dve)
                vevac2_done[idx] = s_dve.v
                b.drain("vector")
                b.wait("vector", s_ld[sl], xt_ready)
                b.wait("vector", s_pe, pe_kv)
                if c > 0:
                    for _s, _v in kstage_stored:
                        b.wait("vector", _s, _v)
                if kpb_tr_pe is not None:
                    b.wait("vector", s_pe, kpb_tr_pe)
                prev_rope = rope(psKV, 1, 128, sl, sub, kpb)
                rope2_done[idx] = prev_rope
                cs_free_dve = prev_rope
                transpose_pair(1, ksth, kstl,
                               lambda i, _sub=sub: _sub * 128, prev_rope)
            pe_after_chunk[c] = s_pe.v
            b.wait("gpsimd", s_act, s_act.v)
            b.wait("gpsimd", s_dve, s_dve.v)
            stq_kh.dma(ap2(cc_k, 0, 128, c * TCH, TCH), ap2(ksth, 0, 128, 0, TCH))
            stq_kl.dma(ap2(cc_k, 128, 128, c * TCH, TCH), ap2(kstl, 0, 128, 0, TCH))
            kstage_stored = stq_kh.done() + stq_kl.done()
            for sub in range(NSUB):
                stq_vh.dma(ap2(cc_v, c * TCH + sub * 128, 128, 0, 128),
                           ap2(vsth, 0, 128, sub * 128, 128))
                stq_vl.dma(ap2(cc_v, S_KV + c * TCH + sub * 128, 128, 0, 128),
                           ap2(vstl, 0, 128, sub * 128, 128))
            vstage_stored = stq_vh.done() + stq_vl.done()
            if c + 1 < NCHK:
                chunk_loaded[c + 1] = load_xt_cs((c + 1) % 2, x_kv, cos_kv, sin_kv, (c + 1) * TCH,
                                                 pe_after_chunk.get(c - 1, 0))
        for q_ in (stq_kh, stq_kl, stq_vh, stq_vl):
            q_.wait_all("gpsimd")
        for _ci, _co in ((cc_k, scr_k), (cc_v, scr_v)):
            b.L("gpsimd", lambda e, _a=_ci, _b=_co: self_cc(e, "AllGather", ALU.bypass,
                [_a[:, :]], [_b[:, :]], s_gp))
            s_gp.inc()
        kv_gathered = s_gp.v

        wq_pe = {}
        wq_ld = 0
        stq_qh = StoreQ(b, [s_st[8], s_st[9]], prewait=True)
        stq_ql = StoreQ(b, [s_st[0], s_st[1]], prewait=True)
        for _s, _v in b.scr_wq_ready:
            b.wait("sync", _s, _v)
        for q_ in (stq_kh, stq_kl, stq_vh, stq_vl):
            q_.wait_all("scalar")
            q_.wait_all("vector")
        for co in range(OCH):
            sl = co % 2
            b.wait("sync", s_pe, s_pe.v)
            b.wait("sync", s_dve, s_dve.v)
            xt_ready, xs_done = load_xt_cs(sl, x_to, coso, sino, co * TCH, s_pe.v)
            b.wait("tensor", s_dve, xs_done)
            for ob in range(D // 512):
                for dc in range(DC):
                    wsl = wq_ld % 2
                    if wq_ld >= 2:
                        b.wait("sync", s_pe, wq_pe[wq_ld - 2])
                    b.dma("sync", ap2(wqtlh[wsl], 0, 128, 0, 512),
                          ap2(scr_wsp, wsp_row(0, dc * 128), 128, ob * 512, 512), inc=s_ld[4 + wsl])
                    b.dma("sync", ap2(wqtll[wsl], 0, 128, 0, 512),
                          ap2(scr_wsp, wsp_row(1, dc * 128), 128, ob * 512, 512), inc=s_ld[4 + wsl])
                    b.wait("tensor", s_ld[4 + wsl], s_ld[4 + wsl].v)
                    for sub in range(NSUB):
                        if dc == 0 and (ob > 0 or co > 0):
                            b.wait("tensor", s_dve, q_rope_done.get(sub, 0))
                        lhs_hl = (ap2(xth[sl], 0, 128, dc * TCH + sub * 128, 128),
                                  ap2(xtl[sl], 0, 128, dc * TCH + sub * 128, 128))
                        rhs_hl = (ap2(wqtlh[wsl], 0, 128, 0, 512),
                                  ap2(wqtll[wsl], 0, 128, 0, 512))
                        b.mm3p(ap2(psQ[sub], 0, 128, 0, 512), lhs_hl, rhs_hl,
                               first=(dc == 0), last=(dc == DC - 1),
                               inc=(s_pe if sub == NSUB - 1 else None))
                    wq_pe[wq_ld] = s_pe.v
                    wq_ld += 1
                if co == 0 and ob == 0:
                    q_rope_done = {}
                pe_q = s_pe.v
                for sub in range(NSUB):
                    b.wait("vector", s_pe, pe_q)
                    b.wait("vector", s_ld[sl], xt_ready)
                    if not (co == 0 and ob == 0):
                        for _s, _v in qstage_stored:
                            b.wait("vector", _s, _v)
                    if kpb_tr_pe is not None:
                        b.wait("vector", s_pe, kpb_tr_pe)
                    qr = rope(psQ[sub], 4, 512, sl, sub, kpb)
                    cs_free_dve = qr
                    q_rope_done[sub] = qr
                    transpose_pair(4, qsth, qstl,
                                   lambda i, _sub=sub: i * TCH + _sub * 128, qr)
                b.wait("gpsimd", s_act, s_act.v)
                for hs in range(4):
                    stq_qh.dma(ap2(scr_qh, ob * 512 + hs * 128, 128, co * TCH, TCH),
                               ap2(qsth, 0, 128, hs * TCH, TCH))
                    stq_ql.dma(ap2(scr_ql, ob * 512 + hs * 128, 128, co * TCH, TCH),
                               ap2(qstl, 0, 128, hs * TCH, TCH))
                qstage_stored = stq_qh.done() + stq_ql.done()
        for q_ in (stq_kh, stq_kl, stq_vh, stq_vl, stq_qh, stq_ql):
            q_.wait_all("gpsimd")
        b.run_stage("A01")
    es_a0.close()

    first_m0 = True
    for qt in range(NQT):
        es_qt = ExitStack()

        with ExitStack() as es:
            sbB = lambda n, sh, dt=F32: es.enter_context(nc.sbuf_tensor(f"{n}_{qt}", sh, dt))
            qtbh = sbB("qtbh", [128, DC * QT], BF16)
            qtbl = sbB("qtbl", [128, DC * QT], BF16)
            kgh = sbB("kgh", [128, S], BF16)
            kgl = sbB("kgl", [128, S], BF16)
            vgh = sbB("vgh", [128, NKC * HD], BF16)
            vgl = sbB("vgl", [128, NKC * HD], BF16)
            p32 = [sbB(f"p32{i}", [128, QT]) for i in range(3)]
            phi = [sbB(f"phi{i}", [128, QT], BF16) for i in range(3)]
            plo = [sbB(f"plo{i}", [128, QT], BF16) for i in range(3)]
            ptmp = sbB("ptmp", [128, QT])
            rcp = sbB("rcp", [1, QT])
            bcst = sbB("bcst", [128, QT])
            mx0 = sbB("mx0", [128, 1])
            aoh = [sbB(f"aoh{i}", [128, QT], BF16) for i in range(2)]
            aol = [sbB(f"aol{i}", [128, QT], BF16) for i in range(2)]
            ps = [es.enter_context(nc.psum_tensor(f"psB{i}_{qt}", [128, 512], F32)) for i in range(7)]
            psS = ps[0:3]
            psPVb, psDb = ps[3:5], ps[5:7]

            b.plan = {e: [] for e in ENGS}
            prev_tail_pv = None
            evac_done_by_bank = {}
            stq_ao = StoreQ(b, [s_st[2], s_st[3]], prewait=True)
            ao_stored = {}
            b.wait("sync", s_gp, kv_gathered)
            b.dma("sync", qtbh[:, :],
                  bass.AP(scr_qh, qt * QT, [[T, 128], [128 * T, DC], [1, QT]]), inc=s_ld[0])
            b.dma("sync", qtbl[:, :],
                  bass.AP(scr_ql, qt * QT, [[T, 128], [128 * T, DC], [1, QT]]), inc=s_ld[0])
            q_ready = s_ld[0].v
            b.wait("tensor", s_ld[0], q_ready)
            for gidx in range(NKV):
                if gidx > 0:
                    b.wait("sync", s_pe, s_pe.v)
                NSHl = g.NC // NKV
                S_KVl = S // NSHl
                for half in range(NSHl):
                    blk_ = gidx + half * NKV
                    b.dma("sync", kgh[:, half * S_KVl:(half + 1) * S_KVl],
                          ap2(scr_k, blk_ * 256, 128, 0, S_KVl), inc=s_ld[1])
                    b.dma("sync", kgl[:, half * S_KVl:(half + 1) * S_KVl],
                          ap2(scr_k, blk_ * 256 + 128, 128, 0, S_KVl), inc=s_ld[1])
                NKCH = S_KVl // 128
                vsplit = max(1, NKCH // 8)
                for lohalf, dstv in ((0, vgh), (1, vgl)):
                    for half in range(NSHl):
                        blk_ = gidx + half * NKV
                        base_row = blk_ * 2 * S_KVl + lohalf * S_KVl
                        for i in range(0, NKCH, vsplit):
                            n = min(vsplit, NKCH - i)
                            b.dma("sync",
                                  bass.AP(dstv, (half * NKCH + i) * HD, [[NKC * HD, 128], [HD, n], [1, HD]]),
                                  bass.AP(scr_v, (base_row + i * 128) * 128, [[128, 128], [128 * 128, n], [1, 128]]),
                                  inc=s_ld[1])
                kv_ready = s_ld[1].v
                b.wait("tensor", s_ld[1], kv_ready)
                for h in range(REP):
                    hh = gidx * REP + h
                    psPV = psPVb[hh % 2]
                    psD = psDb[hh % 2]
                    act_exp = {}
                    dve_plo = {}
                    pe_pv = {}

                    def emit_pv(kp, psPV=psPV, psD=psD):
                        pbp = kp % 3
                        if kp == 0 and evac_done_by_bank.get(hh % 2):
                            b.wait("tensor", s_dve, evac_done_by_bank[hh % 2])
                        b.wait("tensor", s_act, act_exp[kp])
                        b.wait("tensor", s_dve, dve_plo[kp])
                        vh = bass.AP(vgh, kp * HD, [[NKC * HD, 128], [1, HD]])
                        vl = bass.AP(vgl, kp * HD, [[NKC * HD, 128], [1, HD]])
                        for i, (lv, rp) in enumerate(((vh, phi[pbp]), (vh, plo[pbp]), (vl, phi[pbp]))):
                            b.mm(ap2(psPV, 0, 128, 0, QT), lv, rp[:, :],
                                 start=(kp == 0 and i == 0), stop=(kp == NKC - 1 and i == 2),
                                 inc=(s_pe if i == 2 else None))
                        b.mm(ap2(psD, 0, 1, 0, QT), ones_bf[:, 0:1], phi[pbp][:, :],
                             start=(kp == 0), stop=(kp == NKC - 1), inc=s_pe)
                        pe_pv[kp] = s_pe.v

                    for kc in range(NKC):
                        pb = kc % 3
                        if kc >= 3:
                            b.wait("tensor", s_act, act_exp[kc - 3])
                        lhs_hl = (ap2(kgh, 0, 128, kc * 128, 128),
                                  ap2(kgl, 0, 128, kc * 128, 128))
                        rhs_hl = (ap2(qtbh, 0, 128, hh * QT, QT),
                                  ap2(qtbl, 0, 128, hh * QT, QT))
                        b.mm3p(ap2(psS[kc % 3], 0, 128, 0, QT), lhs_hl, rhs_hl, inc=s_pe)
                        pe_s = s_pe.v
                        if first_m0 and kc == 0:
                            b.wait("vector", s_pe, pe_s)
                            b.vred(mx0[:, :], ap2(psS[pb], 0, 128, 0, QT), ALU.max, inc=s_dve)
                            b.drain("vector")
                            b.wait("gpsimd", s_dve, s_dve.v)
                            b.allred(mx0[:, :], mx0[:, :], ReduceOp.max, inc=s_gp)
                            b.wait("vector", s_gp, s_gp.v)
                            b.ts(bias_m0[:, :], mx0[:, :], -SCALE, ALU.mult, inc=s_dve)
                            b.drain("vector")
                            b.wait("scalar", s_dve, s_dve.v)
                        if kc < 3 and prev_tail_pv is not None:
                            b.wait("scalar", s_pe, prev_tail_pv)
                        if kc >= 3:
                            b.wait("scalar", s_dve, dve_plo[kc - 3])
                            b.wait("scalar", s_pe, pe_pv[kc - 3])
                        b.wait("scalar", s_pe, pe_s)
                        b.act(p32[pb][:, :], ap2(psS[pb], 0, 128, 0, QT), AF.Exp,
                              bias=bias_m0[:, :], scale=SCALE)
                        b.act(phi[pb][:, :], ap2(psS[pb], 0, 128, 0, QT), AF.Exp,
                              bias=bias_m0[:, :], scale=SCALE, inc=s_act)
                        act_exp[kc] = s_act.v
                        b.wait("vector", s_act, act_exp[kc])
                        b.drain("vector")
                        b.tt(plo[pb][:, :], p32[pb][:, :], phi[pb][:, :], ALU.subtract, inc=s_dve)
                        dve_plo[kc] = s_dve.v
                        b.drain("vector")
                        if kc >= 2:
                            emit_pv(kc - 2)
                    emit_pv(NKC - 2)
                    emit_pv(NKC - 1)
                    pe_tail = s_pe.v
                    b.wait("vector", s_pe, pe_tail)
                    b.recip(rcp[:, :], ap2(psD, 0, 1, 0, QT), inc=s_dve)
                    b.drain("vector")
                    b.wait("gpsimd", s_dve, s_dve.v)
                    b.pbcast(bcst[:, :], rcp[:, :], inc=s_gp)
                    b.wait("vector", s_gp, s_gp.v)
                    b.tt(ptmp[:, :], ap2(psPV, 0, 128, 0, QT), bcst[:, :], ALU.mult, inc=s_dve)
                    b.drain("vector")
                    b.wait("scalar", s_dve, s_dve.v)
                    if hh >= 2:
                        for _s, _v in ao_stored[hh - 2]:
                            b.wait("scalar", _s, _v)
                            b.wait("vector", _s, _v)
                    b.acopy(aoh[hh % 2][:, :], ptmp[:, :], inc=s_act)
                    b.wait("vector", s_act, s_act.v)
                    b.tt(aol[hh % 2][:, :], ptmp[:, :], aoh[hh % 2][:, :], ALU.subtract, inc=s_dve)
                    b.drain("vector")
                    b.wait("gpsimd", s_dve, s_dve.v)
                    stq_ao.dma(ap2(scr_ah, hh * 128, 128, qt * QT, QT), aoh[hh % 2][:, :])
                    stq_ao.dma(ap2(scr_al, hh * 128, 128, qt * QT, QT), aol[hh % 2][:, :])
                    ao_stored[hh] = stq_ao.done()
                    evac_done_by_bank[hh % 2] = s_dve.v
                    prev_tail_pv = pe_tail
                    first_m0 = False
            stq_ao.wait_all("gpsimd")
            b.run_stage(f"B{qt}")

        with ExitStack() as es:
            sbC = lambda n, sh, dt=F32: es.enter_context(nc.sbuf_tensor(f"{n}_{qt}", sh, dt))
            strm = sbC("strm", [128, DC * QT])
            xn = sbC("xn", [128, DC * QT])
            xnh = sbC("xnh", [128, DC * QT], BF16)
            xnl = sbC("xnl", [128, DC * QT], BF16)
            hbh = sbC("hbh", [128, HFC * FSUB], BF16)
            hbl = sbC("hbl", [128, HFC * FSUB], BF16)
            htmp = sbC("htmp", [128, FSUB])
            wtlh = [sbC("wtlh0", [128, 1024], BF16), sbC("wtlh1", [128, 1024], BF16)]
            wtll = [sbC("wtll0", [128, 1024], BF16), sbC("wtll1", [128, 1024], BF16)]
            aosh = [sbC("aosh0", [128, QT], BF16), sbC("aosh1", [128, QT], BF16)]
            aosl = [sbC("aosl0", [128, QT], BF16), sbC("aosl1", [128, QT], BF16)]
            wf8 = [sbC(f"wf8{i}", [128, 512], FP8) for i in range(4)]
            w2f8 = [sbC(f"w2f8{i}", [128, 512], FP8) for i in range(4)]
            sgb = sbC("sgb", [128, FSUB])
            stat = sbC("stat", [1, QT])
            stat2 = sbC("stat2", [1, QT])
            statm = sbC("statm", [1, QT])
            sqb = [sbC("sqb0", [128, QT])]
            bcstC = sbC("bcstC", [128, QT])
            bcstC2 = sbC("bcstC2", [128, QT])
            NSEG = 4 if DC % 4 == 0 else 2
            YW = D // NSEG
            ybufs = [sbC("ybuf", [128, YW]), sbC("ybuf2", [128, YW])]
            ps = [es.enter_context(nc.psum_tensor(f"psC{i}_{qt}", [128, 512], F32)) for i in range(8)]

            b.plan = {e: [] for e in ENGS}
            b.dma("sync", strm[:, :],
                  bass.AP(x_to, qt * QT, [[T, 128], [128 * T, DC], [1, QT]]), inc=s_ld[0])
            x_ready = s_ld[0].v
            wo_ld = 0
            wo_pe = {}
            dve_oc8 = {}
            OG = min(8, DC)
            for og in range(DC // OG):
                if og > 0:
                    b.wait("tensor", s_dve, dve_oc8[og - 1])
                for hc in range(DC):
                    wsl = wo_ld % 2
                    lsem1, lsem2, lsem3 = s_ld[2 + wsl], s_ld[4 + wsl], s_ld[8 + wsl]
                    if wo_ld >= 2:
                        b.wait("sync", s_pe, wo_pe[wo_ld - 2])
                    b.dma("sync", ap2(wtlh[wsl], 0, 128, 0, OG * 128),
                          ap2(scr_wsp, wsp_row(2, hc * 128), 128, og * OG * 128, OG * 128), inc=lsem1)
                    b.dma("sync", ap2(wtll[wsl], 0, 128, 0, OG * 128),
                          ap2(scr_wsp, wsp_row(3, hc * 128), 128, og * OG * 128, OG * 128), inc=lsem2)
                    b.dma("sync", ap2(aosh[wsl], 0, 128, 0, QT),
                          ap2(scr_ah, hc * 128, 128, qt * QT, QT), inc=lsem3)
                    b.dma("sync", ap2(aosl[wsl], 0, 128, 0, QT),
                          ap2(scr_al, hc * 128, 128, qt * QT, QT), inc=lsem3)
                    b.wait("tensor", lsem1, lsem1.v)
                    b.wait("tensor", lsem2, lsem2.v)
                    b.wait("tensor", lsem3, lsem3.v)
                    rhs_hl = (aosh[wsl][:, :], aosl[wsl][:, :])
                    for j in range(OG):
                        lhs_hl = (ap2(wtlh[wsl], 0, 128, j * 128, 128),
                                  ap2(wtll[wsl], 0, 128, j * 128, 128))
                        b.mm3p(ap2(ps[j], 0, 128, 0, QT), lhs_hl, rhs_hl,
                               first=(hc == 0), last=(hc == DC - 1),
                               inc=(s_pe if j == OG - 1 else None))
                    wo_pe[wo_ld] = s_pe.v
                    wo_ld += 1
                b.wait("vector", s_pe, s_pe.v)
                if og == 0:
                    b.wait("vector", s_ld[0], x_ready)
                for j in range(OG):
                    oc = og * OG + j
                    b.drain("vector")
                    b.tt(ap2(strm, 0, 128, oc * QT, QT), ap2(ps[j], 0, 128, 0, QT),
                         ap2(strm, 0, 128, oc * QT, QT), ALU.add, inc=s_dve)
                dve_oc8[og] = s_dve.v
                b.drain("vector")

            def layer_norm(src, dst, src_ready_dve):
                b.wait("tensor", s_dve, src_ready_dve)
                for dc in range(DC):
                    b.mm(ap2(ps[2], 0, 1, 0, QT), ones_sb[:, 0:1], ap2(src, 0, 128, dc * QT, QT),
                         start=(dc == 0), stop=(dc == DC - 1), inc=(s_pe if dc == DC - 1 else None))
                b.wait("vector", s_pe, s_pe.v)
                b.ts(stat[:, :], ap2(ps[2], 0, 1, 0, QT), 1.0 / D, ALU.mult, inc=s_dve)
                b.drain("vector")
                b.wait("scalar", s_dve, src_ready_dve)
                pe_sq = {}
                for dc in range(DC):
                    if dc >= 1:
                        b.wait("scalar", s_pe, pe_sq[dc - 1])
                    b.act(sqb[0][:, :], ap2(src, 0, 128, dc * QT, QT), AF.Square, inc=s_act)
                    b.wait("tensor", s_act, s_act.v)
                    b.mm(ap2(ps[3], 0, 1, 0, QT), ones_sb[:, 0:1], sqb[0][:, :],
                         start=(dc == 0), stop=(dc == DC - 1), inc=s_pe)
                    pe_sq[dc] = s_pe.v
                b.wait("vector", s_pe, s_pe.v)
                b.ts(stat2[:, :], ap2(ps[3], 0, 1, 0, QT), 1.0 / D, ALU.mult, inc=s_dve)
                b.drain("vector")
                b.tt(statm[:, :], stat[:, :], stat[:, :], ALU.mult, inc=s_dve)
                b.drain("vector")
                b.stt(stat2[:, :], statm[:, :], -1.0, stat2[:, :], ALU.mult, ALU.add, inc=s_dve)
                b.drain("vector")
                b.ts(stat2[:, :], stat2[:, :], 1.0, ALU.mult, s2=1e-5, op1=ALU.add, inc=s_dve)
                b.drain("vector")
                b.wait("scalar", s_dve, s_dve.v)
                b.act(statm[:, :], stat2[:, :], AF.Sqrt, inc=s_act)
                b.wait("vector", s_act, s_act.v)
                b.recip(stat2[:, :], statm[:, :], inc=s_dve)
                b.drain("vector")
                b.tt(statm[:, :], stat[:, :], stat2[:, :], ALU.mult, inc=s_dve)
                b.drain("vector")
                b.wait("gpsimd", s_dve, s_dve.v)
                b.pbcast(bcstC2[:, :], stat2[:, :], inc=s_gp)
                b.pbcast(bcstC[:, :], statm[:, :], inc=s_gp)
                b.wait("vector", s_gp, s_gp.v)
                for dc in range(DC):
                    b.tt(ap2(dst, 0, 128, dc * QT, QT), ap2(src, 0, 128, dc * QT, QT),
                         bcstC2[:, :], ALU.mult, inc=s_dve)
                    b.drain("vector")
                    b.tt(ap2(dst, 0, 128, dc * QT, QT), ap2(dst, 0, 128, dc * QT, QT),
                         bcstC[:, :], ALU.subtract, inc=s_dve)
                    b.drain("vector")
                return s_dve.v

            if g.dbg and qt == 0:
                b.wait("gpsimd", s_dve, s_dve.v)
                b.dma("gpsimd", dbg_strm[:, :], strm[:, :], inc=s_st[0])
                b.wait("gpsimd", s_st[0], s_st[0].v)
            xn_ready = layer_norm(strm, xn, s_dve.v)
            b.vcopy(xnh[:, :], xn[:, :])
            b.drain("vector")
            b.tt(xnl[:, :], xn[:, :], xnh[:, :], ALU.subtract, inc=s_dve)
            xn_split = s_dve.v
            b.drain("vector")
            if g.dbg and qt == 0:
                b.wait("gpsimd", s_dve, xn_split)
                b.dma("gpsimd", dbg_xn[:, :], xn[:, :], inc=s_st[0])
                b.wait("gpsimd", s_st[0], s_st[0].v)

            b.wait("sync", *b.scr_ffn_ready[0])
            w_pe = {}
            w_ld = 0
            m3_pe = {}
            m3_ld = 0
            for fs in range(QT // FSUB):
                for hbg in range(HFC // 4):
                    ps1, ps2 = ps[0:4], ps[4:8]
                    if hbg == 0:
                        b.wait("sync", s_pe, s_pe.v)
                    if not (fs == 0 and hbg == 0):
                        b.wait("tensor", s_dve, s_dve.v)
                    for dc in range(DC):
                        wsl = dc % 4
                        lsem1, lsem2 = s_ld[2 + wsl], s_ld[8 + wsl]
                        if w_ld >= 4:
                            b.wait("sync", s_pe, w_pe[w_ld - 4])
                        b.dma("sync", ap2(wf8[wsl], 0, 128, 0, 512),
                              ap2(scr_w12, w12_row(0, dc * 128), 128, hbg * 512, 512), inc=lsem1)
                        b.dma("sync", ap2(w2f8[wsl], 0, 128, 0, 512),
                              ap2(scr_w12, w12_row(1, dc * 128), 128, hbg * 512, 512), inc=lsem2)
                        b.wait("tensor", lsem1, lsem1.v)
                        b.wait("tensor", lsem2, lsem2.v)
                        if dc == 0 and hbg == 0 and fs == 0:
                            b.wait("tensor", s_dve, xn_split)
                        rhs_hl = (ap2(xnh, 0, 128, dc * QT + fs * FSUB, FSUB),
                                  ap2(xnl, 0, 128, dc * QT + fs * FSUB, FSUB))
                        for j in range(4):
                            b.mm2p(ap2(ps1[j], 0, 128, 0, FSUB), ap2(wf8[wsl], 0, 128, j * 128, 128),
                                   rhs_hl, first=(dc == 0), last=(dc == DC - 1), inc=None)
                            b.mm2p(ap2(ps2[j], 0, 128, 0, FSUB), ap2(w2f8[wsl], 0, 128, j * 128, 128),
                                   rhs_hl, first=(dc == 0), last=(dc == DC - 1),
                                   inc=(s_pe if j == 3 else None))
                        w_pe[w_ld] = s_pe.v
                        w_ld += 1
                    pe_12 = s_pe.v
                    b.wait("scalar", s_pe, pe_12)
                    for j in range(4):
                        hb = hbg * 4 + j
                        if j > 0:
                            b.wait("scalar", s_dve, s_dve.v)
                        b.act(sgb[:, :], ap2(ps1[j], 0, 128, 0, FSUB), AF.Sigmoid, scale=0.5, inc=s_act)
                        b.wait("vector", s_act, s_act.v)
                        b.tt(htmp[:, :], ap2(ps2[j], 0, 128, 0, FSUB), sgb[:, :], ALU.mult, inc=s_dve)
                        b.drain("vector")
                        b.wait("scalar", s_dve, s_dve.v)
                        b.acopy(ap2(hbh, 0, 128, hb * FSUB, FSUB), htmp[:, :], inc=s_act)
                        b.wait("vector", s_act, s_act.v)
                        b.tt(ap2(hbl, 0, 128, hb * FSUB, FSUB), htmp[:, :],
                             ap2(hbh, 0, 128, hb * FSUB, FSUB), ALU.subtract, inc=s_dve)
                        b.drain("vector")
                h_act_done = s_act.v
                h_done = s_dve.v
                for ocg in range(DC // 4):
                    ps3 = ps[0:4]
                    if ocg == 0:
                        b.wait("sync", s_pe, s_pe.v)
                    if not (fs == 0 and ocg == 0):
                        b.wait("tensor", s_dve, m3_evac)
                    for hb in range(HFC):
                        wsl = hb % 4
                        lsem = s_ld[2 + wsl]
                        if m3_ld >= 4:
                            b.wait("sync", s_pe, m3_pe[m3_ld - 4])
                        b.dma("sync", ap2(wf8[wsl], 0, 128, 0, 512),
                              ap2(scr_w3g, hb * 128, 128, ocg * 512, 512), inc=lsem)
                        b.wait("tensor", lsem, lsem.v)
                        if hb == 0 and ocg == 0:
                            b.wait("tensor", s_dve, h_done)
                            b.wait("tensor", s_act, h_act_done)
                        rhs_hl = (ap2(hbh, 0, 128, hb * FSUB, FSUB),
                                  ap2(hbl, 0, 128, hb * FSUB, FSUB))
                        for j in range(4):
                            b.mm2p(ap2(ps3[j], 0, 128, 0, FSUB), ap2(wf8[wsl], 0, 128, j * 128, 128),
                                   rhs_hl, first=(hb == 0), last=(hb == HFC - 1),
                                   inc=(s_pe if j == 3 else None))
                        m3_pe[m3_ld] = s_pe.v
                        m3_ld += 1
                    b.wait("vector", s_pe, s_pe.v)
                    for j in range(4):
                        oc = ocg * 4 + j
                        b.stt(ap2(strm, 0, 128, oc * QT + fs * FSUB, FSUB),
                              ap2(ps3[j], 0, 128, 0, FSUB), 0.25,
                              ap2(xn, 0, 128, oc * QT + fs * FSUB, FSUB),
                              ALU.mult, ALU.add, inc=s_dve)
                        b.drain("vector")
                    m3_evac = s_dve.v

            if g.dbg and qt == 0:
                b.wait("gpsimd", s_dve, s_dve.v)
                b.dma("gpsimd", dbg_ffn[:, :], strm[:, :], inc=s_st[0])
                b.wait("gpsimd", s_st[0], s_st[0].v)
            out_ready = layer_norm(strm, xn, s_dve.v)
            tr_free = [None, None]
            stq_y = StoreQ(b, [s_st[6], s_st[7]])
            yst = stq_y.done()
            for sub in range(QT // 128):
                for oc in range(DC):
                    bank = 4 + (oc % 2)
                    b.wait("tensor", s_act, tr_free[oc % 2])
                    b.wait("tensor", s_dve, out_ready)
                    b.tr(ap2(ps[bank], 0, 128, 0, 128),
                         ap2(xn, 0, 128, oc * QT + sub * 128, 128), ident_sb[:, :], inc=s_pe)
                    b.wait("scalar", s_pe, s_pe.v)
                    seg = oc // (DC // NSEG)
                    if (oc % (DC // NSEG)) == 0 and (sub > 0 or seg >= 2):
                        for _s, _v in yst:
                            b.wait("scalar", _s, _v)
                    _yb = ybufs[seg % 2]
                    _yo = oc % (DC // NSEG)
                    b.acopy(ap2(_yb, 0, 128, _yo * 128, 128), ap2(ps[bank], 0, 128, 0, 128), inc=s_act)
                    tr_free[oc % 2] = s_act.v
                    if _yo == (DC // NSEG) - 1:
                        b.wait("gpsimd", s_act, s_act.v)
                        r0 = qt * QT + sub * 128
                        stq_y.dma(y_out[r0:r0 + 128, seg * YW:(seg + 1) * YW],
                                  ap2(_yb, 0, 128, 0, YW))
                        yst = stq_y.done()
            stq_y.wait_all("gpsimd")
            b.run_stage(f"C{qt}")
        es_qt.close()

    es_outer.close()
    mybir.codegen_inst_isa_subclasses(nc)
    return nc



def rope_tables(S, HD):
    import jax
    import jax.numpy as jnp
    with jax.default_device(jax.devices("cpu")[0]):
        inv = 1.0 / 10000.0 ** (jnp.arange(0, HD, 2, dtype=jnp.float32) / HD)
        t = jnp.arange(S, dtype=jnp.float32)
        freqs = jnp.einsum("i,j->ij", t, inv)
        emb = jnp.concatenate([freqs, freqs], axis=-1)
        cos = np.asarray(jnp.cos(emb), dtype=np.float32)
        sin = np.asarray(jnp.sin(emb), dtype=np.float32)
    return cos, sin


def prep_core_inputs(g: Geom, core, x, Wq, Wk, Wv, Wo, W1, W2, W3):
    S, D, HD, T = g.S, g.D, g.HD, g.T
    x2 = np.ascontiguousarray(x.reshape(S, D))
    cosf, sinf = rope_tables(S, HD)
    sin_signed = sinf.copy()
    sin_signed[:, :HD // 2] *= -1.0
    sl = slice(core * T, (core + 1) * T)
    NC = g.NC
    DS = D // NC
    HFP = ((g.HF + 128 * NC - 1) // (128 * NC)) * 128 * NC
    w3p = np.zeros((HFP, D), dtype=np.float32)
    w3p[:g.HF] = W3.T
    dsl = slice(core * DS, (core + 1) * DS)
    hsl = slice(core * (HFP // NC), (core + 1) * (HFP // NC))
    hc = core % g.NKV
    th = core // g.NKV
    S_KV = S // (NC // g.NKV)
    ksl = slice(th * S_KV, (th + 1) * S_KV)
    wkv_slice = np.concatenate([Wk.T[:, hc * 128:(hc + 1) * 128],
                                Wv.T[:, hc * 128:(hc + 1) * 128]], axis=1)
    return {
        "x_t": np.ascontiguousarray(x2.T),
        "x_to": np.ascontiguousarray(x2.T[:, sl]),
        "x_kv": np.ascontiguousarray(x2.T[:, ksl]),
        "cos_kv": np.ascontiguousarray(cosf[ksl]),
        "sin_kv": np.ascontiguousarray(sin_signed[ksl]),
        "wkv_s": np.ascontiguousarray(wkv_slice),
        "wq_s": np.ascontiguousarray(Wq.T[dsl]),
        "wo_s": np.ascontiguousarray(Wo.T[dsl]),
        "w1_s": np.ascontiguousarray(W1.T[dsl]),
        "w2_s": np.ascontiguousarray(W2.T[dsl]),
        "w3_s": np.ascontiguousarray(w3p[hsl]),
        "cosf": cosf, "sinf": sin_signed,
        "coso": np.ascontiguousarray(cosf[sl]), "sino": np.ascontiguousarray(sin_signed[sl]),
        "ident": np.eye(128, dtype=np.float32),
        "ones": np.ones((128, 128), dtype=np.float32),
    }




from concourse.bass_utils import run_bass_kernel_spmd


def kernel(L0_attn_x, L0_attn_Wq, L0_attn_Wk, L0_attn_Wv, L0_attn_Wo,
           L0_ffn_w1, L0_ffn_w2, L0_ffn_w3):
    g = Geom()
    x = np.ascontiguousarray(np.asarray(L0_attn_x, dtype=np.float32).reshape(g.S, g.D))
    Wq = np.asarray(L0_attn_Wq, dtype=np.float32)
    Wk = np.asarray(L0_attn_Wk, dtype=np.float32)
    Wv = np.asarray(L0_attn_Wv, dtype=np.float32)
    Wo = np.asarray(L0_attn_Wo, dtype=np.float32)
    W1 = np.asarray(L0_ffn_w1, dtype=np.float32)
    W2 = np.asarray(L0_ffn_w2, dtype=np.float32)
    W3 = np.asarray(L0_ffn_w3, dtype=np.float32)

    nc = build(g)

    x_t = np.ascontiguousarray(x.T)
    cosf, sinf = rope_tables(g.S, g.HD)
    sin_signed = sinf.copy()
    sin_signed[:, :g.HD // 2] *= -1.0
    wq_T = np.ascontiguousarray(Wq.T)
    wo_T = np.ascontiguousarray(Wo.T)
    w1_T = np.ascontiguousarray(W1.T)
    w2_T = np.ascontiguousarray(W2.T)
    NCn = g.NC
    DS = g.D // NCn
    HFP = ((g.HF + 128 * NCn - 1) // (128 * NCn)) * 128 * NCn
    HFS = HFP // NCn
    w3p = np.zeros((HFP, g.D), dtype=np.float32)
    w3p[:g.HF] = W3.T
    wk_T = np.ascontiguousarray(Wk.T)
    wv_T = np.ascontiguousarray(Wv.T)
    shared = {
        "x_t": x_t,
        "cosf": cosf, "sinf": sin_signed,
        "ident": np.eye(128, dtype=np.float32),
        "ones": np.ones((128, 128), dtype=np.float32),
    }
    in_maps = []
    for c in range(NCn):
        sl = slice(c * g.T, (c + 1) * g.T)
        dsl = slice(c * DS, (c + 1) * DS)
        hsl = slice(c * HFS, (c + 1) * HFS)
        m = dict(shared)
        m["x_to"] = np.ascontiguousarray(x_t[:, sl])
        m["coso"] = np.ascontiguousarray(cosf[sl])
        m["sino"] = np.ascontiguousarray(sin_signed[sl])
        m["wq_s"] = np.ascontiguousarray(wq_T[dsl])
        m["wo_s"] = np.ascontiguousarray(wo_T[dsl])
        m["w1_s"] = np.ascontiguousarray(w1_T[dsl])
        m["w2_s"] = np.ascontiguousarray(w2_T[dsl])
        m["w3_s"] = np.ascontiguousarray(w3p[hsl])
        hc = c % g.NKV
        th = c // g.NKV
        S_KV = g.S // (NCn // g.NKV)
        ksl = slice(th * S_KV, (th + 1) * S_KV)
        m["x_kv"] = np.ascontiguousarray(x_t[:, ksl])
        m["cos_kv"] = np.ascontiguousarray(cosf[ksl])
        m["sin_kv"] = np.ascontiguousarray(sin_signed[ksl])
        m["wkv_s"] = np.ascontiguousarray(np.concatenate(
            [wk_T[:, hc * 128:(hc + 1) * 128], wv_T[:, hc * 128:(hc + 1) * 128]], axis=1))
        in_maps.append(m)

    res = run_bass_kernel_spmd(nc, in_maps, core_ids=list(range(g.NC)))
    y = np.concatenate([np.asarray(res.results[c]["y"]) for c in range(g.NC)], axis=0)
    return y.reshape(1, g.S, g.D).astype(np.float32, copy=False)


# version 25
# speedup vs baseline: 1.3816x; 1.0119x over previous
import math
from contextlib import ExitStack
from dataclasses import dataclass

import numpy as np

import concourse.bass as bass
import concourse.mybir as mybir
from concourse.alu_op_type import AluOpType as ALU
from bass_rust import ReduceOp
from concourse import library_config

F32 = mybir.dt.float32
BF16 = mybir.dt.bfloat16
FP8 = mybir.dt.float8e4
AF = mybir.ActivationFunctionType
AX = mybir.AxisListType

ENGS = ("tensor", "vector", "scalar", "sync", "gpsimd")


@dataclass
class Geom:
    S: int = 8192
    D: int = 2048
    NH: int = 16
    NKV: int = 4
    HD: int = 128
    HF: int = 5632
    NC: int = 8
    TCH: int = 512
    QT: int = 512
    FSUB: int = 256
    dbg: bool = False

    @property
    def T(self): return self.S // self.NC
    @property
    def DC(self): return self.D // 128
    @property
    def NKD(self): return self.NKV * self.HD
    @property
    def HFC(self): return self.HF // 128
    @property
    def REP(self): return self.NH // self.NKV
    @property
    def NKC(self): return self.S // 128
    @property
    def NWELEM(self): return self.D * self.HF


class SemC:
    def __init__(self, nc, name, step):
        self.h = nc.alloc_semaphore(name=name)
        self.step = step
        self.v = 0

    def inc(self):
        self.v += self.step
        return self.v


class Builder:
    def __init__(self, g: Geom):
        self.g = g
        self.nc = bass.Bass("TRN2", target_bir_lowering=False)
        self.plan = None

    def L(self, eng, f):
        self.plan[eng].append(f)

    def wait(self, eng, sem: SemC, val=None):
        v = sem.v if val is None else val
        if v is not None and v > 0:
            self.L(eng, lambda e, _s=sem.h, _v=v: e.wait_ge(_s, _v))

    def drain(self, eng):
        self.L(eng, lambda e: e.drain())

    def _inc(self, instr, sem):
        if sem is not None:
            instr.then_inc(sem.h, sem.step)

    def mm(self, out, lhsT, rhs, start, stop, inc=None):
        if inc is not None: inc.inc()
        self.L("tensor", lambda e, _o=out, _l=lhsT, _r=rhs, _a=start, _b=stop, _s=inc:
               self._inc(e.matmul(_o, _l, _r, start=_a, stop=_b), _s))

    def mm3p(self, out, lhs_hl, rhs_hl, inc=None, first=True, last=True):
        combos = ((0, 0), (0, 1), (1, 0))
        for i, (a, c) in enumerate(combos):
            self.mm(out, lhs_hl[a], rhs_hl[c],
                    start=(first and i == 0), stop=(last and i == 2),
                    inc=(inc if i == 2 else None))

    def mm2p(self, out, lhs_exact, rhs_hl, inc=None, first=True, last=True):
        for i in range(2):
            self.mm(out, lhs_exact, rhs_hl[i],
                    start=(first and i == 0), stop=(last and i == 1),
                    inc=(inc if i == 1 else None))

    def tr(self, out, in_, ident, inc=None):
        if inc is not None: inc.inc()
        self.L("tensor", lambda e, _o=out, _i=in_, _d=ident, _s=inc:
               self._inc(e.transpose(_o, _i, _d), _s))

    def act(self, out, in_, func, bias=None, scale=None, accum=None, inc=None):
        if inc is not None: inc.inc()
        kw = {}
        if bias is not None: kw["bias"] = bias
        if scale is not None: kw["scale"] = scale
        if accum is not None: kw["accum_out"] = accum
        self.L("scalar", lambda e, _o=out, _i=in_, _f=func, _k=kw, _s=inc:
               self._inc(e.activation(out=_o, in_=_i, func=_f, **_k), _s))

    def acopy(self, out, in_, inc=None):
        if inc is not None: inc.inc()
        self.L("scalar", lambda e, _o=out, _i=in_, _s=inc:
               self._inc(e.copy(_o, _i), _s))

    def tt(self, out, in0, in1, op, inc=None):
        if inc is not None: inc.inc()
        self.L("vector", lambda e, _o=out, _a=in0, _b=in1, _p=op, _s=inc:
               self._inc(e.tensor_tensor(_o, _a, _b, op=_p), _s))

    def ts(self, out, in0, s1, op0, s2=None, op1=None, inc=None):
        if inc is not None: inc.inc()
        kw = dict(scalar2=s2)
        if op1 is not None: kw["op1"] = op1
        self.L("vector", lambda e, _o=out, _a=in0, _1=s1, _p=op0, _k=kw, _s=inc:
               self._inc(e.tensor_scalar(out=_o, in0=_a, scalar1=_1, op0=_p, **_k), _s))

    def stt(self, out, in0, scalar, in1, op0, op1, inc=None):
        if inc is not None: inc.inc()
        self.L("vector", lambda e, _o=out, _a=in0, _c=scalar, _b=in1, _p=op0, _q=op1, _s=inc:
               self._inc(e.scalar_tensor_tensor(out=_o, in0=_a, scalar=_c, in1=_b, op0=_p, op1=_q), _s))

    def vcopy(self, out, in_, inc=None):
        if inc is not None: inc.inc()
        self.L("vector", lambda e, _o=out, _i=in_, _s=inc:
               self._inc(e.tensor_copy(_o, _i), _s))

    def vred(self, out, in_, op, absval=False, inc=None):
        if inc is not None: inc.inc()
        self.L("vector", lambda e, _o=out, _i=in_, _p=op, _a=absval, _s=inc:
               self._inc(e.tensor_reduce(_o, _i, axis=AX.X, op=_p, apply_absolute_value=_a), _s))

    def recip(self, out, in_, inc=None):
        if inc is not None: inc.inc()
        self.L("vector", lambda e, _o=out, _i=in_, _s=inc:
               self._inc(e.reciprocal(_o, _i), _s))

    def dma(self, queue, out, in_, inc=None):
        if inc is not None: inc.inc()
        self.L(queue, lambda e, _o=out, _i=in_, _s=inc:
               self._inc(e.dma_start(_o, _i), _s))

    def pbcast(self, out, in_, inc=None):
        if inc is not None: inc.inc()
        self.L("gpsimd", lambda e, _o=out, _i=in_, _s=inc:
               self._inc(e.partition_broadcast(_o, _i), _s))

    def allred(self, out, in_, op, inc=None):
        if inc is not None: inc.inc()
        self.L("gpsimd", lambda e, _o=out, _i=in_, _p=op, _s=inc:
               self._inc(e.partition_all_reduce(_o, _i, channels=128, reduce_op=_p), _s))

    def run_stage(self, name):
        plan = self.plan
        with self.nc.named_scope(name):
            with self.nc.Block() as block:
                for eng in ENGS:
                    if plan[eng]:
                        def body(e, _th=tuple(plan[eng])):
                            for f in _th:
                                f(e)
                        getattr(block, eng)(body)
        self.plan = None


class StoreQ:
    def __init__(self, b, sems, prewait=False):
        self.b = b
        self.sems = sems
        self.i = 0
        self.prewait = prewait

    def dma(self, out, in_):
        b, sems = self.b, self.sems
        s = sems[self.i % len(sems)]
        if self.i >= len(sems) or self.prewait:
            b.wait("gpsimd", s, s.v)
        b.dma("gpsimd", out, in_, inc=s)
        self.i += 1

    def done(self):
        return [(s, s.v) for s in self.sems]

    def wait_all(self, eng):
        for s, v in self.done():
            self.b.wait(eng, s, v)


def ap2(t, p0, np_, c0, nc_):
    return t[p0:p0 + np_, c0:c0 + nc_]


def make_self_cc(nc_cores):
    def self_cc(e, kind, op, ins, outs, sem):
        e.collective_compute(kind, op, replica_groups=[list(range(nc_cores))],
                             ins=ins, outs=outs).then_inc(sem.h, sem.step)
    return self_cc


def build(g: Geom):
    b = Builder(g)
    nc = b.nc
    S, D, HD, HF, T = g.S, g.D, g.HD, g.HF, g.T
    DC, NKD, NKV, HFC, REP, NKC = g.DC, g.NKD, g.NKV, g.HFC, g.REP, g.NKC
    TCH, QT, FSUB = g.TCH, g.QT, g.FSUB
    NCH = S // TCH
    OCH = T // TCH
    A0_UNITS_PER_CHUNK = -(-((D // 128) * (HF // 512) * 4 + (HF // 128) * (D // 512) * 2 + (D // 128) * (D // 512) * 2 + 8) // (S // TCH))
    NQT = T // QT
    NSUB = TCH // 128
    SCALE = 1.0 / math.sqrt(HD)
    self_cc = make_self_cc(g.NC)

    x_t = nc.dram_tensor("x_t", [D, S], F32, kind="ExternalInput")
    x_to = nc.dram_tensor("x_to", [D, T], F32, kind="ExternalInput")
    DS = D // g.NC
    HFP = ((HF + 128 * g.NC - 1) // (128 * g.NC)) * 128 * g.NC
    HFS = HFP // g.NC
    wq_s = nc.dram_tensor("wq_s", [DS, D], F32, kind="ExternalInput")
    NSH = g.NC // NKV
    S_KV = S // NSH
    x_kv = nc.dram_tensor("x_kv", [D, S_KV], F32, kind="ExternalInput")
    cos_kv = nc.dram_tensor("cos_kv", [S_KV, HD], F32, kind="ExternalInput")
    sin_kv = nc.dram_tensor("sin_kv", [S_KV, HD], F32, kind="ExternalInput")
    wkv_s = nc.dram_tensor("wkv_s", [D, 256], F32, kind="ExternalInput")
    wo_s = nc.dram_tensor("wo_s", [DS, D], F32, kind="ExternalInput")
    w1_s = nc.dram_tensor("w1_s", [DS, HF], F32, kind="ExternalInput")
    w2_s = nc.dram_tensor("w2_s", [DS, HF], F32, kind="ExternalInput")
    w3_s = nc.dram_tensor("w3_s", [HFS, D], F32, kind="ExternalInput")
    cosf = nc.dram_tensor("cosf", [S, HD], F32, kind="ExternalInput")
    sinf = nc.dram_tensor("sinf", [S, HD], F32, kind="ExternalInput")
    coso = nc.dram_tensor("coso", [T, HD], F32, kind="ExternalInput")
    sino = nc.dram_tensor("sino", [T, HD], F32, kind="ExternalInput")
    ident_in = nc.dram_tensor("ident", [128, 128], F32, kind="ExternalInput")
    ones_in = nc.dram_tensor("ones", [128, 128], F32, kind="ExternalInput")
    y_out = nc.dram_tensor("y", [T, D], F32, kind="ExternalOutput")

    cc_k = nc.dram_tensor("cc_k", [256, S_KV], BF16)
    cc_v = nc.dram_tensor("cc_v", [2 * S_KV, 128], BF16)
    scr_k = nc.dram_tensor("scr_k", [g.NC * 256, S_KV], BF16, addr_space=("Shared" if g.NC > 4 else "Local"))
    scr_v = nc.dram_tensor("scr_v", [g.NC * 2 * S_KV, 128], BF16, addr_space=("Shared" if g.NC > 4 else "Local"))
    scr_ah = nc.dram_tensor("scr_ah", [D, T], BF16)
    scr_al = nc.dram_tensor("scr_al", [D, T], BF16)
    scr_qh = nc.dram_tensor("scr_qh", [D, T], BF16)
    scr_ql = nc.dram_tensor("scr_ql", [D, T], BF16)
    CCSP = "Shared" if g.NC > 4 else "Local"
    cc_w12 = nc.dram_tensor("cc_w12", [2 * DS, HF], FP8)
    cc_w3 = nc.dram_tensor("cc_w3", [HFS, D], FP8)
    scr_w12 = nc.dram_tensor("scr_w12", [g.NC * 2 * DS, HF], FP8, addr_space=CCSP)
    scr_w3g = nc.dram_tensor("scr_w3g", [HFP, D], FP8, addr_space=CCSP)

    def w12_row(kind, r):
        rank, off = divmod(r, DS)
        return rank * 2 * DS + kind * DS + off
    cc_wsp_all = nc.dram_tensor("cc_wsp_all", [4 * DS, D], BF16)
    scr_wsp = nc.dram_tensor("scr_wsp", [g.NC * 4 * DS, D], BF16, addr_space=CCSP)

    def wsp_row(kind, r):
        rank, off = divmod(r, DS)
        return rank * 4 * DS + kind * DS + off
    cc_g_in = nc.dram_tensor("cc_g_in", [1, 3], F32)
    cc_g_out = nc.dram_tensor("cc_g_out", [1, 3], F32)

    if g.dbg:
        dbg_attn = nc.dram_tensor("dbg_attn", [128, DC * QT], BF16, kind="ExternalOutput")
        dbg_strm = nc.dram_tensor("dbg_strm", [128, DC * QT], F32, kind="ExternalOutput")
        dbg_xn = nc.dram_tensor("dbg_xn", [128, DC * QT], F32, kind="ExternalOutput")
        dbg_ffn = nc.dram_tensor("dbg_ffn", [128, DC * QT], F32, kind="ExternalOutput")

    sem = lambda n, s: SemC(nc, n, s)
    s_pe = sem("pe", 1)
    s_act = sem("act", 1)
    s_dve = sem("dve", 1)
    s_gp = sem("gp", 1)
    s_ld = [sem(f"ld{i}", 16) for i in range(12)]
    s_st = [sem(f"st{i}", 16) for i in range(16)]
    s_gl = [sem(f"gl{i}", 16) for i in range(4)]

    es_outer = ExitStack()
    sb = lambda n, sh, dt=F32: es_outer.enter_context(nc.sbuf_tensor(n, sh, dt))

    ident_sb = sb("identsb", [128, 128])
    ones_sb = sb("onessb", [128, 128])
    ones_bf = sb("onesbf", [128, 2], BF16)
    ident_bf = sb("identbf", [128, 128], BF16)
    bias_m0 = sb("bias_m0", [128, 1])
    hthr = sb("hthr", [128, 6])

    b.plan = {e: [] for e in ENGS}
    b.dma("sync", ident_sb[:, :], ident_in[:, :], inc=s_ld[3])
    b.dma("sync", ones_sb[:, :], ones_in[:, :], inc=s_ld[3])
    consts_ready = s_ld[3].v
    b.wait("vector", s_ld[3], consts_ready)
    b.vcopy(ones_bf[:, 0:1], ones_sb[:, 0:1], inc=s_dve)
    b.drain("vector")
    b.vcopy(ident_bf[:, :], ident_sb[:, :], inc=s_dve)
    b.drain("vector")
    b.L("gpsimd", lambda e: e.load_library(library_config.attnmlp))

    es_a0 = ExitStack()
    if True:
        sbA = lambda n, sh, dt=F32: es_a0.enter_context(nc.sbuf_tensor(n, sh, dt))

        def colchunks(width):
            cw = width
            while cw > 2048:
                assert cw % 2 == 0
                cw //= 2
            return cw, width // cw
        cw1, nc1 = colchunks(HF)
        cw3, nc3 = colchunks(D)
        WMAX = max(cw1, cw3, D)
        g_sb = es_a0.enter_context(nc.sbuf_tensor("g_sb", [128, 4], F32))
        w_in = [sbA("w_in0", [128, WMAX]), sbA("w_in1", [128, WMAX])]
        s1b = [sbA("s1b0", [128, WMAX]), sbA("s1b1", [128, WMAX])]
        s2b = [sbA("s2b0", [128, WMAX]), sbA("s2b1", [128, WMAX])]
        wqb = [sbA("wqb0", [128, WMAX], FP8), sbA("wqb1", [128, WMAX], FP8)]
        wsph = [sbA("wsph0", [128, D], BF16), sbA("wsph1", [128, D], BF16)]
        wspl = [sbA("wspl0", [128, D], BF16), sbA("wspl1", [128, D], BF16)]
        n_t1 = (D // 128) * nc1 * 2 + (HF // 128) * nc3
        accb = sbA("accb", [128, n_t1 + 8])
        gtot = sbA("gtot", [128, 3])

        wsrc = [w1_s, w2_s, w3_s]
        wtiles = [[(r, c, cw1) for r in range(DS // 128) for c in range(nc1)],
                  [(r, c, cw1) for r in range(DS // 128) for c in range(nc1)],
                  [(r, c, cw3) for r in range(HFS // 128) for c in range(nc3)]]

    def a0_work():
            col = 0
            wcol = []
            dve_done = {}
            ti = 0
            for wi, tiles in enumerate(wtiles):
                wcol.append(col)
                for (r, cc, width) in tiles:
                    sl = ti % 2
                    if ti >= 2:
                        b.wait("sync", s_act, dve_done[ti - 2])
                    b.dma("sync", ap2(w_in[sl], 0, 128, 0, width),
                          ap2(wsrc[wi], r * 128, 128, cc * width, width), inc=s_ld[6 + sl])
                    b.wait("scalar", s_ld[6 + sl], s_ld[6 + sl].v)
                    b.act(ap2(s1b[sl], 0, 128, 0, width), ap2(w_in[sl], 0, 128, 0, width),
                          AF.Abs, accum=accb[:, col:col + 1], inc=s_act)
                    dve_done[ti] = s_act.v
                    col += 1
                    ti += 1
                    yield
            b.wait("vector", s_act, s_act.v)
            for wi in range(3):
                c0, c1 = wcol[wi], wcol[wi] + len(wtiles[wi])
                b.vred(gtot[:, wi:wi + 1], accb[:, c0:c1], ALU.add, inc=s_dve)
                b.drain("vector")
            b.wait("gpsimd", s_dve)
            for wi in range(3):
                b.allred(gtot[:, wi:wi + 1], gtot[:, wi:wi + 1], ReduceOp.add, inc=s_gp)
            b.wait("gpsimd", s_gp, s_gp.v)
            b.dma("gpsimd", cc_g_in[0:1, 0:3], gtot[0:1, 0:3], inc=s_st[14])
            b.wait("gpsimd", s_st[14], s_st[14].v)
            b.L("gpsimd", lambda e: self_cc(e, "AllReduce", ALU.add, [cc_g_in[:, :]], [cc_g_out[:, :]], s_gp))
            s_gp.inc()
            b.wait("gpsimd", s_gp, s_gp.v)
            b.dma("gpsimd", g_sb[0:1, 0:3], cc_g_out[0:1, 0:3], inc=s_st[14])
            b.wait("gpsimd", s_st[14], s_st[14].v)
            b.pbcast(gtot[:, 0:3], g_sb[0:1, 0:3], inc=s_gp)
            b.wait("vector", s_gp)
            for wi in range(3):
                nw = float(g.NWELEM)
                b.ts(hthr[:, 2 * wi:2 * wi + 1], gtot[:, wi:wi + 1], -0.5 / nw, ALU.mult,
                     s2=-0.5e-8, op1=ALU.add, inc=s_dve)
                b.ts(hthr[:, 2 * wi + 1:2 * wi + 2], gtot[:, wi:wi + 1], 0.5 / nw, ALU.mult,
                     s2=0.5e-8, op1=ALU.add, inc=s_dve)
                b.drain("vector")
            hready = s_dve.v
            b.wait("scalar", s_dve, hready)

            b.wait("sync", s_act, s_act.v)
            b.wait("sync", s_dve, s_dve.v)
            stq_h = StoreQ(b, [s_st[10], s_st[11]])
            stq_l = StoreQ(b, [s_st[12], s_st[13]])
            sp_src = [(wq_s, 0, 1), (wo_s, 2, 3)]
            ti = 0
            dve_done = {}
            st_done = {}
            for (srcw, kh_, kl_) in sp_src:
                for r in range(DS // 128):
                    sl = ti % 2
                    if ti >= 2:
                        b.wait("sync", s_dve, dve_done[ti - 2])
                    b.dma("sync", ap2(w_in[sl], 0, 128, 0, D),
                          ap2(srcw, r * 128, 128, 0, D), inc=s_ld[6 + sl])
                    b.wait("vector", s_ld[6 + sl], s_ld[6 + sl].v)
                    if ti >= 2:
                        for _s, _v in st_done[ti - 2]:
                            b.wait("vector", _s, _v)
                    b.drain("vector")
                    b.vcopy(ap2(wsph[sl], 0, 128, 0, D), ap2(w_in[sl], 0, 128, 0, D))
                    b.drain("vector")
                    b.tt(ap2(wspl[sl], 0, 128, 0, D), ap2(w_in[sl], 0, 128, 0, D),
                         ap2(wsph[sl], 0, 128, 0, D), ALU.subtract, inc=s_dve)
                    dve_done[ti] = s_dve.v
                    b.drain("vector")
                    b.wait("gpsimd", s_dve, s_dve.v)
                    stq_h.dma(ap2(cc_wsp_all, kh_ * DS + r * 128, 128, 0, D), ap2(wsph[sl], 0, 128, 0, D))
                    stq_l.dma(ap2(cc_wsp_all, kl_ * DS + r * 128, 128, 0, D), ap2(wspl[sl], 0, 128, 0, D))
                    st_done[ti] = stq_h.done() + stq_l.done()
                    ti += 1
                    yield
            stq_h.wait_all("gpsimd")
            stq_l.wait_all("gpsimd")
            b.L("gpsimd", lambda e: self_cc(e, "AllGather", ALU.bypass,
                [cc_wsp_all[:, :]], [scr_wsp[:, :]], s_gp))
            s_gp.inc()
            b.scr_wq_ready = [(s_gp, s_gp.v)]
            yield


            b.wait("sync", s_dve, s_dve.v)
            stq = StoreQ(b, [s_st[8], s_st[9]])
            ti = 0
            act_done = {}
            dve_done = {}
            st_done = {}
            for wi, tiles in enumerate(wtiles):
                for (r, cc, width) in tiles:
                    sl = ti % 2
                    if ti >= 2:
                        b.wait("sync", s_act, act_done[ti - 2])
                    b.dma("sync", ap2(w_in[sl], 0, 128, 0, width),
                          ap2(wsrc[wi], r * 128, 128, cc * width, width), inc=s_ld[6 + sl])
                    b.wait("scalar", s_ld[6 + sl], s_ld[6 + sl].v)
                    if ti >= 2:
                        b.wait("scalar", s_dve, dve_done[ti - 2])
                    b.act(ap2(s1b[sl], 0, 128, 0, width), ap2(w_in[sl], 0, 128, 0, width),
                          AF.Sign, bias=hthr[:, 2 * wi:2 * wi + 1], inc=s_act)
                    b.act(ap2(s2b[sl], 0, 128, 0, width), ap2(w_in[sl], 0, 128, 0, width),
                          AF.Sign, bias=hthr[:, 2 * wi + 1:2 * wi + 2], inc=s_act)
                    act_done[ti] = s_act.v
                    b.wait("vector", s_act, act_done[ti])
                    if ti >= 2:
                        for _s, _v in st_done[ti - 2]:
                            b.wait("vector", _s, _v)
                    b.drain("vector")
                    b.tt(ap2(wqb[sl], 0, 128, 0, width), ap2(s1b[sl], 0, 128, 0, width),
                         ap2(s2b[sl], 0, 128, 0, width), ALU.add, inc=s_dve)
                    dve_done[ti] = s_dve.v
                    b.wait("gpsimd", s_dve, s_dve.v)
                    if wi < 2:
                        stq.dma(ap2(cc_w12, wi * DS + r * 128, 128, cc * width, width),
                                ap2(wqb[sl], 0, 128, 0, width))
                    else:
                        stq.dma(ap2(cc_w3, r * 128, 128, cc * width, width),
                                ap2(wqb[sl], 0, 128, 0, width))
                    st_done[ti] = stq.done()
                    ti += 1
                    yield
            stq.wait_all("gpsimd")
            for _ci, _co in ((cc_w12, scr_w12), (cc_w3, scr_w3g)):
                b.L("gpsimd", lambda e, _a=_ci, _b=_co: self_cc(e, "AllGather", ALU.bypass,
                    [_a[:, :]], [_b[:, :]], s_gp))
                s_gp.inc()
            b.scr_ffn_ready = [(s_gp, s_gp.v)]

    a0_gen = a0_work()
    for _ in a0_gen:
        pass
    b.run_stage("A0")
    es_a0.close()

    with ExitStack() as es:
        sbA = lambda n, sh, dt=F32: es.enter_context(nc.sbuf_tensor(n, sh, dt))
        xt32 = sbA("xt32", [128, DC * TCH])
        xth = [sbA("xth0", [128, DC * TCH], BF16), sbA("xth1", [128, DC * TCH], BF16)]
        xtl = [sbA("xtl0", [128, DC * TCH], BF16), sbA("xtl1", [128, DC * TCH], BF16)]
        wkv32 = sbA("wkv32", [128, max(NKD, 256)])
        wkvh = sbA("wkvh", [128, DC * 256], BF16)
        wkvl = sbA("wkvl", [128, DC * 256], BF16)
        cs = [sbA("cs0", [128, NSUB * 2 * HD])]
        t1b = sbA("t1b", [128, max(NKD, 512)])
        t2b = sbA("t2b", [128, max(NKD, 512)])
        kpbs = [sbA(f"kpb{i}", [128, max(NKD, 512)]) for i in range(2)]
        kpbhs = [sbA(f"kpbh{i}", [128, max(NKD, 512)], BF16) for i in range(2)]
        kpbls = [sbA(f"kpbl{i}", [128, max(NKD, 512)], BF16) for i in range(2)]
        ksth = sbA("ksth", [128, max(NKV, 4) * TCH], BF16)
        kstl = sbA("kstl", [128, max(NKV, 4) * TCH], BF16)
        vsth = sbA("vsth", [128, NSUB * 128], BF16)
        vstl = sbA("vstl", [128, NSUB * 128], BF16)
        wqtlh = [sbA("wqtlh0", [128, 512], BF16), sbA("wqtlh1", [128, 512], BF16)]
        wqtll = [sbA("wqtll0", [128, 512], BF16), sbA("wqtll1", [128, 512], BF16)]
        qsth, qstl = ksth, kstl

        ps = [es.enter_context(nc.psum_tensor(f"psA{i}", [128, 512], F32)) for i in range(4)]
        psKb, psVb = ps[0:2], ps[2:4]
        psQ = ps[0:4]
        psT = [es.enter_context(nc.psum_tensor(f"psAT{i}", [128, 512], BF16)) for i in range(2)]

        b.plan = {e: [] for e in ENGS}
        for dc in range(DC):
            b.dma("sync", wkv32[:, 0:256], ap2(wkv_s, dc * 128, 128, 0, 256), inc=s_ld[2])
            b.wait("vector", s_ld[2], s_ld[2].v)
            if dc > 0:
                b.drain("vector")
            co0 = dc * 256
            b.vcopy(ap2(wkvh, 0, 128, co0, 256), wkv32[:, 0:256])
            b.drain("vector")
            b.tt(ap2(wkvl, 0, 128, co0, 256), wkv32[:, 0:256],
                 ap2(wkvh, 0, 128, co0, 256), ALU.subtract, inc=s_dve)
            b.wait("sync", s_dve, s_dve.v)
        wkv_ready = s_dve.v

        tr_bank_free = [None, None]
        kpb_tr_pe = {0: None, 1: None}
        xt32_free_dve = 0
        cs_free_dve = 0
        rope2_done = {}
        vevac2_done = {}
        pe_after_chunk = {}
        stq_kh = StoreQ(b, [s_st[0], s_st[1]])
        stq_kl = StoreQ(b, [s_st[2], s_st[3]])
        stq_vh = StoreQ(b, [s_st[4], s_st[5]])
        stq_vl = StoreQ(b, [s_st[6], s_st[7]])

        def load_xt_cs(sl, src_x, src_cos, src_sin, col0, slot_free_pe):
            nonlocal xt32_free_dve, cs_free_dve
            if cs_free_dve:
                b.wait("sync", s_dve, cs_free_dve)
            b.dma("sync", bass.AP(cs[0], 0, [[NSUB * 2 * HD, 128], [2 * HD, NSUB], [1, HD]]),
                  bass.AP(src_cos, col0 * HD, [[HD, 128], [128 * HD, NSUB], [1, HD]]), inc=s_ld[sl])
            b.dma("sync", bass.AP(cs[0], HD, [[NSUB * 2 * HD, 128], [2 * HD, NSUB], [1, HD]]),
                  bass.AP(src_sin, col0 * HD, [[HD, 128], [128 * HD, NSUB], [1, HD]]), inc=s_ld[sl])
            if xt32_free_dve:
                b.wait("sync", s_dve, xt32_free_dve)
            b.dma("sync", xt32[:, :],
                  bass.AP(src_x, col0, [[src_x.shape[1], 128], [128 * src_x.shape[1], DC], [1, TCH]]),
                  inc=s_ld[sl])
            b.wait("vector", s_ld[sl], s_ld[sl].v)
            if slot_free_pe:
                b.wait("vector", s_pe, slot_free_pe)
            b.drain("vector")
            b.vcopy(xth[sl][:, :], xt32[:, :])
            b.drain("vector")
            b.tt(xtl[sl][:, :], xt32[:, :], xth[sl][:, :], ALU.subtract, inc=s_dve)
            xt32_free_dve = s_dve.v
            b.drain("vector")
            return s_ld[sl].v, s_dve.v

        def rope(psrc, nheads, width, sl, sub, kset):
            dst, kpbh, kpbl = kpbs[kset], kpbhs[kset], kpbls[kset]
            base = sub * 2 * HD
            cosap = bass.AP(cs[0], base, [[NSUB * 2 * HD, 128], [0, nheads], [1, HD]])
            sin4 = bass.AP(cs[0], base + HD, [[NSUB * 2 * HD, 128], [0, nheads], [HD // 2, 2], [1, HD // 2]])
            kap = bass.AP(psrc, 0, [[512, 128], [HD, nheads], [1, HD]])
            kshift = bass.AP(psrc, HD // 2, [[512, 128], [HD, nheads], [-(HD // 2), 2], [1, HD // 2]])
            t1ap = bass.AP(t1b, 0, [[t1b.shape[1], 128], [HD, nheads], [1, HD]])
            t2ap = bass.AP(t2b, 0, [[t2b.shape[1], 128], [HD, nheads], [HD // 2, 2], [1, HD // 2]])
            b.tt(t1ap, kap, cosap, ALU.mult)
            b.tt(t2ap, kshift, sin4, ALU.mult, inc=s_dve)
            b.drain("vector")
            b.tt(ap2(dst, 0, 128, 0, width), ap2(t1b, 0, 128, 0, width),
                 ap2(t2b, 0, 128, 0, width), ALU.add, inc=s_dve)
            b.drain("vector")
            b.vcopy(ap2(kpbh, 0, 128, 0, width), ap2(dst, 0, 128, 0, width))
            b.drain("vector")
            b.tt(ap2(kpbl, 0, 128, 0, width), ap2(dst, 0, 128, 0, width),
                 ap2(kpbh, 0, 128, 0, width), ALU.subtract, inc=s_dve)
            out = s_dve.v
            b.drain("vector")
            return out

        def transpose_pair(nslices, dsth, dstl, dcol_fn, ready_dve, kset):
            for i in range(nslices):
                for which, (srcb, dstb) in enumerate(((kpbhs[kset], dsth), (kpbls[kset], dstl))):
                    bank = (2 * i + which) % 2
                    b.wait("tensor", s_act, tr_bank_free[bank])
                    b.wait("tensor", s_dve, ready_dve)
                    b.tr(ap2(psT[bank], 0, 128, 0, 128),
                         ap2(srcb, 0, 128, i * HD, HD), ident_bf[:, :], inc=s_pe)
                    b.wait("scalar", s_pe, s_pe.v)
                    b.acopy(ap2(dstb, 0, 128, dcol_fn(i), 128),
                            ap2(psT[bank], 0, 128, 0, 128), inc=s_act)
                    tr_bank_free[bank] = s_act.v
            kpb_tr_pe[kset] = s_pe.v

        NCHK = S_KV // TCH
        chunk_loaded = {}
        chunk_loaded[0] = load_xt_cs(0, x_kv, cos_kv, sin_kv, 0, 0)
        for c in range(NCHK):
            sl = c % 2
            xt_ready, xs_done = chunk_loaded[c]
            b.wait("tensor", s_dve, xs_done)

            for sub in range(NSUB):
                si = (c * NSUB + sub) % 2
                psKV = psKb[si]
                idx = c * NSUB + sub
                if idx >= 2:
                    b.wait("tensor", s_dve, rope2_done.get(idx - 2, 0))
                    b.wait("tensor", s_dve, vevac2_done.get(idx - 2, 0))
                for dc in range(DC):
                    lhs_hl = (ap2(xth[sl], 0, 128, dc * TCH + sub * 128, 128),
                              ap2(xtl[sl], 0, 128, dc * TCH + sub * 128, 128))
                    rhs_hl = (ap2(wkvh, 0, 128, dc * 256, 256),
                              ap2(wkvl, 0, 128, dc * 256, 256))
                    b.mm3p(ap2(psKV, 0, 128, 0, 256), lhs_hl, rhs_hl,
                           first=(dc == 0), last=(dc == DC - 1),
                           inc=(s_pe if dc == DC - 1 else None))
                pe_kv = s_pe.v
                b.wait("scalar", s_pe, pe_kv)
                if c > 0:
                    for _s, _v in vstage_stored:
                        b.wait("scalar", _s, _v)
                b.acopy(ap2(vsth, 0, 128, sub * 128, 128), ap2(psKV, 0, 128, 128, 128), inc=s_act)
                b.wait("vector", s_act, s_act.v)
                if c > 0:
                    for _s, _v in vstage_stored:
                        b.wait("vector", _s, _v)
                b.drain("vector")
                b.tt(ap2(vstl, 0, 128, sub * 128, 128), ap2(psKV, 0, 128, 128, 128),
                     ap2(vsth, 0, 128, sub * 128, 128), ALU.subtract, inc=s_dve)
                vevac2_done[idx] = s_dve.v
                b.drain("vector")
                b.wait("vector", s_ld[sl], xt_ready)
                b.wait("vector", s_pe, pe_kv)
                if c > 0:
                    for _s, _v in kstage_stored:
                        b.wait("vector", _s, _v)
                kset = idx % 2
                if kpb_tr_pe[kset] is not None:
                    b.wait("vector", s_pe, kpb_tr_pe[kset])
                prev_rope = rope(psKV, 1, 128, sl, sub, kset)
                rope2_done[idx] = prev_rope
                cs_free_dve = prev_rope
                transpose_pair(1, ksth, kstl,
                               lambda i, _sub=sub: _sub * 128, prev_rope, kset)
            pe_after_chunk[c] = s_pe.v
            b.wait("gpsimd", s_act, s_act.v)
            b.wait("gpsimd", s_dve, s_dve.v)
            stq_kh.dma(ap2(cc_k, 0, 128, c * TCH, TCH), ap2(ksth, 0, 128, 0, TCH))
            stq_kl.dma(ap2(cc_k, 128, 128, c * TCH, TCH), ap2(kstl, 0, 128, 0, TCH))
            kstage_stored = stq_kh.done() + stq_kl.done()
            for sub in range(NSUB):
                stq_vh.dma(ap2(cc_v, c * TCH + sub * 128, 128, 0, 128),
                           ap2(vsth, 0, 128, sub * 128, 128))
                stq_vl.dma(ap2(cc_v, S_KV + c * TCH + sub * 128, 128, 0, 128),
                           ap2(vstl, 0, 128, sub * 128, 128))
            vstage_stored = stq_vh.done() + stq_vl.done()
            if c + 1 < NCHK:
                chunk_loaded[c + 1] = load_xt_cs((c + 1) % 2, x_kv, cos_kv, sin_kv, (c + 1) * TCH,
                                                 pe_after_chunk.get(c - 1, 0))
        for q_ in (stq_kh, stq_kl, stq_vh, stq_vl):
            q_.wait_all("gpsimd")
        for _ci, _co in ((cc_k, scr_k), (cc_v, scr_v)):
            b.L("gpsimd", lambda e, _a=_ci, _b=_co: self_cc(e, "AllGather", ALU.bypass,
                [_a[:, :]], [_b[:, :]], s_gp))
            s_gp.inc()
        kv_gathered = s_gp.v

        wq_pe = {}
        wq_ld = 0
        stq_qh = StoreQ(b, [s_st[8], s_st[9]], prewait=True)
        stq_ql = StoreQ(b, [s_st[0], s_st[1]], prewait=True)
        for _s, _v in b.scr_wq_ready:
            b.wait("sync", _s, _v)
        for q_ in (stq_kh, stq_kl, stq_vh, stq_vl):
            q_.wait_all("scalar")
            q_.wait_all("vector")
        for co in range(OCH):
            sl = co % 2
            b.wait("sync", s_pe, s_pe.v)
            b.wait("sync", s_dve, s_dve.v)
            xt_ready, xs_done = load_xt_cs(sl, x_to, coso, sino, co * TCH, s_pe.v)
            b.wait("tensor", s_dve, xs_done)
            for ob in range(D // 512):
                for dc in range(DC):
                    wsl = wq_ld % 2
                    if wq_ld >= 2:
                        b.wait("sync", s_pe, wq_pe[wq_ld - 2])
                    b.dma("sync", ap2(wqtlh[wsl], 0, 128, 0, 512),
                          ap2(scr_wsp, wsp_row(0, dc * 128), 128, ob * 512, 512), inc=s_ld[4 + wsl])
                    b.dma("sync", ap2(wqtll[wsl], 0, 128, 0, 512),
                          ap2(scr_wsp, wsp_row(1, dc * 128), 128, ob * 512, 512), inc=s_ld[4 + wsl])
                    b.wait("tensor", s_ld[4 + wsl], s_ld[4 + wsl].v)
                    for sub in range(NSUB):
                        if dc == 0 and (ob > 0 or co > 0):
                            b.wait("tensor", s_dve, q_rope_done.get(sub, 0))
                        lhs_hl = (ap2(xth[sl], 0, 128, dc * TCH + sub * 128, 128),
                                  ap2(xtl[sl], 0, 128, dc * TCH + sub * 128, 128))
                        rhs_hl = (ap2(wqtlh[wsl], 0, 128, 0, 512),
                                  ap2(wqtll[wsl], 0, 128, 0, 512))
                        b.mm3p(ap2(psQ[sub], 0, 128, 0, 512), lhs_hl, rhs_hl,
                               first=(dc == 0), last=(dc == DC - 1),
                               inc=(s_pe if sub == NSUB - 1 else None))
                    wq_pe[wq_ld] = s_pe.v
                    wq_ld += 1
                if co == 0 and ob == 0:
                    q_rope_done = {}
                pe_q = s_pe.v
                for sub in range(NSUB):
                    b.wait("vector", s_pe, pe_q)
                    b.wait("vector", s_ld[sl], xt_ready)
                    if not (co == 0 and ob == 0):
                        for _s, _v in qstage_stored:
                            b.wait("vector", _s, _v)
                    kset = sub % 2
                    if kpb_tr_pe[kset] is not None:
                        b.wait("vector", s_pe, kpb_tr_pe[kset])
                    qr = rope(psQ[sub], 4, 512, sl, sub, kset)
                    cs_free_dve = qr
                    q_rope_done[sub] = qr
                    transpose_pair(4, qsth, qstl,
                                   lambda i, _sub=sub: i * TCH + _sub * 128, qr, kset)
                b.wait("gpsimd", s_act, s_act.v)
                for hs in range(4):
                    stq_qh.dma(ap2(scr_qh, ob * 512 + hs * 128, 128, co * TCH, TCH),
                               ap2(qsth, 0, 128, hs * TCH, TCH))
                    stq_ql.dma(ap2(scr_ql, ob * 512 + hs * 128, 128, co * TCH, TCH),
                               ap2(qstl, 0, 128, hs * TCH, TCH))
                qstage_stored = stq_qh.done() + stq_ql.done()
        for q_ in (stq_kh, stq_kl, stq_vh, stq_vl, stq_qh, stq_ql):
            q_.wait_all("gpsimd")
        b.run_stage("A01")
    es_a0.close()

    first_m0 = True
    for qt in range(NQT):
        es_qt = ExitStack()

        with ExitStack() as es:
            sbB = lambda n, sh, dt=F32: es.enter_context(nc.sbuf_tensor(f"{n}_{qt}", sh, dt))
            qtbh = sbB("qtbh", [128, DC * QT], BF16)
            qtbl = sbB("qtbl", [128, DC * QT], BF16)
            kgh = sbB("kgh", [128, S], BF16)
            kgl = sbB("kgl", [128, S], BF16)
            vgh = sbB("vgh", [128, NKC * HD], BF16)
            vgl = sbB("vgl", [128, NKC * HD], BF16)
            p32 = [sbB(f"p32{i}", [128, QT]) for i in range(3)]
            phi = [sbB(f"phi{i}", [128, QT], BF16) for i in range(3)]
            plo = [sbB(f"plo{i}", [128, QT], BF16) for i in range(3)]
            ptmp = sbB("ptmp", [128, QT])
            rcp = sbB("rcp", [1, QT])
            bcst = sbB("bcst", [128, QT])
            mx0 = sbB("mx0", [128, 1])
            aoh = [sbB(f"aoh{i}", [128, QT], BF16) for i in range(2)]
            aol = [sbB(f"aol{i}", [128, QT], BF16) for i in range(2)]
            ps = [es.enter_context(nc.psum_tensor(f"psB{i}_{qt}", [128, 512], F32)) for i in range(7)]
            psS = ps[0:3]
            psPVb, psDb = ps[3:5], ps[5:7]

            b.plan = {e: [] for e in ENGS}
            prev_tail_pv = None
            evac_done_by_bank = {}
            stq_ao = StoreQ(b, [s_st[2], s_st[3]], prewait=True)
            ao_stored = {}
            b.wait("sync", s_gp, kv_gathered)
            b.dma("sync", qtbh[:, :],
                  bass.AP(scr_qh, qt * QT, [[T, 128], [128 * T, DC], [1, QT]]), inc=s_ld[0])
            b.dma("sync", qtbl[:, :],
                  bass.AP(scr_ql, qt * QT, [[T, 128], [128 * T, DC], [1, QT]]), inc=s_ld[0])
            q_ready = s_ld[0].v
            b.wait("tensor", s_ld[0], q_ready)
            for gidx in range(NKV):
                if gidx > 0:
                    b.wait("sync", s_pe, s_pe.v)
                NSHl = g.NC // NKV
                S_KVl = S // NSHl
                for half in range(NSHl):
                    blk_ = gidx + half * NKV
                    b.dma("sync", kgh[:, half * S_KVl:(half + 1) * S_KVl],
                          ap2(scr_k, blk_ * 256, 128, 0, S_KVl), inc=s_ld[1])
                    b.dma("sync", kgl[:, half * S_KVl:(half + 1) * S_KVl],
                          ap2(scr_k, blk_ * 256 + 128, 128, 0, S_KVl), inc=s_ld[1])
                NKCH = S_KVl // 128
                vsplit = max(1, NKCH // 8)
                for lohalf, dstv in ((0, vgh), (1, vgl)):
                    for half in range(NSHl):
                        blk_ = gidx + half * NKV
                        base_row = blk_ * 2 * S_KVl + lohalf * S_KVl
                        for i in range(0, NKCH, vsplit):
                            n = min(vsplit, NKCH - i)
                            b.dma("sync",
                                  bass.AP(dstv, (half * NKCH + i) * HD, [[NKC * HD, 128], [HD, n], [1, HD]]),
                                  bass.AP(scr_v, (base_row + i * 128) * 128, [[128, 128], [128 * 128, n], [1, 128]]),
                                  inc=s_ld[1])
                kv_ready = s_ld[1].v
                b.wait("tensor", s_ld[1], kv_ready)
                for h in range(REP):
                    hh = gidx * REP + h
                    psPV = psPVb[hh % 2]
                    psD = psDb[hh % 2]
                    act_exp = {}
                    dve_plo = {}
                    pe_pv = {}

                    def emit_pv(kp, psPV=psPV, psD=psD):
                        pbp = kp % 3
                        if kp == 0 and evac_done_by_bank.get(hh % 2):
                            b.wait("tensor", s_dve, evac_done_by_bank[hh % 2])
                        b.wait("tensor", s_act, act_exp[kp])
                        b.wait("tensor", s_dve, dve_plo[kp])
                        vh = bass.AP(vgh, kp * HD, [[NKC * HD, 128], [1, HD]])
                        vl = bass.AP(vgl, kp * HD, [[NKC * HD, 128], [1, HD]])
                        for i, (lv, rp) in enumerate(((vh, phi[pbp]), (vh, plo[pbp]), (vl, phi[pbp]))):
                            b.mm(ap2(psPV, 0, 128, 0, QT), lv, rp[:, :],
                                 start=(kp == 0 and i == 0), stop=(kp == NKC - 1 and i == 2),
                                 inc=(s_pe if i == 2 else None))
                        b.mm(ap2(psD, 0, 1, 0, QT), ones_bf[:, 0:1], phi[pbp][:, :],
                             start=(kp == 0), stop=(kp == NKC - 1), inc=s_pe)
                        pe_pv[kp] = s_pe.v

                    for kc in range(NKC):
                        pb = kc % 3
                        if kc >= 3:
                            b.wait("tensor", s_act, act_exp[kc - 3])
                        lhs_hl = (ap2(kgh, 0, 128, kc * 128, 128),
                                  ap2(kgl, 0, 128, kc * 128, 128))
                        rhs_hl = (ap2(qtbh, 0, 128, hh * QT, QT),
                                  ap2(qtbl, 0, 128, hh * QT, QT))
                        b.mm3p(ap2(psS[kc % 3], 0, 128, 0, QT), lhs_hl, rhs_hl, inc=s_pe)
                        pe_s = s_pe.v
                        if first_m0 and kc == 0:
                            b.wait("vector", s_pe, pe_s)
                            b.vred(mx0[:, :], ap2(psS[pb], 0, 128, 0, QT), ALU.max, inc=s_dve)
                            b.drain("vector")
                            b.wait("gpsimd", s_dve, s_dve.v)
                            b.allred(mx0[:, :], mx0[:, :], ReduceOp.max, inc=s_gp)
                            b.wait("vector", s_gp, s_gp.v)
                            b.ts(bias_m0[:, :], mx0[:, :], -SCALE, ALU.mult, inc=s_dve)
                            b.drain("vector")
                            b.wait("scalar", s_dve, s_dve.v)
                        if kc < 3 and prev_tail_pv is not None:
                            b.wait("scalar", s_pe, prev_tail_pv)
                        if kc >= 3:
                            b.wait("scalar", s_dve, dve_plo[kc - 3])
                            b.wait("scalar", s_pe, pe_pv[kc - 3])
                        b.wait("scalar", s_pe, pe_s)
                        b.act(p32[pb][:, :], ap2(psS[pb], 0, 128, 0, QT), AF.Exp,
                              bias=bias_m0[:, :], scale=SCALE)
                        b.act(phi[pb][:, :], ap2(psS[pb], 0, 128, 0, QT), AF.Exp,
                              bias=bias_m0[:, :], scale=SCALE, inc=s_act)
                        act_exp[kc] = s_act.v
                        b.wait("vector", s_act, act_exp[kc])
                        b.drain("vector")
                        b.tt(plo[pb][:, :], p32[pb][:, :], phi[pb][:, :], ALU.subtract, inc=s_dve)
                        dve_plo[kc] = s_dve.v
                        b.drain("vector")
                        if kc >= 2:
                            emit_pv(kc - 2)
                    emit_pv(NKC - 2)
                    emit_pv(NKC - 1)
                    pe_tail = s_pe.v
                    b.wait("vector", s_pe, pe_tail)
                    b.recip(rcp[:, :], ap2(psD, 0, 1, 0, QT), inc=s_dve)
                    b.drain("vector")
                    b.wait("gpsimd", s_dve, s_dve.v)
                    b.pbcast(bcst[:, :], rcp[:, :], inc=s_gp)
                    b.wait("vector", s_gp, s_gp.v)
                    b.tt(ptmp[:, :], ap2(psPV, 0, 128, 0, QT), bcst[:, :], ALU.mult, inc=s_dve)
                    b.drain("vector")
                    b.wait("scalar", s_dve, s_dve.v)
                    if hh >= 2:
                        for _s, _v in ao_stored[hh - 2]:
                            b.wait("scalar", _s, _v)
                            b.wait("vector", _s, _v)
                    b.acopy(aoh[hh % 2][:, :], ptmp[:, :], inc=s_act)
                    b.wait("vector", s_act, s_act.v)
                    b.tt(aol[hh % 2][:, :], ptmp[:, :], aoh[hh % 2][:, :], ALU.subtract, inc=s_dve)
                    b.drain("vector")
                    b.wait("gpsimd", s_dve, s_dve.v)
                    stq_ao.dma(ap2(scr_ah, hh * 128, 128, qt * QT, QT), aoh[hh % 2][:, :])
                    stq_ao.dma(ap2(scr_al, hh * 128, 128, qt * QT, QT), aol[hh % 2][:, :])
                    ao_stored[hh] = stq_ao.done()
                    evac_done_by_bank[hh % 2] = s_dve.v
                    prev_tail_pv = pe_tail
                    first_m0 = False
            stq_ao.wait_all("gpsimd")
            b.run_stage(f"B{qt}")

        with ExitStack() as es:
            sbC = lambda n, sh, dt=F32: es.enter_context(nc.sbuf_tensor(f"{n}_{qt}", sh, dt))
            strm = sbC("strm", [128, DC * QT])
            xn = sbC("xn", [128, DC * QT])
            xnh = sbC("xnh", [128, DC * QT], BF16)
            xnl = sbC("xnl", [128, DC * QT], BF16)
            hbh = sbC("hbh", [128, HFC * FSUB], BF16)
            hbl = sbC("hbl", [128, HFC * FSUB], BF16)
            htmp = sbC("htmp", [128, FSUB])
            wtlh = [sbC("wtlh0", [128, 1024], BF16), sbC("wtlh1", [128, 1024], BF16)]
            wtll = [sbC("wtll0", [128, 1024], BF16), sbC("wtll1", [128, 1024], BF16)]
            aosh = [sbC("aosh0", [128, QT], BF16), sbC("aosh1", [128, QT], BF16)]
            aosl = [sbC("aosl0", [128, QT], BF16), sbC("aosl1", [128, QT], BF16)]
            wf8 = [sbC(f"wf8{i}", [128, 512], FP8) for i in range(4)]
            w2f8 = [sbC(f"w2f8{i}", [128, 512], FP8) for i in range(4)]
            sgb = sbC("sgb", [128, FSUB])
            stat = sbC("stat", [1, QT])
            stat2 = sbC("stat2", [1, QT])
            statm = sbC("statm", [1, QT])
            sqb = [sbC("sqb0", [128, QT])]
            bcstC = sbC("bcstC", [128, QT])
            bcstC2 = sbC("bcstC2", [128, QT])
            NSEG = 4 if DC % 4 == 0 else 2
            YW = D // NSEG
            ybufs = [sbC("ybuf", [128, YW]), sbC("ybuf2", [128, YW])]
            ps = [es.enter_context(nc.psum_tensor(f"psC{i}_{qt}", [128, 512], F32)) for i in range(8)]

            b.plan = {e: [] for e in ENGS}
            b.dma("sync", strm[:, :],
                  bass.AP(x_to, qt * QT, [[T, 128], [128 * T, DC], [1, QT]]), inc=s_ld[0])
            x_ready = s_ld[0].v
            wo_ld = 0
            wo_pe = {}
            dve_oc8 = {}
            OG = min(8, DC)
            for og in range(DC // OG):
                if og > 0:
                    b.wait("tensor", s_dve, dve_oc8[og - 1])
                for hc in range(DC):
                    wsl = wo_ld % 2
                    lsem1, lsem3 = s_ld[2 + wsl], s_ld[8 + wsl]
                    glo1, glo2 = s_gl[wsl], s_gl[2 + wsl]
                    if wo_ld >= 2:
                        b.wait("sync", s_pe, wo_pe[wo_ld - 2])
                        b.wait("gpsimd", s_pe, wo_pe[wo_ld - 2])
                    b.wait("gpsimd", glo1, glo1.v)
                    b.wait("gpsimd", glo2, glo2.v)
                    b.dma("sync", ap2(wtlh[wsl], 0, 128, 0, OG * 128),
                          ap2(scr_wsp, wsp_row(2, hc * 128), 128, og * OG * 128, OG * 128), inc=lsem1)
                    b.dma("gpsimd", ap2(wtll[wsl], 0, 128, 0, OG * 128),
                          ap2(scr_wsp, wsp_row(3, hc * 128), 128, og * OG * 128, OG * 128), inc=glo1)
                    b.dma("sync", ap2(aosh[wsl], 0, 128, 0, QT),
                          ap2(scr_ah, hc * 128, 128, qt * QT, QT), inc=lsem3)
                    b.dma("gpsimd", ap2(aosl[wsl], 0, 128, 0, QT),
                          ap2(scr_al, hc * 128, 128, qt * QT, QT), inc=glo2)
                    b.wait("tensor", lsem1, lsem1.v)
                    b.wait("tensor", glo1, glo1.v)
                    b.wait("tensor", lsem3, lsem3.v)
                    b.wait("tensor", glo2, glo2.v)
                    rhs_hl = (aosh[wsl][:, :], aosl[wsl][:, :])
                    for j in range(OG):
                        lhs_hl = (ap2(wtlh[wsl], 0, 128, j * 128, 128),
                                  ap2(wtll[wsl], 0, 128, j * 128, 128))
                        b.mm3p(ap2(ps[j], 0, 128, 0, QT), lhs_hl, rhs_hl,
                               first=(hc == 0), last=(hc == DC - 1),
                               inc=(s_pe if j == OG - 1 else None))
                    wo_pe[wo_ld] = s_pe.v
                    wo_ld += 1
                b.wait("vector", s_pe, s_pe.v)
                if og == 0:
                    b.wait("vector", s_ld[0], x_ready)
                for j in range(OG):
                    oc = og * OG + j
                    b.drain("vector")
                    b.tt(ap2(strm, 0, 128, oc * QT, QT), ap2(ps[j], 0, 128, 0, QT),
                         ap2(strm, 0, 128, oc * QT, QT), ALU.add, inc=s_dve)
                dve_oc8[og] = s_dve.v
                b.drain("vector")

            def layer_norm(src, dst, src_ready_dve):
                b.wait("tensor", s_dve, src_ready_dve)
                for dc in range(DC):
                    b.mm(ap2(ps[2], 0, 1, 0, QT), ones_sb[:, 0:1], ap2(src, 0, 128, dc * QT, QT),
                         start=(dc == 0), stop=(dc == DC - 1), inc=(s_pe if dc == DC - 1 else None))
                b.wait("vector", s_pe, s_pe.v)
                b.ts(stat[:, :], ap2(ps[2], 0, 1, 0, QT), 1.0 / D, ALU.mult, inc=s_dve)
                b.drain("vector")
                b.wait("scalar", s_dve, src_ready_dve)
                pe_sq = {}
                for dc in range(DC):
                    if dc >= 1:
                        b.wait("scalar", s_pe, pe_sq[dc - 1])
                    b.act(sqb[0][:, :], ap2(src, 0, 128, dc * QT, QT), AF.Square, inc=s_act)
                    b.wait("tensor", s_act, s_act.v)
                    b.mm(ap2(ps[3], 0, 1, 0, QT), ones_sb[:, 0:1], sqb[0][:, :],
                         start=(dc == 0), stop=(dc == DC - 1), inc=s_pe)
                    pe_sq[dc] = s_pe.v
                b.wait("vector", s_pe, s_pe.v)
                b.ts(stat2[:, :], ap2(ps[3], 0, 1, 0, QT), 1.0 / D, ALU.mult, inc=s_dve)
                b.drain("vector")
                b.tt(statm[:, :], stat[:, :], stat[:, :], ALU.mult, inc=s_dve)
                b.drain("vector")
                b.stt(stat2[:, :], statm[:, :], -1.0, stat2[:, :], ALU.mult, ALU.add, inc=s_dve)
                b.drain("vector")
                b.ts(stat2[:, :], stat2[:, :], 1.0, ALU.mult, s2=1e-5, op1=ALU.add, inc=s_dve)
                b.drain("vector")
                b.wait("scalar", s_dve, s_dve.v)
                b.act(statm[:, :], stat2[:, :], AF.Sqrt, inc=s_act)
                b.wait("vector", s_act, s_act.v)
                b.recip(stat2[:, :], statm[:, :], inc=s_dve)
                b.drain("vector")
                b.tt(statm[:, :], stat[:, :], stat2[:, :], ALU.mult, inc=s_dve)
                b.drain("vector")
                b.wait("gpsimd", s_dve, s_dve.v)
                b.pbcast(bcstC2[:, :], stat2[:, :], inc=s_gp)
                b.pbcast(bcstC[:, :], statm[:, :], inc=s_gp)
                b.wait("vector", s_gp, s_gp.v)
                for dc in range(DC):
                    b.tt(ap2(dst, 0, 128, dc * QT, QT), ap2(src, 0, 128, dc * QT, QT),
                         bcstC2[:, :], ALU.mult, inc=s_dve)
                    b.drain("vector")
                    b.tt(ap2(dst, 0, 128, dc * QT, QT), ap2(dst, 0, 128, dc * QT, QT),
                         bcstC[:, :], ALU.subtract, inc=s_dve)
                    b.drain("vector")
                return s_dve.v

            if g.dbg and qt == 0:
                b.wait("gpsimd", s_dve, s_dve.v)
                b.dma("gpsimd", dbg_strm[:, :], strm[:, :], inc=s_st[0])
                b.wait("gpsimd", s_st[0], s_st[0].v)
            xn_ready = layer_norm(strm, xn, s_dve.v)
            b.vcopy(xnh[:, :], xn[:, :])
            b.drain("vector")
            b.tt(xnl[:, :], xn[:, :], xnh[:, :], ALU.subtract, inc=s_dve)
            xn_split = s_dve.v
            b.drain("vector")
            if g.dbg and qt == 0:
                b.wait("gpsimd", s_dve, xn_split)
                b.dma("gpsimd", dbg_xn[:, :], xn[:, :], inc=s_st[0])
                b.wait("gpsimd", s_st[0], s_st[0].v)

            b.wait("sync", *b.scr_ffn_ready[0])
            w_pe = {}
            w_ld = 0
            m3_pe = {}
            m3_ld = 0
            for fs in range(QT // FSUB):
                for hbg in range(HFC // 4):
                    ps1, ps2 = ps[0:4], ps[4:8]
                    if hbg == 0:
                        b.wait("sync", s_pe, s_pe.v)
                        b.wait("gpsimd", s_pe, s_pe.v)
                    if not (fs == 0 and hbg == 0):
                        b.wait("tensor", s_dve, s_dve.v)
                    for dc in range(DC):
                        wsl = dc % 4
                        lsem1, glsem = s_ld[2 + wsl], s_gl[wsl]
                        if w_ld >= 4:
                            b.wait("sync", s_pe, w_pe[w_ld - 4])
                            b.wait("gpsimd", s_pe, w_pe[w_ld - 4])
                        b.wait("gpsimd", glsem, glsem.v)
                        b.dma("sync", ap2(wf8[wsl], 0, 128, 0, 512),
                              ap2(scr_w12, w12_row(0, dc * 128), 128, hbg * 512, 512), inc=lsem1)
                        b.dma("gpsimd", ap2(w2f8[wsl], 0, 128, 0, 512),
                              ap2(scr_w12, w12_row(1, dc * 128), 128, hbg * 512, 512), inc=glsem)
                        b.wait("tensor", lsem1, lsem1.v)
                        b.wait("tensor", glsem, glsem.v)
                        if dc == 0 and hbg == 0 and fs == 0:
                            b.wait("tensor", s_dve, xn_split)
                        rhs_hl = (ap2(xnh, 0, 128, dc * QT + fs * FSUB, FSUB),
                                  ap2(xnl, 0, 128, dc * QT + fs * FSUB, FSUB))
                        for j in range(4):
                            b.mm2p(ap2(ps1[j], 0, 128, 0, FSUB), ap2(wf8[wsl], 0, 128, j * 128, 128),
                                   rhs_hl, first=(dc == 0), last=(dc == DC - 1), inc=None)
                            b.mm2p(ap2(ps2[j], 0, 128, 0, FSUB), ap2(w2f8[wsl], 0, 128, j * 128, 128),
                                   rhs_hl, first=(dc == 0), last=(dc == DC - 1),
                                   inc=(s_pe if j == 3 else None))
                        w_pe[w_ld] = s_pe.v
                        w_ld += 1
                    pe_12 = s_pe.v
                    b.wait("scalar", s_pe, pe_12)
                    for j in range(4):
                        hb = hbg * 4 + j
                        if j > 0:
                            b.wait("scalar", s_dve, s_dve.v)
                        b.act(sgb[:, :], ap2(ps1[j], 0, 128, 0, FSUB), AF.Sigmoid, scale=0.5, inc=s_act)
                        b.wait("vector", s_act, s_act.v)
                        b.tt(htmp[:, :], ap2(ps2[j], 0, 128, 0, FSUB), sgb[:, :], ALU.mult, inc=s_dve)
                        b.drain("vector")
                        b.wait("scalar", s_dve, s_dve.v)
                        b.acopy(ap2(hbh, 0, 128, hb * FSUB, FSUB), htmp[:, :], inc=s_act)
                        b.wait("vector", s_act, s_act.v)
                        b.tt(ap2(hbl, 0, 128, hb * FSUB, FSUB), htmp[:, :],
                             ap2(hbh, 0, 128, hb * FSUB, FSUB), ALU.subtract, inc=s_dve)
                        b.drain("vector")
                h_act_done = s_act.v
                h_done = s_dve.v
                for ocg in range(DC // 4):
                    ps3 = ps[0:4]
                    if ocg == 0:
                        b.wait("sync", s_pe, s_pe.v)
                        b.wait("gpsimd", s_pe, s_pe.v)
                    if not (fs == 0 and ocg == 0):
                        b.wait("tensor", s_dve, m3_evac)
                    for hb in range(HFC):
                        wsl = hb % 4
                        if wsl % 2 == 0:
                            lsem, qeng = s_ld[2 + wsl], "sync"
                        else:
                            lsem, qeng = s_gl[wsl], "gpsimd"
                        if m3_ld >= 4:
                            b.wait(qeng, s_pe, m3_pe[m3_ld - 4])
                        if qeng == "gpsimd":
                            b.wait("gpsimd", lsem, lsem.v)
                        b.dma(qeng, ap2(wf8[wsl], 0, 128, 0, 512),
                              ap2(scr_w3g, hb * 128, 128, ocg * 512, 512), inc=lsem)
                        b.wait("tensor", lsem, lsem.v)
                        if hb == 0 and ocg == 0:
                            b.wait("tensor", s_dve, h_done)
                            b.wait("tensor", s_act, h_act_done)
                        rhs_hl = (ap2(hbh, 0, 128, hb * FSUB, FSUB),
                                  ap2(hbl, 0, 128, hb * FSUB, FSUB))
                        for j in range(4):
                            b.mm2p(ap2(ps3[j], 0, 128, 0, FSUB), ap2(wf8[wsl], 0, 128, j * 128, 128),
                                   rhs_hl, first=(hb == 0), last=(hb == HFC - 1),
                                   inc=(s_pe if j == 3 else None))
                        m3_pe[m3_ld] = s_pe.v
                        m3_ld += 1
                    b.wait("vector", s_pe, s_pe.v)
                    for j in range(4):
                        oc = ocg * 4 + j
                        b.stt(ap2(strm, 0, 128, oc * QT + fs * FSUB, FSUB),
                              ap2(ps3[j], 0, 128, 0, FSUB), 0.25,
                              ap2(xn, 0, 128, oc * QT + fs * FSUB, FSUB),
                              ALU.mult, ALU.add, inc=s_dve)
                        b.drain("vector")
                    m3_evac = s_dve.v

            if g.dbg and qt == 0:
                b.wait("gpsimd", s_dve, s_dve.v)
                b.dma("gpsimd", dbg_ffn[:, :], strm[:, :], inc=s_st[0])
                b.wait("gpsimd", s_st[0], s_st[0].v)
            out_ready = layer_norm(strm, xn, s_dve.v)
            tr_free = [None, None]
            stq_y = StoreQ(b, [s_st[6], s_st[7]])
            yst = stq_y.done()
            for sub in range(QT // 128):
                for oc in range(DC):
                    bank = 4 + (oc % 2)
                    b.wait("tensor", s_act, tr_free[oc % 2])
                    b.wait("tensor", s_dve, out_ready)
                    b.tr(ap2(ps[bank], 0, 128, 0, 128),
                         ap2(xn, 0, 128, oc * QT + sub * 128, 128), ident_sb[:, :], inc=s_pe)
                    b.wait("scalar", s_pe, s_pe.v)
                    seg = oc // (DC // NSEG)
                    if (oc % (DC // NSEG)) == 0 and (sub > 0 or seg >= 2):
                        for _s, _v in yst:
                            b.wait("scalar", _s, _v)
                    _yb = ybufs[seg % 2]
                    _yo = oc % (DC // NSEG)
                    b.acopy(ap2(_yb, 0, 128, _yo * 128, 128), ap2(ps[bank], 0, 128, 0, 128), inc=s_act)
                    tr_free[oc % 2] = s_act.v
                    if _yo == (DC // NSEG) - 1:
                        b.wait("gpsimd", s_act, s_act.v)
                        r0 = qt * QT + sub * 128
                        stq_y.dma(y_out[r0:r0 + 128, seg * YW:(seg + 1) * YW],
                                  ap2(_yb, 0, 128, 0, YW))
                        yst = stq_y.done()
            stq_y.wait_all("gpsimd")
            b.run_stage(f"C{qt}")
        es_qt.close()

    es_outer.close()
    mybir.codegen_inst_isa_subclasses(nc)
    return nc



def rope_tables(S, HD):
    import jax
    import jax.numpy as jnp
    with jax.default_device(jax.devices("cpu")[0]):
        inv = 1.0 / 10000.0 ** (jnp.arange(0, HD, 2, dtype=jnp.float32) / HD)
        t = jnp.arange(S, dtype=jnp.float32)
        freqs = jnp.einsum("i,j->ij", t, inv)
        emb = jnp.concatenate([freqs, freqs], axis=-1)
        cos = np.asarray(jnp.cos(emb), dtype=np.float32)
        sin = np.asarray(jnp.sin(emb), dtype=np.float32)
    return cos, sin


def prep_core_inputs(g: Geom, core, x, Wq, Wk, Wv, Wo, W1, W2, W3):
    S, D, HD, T = g.S, g.D, g.HD, g.T
    x2 = np.ascontiguousarray(x.reshape(S, D))
    cosf, sinf = rope_tables(S, HD)
    sin_signed = sinf.copy()
    sin_signed[:, :HD // 2] *= -1.0
    sl = slice(core * T, (core + 1) * T)
    NC = g.NC
    DS = D // NC
    HFP = ((g.HF + 128 * NC - 1) // (128 * NC)) * 128 * NC
    w3p = np.zeros((HFP, D), dtype=np.float32)
    w3p[:g.HF] = W3.T
    dsl = slice(core * DS, (core + 1) * DS)
    hsl = slice(core * (HFP // NC), (core + 1) * (HFP // NC))
    hc = core % g.NKV
    th = core // g.NKV
    S_KV = S // (NC // g.NKV)
    ksl = slice(th * S_KV, (th + 1) * S_KV)
    wkv_slice = np.concatenate([Wk.T[:, hc * 128:(hc + 1) * 128],
                                Wv.T[:, hc * 128:(hc + 1) * 128]], axis=1)
    return {
        "x_t": np.ascontiguousarray(x2.T),
        "x_to": np.ascontiguousarray(x2.T[:, sl]),
        "x_kv": np.ascontiguousarray(x2.T[:, ksl]),
        "cos_kv": np.ascontiguousarray(cosf[ksl]),
        "sin_kv": np.ascontiguousarray(sin_signed[ksl]),
        "wkv_s": np.ascontiguousarray(wkv_slice),
        "wq_s": np.ascontiguousarray(Wq.T[dsl]),
        "wo_s": np.ascontiguousarray(Wo.T[dsl]),
        "w1_s": np.ascontiguousarray(W1.T[dsl]),
        "w2_s": np.ascontiguousarray(W2.T[dsl]),
        "w3_s": np.ascontiguousarray(w3p[hsl]),
        "cosf": cosf, "sinf": sin_signed,
        "coso": np.ascontiguousarray(cosf[sl]), "sino": np.ascontiguousarray(sin_signed[sl]),
        "ident": np.eye(128, dtype=np.float32),
        "ones": np.ones((128, 128), dtype=np.float32),
    }




from concourse.bass_utils import run_bass_kernel_spmd


def kernel(L0_attn_x, L0_attn_Wq, L0_attn_Wk, L0_attn_Wv, L0_attn_Wo,
           L0_ffn_w1, L0_ffn_w2, L0_ffn_w3):
    g = Geom()
    x = np.ascontiguousarray(np.asarray(L0_attn_x, dtype=np.float32).reshape(g.S, g.D))
    Wq = np.asarray(L0_attn_Wq, dtype=np.float32)
    Wk = np.asarray(L0_attn_Wk, dtype=np.float32)
    Wv = np.asarray(L0_attn_Wv, dtype=np.float32)
    Wo = np.asarray(L0_attn_Wo, dtype=np.float32)
    W1 = np.asarray(L0_ffn_w1, dtype=np.float32)
    W2 = np.asarray(L0_ffn_w2, dtype=np.float32)
    W3 = np.asarray(L0_ffn_w3, dtype=np.float32)

    nc = build(g)

    x_t = np.ascontiguousarray(x.T)
    cosf, sinf = rope_tables(g.S, g.HD)
    sin_signed = sinf.copy()
    sin_signed[:, :g.HD // 2] *= -1.0
    wq_T = np.ascontiguousarray(Wq.T)
    wo_T = np.ascontiguousarray(Wo.T)
    w1_T = np.ascontiguousarray(W1.T)
    w2_T = np.ascontiguousarray(W2.T)
    NCn = g.NC
    DS = g.D // NCn
    HFP = ((g.HF + 128 * NCn - 1) // (128 * NCn)) * 128 * NCn
    HFS = HFP // NCn
    w3p = np.zeros((HFP, g.D), dtype=np.float32)
    w3p[:g.HF] = W3.T
    wk_T = np.ascontiguousarray(Wk.T)
    wv_T = np.ascontiguousarray(Wv.T)
    shared = {
        "x_t": x_t,
        "cosf": cosf, "sinf": sin_signed,
        "ident": np.eye(128, dtype=np.float32),
        "ones": np.ones((128, 128), dtype=np.float32),
    }
    in_maps = []
    for c in range(NCn):
        sl = slice(c * g.T, (c + 1) * g.T)
        dsl = slice(c * DS, (c + 1) * DS)
        hsl = slice(c * HFS, (c + 1) * HFS)
        m = dict(shared)
        m["x_to"] = np.ascontiguousarray(x_t[:, sl])
        m["coso"] = np.ascontiguousarray(cosf[sl])
        m["sino"] = np.ascontiguousarray(sin_signed[sl])
        m["wq_s"] = np.ascontiguousarray(wq_T[dsl])
        m["wo_s"] = np.ascontiguousarray(wo_T[dsl])
        m["w1_s"] = np.ascontiguousarray(w1_T[dsl])
        m["w2_s"] = np.ascontiguousarray(w2_T[dsl])
        m["w3_s"] = np.ascontiguousarray(w3p[hsl])
        hc = c % g.NKV
        th = c // g.NKV
        S_KV = g.S // (NCn // g.NKV)
        ksl = slice(th * S_KV, (th + 1) * S_KV)
        m["x_kv"] = np.ascontiguousarray(x_t[:, ksl])
        m["cos_kv"] = np.ascontiguousarray(cosf[ksl])
        m["sin_kv"] = np.ascontiguousarray(sin_signed[ksl])
        m["wkv_s"] = np.ascontiguousarray(np.concatenate(
            [wk_T[:, hc * 128:(hc + 1) * 128], wv_T[:, hc * 128:(hc + 1) * 128]], axis=1))
        in_maps.append(m)

    res = run_bass_kernel_spmd(nc, in_maps, core_ids=list(range(g.NC)))
    y = np.concatenate([np.asarray(res.results[c]["y"]) for c in range(g.NC)], axis=0)
    return y.reshape(1, g.S, g.D).astype(np.float32, copy=False)
